# version 01
# baseline (speedup 1.0000x reference)
import os
import numpy as np
import ml_dtypes
from contextlib import ExitStack
import concourse.bass as bass
import concourse.mybir as mybir
from concourse.bass_utils import run_bass_kernel_spmd

F32 = mybir.dt.float32
BF16 = mybir.dt.bfloat16
AF = mybir.ActivationFunctionType
ALU = mybir.AluOpType
AX = mybir.AxisListType
ENGS = ("sync", "tensor", "vector", "scalar", "gpsimd")

D = 2048
T = 2048
HALO = 128
TC = T + HALO
NK = D // 128
CH = 128
NCH = T // CH
LN_EPS = 1e-5

A_INTER, A_HEADS, A_HD, A_STATE = 4096, 32, 128, 128
A_CONVD = A_INTER + 2 * A_STATE
A_PROJ = A_INTER + A_CONVD + A_HEADS
A_RMS_EPS = 1e-5

B_INNER, B_HEADS, B_HD, B_STATE = 4096, 64, 64, 64
B_PROJ = 2 * B_INNER + 2 * B_HEADS * B_STATE + B_HEADS
B_SCALE = float(B_STATE) ** -0.5
B_RMS_EPS = 1.1920929e-07

SUBLAYERS = tuple(os.environ.get("P_SUB", "attn_a,ffn1,attn_b,ffn2").split(","))
CUT = "full"


class Sem:
    def __init__(self, h):
        self.h = h
        self.n = 0

    def inc(self, k=1):
        self.n += k
        return self.n


def build_nc(sublayers=SUBLAYERS):
    nc = bass.Bass("TRN2", target_bir_lowering=False, debug=False)
    Q = {e: [] for e in ENGS}

    def on(e, fn):
        Q[e].append(fn)

    def din(name, shape, dt=F32):
        return nc.dram_tensor(name, list(shape), dt, kind="ExternalInput").ap()

    def dint(name, shape, dt=F32):
        return nc.dram_tensor(name, list(shape), dt, kind="Internal").ap()

    x_in = din("x_in", (D, TC))
    out_t = nc.dram_tensor("out", [D, T], F32, kind="ExternalOutput").ap()
    x_state = dint("x_state", (D, T))

    wa_inT = din("wa_inT", (D, A_PROJ))
    wa_outT = din("wa_outT", (A_INTER, D))
    conv_a_w = din("conv_a_w", (A_CONVD, 4))
    conv_a_b = din("conv_a_b", (A_CONVD, 1))
    a_Alog = din("a_Alog", (A_HEADS, 1))
    a_dtb = din("a_dtb", (A_HEADS, 1))
    a_Drep = din("a_Drep", (1, A_INTER))
    a_nw = din("a_nw", (1, A_INTER))

    wb_inT = din("wb_inT", (D, B_PROJ))
    wb_outT = din("wb_outT", (B_INNER, D))
    conv_b_w = din("conv_b_w", (B_INNER, 4))
    b_Alog = din("b_Alog", (B_HEADS, 1))
    b_dtb = din("b_dtb", (B_HEADS, 1))
    hsel_in = din("hsel_in", (8, 1))

    wf_T = din("wf_T", (D, D))
    f_biasT = din("f_biasT", (16, 128))
    f_rw1T = din("f_rw1T", (D, 64))
    f_rw2T = din("f_rw2T", (64, 32))
    f_rb1 = din("f_rb1", (64, 1))
    f_rb2 = din("f_rb2", (32, 1))
    f_gates = din("f_gates", (16, 1))
    f_AT = din("f_AT", (D, 256))
    f_BT2 = din("f_BT2", (256, D))
    f_sel = din("f_sel", (16, 256), BF16)

    eye_f = din("eye_f", (128, 128))
    ones_f_row = din("ones_f_row", (1, 128))
    eye_bf = din("eye_bf", (128, 128), BF16)
    su_bf = din("su_bf", (128, 128), BF16)
    negbig_bf = din("negbig_bf", (128, 128), BF16)
    ones_bf_col = din("ones_bf_col", (128, 1), BF16)
    cmask = din("cmask", (1, 24))

    scr_conv_a = dint("scr_conv_a", (A_CONVD, T), BF16)
    scr_gate_a = dint("scr_gate_a", (A_INTER, T), BF16)
    scr_dt_a = dint("scr_dt_a", (A_HEADS, T))
    scr_y = dint("scr_y", (NCH * 128, A_INTER), BF16)
    scr_cum = dint("scr_cum", (4, 8192))
    scr_conv_b = dint("scr_conv_b", (B_INNER, T), BF16)
    scr_z_b = dint("scr_z_b", (B_INNER, T), BF16)
    scr_B_b = dint("scr_B_b", (B_HEADS * B_STATE, T), BF16)
    scr_C_b = dint("scr_C_b", (B_HEADS * B_STATE, T), BF16)
    scr_dt_b = dint("scr_dt_b", (B_HEADS, T))
    CCW_B = B_HEADS * B_HD + 64
    cc_b_in = dint("cc_b_in", (64, CCW_B), BF16)
    cc_b_out = dint("cc_b_out", (8 * 64, CCW_B), BF16)
    cc_h_in = dint("cc_h_in", (128, 64))
    cc_h_out = dint("cc_h_out", (8 * 128, 64))
    x_halo_b = dint("x_halo_b", (D, HALO))
    CCW_A = A_HEADS * A_HD + 64
    cc_a_in = dint("cc_a_in", (128, CCW_A), BF16)
    cc_a_out = dint("cc_a_out", (8 * 128, CCW_A), BF16)
    scr_dec = dint("scr_dec", (1, 128), BF16)

    with ExitStack() as ctx:
        AB_t = ctx.enter_context(nc.sbuf_tensor("AB", [128, 68864], BF16))
        AF_t = ctx.enter_context(nc.sbuf_tensor("AFL", [128, 9216], F32))
        XF_t = ctx.enter_context(nc.sbuf_tensor("XF2", [128, 2048], F32))

        ob = [0]
        def ab(n):
            r = AB_t[:, ob[0]:ob[0] + n]; ob[0] += n
            assert ob[0] <= 68864, ob[0]
            return r
        eye_b = ab(128); su_b = ab(128); negbig_b = ab(128); onescol_b = ab(2)
        dexp_b = ab(A_INTER); nw_b = ab(A_INTER)
        S_sb = ab(4096)
        ECCSb = ab(1024)
        DECb = ab(64); DROWb = ab(64)
        TAP2 = ab(512); TAP0 = ab(512); SS2 = ab(512)
        _pad = ab((16 - ob[0] % 16) % 16)
        R1 = ab(36864)
        R2 = ab(16384)

        ofl = [0]
        def af(n):
            r = AF_t[:, ofl[0]:ofl[0] + n]; ofl[0] += n
            assert ofl[0] <= 9216, ofl[0]
            return r
        eye_s = af(128); ones_row = af(128); cmask_s = af(24)
        RCC = af(4); ACOLf = af(2); DTBCf = af(2)
        EPSc = af(2); EPSBc = af(2)
        SCLc = af(2); RCCSf = af(2)
        SSQb = af(2); MSb = af(2); RSTDb = af(2)
        HSLf = af(2); ZEROc = af(128)
        FR = af(8704)

        lnb = R1[:, 0:34816]
        TAP3 = R1[:, 34816:35328]; TAP1 = R1[:, 35328:35840]
        COt = R1[:, 35840:36352]; GTT = R1[:, 36352:36864]
        XSTv = [R1[:, 0:4096], R1[:, 4096:8192]]
        YTMv = [R1[:, 8192:12288], R1[:, 12288:16384]]
        XDTv = R1[:, 16384:20480]
        BDv = R1[:, 20480:24576]
        DIv = R1[:, 24576:28672]
        BTv = [R1[:, 28672:32768], R1[:, 32768:36864]]
        SJv = [R1[:, 0:4096], R1[:, 4096:8192]]
        T2c = R1[:, 8192:12288]
        YFMh = R1[:, 0:32768]
        YTg = R1[:, 32768:36864]
        XB = R2[:, 0:4096]; SQ = R2[:, 4096:8192]
        WTb = [R2[:, 8192:10240], R2[:, 10240:12288]]
        LTv = R2[:, 0:512]; MTv = R2[:, 512:1024]
        Gv = R2[:, 1024:1152]
        BFv = [R2[:, 1152:1280], R2[:, 1280:1408]]
        CFv = [R2[:, 1408:1536], R2[:, 1536:1664]]
        DTTMb = R2[:, 1664:1728]; DDTTMb = R2[:, 1728:1792]
        ECTMb = R2[:, 1792:1856]; DTOTb = R2[:, 1856:1920]
        DBCb = R2[:, 1920:1984]; MBCb = R2[:, 1984:2048]
        GXS = R2[:, 0:4096]; GGT = R2[:, 4096:8192]
        GB1 = R2[:, 8192:12288]; GB2 = R2[:, 12288:16384]
        WTOv = [R2[:, 8192:12288], R2[:, 12288:16384]]
        XSTb = R1[:, 0:4096]
        BTb = R1[:, 4096:8192]
        XDTb = R1[:, 8192:12288]
        BDb = R1[:, 12288:16384]
        YTMb = R1[:, 16384:20480]
        BFb = R1[:, 20480:28672]
        CFb = R1[:, 28672:36864]
        LTb = R2[:, 0:512]; MTb = R2[:, 512:1024]
        DIb = R2[:, 2048:6144]
        CFg = R2[:, 0:8192]
        GZTb = R2[:, 8192:12288]
        XOLDb = [XF_t[:, 0:512], XF_t[:, 512:1024]]
        XNEWb = [XF_t[:, 1024:1536], XF_t[:, 1536:2048]]
        XT = FR[:, 0:8192]
        SRr = FR[:, 8192:8704]
        DTRf = FR[:, 0:128]; Uf = FR[:, 128:256]; NAf = FR[:, 256:384]
        Ef = FR[:, 384:512]; MX2f = FR[:, 512:640]; DTf = FR[:, 640:768]
        ADf = FR[:, 768:896]; CUMf = FR[:, 896:1024]; NCf = FR[:, 1024:1152]
        REVIf = FR[:, 1152:1280]; DTEf = FR[:, 1280:1408]; DDTf = FR[:, 1408:1536]
        ECf = FR[:, 1536:1664]; ECCf = FR[:, 1664:1792]; ONESf = FR[:, 1792:1920]
        TOTf = FR[:, 1920:1922]; TROWf = FR[:, 1922:1986]; DTILf = FR[:, 1986:2050]
        LTMPf = FR[:, 2056:2568]
        LdL = [FR[:, 2568:3080], FR[:, 3080:3592]]
        LdR = [FR[:, 3592:4104], FR[:, 4104:4616]]
        SSQf = FR[:, 4616:4618]; MSf = FR[:, 4618:4620]; RSTDf = FR[:, 4620:4622]
        XOLDf = [FR[:, 4624:5136], FR[:, 5136:5648]]
        XNEWf = [FR[:, 5648:6160], FR[:, 6160:6672]]
        SILZb = FR[:, 0:4096]
        SQDb = FR[:, 4096:8192]
        HSLT = FR[:, 0:8192]

        sem_names = ("lx0", "lx1", "ldw0", "ldw1", "mc0", "mc1", "lds0", "lds1",
                     "sj0", "sj1", "glx", "xl0", "xl1", "sy0", "sy1", "str_", "stw",
                     "stg", "scs", "mm", "ev", "vv", "pp", "cc", "bar", "sed", "sea")
        SM = {n: Sem(ctx.enter_context(nc.semaphore(n))) for n in sem_names}
        lx = [SM["lx0"], SM["lx1"]]
        ldwp = [SM["ldw0"], SM["ldw1"]]
        mc = [SM["mc0"], SM["mc1"]]
        ldsp = [SM["lds0"], SM["lds1"]]
        sj = [SM["sj0"], SM["sj1"]]
        glx = SM["glx"]
        xlp = [SM["xl0"], SM["xl1"]]
        syp = [SM["sy0"], SM["sy1"]]
        strs = SM["str_"]
        stw = SM["stw"]
        stg = SM["stg"]
        scs = SM["scs"]
        smm, sev, svv, spp, scc, bar = (SM[n] for n in ("mm", "ev", "vv", "pp", "cc", "bar"))
        sed = SM["sed"]; sea = SM["sea"]
        DRAIN = {
            "sync": [SM["lx0"], SM["lx1"], SM["mc0"], SM["mc1"], SM["sj0"], SM["sj1"],
                     glx, SM["xl0"], SM["xl1"], strs, scs],
            "scalar": [stw, SM["sy0"], SM["sy1"]],
            "gpsimd": [SM["ldw0"], SM["ldw1"], SM["lds0"], SM["lds1"], stg],
        }

        PS = [ctx.enter_context(nc.psum_tensor(f"ps{i}", [128, 512], F32)) for i in range(8)]
        psguard = {i: [] for i in range(8)}

        def gpop(*banks):
            out = []
            for b in banks:
                out.extend(psguard[b])
                psguard[b] = []
            return out

        def gwait(eng, gl):
            for s_, v in gl:
                eng.wait_ge(s_.h, v)

        def setguard(b, s_, v):
            psguard[b].append((s_, v))

        def Bar():
            tgt = bar.n + len(ENGS)
            bar.n = tgt
            for e in ENGS:
                drains = [(s_, s_.n) for s_ in DRAIN.get(e, [])]
                def f(eng, t=tgt, drains=drains):
                    for s_, v in drains:
                        eng.wait_ge(s_.h, v)
                    eng.drain().then_inc(bar.h, 1)
                    eng.wait_ge(bar.h, t)
                on(e, f)
            for b in range(8):
                psguard[b] = []

        init_vc = [0]

        def emit_init():
            def f_sync(eng, tgt=lx[0].n + 16 * 7):
                for dst, src in [(eye_s, eye_f), (ones_row[0:1, :], ones_f_row),
                                 (cmask_s[0:1, :], cmask),
                                 (eye_b, eye_bf), (su_b, su_bf), (negbig_b, negbig_bf),
                                 (onescol_b[:, 0:1], ones_bf_col)]:
                    eng.dma_start(out=dst, in_=src).then_inc(lx[0].h, 16)
                eng.wait_ge(lx[0].h, tgt)
            on("sync", f_sync)
            lx[0].inc(16 * 7)

            def f_g(eng, tgt=ldwp[0].n + 32):
                eng.dma_start(out=dexp_b, in_=a_Drep.partition_broadcast(128)).then_inc(ldwp[0].h, 16)
                eng.dma_start(out=nw_b, in_=a_nw.partition_broadcast(128)).then_inc(ldwp[0].h, 16)
                eng.wait_ge(ldwp[0].h, tgt)
            on("gpsimd", f_g)
            ldwp[0].inc(32)

            def f_vc(eng):
                eng.memset(EPSc[:, 0:1], 1e-5)
                eng.memset(EPSBc[:, 0:1], B_RMS_EPS)
                eng.memset(SCLc[:, 0:1], float(np.log(B_SCALE)))
                eng.memset(ZEROc[:, :], 0.0)
                eng.drain().then_inc(svv.h, 1)
            on("vector", f_vc)
            init_vc[0] = svv.inc()

            def f_hz0(eng, v0=init_vc[0], tgt=scs.n + 16 * NK):
                eng.wait_ge(svv.h, v0)
                for k in range(NK):
                    eng.dma_start(out=x_halo_b[k * 128:(k + 1) * 128, :],
                                  in_=ZEROc[:, :]).then_inc(scs.h, 16)
                eng.wait_ge(scs.h, tgt)
            on("sync", f_hz0)
            scs.inc(16 * NK)
            Bar()

        def emit_ln(lnw, tt_list, slotw=256, lnb_t=None, post_tt=None):
            if lnb_t is None:
                lnb_t = lnb
            applies = {}
            xtg = {}
            for it, (doff, soff, w, src) in enumerate(tt_list):
                xs = (it % 2) * 16 * slotw
                ap_prev = applies.get(it - 2, 0)
                xg_prev = xtg.get(it - 2, None)

                ls = lx[it % 2]
                def f_ld(eng, xs=xs, soff=soff, w=w, src=src, ap=ap_prev, ls=ls, slotw=slotw,
                         xg=xg_prev):
                    eng.wait_ge(svv.h, ap)
                    if xg is not None:
                        eng.wait_ge(xg[0].h, xg[1])
                    for k in range(NK):
                        eng.dma_start(out=XT[:, xs + k * slotw:xs + k * slotw + w],
                                      in_=src[k * 128:(k + 1) * 128, soff:soff + w]).then_inc(ls.h, 16)
                on("sync", f_ld)
                ld_t = ls.inc(16 * NK)

                mm_prev = smm.n
                def f_sc(eng, xs=xs, w=w, ld_t=ld_t, mm_prev=mm_prev, it=it, ls=ls, slotw=slotw):
                    eng.wait_ge(ls.h, ld_t)
                    if it > 0:
                        eng.wait_ge(smm.h, mm_prev)
                    for k in range(NK):
                        eng.activation(XB[:, k * slotw:k * slotw + w],
                                       XT[:, xs + k * slotw:xs + k * slotw + w], AF.Copy)
                    for k in range(NK):
                        eng.activation(SQ[:, k * slotw:k * slotw + w],
                                       XT[:, xs + k * slotw:xs + k * slotw + w], AF.Square)
                    eng.drain().then_inc(sev.h, 1)
                on("scalar", f_sc)
                ev_t = sev.inc()

                g45 = gpop(4, 5)
                def f_t1(eng, w=w, ev_t=ev_t, g45=g45, slotw=slotw):
                    eng.wait_ge(sev.h, ev_t)
                    gwait(eng, g45)
                    for k in range(NK):
                        eng.matmul(PS[4][0:1, 0:w], onescol_b[:, 0:1], XB[:, k * slotw:k * slotw + w],
                                   start=(k == 0), stop=(k == NK - 1))
                    for k in range(NK):
                        eng.matmul(PS[5][0:1, 0:w], onescol_b[:, 0:1], SQ[:, k * slotw:k * slotw + w],
                                   start=(k == 0), stop=(k == NK - 1))
                    eng.drain().then_inc(smm.h, 1)
                on("tensor", f_t1)
                mm_t = smm.inc()

                def f_v1(eng, w=w, mm_t=mm_t):
                    eng.wait_ge(smm.h, mm_t)
                    eng.tensor_scalar(SRr[0:1, 0:w], PS[4][0:1, 0:w], 1.0 / D, None, ALU.mult)
                    eng.tensor_scalar(SRr[0:1, 256:256 + w], PS[5][0:1, 0:w], 1.0 / D, None, ALU.mult)
                    eng.drain().then_inc(svv.h, 1)
                on("vector", f_v1)
                v1_t = svv.inc()

                g6 = gpop(6)
                def f_t2(eng, w=w, v1_t=v1_t, g6=g6):
                    eng.wait_ge(svv.h, v1_t)
                    gwait(eng, g6)
                    eng.matmul(PS[6][:, 0:w], ones_row[0:1, :], SRr[0:1, 0:w], start=True, stop=True)
                    eng.drain().then_inc(smm.h, 1)
                on("tensor", f_t2)
                t2_t = smm.inc()

                def f_v2(eng, w=w, v1_t=v1_t, vmid=svv.n + 1):
                    eng.wait_ge(svv.h, v1_t)
                    eng.tensor_tensor(PS[4][0:1, 0:w], SRr[0:1, 0:w], SRr[0:1, 0:w],
                                      ALU.mult).then_inc(svv.h, 1)
                    eng.wait_ge(svv.h, vmid)
                    eng.tensor_tensor(SRr[0:1, 256:256 + w], SRr[0:1, 256:256 + w],
                                      PS[4][0:1, 0:w], ALU.subtract)
                    eng.drain().then_inc(svv.h, 1)
                on("vector", f_v2)
                svv.inc()
                v2_t = svv.inc()
                setguard(4, svv, v2_t)

                def f_s2(eng, w=w, v2_t=v2_t, ea=sea.n + 1):
                    eng.wait_ge(svv.h, v2_t)
                    eng.activation(SRr[0:1, 256:256 + w], SRr[0:1, 256:256 + w], AF.Ln,
                                   bias=EPSc[0:1, 0:1]).then_inc(sea.h, 1)
                    eng.wait_ge(sea.h, ea)
                    eng.activation(SRr[0:1, 256:256 + w], SRr[0:1, 256:256 + w], AF.Exp, scale=-0.5)
                    eng.drain().then_inc(sev.h, 1)
                on("scalar", f_s2)
                sea.inc()
                s2_t = sev.inc()

                g7 = gpop(7)
                def f_t3(eng, w=w, s2_t=s2_t, g7=g7):
                    eng.wait_ge(sev.h, s2_t)
                    gwait(eng, g7)
                    eng.matmul(PS[7][:, 0:w], ones_row[0:1, :], SRr[0:1, 256:256 + w],
                               start=True, stop=True)
                    eng.drain().then_inc(smm.h, 1)
                on("tensor", f_t3)
                t3_t = smm.inc()

                def f_v3(eng, xs=xs, doff=doff, w=w, t2_t=t2_t, t3_t=t3_t, lnw=lnw,
                         em=sed.n + 1, slotw=slotw, lnb_t=lnb_t):
                    eng.wait_ge(smm.h, t3_t)
                    for k in range(NK):
                        xt = XT[:, xs + k * slotw:xs + k * slotw + w]
                        ins = eng.tensor_tensor(xt, xt, PS[6][:, 0:w], ALU.subtract)
                    ins.then_inc(sed.h, 1)
                    eng.wait_ge(sed.h, em)
                    for k in range(NK):
                        xt = XT[:, xs + k * slotw:xs + k * slotw + w]
                        eng.tensor_tensor(lnb_t[:, k * lnw + doff:k * lnw + doff + w],
                                          xt, PS[7][:, 0:w], ALU.mult)
                    eng.drain().then_inc(svv.h, 1)
                on("vector", f_v3)
                sed.inc()
                av = svv.inc()
                applies[it] = av
                setguard(6, svv, av)
                setguard(7, svv, av)
                if post_tt is not None:
                    xtg[it] = post_tt(it, xs, doff, w, av)
            return applies

        CWp = af(20)

        def emit_inproj2(wT, Mlist, lnw, ln_tts, applies, convw_dram, convb_dram):
            def ln_wait(o, w):
                tgt = 0
                for idx, (doff, _s, ww, _src) in enumerate(ln_tts):
                    if doff < o + w and o < doff + ww:
                        tgt = max(tgt, applies[idx])
                return tgt

            mmdone = {}
            st_guard = [0]
            emit_inproj2.ptv = [0]
            halo_ln = lnw == TC
            for mi, M in enumerate(Mlist):
                p = mi % 2
                mcol0, mw = M["mcol0"], M["mw"]
                wfree = smm.n if mi < 2 else mmdone[mi - 2]
                ws = ldwp[p]
                def f_gw(eng, p=p, mcol0=mcol0, mw=mw, wfree=wfree, M=M, ws=ws):
                    eng.wait_ge(smm.h, wfree)
                    src = wT.rearrange("(k p) m -> p k m", p=128)[:, :, mcol0:mcol0 + mw]
                    eng.dma_start(out=WTb[p].rearrange("p (k j) -> p k j", k=16)[:, :, 0:mw],
                                  in_=src).then_inc(ws.h, 16)
                    if M["conv"]:
                        eng.dma_start(out=CWp[:, p * 10:p * 10 + 4],
                                      in_=convw_dram[M["crow0"]:M["crow0"] + 128, :]).then_inc(ws.h, 16)
                        if convb_dram is not None:
                            eng.dma_start(out=CWp[:, p * 10 + 4:p * 10 + 5],
                                          in_=convb_dram[M["crow0"]:M["crow0"] + 128, :]).then_inc(ws.h, 16)
                on("gpsimd", f_gw)
                nw_dma = 1 + ((2 if convb_dram is not None else 1) if M["conv"] else 0)
                w_t = ws.inc(16 * nw_dma)

                tts = ([("halo", 0, -1, 128)] if (M["conv"] and halo_ln) else [])
                off0 = HALO if halo_ln else 0
                tts += [("real", off0 + i * 512, i * 512, 512) for i in range(4)]
                prev_bank = None
                prev_w = None
                for ti, (knd, lno, so, w) in enumerate(tts):
                    bk = ti % 3
                    lnv = ln_wait(lno, w)
                    gb = gpop(bk)
                    def f_mm(eng, p=p, mw=mw, lno=lno, w=w, bk=bk, w_t=w_t, lnv=lnv, lnw=lnw, ws=ws, gb=gb):
                        eng.wait_ge(ws.h, w_t)
                        eng.wait_ge(svv.h, lnv)
                        gwait(eng, gb)
                        for k in range(NK):
                            eng.matmul(PS[bk][0:mw, 0:w],
                                       WTb[p][:, k * 128:k * 128 + mw],
                                       lnb[:, k * lnw + lno:k * lnw + lno + w],
                                       start=(k == 0), stop=(k == NK - 1))
                        eng.drain().then_inc(smm.h, 1)
                    on("tensor", f_mm)
                    g_t = smm.inc()
                    if ti == len(tts) - 1:
                        mmdone[mi] = g_t
                    if knd == "halo":
                        prev_bank = bk
                        prev_w = w
                        setguard(bk, smm, g_t)
                        continue

                    if M["conv"]:
                        pb = prev_bank
                        cw = CWp[:, p * 10:p * 10 + 4]
                        cb = CWp[:, p * 10 + 4:p * 10 + 5] if convb_dram is not None else None
                        fr = pb is None
                        pw = prev_w
                        def f_ta(eng, b=bk, pb=pb, g_t=g_t, cw=cw, fr=fr, gg=st_guard[0], pw=pw,
                                 ptv=emit_inproj2.ptv[0]):
                            eng.wait_ge(smm.h, g_t)
                            eng.wait_ge(stw.h, gg)
                            eng.wait_ge(svv.h, ptv)
                            eng.activation(TAP3[:, 0:512], PS[b][:, 0:512], AF.Copy, scale=cw[:, 3:4])
                            eng.activation(TAP1[:, 2:512], PS[b][:, 0:510], AF.Copy, scale=cw[:, 1:2])
                            if fr:
                                eng.memset(TAP1[:, 0:2], 0.0)
                            else:
                                eng.activation(TAP1[:, 0:2], PS[pb][:, pw - 2:pw], AF.Copy, scale=cw[:, 1:2])
                            eng.drain().then_inc(sev.h, 1)
                        on("scalar", f_ta)
                        ta = sev.inc()

                        def f_tv(eng, b=bk, pb=pb, g_t=g_t, cw=cw, fr=fr, ppg=spp.n, pw=pw, ta=ta):
                            eng.wait_ge(smm.h, g_t)
                            eng.wait_ge(spp.h, ppg)
                            eng.wait_ge(sev.h, ta)
                            eng.tensor_scalar(TAP2[:, 1:512], PS[b][:, 0:511], cw[:, 2:3], None, ALU.mult)
                            eng.tensor_scalar(TAP0[:, 3:512], PS[b][:, 0:509], cw[:, 0:1], None, ALU.mult)
                            if fr:
                                eng.memset(TAP2[:, 0:1], 0.0)
                                eng.memset(TAP0[:, 0:3], 0.0)
                            else:
                                eng.tensor_scalar(TAP2[:, 0:1], PS[pb][:, pw - 1:pw], cw[:, 2:3], None, ALU.mult)
                                eng.tensor_scalar(TAP0[:, 0:3], PS[pb][:, pw - 3:pw], cw[:, 0:1], None, ALU.mult)
                            eng.drain().then_inc(svv.h, 1)
                        on("vector", f_tv)
                        tv = svv.inc()
                        emit_inproj2.ptv[0] = tv
                        setguard(bk, sev, ta); setguard(bk, svv, tv)
                        if pb is not None:
                            setguard(pb, sev, ta); setguard(pb, svv, tv)

                        def f_pp(eng, ta=ta, tv=tv):
                            eng.wait_ge(sev.h, ta)
                            eng.wait_ge(svv.h, tv)
                            eng.tensor_tensor(SS2[:, 0:512], TAP1[:, 0:512], TAP0[:, 0:512], ALU.add)
                            eng.drain().then_inc(spp.h, 1)
                        on("gpsimd", f_pp)
                        pp_t = spp.inc()

                        def f_av(eng, ta=ta, tv=tv, pp_t=pp_t, em=sed.n + 1):
                            eng.wait_ge(sev.h, ta)
                            eng.tensor_tensor(COt[:, 0:512], TAP3[:, 0:512], TAP2[:, 0:512],
                                              ALU.add).then_inc(sed.h, 1)
                            eng.wait_ge(spp.h, pp_t)
                            eng.wait_ge(sed.h, em)
                            eng.tensor_tensor(COt[:, 0:512], COt[:, 0:512], SS2[:, 0:512], ALU.add)
                            eng.drain().then_inc(svv.h, 1)
                        on("vector", f_av)
                        sed.inc()
                        cv = svv.inc()

                        def f_si(eng, cv=cv, cb=cb, gg=st_guard[0]):
                            eng.wait_ge(svv.h, cv)
                            eng.wait_ge(stw.h, gg)
                            if cb is not None:
                                eng.activation(GTT[:, 0:512], COt[:, 0:512], AF.Silu, bias=cb)
                            else:
                                eng.activation(GTT[:, 0:512], COt[:, 0:512], AF.Silu)
                            eng.drain().then_inc(sev.h, 1)
                        on("scalar", f_si)
                        si = sev.inc()

                        def f_st(eng, M=M, so=so, si=si):
                            eng.wait_ge(sev.h, si)
                            eng.dma_start(out=M["dst"][M["drow0"]:M["drow0"] + 128, so:so + 512],
                                          in_=GTT[:, 0:512]).then_inc(stw.h, 16)
                        on("scalar", f_st)
                        st_guard[0] = stw.inc(16)
                    else:
                        isdt = M["kind"] == "dt"
                        def f_ev(eng, b=bk, g_t=g_t, mw=mw, isdt=isdt, gg=st_guard[0]):
                            eng.wait_ge(smm.h, g_t)
                            eng.wait_ge(stw.h, gg)
                            if isdt:
                                eng.activation(XT[0:mw, 0:512], PS[b][0:mw, 0:512], AF.Copy)
                            else:
                                eng.activation(GTT[:, 0:512], PS[b][:, 0:512], AF.Copy)
                            eng.drain().then_inc(sev.h, 1)
                        on("scalar", f_ev)
                        e_t = sev.inc()
                        setguard(bk, sev, e_t)

                        def f_st2(eng, M=M, so=so, e_t=e_t, mw=mw, isdt=isdt):
                            eng.wait_ge(sev.h, e_t)
                            src = XT[0:mw, 0:512] if isdt else GTT[:, 0:512]
                            eng.dma_start(out=M["dst"][M["drow0"]:M["drow0"] + mw, so:so + 512],
                                          in_=src).then_inc(stw.h, 16)
                        on("scalar", f_st2)
                        st_guard[0] = stw.inc(16)
                    prev_bank = bk
                    prev_w = w

        sevac = {}
        chunk_mm_done = {}
        qmm_ld = {}
        chunk_ldq = {}
        ytm_store = {}
        mixer_pq = [0, 0]

        def emit_mixer_a():
            H, HD = A_HEADS, A_HD
            HP = H * HD
            nq = H // 4

            def f_ld0(eng):
                eng.dma_start(out=DTRf[0:H, 0:1], in_=a_Alog[:, :]).then_inc(lx[0].h, 16)
                eng.dma_start(out=DTBCf[0:H, 0:1], in_=a_dtb[:, :]).then_inc(lx[0].h, 16)
            on("sync", f_ld0)
            l0 = lx[0].inc(32)

            def f_i_s(eng, l0=l0):
                eng.wait_ge(lx[0].h, l0)
                eng.activation(ACOLf[0:H, 0:1], DTRf[0:H, 0:1], AF.Exp)
                eng.drain().then_inc(sev.h, 1)
            on("scalar", f_i_s)
            e0 = sev.inc()

            def f_i_v(eng, e0=e0):
                eng.wait_ge(sev.h, e0)
                eng.tensor_scalar(ACOLf[0:H, 0:1], ACOLf[0:H, 0:1], -1.0, None, ALU.mult)
                eng.memset(RCC[:, 0:4], 0.0)
                eng.memset(S_sb[:, :], 0.0)
                eng.memset(ONESf[0:H, :], 1.0)
                for pr in range(2):
                    eng.memset(LdL[pr][0:2, 0:512], 1.0)
                    eng.memset(LdR[pr][0:2, 0:512], 1.0)
                eng.drain().then_inc(svv.h, 1)
            on("vector", f_i_v)
            v0 = svv.inc()

            dtv2_of = {}
            g1_of = {}
            for c in range(NCH):
                cp = c % 2
                c0 = c * CH
                g_prev = smm.n if c < 2 else chunk_mm_done[c - 2]
                ms_ = mc[cp]
                def f_cin(eng, cp=cp, c0=c0, g_prev=g_prev, v0=v0, ms_=ms_,
                          dtg=dtv2_of.get(c - 1, 0)):
                    eng.wait_ge(svv.h, v0)
                    eng.wait_ge(smm.h, g_prev)
                    eng.wait_ge(svv.h, dtg)
                    eng.dma_start_transpose(out=XSTv[cp][:, :],
                                            in_=scr_conv_a[0:HP, c0:c0 + CH]).then_inc(ms_.h, 16)
                    eng.dma_start(out=BFv[cp][:, :],
                                  in_=scr_conv_a[A_INTER:A_INTER + 128, c0:c0 + CH]).then_inc(ms_.h, 16)
                    eng.dma_start(out=CFv[cp][:, :],
                                  in_=scr_conv_a[A_INTER + 128:A_INTER + 256, c0:c0 + CH]).then_inc(ms_.h, 16)
                    eng.dma_start_transpose(out=BTv[cp][:, 0:128],
                                            in_=scr_conv_a[A_INTER:A_INTER + 128, c0:c0 + CH]).then_inc(ms_.h, 16)
                    eng.dma_start(out=DTRf[0:H, 0:128],
                                  in_=scr_dt_a[0:H, c0:c0 + CH]).then_inc(ms_.h, 16)
                on("sync", f_cin)
                cin_t = ms_.inc(16 * 5)

                def f_dtv1(eng, cin_t=cin_t, ms_=ms_, e0=sed.n + 1, e1=sed.n + 2):
                    eng.wait_ge(ms_.h, cin_t)
                    eng.tensor_scalar(Uf[0:H, :], DTRf[0:H, 0:128], DTBCf[0:H, 0:1], None,
                                      ALU.add).then_inc(sed.h, 1)
                    eng.wait_ge(sed.h, e0)
                    eng.tensor_scalar(NAf[0:H, :], Uf[0:H, :], -1.0, None,
                                      ALU.mult).then_inc(sed.h, 1)
                    eng.wait_ge(sed.h, e1)
                    eng.tensor_tensor(NAf[0:H, :], Uf[0:H, :], NAf[0:H, :], ALU.min)
                    eng.drain().then_inc(svv.h, 1)
                on("vector", f_dtv1)
                sed.inc(2)
                t1 = svv.inc()

                def f_dts1(eng, t1=t1, ea=sea.n + 1):
                    eng.wait_ge(svv.h, t1)
                    eng.activation(Ef[0:H, :], NAf[0:H, :], AF.Exp).then_inc(sea.h, 1)
                    eng.wait_ge(sea.h, ea)
                    eng.activation(Ef[0:H, :], Ef[0:H, :], AF.Ln, bias=1.0)
                    eng.drain().then_inc(sev.h, 1)
                on("scalar", f_dts1)
                sea.inc()
                t2 = sev.inc()

                def f_dtv2(eng, t2=t2, cp=cp, eb=sed.n):
                    eng.wait_ge(sev.h, t2)
                    eng.tensor_scalar(MX2f[0:H, :], Uf[0:H, :], 0.0, None,
                                      ALU.max).then_inc(sed.h, 1)
                    eng.wait_ge(sed.h, eb + 1)
                    eng.tensor_tensor(DTf[0:H, :], MX2f[0:H, :], Ef[0:H, :],
                                      ALU.add).then_inc(sed.h, 1)
                    eng.wait_ge(sed.h, eb + 2)
                    eng.tensor_scalar(DTf[0:H, :], DTf[0:H, :], 100.0, 0.0, ALU.min,
                                      ALU.max).then_inc(sed.h, 1)
                    eng.wait_ge(sed.h, eb + 3)
                    eng.tensor_scalar(ADf[0:H, :], DTf[0:H, :], ACOLf[0:H, 0:1], None,
                                      ALU.mult).then_inc(sed.h, 1)
                    eng.wait_ge(sed.h, eb + 4)
                    eng.tensor_tensor_scan(REVIf[0:H, :][:, ::-1], ONESf[0:H, :],
                                           ADf[0:H, :][:, ::-1], 0.0, ALU.mult, ALU.add)
                    eng.tensor_tensor_scan(CUMf[0:H, :], ONESf[0:H, :], ADf[0:H, :],
                                           0.0, ALU.mult, ALU.add).then_inc(sed.h, 1)
                    eng.wait_ge(sed.h, eb + 5)
                    eng.tensor_tensor(DTEf[0:H, :], REVIf[0:H, :], ADf[0:H, :], ALU.subtract)
                    eng.tensor_scalar(NCf[0:H, :], CUMf[0:H, :], -1.0, None, ALU.mult)
                    eng.tensor_reduce(TOTf[0:H, 0:1], ADf[0:H, :], AX.X,
                                      ALU.add).then_inc(sed.h, 1)
                    eng.wait_ge(sed.h, eb + 6)
                    eng.tensor_tensor(RCC[0:H, (1 - cp):(2 - cp)], RCC[0:H, cp:cp + 1],
                                      TOTf[0:H, 0:1], ALU.add)
                    eng.drain().then_inc(svv.h, 1)
                on("vector", f_dtv2)
                sed.inc(6)
                t3 = svv.inc()
                dtv2_of[c] = t3

                def f_dts2(eng, t3=t3, cp=cp):
                    eng.wait_ge(svv.h, t3)
                    eng.activation(DTEf[0:H, :], DTEf[0:H, :], AF.Exp)
                    eng.activation(ECf[0:H, :], CUMf[0:H, :], AF.Exp)
                    eng.activation(ECCf[0:H, :], CUMf[0:H, :], AF.Exp, bias=RCC[0:H, cp:cp + 1])
                    eng.drain().then_inc(sev.h, 1)
                on("scalar", f_dts2)
                t4 = sev.inc()

                def f_g1(eng, cp=cp, t3=t3, ldg=chunk_ldq.get(c - 2, ())):
                    eng.wait_ge(svv.h, t3)
                    for s_, v_ in ldg:
                        eng.wait_ge(s_.h, v_)
                    eng.dma_start(out=scr_cum[2 * cp, 0:H * 128].rearrange("(h l) -> h l", h=H),
                                  in_=CUMf[0:H, :]).then_inc(stg.h, 16)
                    eng.dma_start(out=scr_cum[2 * cp + 1, 0:H * 128].rearrange("(h l) -> h l", h=H),
                                  in_=NCf[0:H, :]).then_inc(stg.h, 16)
                on("gpsimd", f_g1)
                st1 = stg.inc(32)
                g1_of[c] = st1

                g76 = gpop(7, 6)
                def f_tr(eng, t4=t4, g76=g76):
                    eng.wait_ge(sev.h, t4)
                    gwait(eng, g76)
                    eng.transpose(PS[7][:, 0:H], DTf[0:H, :], eye_s[0:H, 0:H])
                    eng.transpose(PS[7][:, 128:128 + H], DTEf[0:H, :], eye_s[0:H, 0:H])
                    eng.transpose(PS[7][:, 256:256 + H], ECf[0:H, :], eye_s[0:H, 0:H])
                    eng.transpose(PS[7][:, 384:384 + H], ECCf[0:H, :], eye_s[0:H, 0:H])
                    eng.transpose(PS[6][0:1, 0:H], TOTf[0:H, 0:1], eye_s[0:H, 0:H])
                    eng.drain().then_inc(smm.h, 1)
                on("tensor", f_tr)
                tr_t = smm.inc()

                def f_tre(eng, tr_t=tr_t, c=c):
                    eng.wait_ge(smm.h, tr_t)
                    eng.activation(DTTMb[:, 0:H], PS[7][:, 0:H], AF.Copy)
                    eng.activation(DDTTMb[:, 0:H], PS[7][:, 128:128 + H], AF.Copy)
                    eng.activation(ECTMb[:, 0:H], PS[7][:, 256:256 + H], AF.Copy)
                    eng.activation(ECCSb[:, c * 64:c * 64 + H], PS[7][:, 384:384 + H], AF.Copy)
                    eng.activation(TROWf[0:1, 0:H], PS[6][0:1, 0:H], AF.Copy)
                    eng.drain().then_inc(sev.h, 1)
                on("scalar", f_tre)
                tre_t = sev.inc()
                setguard(7, sev, tre_t)

                def f_dtm(eng, tre_t=tre_t):
                    eng.wait_ge(sev.h, tre_t)
                    eng.matmul(PS[6][:, 64:64 + H], ones_row[0:1, :], TROWf[0:1, 0:H],
                               start=True, stop=True)
                    eng.drain().then_inc(smm.h, 1)
                on("tensor", f_dtm)
                dtm_t = smm.inc()

                def f_dte(eng, dtm_t=dtm_t):
                    eng.wait_ge(smm.h, dtm_t)
                    eng.activation(DTOTb[:, 0:H], PS[6][:, 64:64 + H], AF.Exp)
                    eng.drain().then_inc(sev.h, 1)
                on("scalar", f_dte)
                dte_t = sev.inc()
                setguard(6, sev, dte_t)

                big_prev = smm.n if c < 1 else chunk_mm_done[c - 1]
                def f_big(eng, cp=cp, cin_t=cin_t, dte_t=dte_t, big_prev=big_prev, ms_=ms_):
                    eng.wait_ge(ms_.h, cin_t)
                    eng.wait_ge(sev.h, dte_t)
                    eng.wait_ge(smm.h, big_prev)
                    eng.tensor_tensor(XDTv.rearrange("p (h d) -> p h d", h=H),
                                      XSTv[cp].rearrange("p (h d) -> p h d", h=H),
                                      DTTMb[:, 0:H].unsqueeze(2).broadcast_to((128, H, HD)),
                                      ALU.mult)
                    eng.tensor_tensor(BDv.rearrange("p (h d) -> p h d", h=H),
                                      BTv[cp][:, 0:128].unsqueeze(1).broadcast_to((128, H, 128)),
                                      DDTTMb[:, 0:H].unsqueeze(2).broadcast_to((128, H, 128)),
                                      ALU.mult)
                    eng.tensor_tensor(DIv.rearrange("p (h d) -> p h d", h=H),
                                      eye_b.unsqueeze(1).broadcast_to((128, H, 128)),
                                      DTOTb[:, 0:H].unsqueeze(2).broadcast_to((128, H, 128)),
                                      ALU.mult)
                    eng.drain().then_inc(svv.h, 1)
                on("vector", f_big)
                big_t = svv.inc()

                g6m = gpop(6)
                def f_gmm(eng, cp=cp, cin_t=cin_t, ms_=ms_, g6m=g6m):
                    eng.wait_ge(ms_.h, cin_t)
                    gwait(eng, g6m)
                    eng.matmul(PS[6][:, 128:256], BFv[cp][:, :], CFv[cp][:, :], start=True, stop=True)
                    eng.drain().then_inc(smm.h, 1)
                on("tensor", f_gmm)
                gm_t = smm.inc()

                def f_gev(eng, gm_t=gm_t):
                    eng.wait_ge(smm.h, gm_t)
                    eng.activation(Gv[:, :], PS[6][:, 128:256], AF.Copy)
                    eng.drain().then_inc(sev.h, 1)
                on("scalar", f_gev)
                ge_t = sev.inc()
                setguard(6, sev, ge_t)

                for q in range(nq):
                    qp = q % 2
                    bL, bY, bO = qp, 2 + qp, 4 + qp

                    ld_guard = qmm_ld.get((c, q - 2), qmm_ld.get((c - 1, nq - 2 + q), 0))
                    qs_ = ldsp[qp]
                    def f_gq(eng, cp=cp, q=q, qp=qp, st1=st1, ld_guard=ld_guard, qs_=qs_):
                        eng.wait_ge(stg.h, st1)
                        eng.wait_ge(smm.h, ld_guard)
                        eng.dma_start(out=LdR[qp][0:1, 0:512],
                                      in_=scr_cum[2 * cp, q * 512:(q + 1) * 512][None, :]).then_inc(qs_.h, 16)
                        eng.dma_start(out=LdL[qp][1:2, 0:512],
                                      in_=scr_cum[2 * cp + 1, q * 512:(q + 1) * 512][None, :]).then_inc(qs_.h, 16)
                    on("gpsimd", f_gq)
                    ldq = qs_.inc(32)

                    gqL = gpop(bL)
                    def f_qL(eng, qp=qp, bL=bL, ldq=ldq, v0=v0, qs_=qs_, gqL=gqL):
                        eng.wait_ge(qs_.h, ldq)
                        eng.wait_ge(svv.h, v0)
                        gwait(eng, gqL)
                        for j in range(4):
                            eng.matmul(PS[bL][:, j * 128:(j + 1) * 128], su_b, negbig_b,
                                       start=True, stop=False)
                            eng.matmul(PS[bL][:, j * 128:(j + 1) * 128],
                                       LdL[qp][0:2, j * 128:(j + 1) * 128],
                                       LdR[qp][0:2, j * 128:(j + 1) * 128],
                                       start=False, stop=True)
                        eng.drain().then_inc(smm.h, 1)
                    on("tensor", f_qL)
                    qL = smm.inc()
                    qmm_ld[(c, q)] = qL

                    def f_qa(eng, bL=bL, qL=qL, ea=sea.n + 1, pqy=mixer_pq[0]):
                        eng.wait_ge(smm.h, qL)
                        eng.wait_ge(svv.h, pqy)
                        eng.activation(LTMPf[:, 0:512], PS[bL][:, 0:512], AF.Relu,
                                       scale=-1.0).then_inc(sea.h, 1)
                        eng.wait_ge(sea.h, ea)
                        eng.activation(LTv[:, 0:512], LTMPf[:, 0:512], AF.Exp, scale=-1.0)
                        eng.drain().then_inc(sev.h, 1)
                    on("scalar", f_qa)
                    sea.inc()
                    qa = sev.inc()
                    setguard(bL, sev, qa)

                    def f_qmt(eng, qa=qa, ge_t=ge_t, pq2=mixer_pq[1]):
                        eng.wait_ge(sev.h, qa)
                        eng.wait_ge(sev.h, ge_t)
                        eng.wait_ge(smm.h, pq2)
                        eng.tensor_tensor(MTv.rearrange("p (j i) -> p j i", j=4),
                                          LTv.rearrange("p (j i) -> p j i", j=4),
                                          Gv.unsqueeze(1).broadcast_to((128, 4, 128)), ALU.mult)
                        eng.drain().then_inc(svv.h, 1)
                    on("vector", f_qmt)
                    qmt = svv.inc()

                    sev_prev = sevac.get((c - 1, q), 0)
                    gq2 = gpop(bY, bO, bL)
                    def f_q2(eng, q=q, cp=cp, bL=bL, bY=bY, bO=bO, qmt=qmt, big_t=big_t,
                             qa=qa, sev_prev=sev_prev, gq=gq2):
                        eng.wait_ge(svv.h, qmt)
                        eng.wait_ge(svv.h, big_t)
                        eng.wait_ge(sev.h, qa)
                        eng.wait_ge(sev.h, sev_prev)
                        gwait(eng, gq)
                        for j in range(4):
                            h = 4 * q + j
                            eng.matmul(PS[bY][:, j * 128:(j + 1) * 128],
                                       MTv[:, j * 128:(j + 1) * 128],
                                       XDTv[:, h * 128:(h + 1) * 128], start=True, stop=True)
                            eng.matmul(PS[bO][:, j * 128:(j + 1) * 128],
                                       CFv[cp][:, :], S_sb[:, h * 128:(h + 1) * 128],
                                       start=True, stop=True)
                            eng.matmul(PS[bL][:, j * 128:(j + 1) * 128],
                                       DIv[:, h * 128:(h + 1) * 128],
                                       S_sb[:, h * 128:(h + 1) * 128], start=True, stop=False)
                            eng.matmul(PS[bL][:, j * 128:(j + 1) * 128],
                                       BDv[:, h * 128:(h + 1) * 128],
                                       XDTv[:, h * 128:(h + 1) * 128], start=False, stop=True)
                        eng.drain().then_inc(smm.h, 1)
                    on("tensor", f_q2)
                    q2 = smm.inc()
                    mixer_pq[1] = q2

                    def f_qy(eng, q=q, cp=cp, bY=bY, bO=bO, q2=q2, ysg=ytm_store.get(c - 2, 0), sy_=syp[cp],
                             em=sed.n + 1):
                        eng.wait_ge(smm.h, q2)
                        eng.wait_ge(sy_.h, ysg)
                        eng.tensor_tensor(LTv.rearrange("p (j i) -> p j i", j=4),
                                          PS[bO].ap().rearrange("p (j i) -> p j i", j=4),
                                          ECTMb[:, 4 * q:4 * q + 4].unsqueeze(2).broadcast_to((128, 4, 128)),
                                          ALU.mult).then_inc(sed.h, 1)
                        eng.wait_ge(sed.h, em)
                        eng.tensor_tensor(YTMv[cp][:, q * 512:(q + 1) * 512],
                                          PS[bY][:, 0:512], LTv[:, 0:512], ALU.add)
                        eng.drain().then_inc(svv.h, 1)
                    on("vector", f_qy)
                    sed.inc()
                    qy = svv.inc()
                    mixer_pq[0] = qy
                    if c == 0 and q == 0 and os.environ.get("P_DBG2", "0") == "1":
                        dbgs = []
                        for nm, tile in [("d_xst", XSTv[0]), ("d_xdt", XDTv), ("d_bd", BDv),
                                          ("d_di", DIv), ("d_lt", LTv), ("d_mt", MTv),
                                          ("d_g", Gv), ("d_bt", BTv[0][:, 0:128]),
                                          ("d_bf", BFv[0]), ("d_cf", CFv[0])]:
                            dt_ = nc.dram_tensor(nm, [128, tile.shape[1]], BF16,
                                                 kind="ExternalOutput").ap()
                            dbgs.append((dt_, tile))
                        for nm, tile in [("d_cum", CUMf), ("d_dtf", DTf), ("d_ltmp", LTMPf)]:
                            dt_ = nc.dram_tensor(nm, [128, tile.shape[1]], F32,
                                                 kind="ExternalOutput").ap()
                            dbgs.append((dt_, tile))
                        def f_dbg2(eng, dbgs=dbgs, w1=qy, w2=sevac.get((0, 0), 0), wq=smm.n,
                                   tgt=scs.n + 16 * len(dbgs)):
                            eng.wait_ge(svv.h, w1)
                            eng.wait_ge(smm.h, wq)
                            for dt_, tile in dbgs:
                                eng.dma_start(out=dt_[0:tile.shape[0], :], in_=tile).then_inc(scs.h, 16)
                            eng.wait_ge(scs.h, tgt)
                        on("sync", f_dbg2)
                        scs.inc(16 * len(dbgs))
                    setguard(bY, svv, qy)
                    setguard(bO, svv, qy)

                    def f_qs(eng, q=q, bL=bL, q2=q2):
                        eng.wait_ge(smm.h, q2)
                        eng.activation(S_sb[:, q * 512:(q + 1) * 512], PS[bL][:, 0:512], AF.Copy)
                        eng.drain().then_inc(sev.h, 1)
                    on("scalar", f_qs)
                    se = sev.inc()
                    sevac[(c, q)] = se
                    setguard(bL, sev, se)
                chunk_mm_done[c] = smm.n
                chunk_ldq[c] = ((ldsp[0], ldsp[0].n), (ldsp[1], ldsp[1].n))

                lastqy = svv.n
                ystg = ytm_store.get(c - 2, 0)
                def f_yst(eng, cp=cp, c=c, lastqy=lastqy, ystg=ystg):
                    eng.wait_ge(svv.h, lastqy)
                    eng.dma_start(out=scr_y[c * 128:(c + 1) * 128, 0:HP],
                                  in_=YTMv[cp][:, :]).then_inc(syp[cp].h, 16)
                on("scalar", f_yst)
                ytm_store[c] = syp[cp].inc(16)

            fin_se = sev.n
            fin_vv = svv.n
            def f_dec(eng, fin_se=fin_se, fin_vv=fin_vv, cpf=NCH % 2):
                eng.wait_ge(sev.h, fin_se)
                eng.wait_ge(svv.h, fin_vv)
                eng.activation(DECb[:, 0:1], RCC[:, cpf:cpf + 1], AF.Exp)
                eng.drain().then_inc(sev.h, 1)
            on("scalar", f_dec)
            dec_t = sev.inc()

            def f_ss(eng, dec_t=dec_t, w1=scs.n + 16 * 3):
                eng.wait_ge(sev.h, dec_t)
                eng.dma_start(out=cc_a_in[:, 0:A_HEADS * A_HD], in_=S_sb[:, :]).then_inc(scs.h, 16)
                eng.dma_start(out=cc_a_in[:, A_HEADS * A_HD:A_HEADS * A_HD + 64],
                              in_=S_sb[:, 0:64]).then_inc(scs.h, 16)
                eng.dma_start(out=scr_dec[0:1, :].rearrange("a p -> p a"),
                              in_=DECb[:, 0:1]).then_inc(scs.h, 16)
                eng.wait_ge(scs.h, w1)
                eng.dma_start(out=cc_a_in[0:1, A_HEADS * A_HD:A_HEADS * A_HD + 64],
                              in_=scr_dec[0:1, 0:64]).then_inc(scs.h, 16)
            on("sync", f_ss)
            scs.inc(64)

        def emit_collective_a():
            H = A_HEADS
            st_t = scs.n
            def f_cc(eng, st_t=st_t):
                eng.wait_ge(scs.h, st_t)
                eng.collective_compute(
                    "AllGather", ALU.bypass,
                    replica_groups=[list(range(8))],
                    ins=[cc_a_in[:, :].opt()],
                    outs=[cc_a_out[:, :].opt()],
                ).then_inc(scc.h, 1)
            on("gpsimd", f_cc)
            cc_t = scc.inc()

            def f_hz(eng, cc_t=cc_t):
                eng.wait_ge(scc.h, cc_t)
                eng.memset(S_sb[:, :], 0.0)
                eng.drain().then_inc(svv.h, 1)
            on("vector", f_hz)
            prev = svv.inc()

            jld = {}
            for j in range(8):
                jp = j % 2
                ld_pv = jld.get(j - 2, 0)
                js_ = sj[jp]
                def f_jl(eng, j=j, jp=jp, cc_t=cc_t, ld_pv=ld_pv, js_=js_):
                    eng.wait_ge(scc.h, cc_t)
                    eng.wait_ge(svv.h, ld_pv)
                    eng.dma_start(out=SJv[jp][:, :],
                                  in_=cc_a_out[j * 128:(j + 1) * 128, 0:H * A_HD]).then_inc(js_.h, 16)
                    eng.dma_start(out=DROWb[0:1, jp * 32:jp * 32 + H],
                                  in_=cc_a_out[j * 128:j * 128 + 1, H * A_HD:H * A_HD + H]).then_inc(js_.h, 16)
                on("sync", f_jl)
                jl = js_.inc(32)

                def f_jr(eng, j=j, jl=jl, prev=prev, js_=js_, em=sed.n + 1):
                    eng.wait_ge(js_.h, jl)
                    eng.wait_ge(svv.h, prev)
                    eng.tensor_scalar(DTILf[0:1, 0:H], DROWb[0:1, (j % 2) * 32:(j % 2) * 32 + H],
                                      cmask_s[0:1, 8 + j:9 + j], None, ALU.mult).then_inc(sed.h, 1)
                    eng.wait_ge(sed.h, em)
                    eng.tensor_scalar(DTILf[0:1, 0:H], DTILf[0:1, 0:H],
                                      cmask_s[0:1, 16 + j:17 + j], None, ALU.add)
                    eng.tensor_scalar(DTILf[0:1, 32:32 + H], ones_row[0:1, 0:H],
                                      cmask_s[0:1, j:j + 1], None, ALU.mult)
                    eng.drain().then_inc(svv.h, 1)
                on("vector", f_jr)
                sed.inc()
                jr = svv.inc()

                gj6 = gpop(6)
                def f_jm(eng, jr=jr, gj6=gj6):
                    eng.wait_ge(svv.h, jr)
                    gwait(eng, gj6)
                    eng.matmul(PS[6][:, 0:H], ones_row[0:1, :], DTILf[0:1, 0:H], start=True, stop=True)
                    eng.matmul(PS[6][:, 64:64 + H], ones_row[0:1, :], DTILf[0:1, 32:32 + H],
                               start=True, stop=True)
                    eng.drain().then_inc(smm.h, 1)
                on("tensor", f_jm)
                jm = smm.inc()

                def f_je(eng, jm=jm):
                    eng.wait_ge(smm.h, jm)
                    eng.activation(DBCb[:, 0:H], PS[6][:, 0:H], AF.Copy)
                    eng.activation(MBCb[:, 0:H], PS[6][:, 64:64 + H], AF.Copy)
                    eng.drain().then_inc(sev.h, 1)
                on("scalar", f_je)
                je = sev.inc()
                setguard(6, sev, je)

                def f_jv(eng, jp=jp, je=je, em=sed.n + 1):
                    eng.wait_ge(sev.h, je)
                    eng.tensor_tensor(S_sb.rearrange("p (h d) -> p h d", h=H),
                                      S_sb.rearrange("p (h d) -> p h d", h=H),
                                      DBCb[:, 0:H].unsqueeze(2).broadcast_to((128, H, 128)), ALU.mult)
                    eng.tensor_tensor(T2c.rearrange("p (h d) -> p h d", h=H),
                                      SJv[jp].rearrange("p (h d) -> p h d", h=H),
                                      MBCb[:, 0:H].unsqueeze(2).broadcast_to((128, H, 128)),
                                      ALU.mult).then_inc(sed.h, 1)
                    eng.wait_ge(sed.h, em)
                    eng.tensor_tensor(S_sb[:, :], S_sb[:, :], T2c[:, :], ALU.add)
                    eng.drain().then_inc(svv.h, 1)
                on("vector", f_jv)
                sed.inc()
                prev = svv.inc()
                jld[j] = prev

        def emit_gate_outproj_a(xprev, xprev_off, xdst):
            H = A_HEADS
            HP = A_INTER
            GCF = SS2[:, 0:128]
            for h2 in range(2):
                gate_done_v = None
                for ci in range(8):
                    c = 8 * h2 + ci
                    c0 = c * CH
                    ysg = ytm_store[c]
                    def f_gl(eng, c=c, c0=c0, ysg=ysg, gp=smm.n, pe1=emit_gate_outproj_a.pe1[0],
                             psg=emit_gate_outproj_a.psg[0], sy_=syp[c % 2]):
                        eng.wait_ge(sy_.h, ysg)
                        eng.wait_ge(smm.h, gp)
                        eng.wait_ge(svv.h, pe1)
                        eng.wait_ge(sev.h, psg)
                        eng.dma_start(out=YTg[:, :], in_=scr_y[c * 128:(c + 1) * 128, 0:HP]).then_inc(glx.h, 16)
                        eng.dma_start_transpose(out=GXS[:, :], in_=scr_conv_a[0:HP, c0:c0 + CH]).then_inc(glx.h, 16)
                        eng.dma_start_transpose(out=GGT[:, :], in_=scr_gate_a[0:HP, c0:c0 + CH]).then_inc(glx.h, 16)
                        eng.dma_start(out=GCF, in_=scr_conv_a[A_INTER + 128:A_INTER + 256, c0:c0 + CH]).then_inc(glx.h, 16)
                    on("sync", f_gl)
                    gl = glx.inc(16 * 4)

                    corr_v = []
                    for q in range(H // 4):
                        bq = q % 2
                        gcm = gpop(bq)
                        def f_cm(eng, q=q, bq=bq, gl=gl, gcm=gcm):
                            eng.wait_ge(glx.h, gl)
                            gwait(eng, gcm)
                            for j in range(4):
                                h = 4 * q + j
                                eng.matmul(PS[bq][:, j * 128:(j + 1) * 128], GCF,
                                           S_sb[:, h * 128:(h + 1) * 128], start=True, stop=True)
                            eng.drain().then_inc(smm.h, 1)
                        on("tensor", f_cm)
                        cm = smm.inc()

                        def f_cv(eng, q=q, bq=bq, cm=cm, c=c, em=sed.n + 1):
                            eng.wait_ge(smm.h, cm)
                            eng.tensor_tensor(TAP2.rearrange("p (j i) -> p j i", j=4),
                                              PS[bq].ap().rearrange("p (j i) -> p j i", j=4),
                                              ECCSb[:, c * 64 + 4 * q:c * 64 + 4 * q + 4]
                                              .unsqueeze(2).broadcast_to((128, 4, 128)),
                                              ALU.mult).then_inc(sed.h, 1)
                            eng.wait_ge(sed.h, em)
                            eng.tensor_tensor(YTg[:, q * 512:(q + 1) * 512],
                                              YTg[:, q * 512:(q + 1) * 512], TAP2[:, 0:512], ALU.add)
                            eng.drain().then_inc(svv.h, 1)
                        on("vector", f_cv)
                        sed.inc()
                        cvt = svv.inc()
                        setguard(bq, svv, cvt)
                        corr_v.append(cvt)

                    def f_sg(eng, gl=gl, pe2=emit_gate_outproj_a.pe2[0]):
                        eng.wait_ge(glx.h, gl)
                        eng.wait_ge(svv.h, pe2)
                        eng.activation(GB1[:, :], GGT[:, :], AF.Silu)
                        eng.drain().then_inc(sev.h, 1)
                    on("scalar", f_sg)
                    sg = sev.inc()

                    cvl = corr_v[-1]
                    def f_e1(eng, gl=gl, cvl=cvl, sg=sg, em=sed.n + 1,
                             ptr=emit_gate_outproj_a.trg.get(c - 1, 0)):
                        eng.wait_ge(glx.h, gl)
                        eng.wait_ge(svv.h, cvl)
                        eng.wait_ge(sev.h, sg)
                        eng.wait_ge(strs.h, ptr)
                        eng.tensor_tensor(GB2[:, :], GXS[:, :], dexp_b[:, :],
                                          ALU.mult).then_inc(sed.h, 1)
                        eng.wait_ge(sed.h, em)
                        eng.tensor_tensor(GXS[:, :], YTg[:, :], GB2[:, :],
                                          ALU.add).then_inc(sed.h, 1)
                        eng.wait_ge(sed.h, em + 1)
                        eng.tensor_tensor(GB2[:, :], GXS[:, :], GB1[:, :], ALU.mult)
                        eng.drain().then_inc(svv.h, 1)
                    on("vector", f_e1)
                    sed.inc(2)
                    e1 = svv.inc()
                    emit_gate_outproj_a.pe1[0] = e1
                    emit_gate_outproj_a.psg[0] = sg

                    def f_sq(eng, e1=e1):
                        eng.wait_ge(svv.h, e1)
                        eng.activation(GB1[:, :], GB2[:, :], AF.Square, accum_out=SSQf[:, 0:1])
                        eng.drain().then_inc(sev.h, 1)
                    on("scalar", f_sq)
                    sq_t = sev.inc()

                    def f_rs1(eng, sq_t=sq_t):
                        eng.wait_ge(sev.h, sq_t)
                        eng.tensor_scalar(MSf[:, 0:1], SSQf[:, 0:1], 1.0 / A_INTER, None, ALU.mult)
                        eng.drain().then_inc(svv.h, 1)
                    on("vector", f_rs1)
                    rs1 = svv.inc()

                    def f_rs2(eng, rs1=rs1, ea=sea.n + 1):
                        eng.wait_ge(svv.h, rs1)
                        eng.activation(RSTDf[:, 0:1], MSf[:, 0:1], AF.Ln,
                                       bias=EPSc[:, 0:1]).then_inc(sea.h, 1)
                        eng.wait_ge(sea.h, ea)
                        eng.activation(RSTDf[:, 0:1], RSTDf[:, 0:1], AF.Exp, scale=-0.5)
                        eng.drain().then_inc(sev.h, 1)
                    on("scalar", f_rs2)
                    sea.inc()
                    rs2 = sev.inc()

                    trg = emit_gate_outproj_a.trg.get(c - 1, 0)
                    def f_e2(eng, rs2=rs2, trg=trg, em=sed.n + 1):
                        eng.wait_ge(sev.h, rs2)
                        eng.wait_ge(strs.h, trg)
                        eng.tensor_scalar(GB1[:, :], GB2[:, :], RSTDf[:, 0:1], None,
                                          ALU.mult).then_inc(sed.h, 1)
                        eng.wait_ge(sed.h, em)
                        eng.tensor_tensor(GB2[:, :], GB1[:, :], nw_b[:, :], ALU.mult)
                        eng.drain().then_inc(svv.h, 1)
                    on("vector", f_e2)
                    sed.inc()
                    e2 = svv.inc()
                    emit_gate_outproj_a.pe2[0] = e2

                    def f_trs(eng, e2=e2, ci=ci, opg=emit_gate_outproj_a.opmm[0]):
                        eng.wait_ge(svv.h, e2)
                        eng.wait_ge(smm.h, opg)
                        for f in range(32):
                            eng.dma_start_transpose(
                                out=YFMh[:, f * 1024 + ci * 128:f * 1024 + ci * 128 + 128],
                                in_=GB2[:, f * 128:(f + 1) * 128]).then_inc(strs.h, 16)
                    on("sync", f_trs)
                    emit_gate_outproj_a.trg[c] = strs.inc(16 * 32)

                half_tr = emit_gate_outproj_a.trg[8 * h2 + 7]
                mmd = {}
                for m in range(16):
                    p = m % 2
                    wfree = smm.n if m < 2 else mmd[m - 2]
                    ws2 = ldwp[p]
                    def f_wl(eng, p=p, m=m, wfree=wfree, lv=svv.n, htr=half_tr, ws2=ws2):
                        eng.wait_ge(smm.h, wfree)
                        eng.wait_ge(svv.h, lv)
                        eng.wait_ge(strs.h, htr)
                        src = wa_outT.rearrange("(k p) m -> p k m", p=128)[:, :, m * 128:(m + 1) * 128]
                        eng.dma_start(out=WTOv[p].rearrange("p (k j) -> p k j", k=32),
                                      in_=src).then_inc(ws2.h, 16)
                    on("gpsimd", f_wl)
                    wt = ws2.inc(16)

                    for tt in range(2):
                        bk = 2 + (m * 2 + tt) % 2
                        tok0 = h2 * 1024 + tt * 512
                        gom = gpop(bk)
                        def f_om(eng, p=p, bk=bk, tt=tt, wt=wt, half_tr=half_tr, ws2=ws2, gom=gom):
                            eng.wait_ge(ws2.h, wt)
                            eng.wait_ge(strs.h, half_tr)
                            gwait(eng, gom)
                            for k in range(32):
                                eng.matmul(PS[bk][:, 0:512],
                                           WTOv[p][:, k * 128:(k + 1) * 128],
                                           YFMh[:, k * 1024 + tt * 512:k * 1024 + tt * 512 + 512],
                                           start=(k == 0), stop=(k == 31))
                            eng.drain().then_inc(smm.h, 1)
                        on("tensor", f_om)
                        om = smm.inc()
                        emit_gate_outproj_a.opmm[0] = om
                        if tt == 1:
                            mmd[m] = om

                        xp = (m * 2 + tt) % 2
                        xs_ = xlp[xp]
                        def f_xl(eng, m=m, tok0=tok0, xp=xp, xprev=xprev, xprev_off=xprev_off,
                                 xg=emit_gate_outproj_a.xld.get(xp, 0), xs_=xs_):
                            eng.wait_ge(svv.h, xg)
                            eng.dma_start(out=XOLDf[xp][:, :],
                                          in_=xprev[m * 128:(m + 1) * 128,
                                                    xprev_off + tok0:xprev_off + tok0 + 512]).then_inc(xs_.h, 16)
                        on("sync", f_xl)
                        xl = xs_.inc(16)

                        def f_xv(eng, bk=bk, xp=xp, om=om, xl=xl, xg2=emit_gate_outproj_a.xst.get(xp, 0),
                                 xs_=xs_, sy_=syp[xp]):
                            eng.wait_ge(smm.h, om)
                            eng.wait_ge(xs_.h, xl)
                            eng.wait_ge(sy_.h, xg2)
                            eng.tensor_tensor(XNEWf[xp][:, :], PS[bk][:, 0:512], XOLDf[xp][:, :], ALU.add)
                            eng.drain().then_inc(svv.h, 1)
                        on("vector", f_xv)
                        xv = svv.inc()
                        setguard(bk, svv, xv)
                        emit_gate_outproj_a.xld[xp] = xv

                        def f_xs(eng, m=m, tok0=tok0, xp=xp, xv=xv, xdst=xdst):
                            eng.wait_ge(svv.h, xv)
                            eng.dma_start(out=xdst[m * 128:(m + 1) * 128, tok0:tok0 + 512],
                                          in_=XNEWf[xp][:, :]).then_inc(syp[xp].h, 16)
                        on("scalar", f_xs)
                        emit_gate_outproj_a.xst[xp] = syp[xp].inc(16)
        emit_gate_outproj_a.trg = {}
        emit_gate_outproj_a.xld = {}
        emit_gate_outproj_a.xst = {}
        emit_gate_outproj_a.pe1 = [0]
        emit_gate_outproj_a.psg = [0]
        emit_gate_outproj_a.pe2 = [0]
        emit_gate_outproj_a.opmm = [0]


        def emit_ffn(xsrc, xsrc_off, xdst):
            LNW = T
            lnbf = R1[:, 0:16 * LNW]
            XA1 = FR[:, 0:512]
            FXOLD = [FR[:, 512:1024], FR[:, 1024:1536]]
            FXNEW = [FR[:, 1536:2048], FR[:, 2048:2560]]
            LGf = FR[:, 2560:3072]
            LGT = FR[:, 3072:3088]
            M8v = FR[:, 3088:3096]
            NV1 = FR[:, 3096:3097]; E2v = FR[:, 3097:3098]; DENv = FR[:, 3098:3099]
            RDv = FR[:, 3099:3100]
            Ev = FR[:, 3104:3120]; G8v = FR[:, 3120:3136]; ESv = FR[:, 3136:3152]
            RW1T = FR[:, 4096:5120]
            RW2Ts = FR[:, 5120:5152]
            RB1c = FR[:, 5152:5153]; RB2G = FR[:, 5153:5154]; GTSc = FR[:, 5154:5155]
            BCOL = FR[:, 5168:5184]
            RT1 = FR[:, 5200:5328]
            WXA = [R2[:, 0:2048], R2[:, 2048:4096]]
            BTW = [R2[:, 12288:12544], R2[:, 12544:12800]]
            SELb = R2[:, 12800:13056]
            COF = R2[:, 13056:15104]
            CEb = [R2[:, 15104:15616], R2[:, 15616:16128]]

            def f_fl(eng, tgt=lx[0].n + 16 * 7):
                eng.dma_start(out=RW1T.rearrange("p (k j) -> p k j", k=16),
                              in_=f_rw1T.rearrange("(k p) m -> p k m", p=128)).then_inc(lx[0].h, 16)
                eng.dma_start(out=RW2Ts[0:64, :], in_=f_rw2T[:, :]).then_inc(lx[0].h, 16)
                eng.dma_start(out=RB1c[0:64, :], in_=f_rb1[:, :]).then_inc(lx[0].h, 16)
                eng.dma_start(out=RB2G[0:32, :], in_=f_rb2[:, :]).then_inc(lx[0].h, 16)
                eng.dma_start(out=GTSc[0:16, 0:1], in_=f_gates[:, :]).then_inc(lx[0].h, 16)
                eng.dma_start(out=RT1[0:16, 0:128], in_=f_biasT[:, :]).then_inc(lx[0].h, 16)
                eng.dma_start(out=SELb[0:16, :], in_=f_sel[:, :]).then_inc(lx[0].h, 16)
                eng.wait_ge(lx[0].h, tgt)
            on("sync", f_fl)
            fl_t = lx[0].inc(16 * 7)

            g3i = gpop(3)
            def f_fb_t(eng, fl_t=fl_t, g3i=g3i):
                eng.wait_ge(lx[0].h, fl_t)
                gwait(eng, g3i)
                eng.transpose(PS[3][:, 0:16], RT1[0:16, 0:128], eye_s[0:16, 0:16])
                eng.drain().then_inc(smm.h, 1)
            on("tensor", f_fb_t)
            fbt = smm.inc()

            def f_fb_e(eng, fbt=fbt):
                eng.wait_ge(smm.h, fbt)
                eng.activation(BCOL[:, :], PS[3][:, 0:16], AF.Copy)
                eng.drain().then_inc(sev.h, 1)
            on("scalar", f_fb_e)
            fbe = sev.inc()
            setguard(3, sev, fbe)

            def f_fb_v(eng, fbe=fbe):
                eng.wait_ge(sev.h, fbe)
                eng.tensor_tensor(RB2G[0:16, :], RB2G[0:16, :], GTSc[0:16, :], ALU.add)
                eng.drain().then_inc(svv.h, 1)
            on("vector", f_fb_v)
            fbv = svv.inc()

            hr_done = []

            def router1(it, xs, doff, w, av):
                g3 = gpop(3)
                def f_r1(eng, xs=xs, w=w, av=av, g3=g3, fl_t=fl_t, fbv=fbv):
                    eng.wait_ge(svv.h, av)
                    eng.wait_ge(svv.h, fbv)
                    eng.wait_ge(lx[0].h, fl_t)
                    gwait(eng, g3)
                    for k in range(NK):
                        eng.matmul(PS[3][0:64, 0:w], RW1T[:, k * 64:(k + 1) * 64],
                                   XT[:, xs + k * 128:xs + k * 128 + w],
                                   start=(k == 0), stop=(k == NK - 1))
                    eng.drain().then_inc(smm.h, 1)
                on("tensor", f_r1)
                r1 = smm.inc()

                def f_r1e(eng, r1=r1, w=w):
                    eng.wait_ge(smm.h, r1)
                    eng.activation(RT1[0:64, 0:w], PS[3][0:64, 0:w], AF.Copy)
                    eng.drain().then_inc(sev.h, 1)
                on("scalar", f_r1e)
                r1e = sev.inc()
                setguard(3, sev, r1e)

                def f_r1v(eng, r1e=r1e, w=w):
                    eng.wait_ge(sev.h, r1e)
                    eng.tensor_tensor(RT1[0:64, 0:w], RT1[0:64, 0:w], PS[7][0:64, 0:w], ALU.mult)
                    eng.drain().then_inc(svv.h, 1)
                on("vector", f_r1v)
                r1v = svv.inc()
                setguard(7, svv, r1v)

                def f_r1s(eng, r1v=r1v, doff=doff, w=w):
                    eng.wait_ge(svv.h, r1v)
                    eng.activation(HRf[0:64, doff:doff + w], RT1[0:64, 0:w], AF.Silu, bias=RB1c[0:64, :])
                    eng.drain().then_inc(sev.h, 1)
                on("scalar", f_r1s)
                hr_done.append(sev.inc())
                return (smm, r1)

            HRf = FR[:, 6144:8192]
            ln_tts = [(j * 128, j * 128, 128, xsrc) for j in range(16)]
            ln_tts = [(j * 128, xsrc_off + j * 128, 128, xsrc) for j in range(16)]
            emit_ln(LNW, ln_tts, slotw=128, lnb_t=lnbf, post_tt=router1)
            Bar()

            for t4 in range(4):
                bk = t4 % 2
                g_ = gpop(bk)
                def f_r2(eng, t4=t4, bk=bk, g_=g_):
                    gwait(eng, g_)
                    eng.matmul(PS[bk][0:32, 0:512], RW2Ts[0:64, :], HRf[0:64, t4 * 512:(t4 + 1) * 512],
                               start=True, stop=True)
                    eng.drain().then_inc(smm.h, 1)
                on("tensor", f_r2)
                r2 = smm.inc()

                def f_lg(eng, r2=r2, t4=t4, bk=bk):
                    eng.wait_ge(smm.h, r2)
                    eng.tensor_scalar(LGf[0:16, 0:512], PS[bk][0:16, 0:512], RB2G[0:16, :],
                                      None, ALU.add)
                    eng.drain().then_inc(svv.h, 1)
                on("vector", f_lg)
                lg = svv.inc()
                setguard(bk, svv, lg)
                if t4 == 0 and os.environ.get("P_DBG3", "0") == "1":
                    d_lg0 = nc.dram_tensor("d_lg0", [16, 512], F32, kind="ExternalOutput").ap()
                    d_hr0 = nc.dram_tensor("d_hr0", [64, 512], F32, kind="ExternalOutput").ap()
                    def f_d30(eng, w=lg, tgt=scs.n + 32):
                        eng.wait_ge(svv.h, w)
                        eng.dma_start(out=d_lg0[:, :], in_=LGf[0:16, 0:512]).then_inc(scs.h, 16)
                        eng.dma_start(out=d_hr0[:, :], in_=HRf[0:64, 0:512]).then_inc(scs.h, 16)
                        eng.wait_ge(scs.h, tgt)
                    on("sync", f_d30)
                    scs.inc(32)

                for i8 in range(4):
                    tok = t4 * 512 + i8 * 128
                    bk2 = 2 + (i8 % 2)
                    g2_ = gpop(bk2)
                    def f_lt(eng, i8=i8, bk2=bk2, lg=lg, g2_=g2_):
                        eng.wait_ge(svv.h, lg)
                        gwait(eng, g2_)
                        eng.transpose(PS[bk2][:, 0:16], LGf[0:16, i8 * 128:(i8 + 1) * 128],
                                      eye_s[0:16, 0:16])
                        eng.drain().then_inc(smm.h, 1)
                    on("tensor", f_lt)
                    lt_ = smm.inc()

                    def f_lte(eng, lt_=lt_, bk2=bk2):
                        eng.wait_ge(smm.h, lt_)
                        eng.activation(LGT[:, :], PS[bk2][:, 0:16], AF.Copy)
                        eng.drain().then_inc(sev.h, 1)
                    on("scalar", f_lte)
                    lte = sev.inc()
                    setguard(bk2, sev, lte)

                    def f_top(eng, lte=lte, eb=sed.n):
                        eng.wait_ge(sev.h, lte)
                        eng.max(out=M8v[:, :], in_=LGT[:, :])
                        eng.drain().then_inc(sed.h, 1)
                        eng.wait_ge(sed.h, eb + 1)
                        eng.tensor_scalar(NV1[:, :], M8v[:, 0:1], -1.0, None, ALU.mult)
                        eng.tensor_scalar(G8v[:, :], LGT[:, :], M8v[:, 1:2], None, ALU.is_ge)
                        eng.drain().then_inc(svv.h, 1)
                    on("vector", f_top)
                    sed.inc()
                    top = svv.inc()

                    def f_te(eng, top=top, eb=sea.n):
                        eng.wait_ge(svv.h, top)
                        eng.activation(Ev[:, :], LGT[:, :], AF.Exp, bias=NV1[:, :])
                        eng.activation(E2v[:, :], M8v[:, 1:2], AF.Exp, bias=NV1[:, :])
                        eng.drain().then_inc(sev.h, 1)
                    on("scalar", f_te)
                    te = sev.inc()

                    def f_co(eng, te=te, eb=sed.n):
                        eng.wait_ge(sev.h, te)
                        eng.tensor_scalar(DENv[:, :], E2v[:, :], 1.0, None, ALU.add).then_inc(sed.h, 1)
                        eng.wait_ge(sed.h, eb + 1)
                        eng.reciprocal(RDv[:, :], DENv[:, :]).then_inc(sed.h, 1)
                        eng.wait_ge(sed.h, eb + 2)
                        eng.tensor_scalar(ESv[:, :], Ev[:, :], RDv[:, :], None, ALU.mult).then_inc(sed.h, 1)
                        eng.wait_ge(sed.h, eb + 3)
                        eng.tensor_tensor(ESv[:, :], ESv[:, :], G8v[:, :], ALU.mult)
                        eng.drain().then_inc(svv.h, 1)
                    on("vector", f_co)
                    sed.inc(3)
                    co = svv.inc()

                    g3_ = gpop(bk2)
                    def f_ct(eng, co=co, bk2=bk2, g3_=g3_):
                        eng.wait_ge(svv.h, co)
                        gwait(eng, g3_)
                        eng.transpose(PS[bk2][0:16, 0:128], ESv[:, :], eye_s[:, :])
                        eng.drain().then_inc(smm.h, 1)
                    on("tensor", f_ct)
                    ct = smm.inc()

                    def f_ce(eng, ct=ct, bk2=bk2, tok=tok):
                        eng.wait_ge(smm.h, ct)
                        eng.activation(COF[0:16, tok:tok + 128], PS[bk2][0:16, 0:128], AF.Copy)
                        eng.drain().then_inc(sev.h, 1)
                    on("scalar", f_ce)
                    cof_t = sev.inc()
                    setguard(bk2, sev, cof_t)
            cof_all = sev.n
            if os.environ.get("P_DBG3", "0") == "1":
                d_cof = nc.dram_tensor("d_cof", [16, 2048], BF16, kind="ExternalOutput").ap()
                d_hr = nc.dram_tensor("d_hr", [64, 2048], F32, kind="ExternalOutput").ap()
                d_lg = nc.dram_tensor("d_lg", [16, 512], F32, kind="ExternalOutput").ap()
                d_lgt = nc.dram_tensor("d_lgt", [128, 16], F32, kind="ExternalOutput").ap()
                d_m8 = nc.dram_tensor("d_m8", [128, 8], F32, kind="ExternalOutput").ap()
                d_es = nc.dram_tensor("d_es", [128, 16], F32, kind="ExternalOutput").ap()
                d_rw2 = nc.dram_tensor("d_rw2", [64, 32], F32, kind="ExternalOutput").ap()
                def f_d3(eng, w=cof_all, tgt=scs.n + 16 * 7):
                    eng.wait_ge(sev.h, w)
                    eng.dma_start(out=d_cof[:, :], in_=COF[0:16, :]).then_inc(scs.h, 16)
                    eng.dma_start(out=d_hr[:, :], in_=HRf[0:64, :]).then_inc(scs.h, 16)
                    eng.dma_start(out=d_lg[:, :], in_=LGf[0:16, 0:512]).then_inc(scs.h, 16)
                    eng.dma_start(out=d_lgt[:, :], in_=LGT[:, :]).then_inc(scs.h, 16)
                    eng.dma_start(out=d_m8[:, :], in_=M8v[:, :]).then_inc(scs.h, 16)
                    eng.dma_start(out=d_es[:, :], in_=ESv[:, :]).then_inc(scs.h, 16)
                    eng.dma_start(out=d_rw2[:, :], in_=RW2Ts[0:64, :]).then_inc(scs.h, 16)
                    eng.wait_ge(scs.h, tgt)
                on("sync", f_d3)
                scs.inc(16 * 7)
            Bar()

            for m2 in range(2):
                ws = ldwp[m2]
                def f_aw(eng, m2=m2, ws=ws, wg=smm.n):
                    eng.wait_ge(smm.h, wg)
                    src_ = f_AT.rearrange("(k p) m -> p k m", p=128)[:, :, m2 * 128:(m2 + 1) * 128]
                    eng.dma_start(out=WTb[m2].rearrange("p (k j) -> p k j", k=16),
                                  in_=src_).then_inc(ws.h, 16)
                on("gpsimd", f_aw)
                aw = ws.inc(16)

                for t4 in range(4):
                    bkx = (t4 % 2)
                    bkc = 2 + (t4 % 2)
                    gx = gpop(bkx); gc = gpop(bkc)
                    def f_xam(eng, m2=m2, t4=t4, bkx=bkx, bkc=bkc, aw=aw, ws=ws, gx=gx, gc=gc):
                        eng.wait_ge(ws.h, aw)
                        gwait(eng, gx); gwait(eng, gc)
                        for k in range(NK):
                            eng.matmul(PS[bkx][:, 0:512], WTb[m2][:, k * 128:(k + 1) * 128],
                                       lnbf[:, k * LNW + t4 * 512:k * LNW + t4 * 512 + 512],
                                       start=(k == 0), stop=(k == NK - 1))
                        eng.matmul(PS[bkc][:, 0:512], SELb[0:16, m2 * 128:(m2 + 1) * 128],
                                   COF[0:16, t4 * 512:(t4 + 1) * 512], start=True, stop=True)
                        eng.drain().then_inc(smm.h, 1)
                    on("tensor", f_xam)
                    xam = smm.inc()

                    def f_cee(eng, xam=xam, bkc=bkc, t4=t4):
                        eng.wait_ge(smm.h, xam)
                        eng.activation(CEb[t4 % 2][:, :], PS[bkc][:, 0:512], AF.Copy)
                        eng.drain().then_inc(sev.h, 1)
                    on("scalar", f_cee)
                    cee = sev.inc()
                    setguard(bkc, sev, cee)

                    def f_wxa(eng, cee=cee, bkx=bkx, m2=m2, t4=t4):
                        eng.wait_ge(sev.h, cee)
                        eng.tensor_tensor(WXA[m2][:, t4 * 512:(t4 + 1) * 512],
                                          PS[bkx][:, 0:512], CEb[t4 % 2][:, :], ALU.mult)
                        eng.drain().then_inc(svv.h, 1)
                    on("vector", f_wxa)
                    wxa = svv.inc()
                    setguard(bkx, svv, wxa)
            Bar()

            mmdF = {}
            for m in range(16):
                p = m % 2
                ws = ldwp[p]
                wfree = smm.n if m < 2 else mmdF[m - 2]
                def f_fw(eng, p=p, m=m, ws=ws, wfree=wfree):
                    eng.wait_ge(smm.h, wfree)
                    src_ = wf_T.rearrange("(k p) m -> p k m", p=128)[:, :, m * 128:(m + 1) * 128]
                    eng.dma_start(out=WTb[p].rearrange("p (k j) -> p k j", k=16),
                                  in_=src_).then_inc(ws.h, 16)
                    src2 = f_BT2.rearrange("(k p) m -> p k m", p=128)[:, :, m * 128:(m + 1) * 128]
                    eng.dma_start(out=BTW[p].rearrange("p (k j) -> p k j", k=2),
                                  in_=src2).then_inc(ws.h, 16)
                on("gpsimd", f_fw)
                fw = ws.inc(32)

                for tt in range(4):
                    bk = (m * 4 + tt) % 4
                    gk = gpop(bk)
                    def f_bm(eng, p=p, tt=tt, bk=bk, fw=fw, ws=ws, gk=gk):
                        eng.wait_ge(ws.h, fw)
                        gwait(eng, gk)
                        for k in range(NK):
                            eng.matmul(PS[bk][:, 0:512], WTb[p][:, k * 128:(k + 1) * 128],
                                       lnbf[:, k * LNW + tt * 512:k * LNW + tt * 512 + 512],
                                       start=(k == 0), stop=False)
                        for k2 in range(2):
                            eng.matmul(PS[bk][:, 0:512], BTW[p][:, k2 * 128:(k2 + 1) * 128],
                                       WXA[k2][:, tt * 512:(tt + 1) * 512],
                                       start=False, stop=(k2 == 1))
                        eng.drain().then_inc(smm.h, 1)
                    on("tensor", f_bm)
                    bm = smm.inc()
                    if tt == 3:
                        mmdF[m] = bm

                    xp = (m * 4 + tt) % 2
                    xs_ = xlp[xp]
                    def f_fxl(eng, m=m, tt=tt, xp=xp, xs_=xs_,
                              xg=emit_ffn.xld.get(xp, 0)):
                        eng.wait_ge(svv.h, xg)
                        eng.dma_start(out=FXOLD[xp][:, :],
                                      in_=xsrc[m * 128:(m + 1) * 128,
                                               xsrc_off + tt * 512:xsrc_off + tt * 512 + 512]).then_inc(xs_.h, 16)
                    on("sync", f_fxl)
                    fxl = xs_.inc(16)

                    def f_fxv(eng, bk=bk, xp=xp, bm=bm, fxl=fxl, m=m, xs_=xs_,
                              xg2=emit_ffn.xst.get(xp, 0), eb=sed.n, sy_=syp[xp]):
                        eng.wait_ge(smm.h, bm)
                        eng.wait_ge(xs_.h, fxl)
                        eng.wait_ge(sy_.h, xg2)
                        eng.tensor_scalar(XA1[:, :], PS[bk][:, 0:512], BCOL[:, m:m + 1],
                                          None, ALU.add).then_inc(sed.h, 1)
                        eng.wait_ge(sed.h, eb + 1)
                        eng.tensor_tensor(FXNEW[xp][:, :], XA1[:, :], FXOLD[xp][:, :], ALU.add)
                        eng.drain().then_inc(svv.h, 1)
                    on("vector", f_fxv)
                    sed.inc()
                    fxv = svv.inc()
                    setguard(bk, svv, fxv)
                    emit_ffn.xld[xp] = fxv

                    def f_fxs(eng, m=m, tt=tt, xp=xp, fxv=fxv, xdst=xdst):
                        eng.wait_ge(svv.h, fxv)
                        eng.dma_start(out=xdst[m * 128:(m + 1) * 128, tt * 512:tt * 512 + 512],
                                      in_=FXNEW[xp][:, :]).then_inc(syp[xp].h, 16)
                    on("scalar", f_fxs)
                    emit_ffn.xst[xp] = syp[xp].inc(16)
        emit_ffn.xld = {}
        emit_ffn.xst = {}

        def emit_halo_b(xsrc, xsrc_off):
            base = lx[0].n
            def f_hga(eng, base=base, xsrc=xsrc, xsrc_off=xsrc_off):
                for k in range(NK):
                    eng.dma_start(out=XF_t[:, k * 4:(k + 1) * 4],
                                  in_=xsrc[k * 128:(k + 1) * 128,
                                           xsrc_off + T - 4:xsrc_off + T]).then_inc(lx[0].h, 16)
                eng.dma_start(out=HSLf[0:8, 0:1], in_=hsel_in[:, :]).then_inc(lx[0].h, 16)
                eng.wait_ge(lx[0].h, base + 16 * 17)
                for k in range(NK):
                    eng.dma_start(out=cc_h_in[k * 8:(k + 1) * 8, :]
                                  .rearrange("P (pp t) -> (P pp) t", pp=16),
                                  in_=XF_t[:, k * 4:(k + 1) * 4]).then_inc(lx[0].h, 16)
            on("sync", f_hga)
            hga = lx[0].inc(16 * 33)

            def f_hcc(eng, hga=hga):
                eng.wait_ge(lx[0].h, hga)
                eng.collective_compute(
                    "AllGather", ALU.bypass,
                    replica_groups=[list(range(8))],
                    ins=[cc_h_in[:, :].opt()],
                    outs=[cc_h_out[:, :].opt()],
                ).then_inc(scc.h, 1)
            on("gpsimd", f_hcc)
            hcc = scc.inc()

            def f_hlt(eng, hcc=hcc):
                eng.wait_ge(scc.h, hcc)
                eng.dma_start(out=HSLT[0:8, :],
                              in_=cc_h_out.rearrange("(j p) q -> j (p q)", p=128)).then_inc(lx[1].h, 16)
            on("sync", f_hlt)
            hlt = lx[1].inc(16)

            halo_st = [0]
            for c in range(NCH):
                bk = 4 + c % 2
                gh = gpop(bk)
                def f_hm(eng, c=c, bk=bk, hlt=hlt, gh=gh, hga=hga):
                    eng.wait_ge(lx[1].h, hlt)
                    eng.wait_ge(lx[0].h, hga)
                    gwait(eng, gh)
                    eng.matmul(PS[bk][0:1, 0:512], HSLf[0:8, 0:1],
                               HSLT[0:8, c * 512:(c + 1) * 512], start=True, stop=True)
                    eng.drain().then_inc(smm.h, 1)
                on("tensor", f_hm)
                hm = smm.inc()

                def f_he(eng, bk=bk, hm=hm, pst=halo_st[0]):
                    eng.wait_ge(smm.h, hm)
                    eng.wait_ge(scs.h, pst)
                    eng.activation(XF_t[0:1, 1024:1536], PS[bk][0:1, 0:512], AF.Copy)
                    eng.drain().then_inc(sev.h, 1)
                on("scalar", f_he)
                he = sev.inc()
                setguard(bk, sev, he)

                def f_hs(eng, c=c, he=he):
                    eng.wait_ge(sev.h, he)
                    eng.dma_start(out=x_halo_b[c * 128:(c + 1) * 128, 124:128].unsqueeze(0),
                                  in_=XF_t[0:1, 1024:1536].rearrange("a (p t) -> a p t", p=128)
                                  ).then_inc(scs.h, 16)
                on("sync", f_hs)
                halo_st[0] = scs.inc(16)

        sevacB = {}
        chunk_mm_doneB = {}
        qmm_ldB = {}
        chunk_ldqB = {}
        ytm_storeB = {}
        bigB = {}
        pqB = [0, 0]
        LNSCALE = float(np.log(B_SCALE))

        def emit_mixer_b():
            H, HD, SP = B_HEADS, B_HD, B_STATE
            nq = H // 4

            def f_ld0(eng):
                eng.dma_start(out=DTRf[0:H, 0:1], in_=b_Alog[:, :]).then_inc(lx[0].h, 16)
                eng.dma_start(out=DTBCf[0:H, 0:1], in_=b_dtb[:, :]).then_inc(lx[0].h, 16)
            on("sync", f_ld0)
            l0 = lx[0].inc(32)

            def f_i_s(eng, l0=l0):
                eng.wait_ge(lx[0].h, l0)
                eng.activation(ACOLf[0:H, 0:1], DTRf[0:H, 0:1], AF.Exp)
                eng.drain().then_inc(sev.h, 1)
            on("scalar", f_i_s)
            e0 = sev.inc()

            def f_i_v(eng, e0=e0):
                eng.wait_ge(sev.h, e0)
                eng.tensor_scalar(ACOLf[0:H, 0:1], ACOLf[0:H, 0:1], -1.0, None, ALU.mult)
                eng.memset(RCC[:, 0:4], 0.0)
                eng.memset(S_sb[:, :], 0.0)
                eng.memset(ONESf[0:H, :], 1.0)
                for pr in range(2):
                    eng.memset(LdL[pr][0:2, 0:512], 1.0)
                    eng.memset(LdR[pr][0:2, 0:512], 1.0)
                eng.drain().then_inc(svv.h, 1)
            on("vector", f_i_v)
            v0 = svv.inc()

            dtv2B = {}
            for c in range(NCH):
                cp = c % 2
                c0 = c * CH
                ms_ = mc[cp]
                g_prev = smm.n if c < 1 else chunk_mm_doneB[c - 1]
                big_g = svv.n if c < 1 else bigB[c - 1]
                def f_cinB(eng, cp=cp, c0=c0, g_prev=g_prev, v0=v0, ms_=ms_, big_g=big_g,
                           dtg=dtv2B.get(c - 1, 0)):
                    eng.wait_ge(svv.h, v0)
                    eng.wait_ge(smm.h, g_prev)
                    eng.wait_ge(svv.h, dtg)
                    eng.wait_ge(svv.h, big_g)
                    eng.dma_start_transpose(out=XSTb[:, :],
                                            in_=scr_conv_b[0:B_INNER, c0:c0 + CH]).then_inc(ms_.h, 16)
                    eng.dma_start_transpose(out=BTb[:, :],
                                            in_=scr_B_b[0:4096, c0:c0 + CH]).then_inc(ms_.h, 16)
                    for kk in range(4):
                        eng.dma_start(
                            out=BFb[0:64, kk * 2048:(kk + 1) * 2048].rearrange("p (h j) -> p h j", h=16),
                            in_=scr_B_b.rearrange("(h n) t -> n h t", n=64)[:, kk * 16:(kk + 1) * 16,
                                                                           c0:c0 + CH]).then_inc(ms_.h, 16)
                        eng.dma_start(
                            out=CFb[0:64, kk * 2048:(kk + 1) * 2048].rearrange("p (h j) -> p h j", h=16),
                            in_=scr_C_b.rearrange("(h n) t -> n h t", n=64)[:, kk * 16:(kk + 1) * 16,
                                                                           c0:c0 + CH]).then_inc(ms_.h, 16)
                    eng.dma_start(out=DTRf[0:H, 0:128],
                                  in_=scr_dt_b[0:H, c0:c0 + CH]).then_inc(ms_.h, 16)
                on("sync", f_cinB)
                cin_t = ms_.inc(16 * 11)

                def f_dtv1(eng, cin_t=cin_t, ms_=ms_, e0=sed.n + 1, e1=sed.n + 2):
                    eng.wait_ge(ms_.h, cin_t)
                    eng.tensor_scalar(Uf[0:H, :], DTRf[0:H, 0:128], DTBCf[0:H, 0:1], None,
                                      ALU.add).then_inc(sed.h, 1)
                    eng.wait_ge(sed.h, e0)
                    eng.tensor_scalar(NAf[0:H, :], Uf[0:H, :], -1.0, None,
                                      ALU.mult).then_inc(sed.h, 1)
                    eng.wait_ge(sed.h, e1)
                    eng.tensor_tensor(NAf[0:H, :], Uf[0:H, :], NAf[0:H, :], ALU.min)
                    eng.drain().then_inc(svv.h, 1)
                on("vector", f_dtv1)
                sed.inc(2)
                t1 = svv.inc()

                def f_dts1(eng, t1=t1, ea=sea.n + 1):
                    eng.wait_ge(svv.h, t1)
                    eng.activation(Ef[0:H, :], NAf[0:H, :], AF.Exp).then_inc(sea.h, 1)
                    eng.wait_ge(sea.h, ea)
                    eng.activation(Ef[0:H, :], Ef[0:H, :], AF.Ln, bias=1.0)
                    eng.drain().then_inc(sev.h, 1)
                on("scalar", f_dts1)
                sea.inc()
                t2 = sev.inc()

                def f_dtv2(eng, t2=t2, cp=cp, eb=sed.n):
                    eng.wait_ge(sev.h, t2)
                    eng.tensor_scalar(MX2f[0:H, :], Uf[0:H, :], 0.0, None,
                                      ALU.max).then_inc(sed.h, 1)
                    eng.wait_ge(sed.h, eb + 1)
                    eng.tensor_tensor(DTf[0:H, :], MX2f[0:H, :], Ef[0:H, :],
                                      ALU.add).then_inc(sed.h, 1)
                    eng.wait_ge(sed.h, eb + 2)
                    eng.tensor_scalar(DTf[0:H, :], DTf[0:H, :], 0.5, 1e-4, ALU.min,
                                      ALU.max).then_inc(sed.h, 1)
                    eng.wait_ge(sed.h, eb + 3)
                    eng.tensor_scalar(ADf[0:H, :], DTf[0:H, :], ACOLf[0:H, 0:1], None,
                                      ALU.mult).then_inc(sed.h, 1)
                    eng.wait_ge(sed.h, eb + 4)
                    eng.tensor_tensor_scan(REVIf[0:H, :][:, ::-1], ONESf[0:H, :],
                                           ADf[0:H, :][:, ::-1], 0.0, ALU.mult, ALU.add)
                    eng.tensor_tensor_scan(CUMf[0:H, :], ONESf[0:H, :], ADf[0:H, :],
                                           0.0, ALU.mult, ALU.add).then_inc(sed.h, 1)
                    eng.wait_ge(sed.h, eb + 5)
                    eng.tensor_tensor(DTEf[0:H, :], REVIf[0:H, :], ADf[0:H, :], ALU.subtract)
                    eng.tensor_scalar(NCf[0:H, :], CUMf[0:H, :], -1.0, None, ALU.mult)
                    eng.tensor_scalar(RCCSf[0:H, 0:1], RCC[0:H, cp:cp + 1], LNSCALE, None, ALU.add)
                    eng.tensor_reduce(TOTf[0:H, 0:1], ADf[0:H, :], AX.X,
                                      ALU.add).then_inc(sed.h, 1)
                    eng.wait_ge(sed.h, eb + 6)
                    eng.tensor_tensor(RCC[0:H, (1 - cp):(2 - cp)], RCC[0:H, cp:cp + 1],
                                      TOTf[0:H, 0:1], ALU.add)
                    eng.drain().then_inc(svv.h, 1)
                on("vector", f_dtv2)
                sed.inc(6)
                t3 = svv.inc()
                dtv2B[c] = t3

                def f_dts2(eng, t3=t3, cp=cp):
                    eng.wait_ge(svv.h, t3)
                    eng.activation(DTEf[0:H, :], DTEf[0:H, :], AF.Exp)
                    eng.activation(ECf[0:H, :], CUMf[0:H, :], AF.Exp, bias=SCLc[0:H, 0:1])
                    eng.activation(ECCf[0:H, :], CUMf[0:H, :], AF.Exp, bias=RCCSf[0:H, 0:1])
                    eng.drain().then_inc(sev.h, 1)
                on("scalar", f_dts2)
                t4 = sev.inc()

                def f_g1(eng, cp=cp, t3=t3, ldg=chunk_ldqB.get(c - 2, ())):
                    eng.wait_ge(svv.h, t3)
                    for s_, v_ in ldg:
                        eng.wait_ge(s_.h, v_)
                    eng.dma_start(out=scr_cum[2 * cp, 0:H * 128].rearrange("(h l) -> h l", h=H),
                                  in_=CUMf[0:H, :]).then_inc(stg.h, 16)
                    eng.dma_start(out=scr_cum[2 * cp + 1, 0:H * 128].rearrange("(h l) -> h l", h=H),
                                  in_=NCf[0:H, :]).then_inc(stg.h, 16)
                on("gpsimd", f_g1)
                st1 = stg.inc(32)

                g76 = gpop(7, 6)
                def f_tr(eng, t4=t4, g76=g76):
                    eng.wait_ge(sev.h, t4)
                    gwait(eng, g76)
                    eng.transpose(PS[7][:, 0:H], DTf[0:H, :], eye_s[0:H, 0:H])
                    eng.transpose(PS[7][:, 64:64 + H], DTEf[0:H, :], eye_s[0:H, 0:H])
                    eng.transpose(PS[7][:, 128:128 + H], ECf[0:H, :], eye_s[0:H, 0:H])
                    eng.transpose(PS[7][:, 192:192 + H], ECCf[0:H, :], eye_s[0:H, 0:H])
                    eng.transpose(PS[6][0:1, 0:H], TOTf[0:H, 0:1], eye_s[0:H, 0:H])
                    eng.drain().then_inc(smm.h, 1)
                on("tensor", f_tr)
                tr_t = smm.inc()

                def f_tre(eng, tr_t=tr_t, c=c):
                    eng.wait_ge(smm.h, tr_t)
                    eng.activation(DTTMb[:, 0:H], PS[7][:, 0:H], AF.Copy)
                    eng.activation(DDTTMb[:, 0:H], PS[7][:, 64:64 + H], AF.Copy)
                    eng.activation(ECTMb[:, 0:H], PS[7][:, 128:128 + H], AF.Copy)
                    eng.activation(ECCSb[:, c * 64:(c + 1) * 64], PS[7][:, 192:192 + H], AF.Copy)
                    eng.activation(TROWf[0:1, 0:H], PS[6][0:1, 0:H], AF.Copy)
                    eng.drain().then_inc(sev.h, 1)
                on("scalar", f_tre)
                tre_t = sev.inc()
                setguard(7, sev, tre_t)

                def f_dtm(eng, tre_t=tre_t):
                    eng.wait_ge(sev.h, tre_t)
                    eng.matmul(PS[6][:, 64:64 + H], ones_row[0:1, :], TROWf[0:1, 0:H],
                               start=True, stop=True)
                    eng.drain().then_inc(smm.h, 1)
                on("tensor", f_dtm)
                dtm_t = smm.inc()

                def f_dte(eng, dtm_t=dtm_t):
                    eng.wait_ge(smm.h, dtm_t)
                    eng.activation(DTOTb[:, 0:H], PS[6][:, 64:64 + H], AF.Exp)
                    eng.drain().then_inc(sev.h, 1)
                on("scalar", f_dte)
                dte_t = sev.inc()
                setguard(6, sev, dte_t)

                big_prev = smm.n if c < 1 else chunk_mm_doneB[c - 1]
                def f_bigB(eng, cp=cp, cin_t=cin_t, dte_t=dte_t, big_prev=big_prev, ms_=ms_):
                    eng.wait_ge(ms_.h, cin_t)
                    eng.wait_ge(sev.h, dte_t)
                    eng.wait_ge(smm.h, big_prev)
                    eng.tensor_tensor(XDTb.rearrange("p (h d) -> p h d", h=H),
                                      XSTb.rearrange("p (h d) -> p h d", h=H),
                                      DTTMb[:, 0:H].unsqueeze(2).broadcast_to((128, H, HD)),
                                      ALU.mult)
                    eng.tensor_tensor(BDb.rearrange("p (h d) -> p h d", h=H),
                                      BTb.rearrange("p (h d) -> p h d", h=H),
                                      DDTTMb[:, 0:H].unsqueeze(2).broadcast_to((128, H, SP)),
                                      ALU.mult)
                    eng.tensor_tensor(DIb[0:64, :].rearrange("p (h d) -> p h d", h=H),
                                      eye_b[0:64, 0:64].unsqueeze(1).broadcast_to((64, H, 64)),
                                      DTOTb[0:64, 0:H].unsqueeze(2).broadcast_to((64, H, 64)),
                                      ALU.mult)
                    eng.drain().then_inc(svv.h, 1)
                on("vector", f_bigB)
                bigB[c] = svv.inc()

                for q in range(nq):
                    qp = q % 2
                    o = q // 2
                    bL, bG = qp, 2 + qp
                    bY, bO = 4 + (o % 2), 6 + (o % 2)
                    yo = (q % 2) * 256

                    ld_guard = qmm_ldB.get((c, q - 2), qmm_ldB.get((c - 1, nq - 2 + q), 0))
                    qs_ = ldsp[qp]
                    def f_gq(eng, cp=cp, q=q, qp=qp, st1=st1, ld_guard=ld_guard, qs_=qs_):
                        eng.wait_ge(stg.h, st1)
                        eng.wait_ge(smm.h, ld_guard)
                        eng.dma_start(out=LdR[qp][0:1, 0:512],
                                      in_=scr_cum[2 * cp, q * 512:(q + 1) * 512][None, :]).then_inc(qs_.h, 16)
                        eng.dma_start(out=LdL[qp][1:2, 0:512],
                                      in_=scr_cum[2 * cp + 1, q * 512:(q + 1) * 512][None, :]).then_inc(qs_.h, 16)
                    on("gpsimd", f_gq)
                    ldq = qs_.inc(32)

                    gqL = gpop(bL)
                    def f_qL(eng, qp=qp, bL=bL, ldq=ldq, v0=v0, qs_=qs_, gqL=gqL):
                        eng.wait_ge(qs_.h, ldq)
                        eng.wait_ge(svv.h, v0)
                        gwait(eng, gqL)
                        for j in range(4):
                            eng.matmul(PS[bL][:, j * 128:(j + 1) * 128], su_b, negbig_b,
                                       start=True, stop=False)
                            eng.matmul(PS[bL][:, j * 128:(j + 1) * 128],
                                       LdL[qp][0:2, j * 128:(j + 1) * 128],
                                       LdR[qp][0:2, j * 128:(j + 1) * 128],
                                       start=False, stop=True)
                        eng.drain().then_inc(smm.h, 1)
                    on("tensor", f_qL)
                    qL = smm.inc()
                    qmm_ldB[(c, q)] = qL

                    gqG = gpop(bG)
                    def f_qG(eng, q=q, bG=bG, cin_t=cin_t, ms_=ms_, gqG=gqG):
                        eng.wait_ge(ms_.h, cin_t)
                        gwait(eng, gqG)
                        for j in range(4):
                            h = 4 * q + j
                            eng.matmul(PS[bG][:, j * 128:(j + 1) * 128],
                                       BFb[0:64, h * 128:(h + 1) * 128],
                                       CFb[0:64, h * 128:(h + 1) * 128], start=True, stop=True)
                        eng.drain().then_inc(smm.h, 1)
                    on("tensor", f_qG)
                    qG = smm.inc()

                    def f_qa(eng, bL=bL, qL=qL, ea=sea.n + 1, pmt=pqB[0]):
                        eng.wait_ge(smm.h, qL)
                        eng.wait_ge(svv.h, pmt)
                        eng.activation(LTMPf[:, 0:512], PS[bL][:, 0:512], AF.Relu,
                                       scale=-1.0).then_inc(sea.h, 1)
                        eng.wait_ge(sea.h, ea)
                        eng.activation(LTb[:, 0:512], LTMPf[:, 0:512], AF.Exp, scale=-1.0,
                                       bias=SCLc[:, 0:1])
                        eng.drain().then_inc(sev.h, 1)
                    on("scalar", f_qa)
                    sea.inc()
                    qa = sev.inc()
                    setguard(bL, sev, qa)

                    def f_qmt(eng, bG=bG, qa=qa, qG=qG, pq2=pqB[1]):
                        eng.wait_ge(sev.h, qa)
                        eng.wait_ge(smm.h, qG)
                        eng.wait_ge(smm.h, pq2)
                        eng.tensor_tensor(MTb[:, 0:512], PS[bG][:, 0:512], LTb[:, 0:512], ALU.mult)
                        eng.drain().then_inc(svv.h, 1)
                    on("vector", f_qmt)
                    qmt = svv.inc()
                    setguard(bG, svv, qmt)
                    pqB[0] = qmt

                    sev_prev = sevacB.get((c - 1, q), 0)
                    gq2 = gpop(bY, bO, bL)
                    def f_q2(eng, q=q, bL=bL, bY=bY, bO=bO, yo=yo, qmt=qmt, big_t=bigB[c],
                             qa=qa, sev_prev=sev_prev, gq=gq2):
                        eng.wait_ge(svv.h, qmt)
                        eng.wait_ge(svv.h, big_t)
                        eng.wait_ge(sev.h, qa)
                        eng.wait_ge(sev.h, sev_prev)
                        gwait(eng, gq)
                        for j in range(4):
                            h = 4 * q + j
                            eng.matmul(PS[bY][:, yo + j * 64:yo + (j + 1) * 64],
                                       MTb[:, j * 128:(j + 1) * 128],
                                       XDTb[:, h * 64:(h + 1) * 64], start=True, stop=True)
                            eng.matmul(PS[bO][:, yo + j * 64:yo + (j + 1) * 64],
                                       CFb[0:64, h * 128:(h + 1) * 128],
                                       S_sb[0:64, h * 64:(h + 1) * 64], start=True, stop=True)
                            eng.matmul(PS[bL][0:64, j * 64:(j + 1) * 64],
                                       DIb[0:64, h * 64:(h + 1) * 64],
                                       S_sb[0:64, h * 64:(h + 1) * 64], start=True, stop=False)
                            eng.matmul(PS[bL][0:64, j * 64:(j + 1) * 64],
                                       BDb[:, h * 64:(h + 1) * 64],
                                       XDTb[:, h * 64:(h + 1) * 64], start=False, stop=True)
                        eng.drain().then_inc(smm.h, 1)
                    on("tensor", f_q2)
                    q2 = smm.inc()
                    pqB[1] = q2

                    if q % 2 == 1:
                        ysg_c = ytm_storeB.get(c - 1, 0)
                        def f_qy(eng, o=o, bY=bY, bO=bO, q2=q2, ysg=ysg_c,
                                 sy_=syp[(c - 1) % 2], em=sed.n + 1):
                            eng.wait_ge(smm.h, q2)
                            eng.wait_ge(sy_.h, ysg)
                            eng.tensor_tensor(TAP2.rearrange("p (h j) -> p h j", h=8),
                                              PS[bO].ap().rearrange("p (h j) -> p h j", h=8),
                                              ECTMb[:, 8 * o:8 * o + 8].unsqueeze(2)
                                              .broadcast_to((128, 8, 64)),
                                              ALU.mult).then_inc(sed.h, 1)
                            eng.wait_ge(sed.h, em)
                            eng.tensor_tensor(YTMb[:, o * 512:(o + 1) * 512],
                                              PS[bY][:, 0:512], TAP2[:, 0:512], ALU.add)
                            eng.drain().then_inc(svv.h, 1)
                        on("vector", f_qy)
                        sed.inc()
                        qy = svv.inc()
                        setguard(bY, svv, qy)
                        setguard(bO, svv, qy)

                    def f_qs(eng, q=q, bL=bL, q2=q2):
                        eng.wait_ge(smm.h, q2)
                        eng.activation(S_sb[0:64, q * 256:(q + 1) * 256], PS[bL][0:64, 0:256],
                                       AF.Copy)
                        eng.drain().then_inc(sev.h, 1)
                    on("scalar", f_qs)
                    se = sev.inc()
                    sevacB[(c, q)] = se
                    setguard(bL, sev, se)
                chunk_mm_doneB[c] = smm.n
                chunk_ldqB[c] = ((ldsp[0], ldsp[0].n), (ldsp[1], ldsp[1].n))

                lastqy = svv.n
                def f_yst(eng, c=c, lastqy=lastqy):
                    eng.wait_ge(svv.h, lastqy)
                    eng.dma_start(out=scr_y[c * 128:(c + 1) * 128, 0:B_INNER],
                                  in_=YTMb[:, :]).then_inc(syp[c % 2].h, 16)
                on("scalar", f_yst)
                ytm_storeB[c] = syp[c % 2].inc(16)

            fin_se = sev.n
            fin_vv = svv.n
            def f_dec(eng, fin_se=fin_se, fin_vv=fin_vv, cpf=NCH % 2):
                eng.wait_ge(sev.h, fin_se)
                eng.wait_ge(svv.h, fin_vv)
                eng.activation(DECb[0:B_HEADS, 0:1], RCC[0:B_HEADS, cpf:cpf + 1], AF.Exp)
                eng.drain().then_inc(sev.h, 1)
            on("scalar", f_dec)
            dec_t = sev.inc()

            def f_ssB(eng, dec_t=dec_t, w1=scs.n + 16 * 3):
                eng.wait_ge(sev.h, dec_t)
                eng.dma_start(out=cc_b_in[:, 0:4096], in_=S_sb[0:64, :]).then_inc(scs.h, 16)
                eng.dma_start(out=cc_b_in[:, 4096:4160], in_=S_sb[0:64, 0:64]).then_inc(scs.h, 16)
                eng.dma_start(out=scr_dec[0:1, 0:64].rearrange("a p -> p a"),
                              in_=DECb[0:64, 0:1]).then_inc(scs.h, 16)
                eng.wait_ge(scs.h, w1)
                eng.dma_start(out=cc_b_in[0:1, 4096:4160],
                              in_=scr_dec[0:1, 0:64]).then_inc(scs.h, 16)
            on("sync", f_ssB)
            scs.inc(64)

        def emit_collective_b():
            H = B_HEADS
            st_t = scs.n
            def f_cc(eng, st_t=st_t):
                eng.wait_ge(scs.h, st_t)
                eng.collective_compute(
                    "AllGather", ALU.bypass,
                    replica_groups=[list(range(8))],
                    ins=[cc_b_in[:, :].opt()],
                    outs=[cc_b_out[:, :].opt()],
                ).then_inc(scc.h, 1)
            on("gpsimd", f_cc)
            cc_t = scc.inc()

            def f_hz(eng, cc_t=cc_t):
                eng.wait_ge(scc.h, cc_t)
                eng.memset(S_sb[:, :], 0.0)
                eng.drain().then_inc(svv.h, 1)
            on("vector", f_hz)
            prev = svv.inc()

            drow_slots = [DROWb[0:1, 0:64], DECb[0:1, 0:64]]
            jld = {}
            for j in range(8):
                jp = j % 2
                ld_pv = jld.get(j - 2, 0)
                js_ = sj[jp]
                def f_jl(eng, j=j, jp=jp, cc_t=cc_t, ld_pv=ld_pv, js_=js_):
                    eng.wait_ge(scc.h, cc_t)
                    eng.wait_ge(svv.h, ld_pv)
                    eng.dma_start(out=SJv[jp][0:64, :],
                                  in_=cc_b_out[j * 64:(j + 1) * 64, 0:4096]).then_inc(js_.h, 16)
                    eng.dma_start(out=drow_slots[jp],
                                  in_=cc_b_out[j * 64:j * 64 + 1, 4096:4096 + H]).then_inc(js_.h, 16)
                on("sync", f_jl)
                jl = js_.inc(32)

                def f_jr(eng, j=j, jp=jp, jl=jl, prev=prev, js_=js_, em=sed.n + 1):
                    eng.wait_ge(js_.h, jl)
                    eng.wait_ge(svv.h, prev)
                    eng.tensor_scalar(DTILf[0:1, 0:H], drow_slots[jp],
                                      cmask_s[0:1, 8 + j:9 + j], None, ALU.mult).then_inc(sed.h, 1)
                    eng.wait_ge(sed.h, em)
                    eng.tensor_scalar(DTILf[0:1, 0:H], DTILf[0:1, 0:H],
                                      cmask_s[0:1, 16 + j:17 + j], None, ALU.add)
                    eng.tensor_scalar(TROWf[0:1, 0:H], ones_row[0:1, 0:H],
                                      cmask_s[0:1, j:j + 1], None, ALU.mult)
                    eng.drain().then_inc(svv.h, 1)
                on("vector", f_jr)
                sed.inc()
                jr = svv.inc()

                gj6 = gpop(6)
                def f_jm(eng, jr=jr, gj6=gj6):
                    eng.wait_ge(svv.h, jr)
                    gwait(eng, gj6)
                    eng.matmul(PS[6][:, 0:H], ones_row[0:1, :], DTILf[0:1, 0:H], start=True, stop=True)
                    eng.matmul(PS[6][:, 64:64 + H], ones_row[0:1, :], TROWf[0:1, 0:H],
                               start=True, stop=True)
                    eng.drain().then_inc(smm.h, 1)
                on("tensor", f_jm)
                jm = smm.inc()

                def f_je(eng, jm=jm):
                    eng.wait_ge(smm.h, jm)
                    eng.activation(DBCb[:, 0:H], PS[6][:, 0:H], AF.Copy)
                    eng.activation(MBCb[:, 0:H], PS[6][:, 64:64 + H], AF.Copy)
                    eng.drain().then_inc(sev.h, 1)
                on("scalar", f_je)
                je = sev.inc()
                setguard(6, sev, je)

                def f_jv(eng, jp=jp, je=je, em=sed.n + 1):
                    eng.wait_ge(sev.h, je)
                    eng.tensor_tensor(S_sb[0:64, :].rearrange("p (h d) -> p h d", h=H),
                                      S_sb[0:64, :].rearrange("p (h d) -> p h d", h=H),
                                      DBCb[0:64, 0:H].unsqueeze(2).broadcast_to((64, H, 64)),
                                      ALU.mult)
                    eng.tensor_tensor(T2c[0:64, :].rearrange("p (h d) -> p h d", h=H),
                                      SJv[jp][0:64, :].rearrange("p (h d) -> p h d", h=H),
                                      MBCb[0:64, 0:H].unsqueeze(2).broadcast_to((64, H, 64)),
                                      ALU.mult).then_inc(sed.h, 1)
                    eng.wait_ge(sed.h, em)
                    eng.tensor_tensor(S_sb[0:64, :], S_sb[0:64, :], T2c[0:64, :], ALU.add)
                    eng.drain().then_inc(svv.h, 1)
                on("vector", f_jv)
                sed.inc()
                prev = svv.inc()
                jld[j] = prev

        def emit_gate_outproj_b(xprev, xprev_off, xdst):
            H = B_HEADS
            gb = emit_gate_outproj_b
            for h2 in range(2):
                for ci in range(8):
                    c = 8 * h2 + ci
                    c0 = c * CH
                    ysg = ytm_storeB[c]
                    def f_gl(eng, c=c, c0=c0, ysg=ysg, gp=smm.n, pe2=gb.pe2[0],
                             psg=gb.psg[0], sy_=syp[c % 2], ptr=gb.trg.get(c - 1, 0)):
                        eng.wait_ge(sy_.h, ysg)
                        eng.wait_ge(smm.h, gp)
                        eng.wait_ge(svv.h, pe2)
                        eng.wait_ge(sev.h, psg)
                        eng.wait_ge(strs.h, ptr)
                        eng.dma_start(out=YTg[:, :],
                                      in_=scr_y[c * 128:(c + 1) * 128, 0:B_INNER]).then_inc(glx.h, 16)
                        eng.dma_start_transpose(out=GZTb[:, :],
                                                in_=scr_z_b[0:B_INNER, c0:c0 + CH]).then_inc(glx.h, 16)
                        for kk in range(4):
                            eng.dma_start(
                                out=CFg[0:64, kk * 2048:(kk + 1) * 2048].rearrange("p (h j) -> p h j", h=16),
                                in_=scr_C_b.rearrange("(h n) t -> n h t", n=64)[:, kk * 16:(kk + 1) * 16,
                                                                               c0:c0 + CH]).then_inc(glx.h, 16)
                    on("sync", f_gl)
                    gl = glx.inc(16 * 6)

                    corr_v = []
                    for q in range(16):
                        bq = q % 2
                        gcm = gpop(bq)
                        def f_cm(eng, q=q, bq=bq, gl=gl, gcm=gcm):
                            eng.wait_ge(glx.h, gl)
                            gwait(eng, gcm)
                            for j in range(4):
                                h = 4 * q + j
                                eng.matmul(PS[bq][:, j * 64:(j + 1) * 64],
                                           CFg[0:64, h * 128:(h + 1) * 128],
                                           S_sb[0:64, h * 64:(h + 1) * 64], start=True, stop=True)
                            eng.drain().then_inc(smm.h, 1)
                        on("tensor", f_cm)
                        cm = smm.inc()

                        def f_cv(eng, q=q, bq=bq, cm=cm, c=c, em=sed.n + 1):
                            eng.wait_ge(smm.h, cm)
                            eng.tensor_tensor(TAP2[:, 0:256].rearrange("p (j i) -> p j i", j=4),
                                              PS[bq].ap()[:, 0:256].rearrange("p (j i) -> p j i", j=4),
                                              ECCSb[:, c * 64 + 4 * q:c * 64 + 4 * q + 4]
                                              .unsqueeze(2).broadcast_to((128, 4, 64)),
                                              ALU.mult).then_inc(sed.h, 1)
                            eng.wait_ge(sed.h, em)
                            eng.tensor_tensor(YTg[:, q * 256:(q + 1) * 256],
                                              YTg[:, q * 256:(q + 1) * 256], TAP2[:, 0:256], ALU.add)
                            eng.drain().then_inc(svv.h, 1)
                        on("vector", f_cv)
                        sed.inc()
                        cvt = svv.inc()
                        setguard(bq, svv, cvt)
                        corr_v.append(cvt)

                    cvl = corr_v[-1]
                    def f_sq(eng, cvl=cvl, pe2=gb.pe2[0]):
                        eng.wait_ge(svv.h, cvl)
                        eng.wait_ge(svv.h, pe2)
                        eng.activation(SQDb[:, :], YTg[:, :], AF.Square, accum_out=SSQb[:, 0:1])
                        eng.drain().then_inc(sev.h, 1)
                    on("scalar", f_sq)
                    sq_t = sev.inc()

                    def f_sg(eng, gl=gl, sq_t=sq_t, pe2=gb.pe2[0]):
                        eng.wait_ge(glx.h, gl)
                        eng.wait_ge(sev.h, sq_t)
                        eng.wait_ge(svv.h, pe2)
                        eng.activation(SILZb[:, :], GZTb[:, :], AF.Silu)
                        eng.drain().then_inc(sev.h, 1)
                    on("scalar", f_sg)
                    sg = sev.inc()
                    gb.psg[0] = sg

                    def f_rs1(eng, sq_t=sq_t):
                        eng.wait_ge(sev.h, sq_t)
                        eng.tensor_scalar(MSb[:, 0:1], SSQb[:, 0:1], 1.0 / B_INNER, None, ALU.mult)
                        eng.drain().then_inc(svv.h, 1)
                    on("vector", f_rs1)
                    rs1 = svv.inc()

                    def f_rs2(eng, rs1=rs1, ea=sea.n + 1):
                        eng.wait_ge(svv.h, rs1)
                        eng.activation(RSTDb[:, 0:1], MSb[:, 0:1], AF.Ln,
                                       bias=EPSBc[:, 0:1]).then_inc(sea.h, 1)
                        eng.wait_ge(sea.h, ea)
                        eng.activation(RSTDb[:, 0:1], RSTDb[:, 0:1], AF.Exp, scale=-0.5)
                        eng.drain().then_inc(sev.h, 1)
                    on("scalar", f_rs2)
                    sea.inc()
                    rs2 = sev.inc()

                    trg = gb.trg.get(c - 1, 0)
                    def f_e2(eng, rs2=rs2, sg=sg, trg=trg, em=sed.n + 1):
                        eng.wait_ge(sev.h, rs2)
                        eng.wait_ge(sev.h, sg)
                        eng.wait_ge(strs.h, trg)
                        eng.tensor_scalar(SQDb[:, :], YTg[:, :], RSTDb[:, 0:1], None,
                                          ALU.mult).then_inc(sed.h, 1)
                        eng.wait_ge(sed.h, em)
                        eng.tensor_tensor(GZTb[:, :], SQDb[:, :], SILZb[:, :], ALU.mult)
                        eng.drain().then_inc(svv.h, 1)
                    on("vector", f_e2)
                    sed.inc()
                    e2 = svv.inc()
                    gb.pe2[0] = e2

                    def f_trs(eng, e2=e2, ci=ci, opg=gb.opmm[0]):
                        eng.wait_ge(svv.h, e2)
                        eng.wait_ge(smm.h, opg)
                        for f in range(32):
                            eng.dma_start_transpose(
                                out=YFMh[:, f * 1024 + ci * 128:f * 1024 + ci * 128 + 128],
                                in_=GZTb[:, f * 128:(f + 1) * 128]).then_inc(strs.h, 16)
                    on("sync", f_trs)
                    gb.trg[c] = strs.inc(16 * 32)

                Bar()
                half_tr = gb.trg[8 * h2 + 7]
                mmd = {}
                for m in range(16):
                    p = m % 2
                    wfree = smm.n if m < 2 else mmd[m - 2]
                    ws2 = ldwp[p]
                    def f_wl(eng, p=p, m=m, wfree=wfree, htr=half_tr, ws2=ws2):
                        eng.wait_ge(smm.h, wfree)
                        eng.wait_ge(strs.h, htr)
                        src = wb_outT.rearrange("(k p) m -> p k m", p=128)[:, :, m * 128:(m + 1) * 128]
                        eng.dma_start(out=WTOv[p].rearrange("p (k j) -> p k j", k=32),
                                      in_=src).then_inc(ws2.h, 16)
                    on("gpsimd", f_wl)
                    wt = ws2.inc(16)

                    for tt in range(2):
                        bk = 2 + (m * 2 + tt) % 2
                        tok0 = h2 * 1024 + tt * 512
                        gom = gpop(bk)
                        def f_om(eng, p=p, bk=bk, tt=tt, wt=wt, half_tr=half_tr, ws2=ws2, gom=gom):
                            eng.wait_ge(ws2.h, wt)
                            eng.wait_ge(strs.h, half_tr)
                            gwait(eng, gom)
                            for k in range(32):
                                eng.matmul(PS[bk][:, 0:512],
                                           WTOv[p][:, k * 128:(k + 1) * 128],
                                           YFMh[:, k * 1024 + tt * 512:k * 1024 + tt * 512 + 512],
                                           start=(k == 0), stop=(k == 31))
                            eng.drain().then_inc(smm.h, 1)
                        on("tensor", f_om)
                        om = smm.inc()
                        gb.opmm[0] = om
                        if tt == 1:
                            mmd[m] = om

                        xp = (m * 2 + tt) % 2
                        xs_ = xlp[xp]
                        def f_xl(eng, m=m, tok0=tok0, xp=xp, xprev=xprev, xprev_off=xprev_off,
                                 xg=gb.xld.get(xp, 0), xs_=xs_):
                            eng.wait_ge(svv.h, xg)
                            eng.dma_start(out=XOLDb[xp][:, :],
                                          in_=xprev[m * 128:(m + 1) * 128,
                                                    xprev_off + tok0:xprev_off + tok0 + 512]).then_inc(xs_.h, 16)
                        on("sync", f_xl)
                        xl = xs_.inc(16)

                        def f_xv(eng, bk=bk, xp=xp, om=om, xl=xl, xg2=gb.xst.get(xp, 0),
                                 xs_=xs_, sy_=syp[xp]):
                            eng.wait_ge(smm.h, om)
                            eng.wait_ge(xs_.h, xl)
                            eng.wait_ge(sy_.h, xg2)
                            eng.tensor_tensor(XNEWb[xp][:, :], PS[bk][:, 0:512], XOLDb[xp][:, :], ALU.add)
                            eng.drain().then_inc(svv.h, 1)
                        on("vector", f_xv)
                        xv = svv.inc()
                        setguard(bk, svv, xv)
                        gb.xld[xp] = xv

                        def f_xs(eng, m=m, tok0=tok0, xp=xp, xv=xv, xdst=xdst):
                            eng.wait_ge(svv.h, xv)
                            eng.dma_start(out=xdst[m * 128:(m + 1) * 128, tok0:tok0 + 512],
                                          in_=XNEWb[xp][:, :]).then_inc(syp[xp].h, 16)
                        on("scalar", f_xs)
                        gb.xst[xp] = syp[xp].inc(16)
                Bar()
        emit_gate_outproj_b.trg = {}
        emit_gate_outproj_b.xld = {}
        emit_gate_outproj_b.xst = {}
        emit_gate_outproj_b.psg = [0]
        emit_gate_outproj_b.pe2 = [0]
        emit_gate_outproj_b.opmm = [0]

        emit_init()

        cur, cur_off = x_in, HALO
        for si, sl in enumerate(sublayers):
            last = si == len(sublayers) - 1
            dst = out_t if last else x_state
            if sl == "attn_a":
                ln_tts = [(0, 0, 128, x_in)] + [(HALO + j * 256, HALO + j * 256, 256, x_in)
                                                for j in range(8)]
                applies = emit_ln(TC, ln_tts)
                if CUT != "ln":
                    NM_CONV = int(os.environ.get("P_NMC", "34"))
                    NM_GATE = int(os.environ.get("P_NMG", "32"))
                    Mlist = []
                    for m in range(NM_CONV):
                        Mlist.append(dict(kind="conv", mcol0=A_INTER + m * 128, mw=128, conv=True,
                                          dst=scr_conv_a, drow0=m * 128, crow0=m * 128))
                    for m in range(NM_GATE):
                        Mlist.append(dict(kind="gate", mcol0=m * 128, mw=128, conv=False,
                                          dst=scr_gate_a, drow0=m * 128))
                    Mlist.append(dict(kind="dt", mcol0=A_INTER + A_CONVD, mw=32, conv=False,
                                      dst=scr_dt_a, drow0=0))
                    emit_inproj2(wa_inT, Mlist, TC, ln_tts, applies, conv_a_w, conv_a_b)
                    Bar()
                if CUT not in ("ln", "inproj"):
                    emit_mixer_a()
                    Bar()
                if CUT not in ("ln", "inproj", "mixer"):
                    emit_collective_a()
                    Bar()
                if CUT not in ("ln", "inproj", "mixer", "collective"):
                    emit_gate_outproj_a(cur, cur_off, dst)
            elif sl in ("ffn1", "ffn2"):
                if CUT != "full":
                    continue
                emit_ffn.xld = {}
                emit_ffn.xst = {}
                emit_ffn(cur, cur_off, dst)
            elif sl == "attn_b":
                if CUT != "full" and "attn_a" in sublayers:
                    continue
                emit_halo_b(cur, cur_off)
                Bar()
                ln_tts_b = [(0, 0, 128, x_halo_b)] + [(HALO + j * 256, cur_off + j * 256, 256, cur)
                                                      for j in range(8)]
                applies_b = emit_ln(TC, ln_tts_b)
                if CUT != "ln":
                    MlistB = []
                    for m in range(32):
                        MlistB.append(dict(kind="conv", mcol0=B_INNER + m * 128, mw=128, conv=True,
                                           dst=scr_conv_b, drow0=m * 128, crow0=m * 128))
                    for m in range(32):
                        MlistB.append(dict(kind="z", mcol0=m * 128, mw=128, conv=False,
                                           dst=scr_z_b, drow0=m * 128))
                    for m in range(32):
                        MlistB.append(dict(kind="B", mcol0=2 * B_INNER + m * 128, mw=128, conv=False,
                                           dst=scr_B_b, drow0=m * 128))
                    for m in range(32):
                        MlistB.append(dict(kind="C", mcol0=2 * B_INNER + 4096 + m * 128, mw=128,
                                           conv=False, dst=scr_C_b, drow0=m * 128))
                    MlistB.append(dict(kind="dt", mcol0=B_PROJ - B_HEADS, mw=64, conv=False,
                                       dst=scr_dt_b, drow0=0))
                    emit_inproj2(wb_inT, MlistB, TC, ln_tts_b, applies_b, conv_b_w, None)
                    Bar()
                if CUT not in ("ln", "inproj"):
                    emit_mixer_b()
                    Bar()
                if CUT not in ("ln", "inproj", "mixer"):
                    emit_collective_b()
                    Bar()
                if CUT not in ("ln", "inproj", "mixer", "collective"):
                    emit_gate_outproj_b(cur, cur_off, dst)
            cur, cur_off = dst, 0
            Bar()

        if os.environ.get("P_DBGB", "0") == "1":
            d_halo = nc.dram_tensor("d_halo", [D, HALO], F32, kind="ExternalOutput").ap()
            d_spre = nc.dram_tensor("d_spre", [64, 4096], BF16, kind="ExternalOutput").ap()
            d_ccin = nc.dram_tensor("d_ccin", [64, CCW_B], BF16, kind="ExternalOutput").ap()
            def f_dbgb(eng, b0=bar.n, tgt=scs.n + 16 * (NK + 2)):
                eng.wait_ge(bar.h, b0)
                for k in range(NK):
                    eng.dma_start(out=d_halo[k * 128:(k + 1) * 128, :],
                                  in_=x_halo_b[k * 128:(k + 1) * 128, :]).then_inc(scs.h, 16)
                eng.dma_start(out=d_spre[:, :], in_=S_sb[0:64, :]).then_inc(scs.h, 16)
                eng.dma_start(out=d_ccin[:, :], in_=cc_b_in[:, :]).then_inc(scs.h, 16)
                eng.wait_ge(scs.h, tgt)
            on("sync", f_dbgb)
            scs.inc(16 * (NK + 2))

        if os.environ.get("P_DBG", "0") == "1":
            dbg_conv = nc.dram_tensor("dbg_conv", [A_CONVD, T], BF16, kind="ExternalOutput").ap()
            dbg_gate = nc.dram_tensor("dbg_gate", [A_INTER, T], BF16, kind="ExternalOutput").ap()
            dbg_dt = nc.dram_tensor("dbg_dt", [A_HEADS, T], F32, kind="ExternalOutput").ap()
            dbg_y = nc.dram_tensor("dbg_y", [NCH * 128, A_INTER], BF16, kind="ExternalOutput").ap()
            def f_dbg(eng, b0=bar.n, tgt=scs.n + 16 * 4):
                eng.wait_ge(bar.h, b0)
                eng.dma_start(out=dbg_conv[:, :], in_=scr_conv_a[:, :]).then_inc(scs.h, 16)
                eng.dma_start(out=dbg_gate[:, :], in_=scr_gate_a[:, :]).then_inc(scs.h, 16)
                eng.dma_start(out=dbg_dt[:, :], in_=scr_dt_a[:, :]).then_inc(scs.h, 16)
                eng.dma_start(out=dbg_y[:, :], in_=scr_y[:, :]).then_inc(scs.h, 16)
                eng.wait_ge(scs.h, tgt)
            on("sync", f_dbg)
            scs.inc(16 * 4)

        if CUT != "full":
            def f_cp(eng, b0=bar.n, tgt=scs.n + 16 * NK):
                eng.wait_ge(bar.h, b0)
                for k in range(NK):
                    eng.dma_start(out=out_t[k * 128:(k + 1) * 128, :],
                                  in_=x_in[k * 128:(k + 1) * 128, HALO:HALO + T]).then_inc(scs.h, 16)
                eng.wait_ge(scs.h, tgt)
            on("sync", f_cp)
            scs.inc(16 * NK)

        fins = [(s_, s_.n) for s_ in (stw, syp[0], syp[1], stg, scs, strs)]
        def f_fin(eng, fins=fins):
            for s_, v in fins:
                eng.wait_ge(s_.h, v)
        on("sync", f_fin)

        with nc.Block() as block:
            block.sync(lambda eng: [f(eng) for f in Q["sync"]])
            block.tensor(lambda eng: [f(eng) for f in Q["tensor"]])
            block.vector(lambda eng: [f(eng) for f in Q["vector"]])
            block.scalar(lambda eng: [f(eng) for f in Q["scalar"]])
            block.gpsimd(lambda eng: [f(eng) for f in Q["gpsimd"]])
    return nc



def _bf(a):
    return np.ascontiguousarray(a).astype(ml_dtypes.bfloat16)


def make_host_inputs(inputs, sublayers=SUBLAYERS):
    x = np.asarray(inputs["L0_attn_a_hidden_states"], dtype=np.float32)
    shared = {
        "wa_inT": np.ascontiguousarray(np.asarray(inputs["L0_attn_a_in_proj_w"], np.float32).T),
        "wa_outT": np.ascontiguousarray(np.asarray(inputs["L0_attn_a_out_proj_w"], np.float32).T),
        "conv_a_w": np.ascontiguousarray(np.asarray(inputs["L0_attn_a_conv_w"], np.float32).reshape(A_CONVD, 4)),
        "conv_a_b": np.ascontiguousarray(np.asarray(inputs["L0_attn_a_conv_b"], np.float32).reshape(A_CONVD, 1)),
        "a_Alog": np.ascontiguousarray(np.asarray(inputs["L0_attn_a_A_log"], np.float32).reshape(A_HEADS, 1)),
        "a_dtb": np.ascontiguousarray(np.asarray(inputs["L0_attn_a_dt_bias"], np.float32).reshape(A_HEADS, 1)),
        "a_Drep": np.ascontiguousarray(np.repeat(np.asarray(inputs["L0_attn_a_D"], np.float32), A_HD).reshape(1, A_INTER)),
        "a_nw": np.ascontiguousarray(np.asarray(inputs["L0_attn_a_norm_w"], np.float32).reshape(1, A_INTER)),
        "eye_f": np.eye(128, dtype=np.float32),
        "ones_f_row": np.ones((1, 128), dtype=np.float32),
        "eye_bf": _bf(np.eye(128, dtype=np.float32)),
        "su_bf": _bf(np.triu(np.ones((128, 128), np.float32), 1)),
        "negbig_bf": _bf(np.eye(128, dtype=np.float32) * -1.0e30),
        "ones_bf_col": _bf(np.ones((128, 1), np.float32)),
        "wb_inT": np.ascontiguousarray(np.asarray(inputs["L0_attn_b_in_proj_w"], np.float32).T),
        "wb_outT": np.ascontiguousarray(np.asarray(inputs["L0_attn_b_out_proj_w"], np.float32).T),
        "conv_b_w": np.ascontiguousarray(np.asarray(inputs["L0_attn_b_conv_w"], np.float32).reshape(B_INNER, 4)),
        "b_Alog": np.ascontiguousarray(np.asarray(inputs["L0_attn_b_A_log"], np.float32).reshape(B_HEADS, 1)),
        "b_dtb": np.ascontiguousarray(np.asarray(inputs["L0_attn_b_dt_bias"], np.float32).reshape(B_HEADS, 1)),
        "wf_T": np.ascontiguousarray(np.asarray(inputs["L0_ffn_W"], np.float32).T),
        "f_biasT": np.ascontiguousarray(np.asarray(inputs["L0_ffn_bias"], np.float32).reshape(16, 128)),
        "f_rw1T": np.ascontiguousarray(np.asarray(inputs["L0_ffn_rw1"], np.float32).T),
        "f_rw2T": np.ascontiguousarray(np.asarray(inputs["L0_ffn_rw2"], np.float32).T),
        "f_rb1": np.ascontiguousarray(np.asarray(inputs["L0_ffn_rb1"], np.float32).reshape(64, 1)),
        "f_rb2": np.ascontiguousarray(np.asarray(inputs["L0_ffn_rb2"], np.float32).reshape(32, 1)),
        "f_gates": np.ascontiguousarray(np.asarray(inputs["L0_ffn_gates"], np.float32).reshape(16, 1)),
        "f_AT": np.ascontiguousarray(np.asarray(inputs["L0_ffn_A"], np.float32).reshape(256, D).T),
        "f_BT2": np.ascontiguousarray(np.asarray(inputs["L0_ffn_B"], np.float32).transpose(0, 2, 1).reshape(256, D)),
        "f_sel": _bf(np.repeat(np.eye(16, dtype=np.float32), 16, axis=1) * 0 +
                     np.kron(np.eye(16, dtype=np.float32), np.ones((1, 16), np.float32))),
    }
    in_maps = []
    for ci in range(8):
        b, seg = divmod(ci, 4)
        s0 = seg * T
        arr = np.zeros((TC, D), dtype=np.float32)
        arr[HALO:] = x[b, s0:s0 + T]
        if seg > 0:
            arr[:HALO] = x[b, s0 - HALO:s0]
        m = np.zeros(8, np.float32)
        m2 = np.zeros(8, np.float32)
        lo = 4 * (ci // 4)
        for j in range(8):
            if lo <= j < ci:
                m[j] = 1.0
                m2[j] = 1.0
        cm = np.concatenate([m, m2, 1.0 - m2]).reshape(1, 24).astype(np.float32)
        hsel = np.zeros((8, 1), np.float32)
        if ci % 4 != 0:
            hsel[ci - 1, 0] = 1.0
        d = dict(shared)
        d["x_in"] = np.ascontiguousarray(arr.T)
        d["cmask"] = cm
        d["hsel_in"] = hsel
        in_maps.append(d)
    return in_maps


_NC_CACHE = {}


def kernel(**inputs):
    sub = SUBLAYERS
    key = (sub, CUT)
    if key not in _NC_CACHE:
        _NC_CACHE[key] = build_nc(sub)
    nc_ = _NC_CACHE[key]
    in_maps = make_host_inputs(inputs, sub)
    res = run_bass_kernel_spmd(nc_, in_maps, core_ids=list(range(8)))
    out = np.zeros((2, 8192, D), dtype=np.float32)
    for ci in range(8):
        b, seg = divmod(ci, 4)
        out[b, seg * T:(seg + 1) * T] = np.asarray(res.results[ci]["out"]).T
    return out


# version 02
# speedup vs baseline: 1.0337x; 1.0337x over previous
import os
import numpy as np
import ml_dtypes
from contextlib import ExitStack
import concourse.bass as bass
import concourse.mybir as mybir
from concourse.bass_utils import run_bass_kernel_spmd

F32 = mybir.dt.float32
BF16 = mybir.dt.bfloat16
AF = mybir.ActivationFunctionType
ALU = mybir.AluOpType
AX = mybir.AxisListType
ENGS = ("sync", "tensor", "vector", "scalar", "gpsimd")

D = 2048
T = 2048
HALO = 128
TC = T + HALO
NK = D // 128
CH = 128
NCH = T // CH
LN_EPS = 1e-5

A_INTER, A_HEADS, A_HD, A_STATE = 4096, 32, 128, 128
A_CONVD = A_INTER + 2 * A_STATE
A_PROJ = A_INTER + A_CONVD + A_HEADS
A_RMS_EPS = 1e-5

B_INNER, B_HEADS, B_HD, B_STATE = 4096, 64, 64, 64
B_PROJ = 2 * B_INNER + 2 * B_HEADS * B_STATE + B_HEADS
B_SCALE = float(B_STATE) ** -0.5
B_RMS_EPS = 1.1920929e-07

SUBLAYERS = tuple(os.environ.get("P_SUB", "attn_a,ffn1,attn_b,ffn2").split(","))
CUT = "full"


class Sem:
    def __init__(self, h):
        self.h = h
        self.n = 0

    def inc(self, k=1):
        self.n += k
        return self.n


def build_nc(sublayers=SUBLAYERS):
    nc = bass.Bass("TRN2", target_bir_lowering=False, debug=False)
    Q = {e: [] for e in ENGS}

    def on(e, fn):
        Q[e].append(fn)

    def din(name, shape, dt=F32):
        return nc.dram_tensor(name, list(shape), dt, kind="ExternalInput").ap()

    def dint(name, shape, dt=F32):
        return nc.dram_tensor(name, list(shape), dt, kind="Internal").ap()

    x_in = din("x_in", (D, TC))
    out_t = nc.dram_tensor("out", [D, T], F32, kind="ExternalOutput").ap()
    x_state = dint("x_state", (D, T))

    wa_inT = din("wa_inT", (D, A_PROJ))
    wa_outT = din("wa_outT", (A_INTER, D))
    conv_a_w = din("conv_a_w", (A_CONVD, 4))
    conv_a_b = din("conv_a_b", (A_CONVD, 1))
    a_Alog = din("a_Alog", (A_HEADS, 1))
    a_dtb = din("a_dtb", (A_HEADS, 1))
    a_Drep = din("a_Drep", (1, A_INTER))
    a_nw = din("a_nw", (1, A_INTER))

    wb_inT = din("wb_inT", (D, B_PROJ))
    wb_outT = din("wb_outT", (B_INNER, D))
    conv_b_w = din("conv_b_w", (B_INNER, 4))
    b_Alog = din("b_Alog", (B_HEADS, 1))
    b_dtb = din("b_dtb", (B_HEADS, 1))
    hsel_in = din("hsel_in", (8, 1))

    wf_T = din("wf_T", (D, D))
    f_biasT = din("f_biasT", (16, 128))
    f_rw1T = din("f_rw1T", (D, 64))
    f_rw2T = din("f_rw2T", (64, 32))
    f_rb1 = din("f_rb1", (64, 1))
    f_rb2 = din("f_rb2", (32, 1))
    f_gates = din("f_gates", (16, 1))
    f_AT = din("f_AT", (D, 256))
    f_BT2 = din("f_BT2", (256, D))
    f_sel = din("f_sel", (16, 256), BF16)

    eye_f = din("eye_f", (128, 128))
    ones_f_row = din("ones_f_row", (1, 128))
    eye_bf = din("eye_bf", (128, 128), BF16)
    su_bf = din("su_bf", (128, 128), BF16)
    negbig_bf = din("negbig_bf", (128, 128), BF16)
    ones_bf_col = din("ones_bf_col", (128, 1), BF16)
    cmask = din("cmask", (1, 24))

    scr_conv_a = dint("scr_conv_a", (A_CONVD, T), BF16)
    scr_gate_a = dint("scr_gate_a", (A_INTER, T), BF16)
    scr_dt_a = dint("scr_dt_a", (A_HEADS, T))
    scr_y = dint("scr_y", (NCH * 128, A_INTER), BF16)
    scr_cum = dint("scr_cum", (4, 8192))
    scr_conv_b = dint("scr_conv_b", (B_INNER, T), BF16)
    scr_z_b = dint("scr_z_b", (B_INNER, T), BF16)
    scr_B_b = dint("scr_B_b", (B_HEADS * B_STATE, T), BF16)
    scr_C_b = dint("scr_C_b", (B_HEADS * B_STATE, T), BF16)
    scr_dt_b = dint("scr_dt_b", (B_HEADS, T))
    CCW_B = B_HEADS * B_HD + 64
    cc_b_in = dint("cc_b_in", (64, CCW_B), BF16)
    cc_b_out = dint("cc_b_out", (8 * 64, CCW_B), BF16)
    cc_h_in = dint("cc_h_in", (128, 64))
    cc_h_out = dint("cc_h_out", (8 * 128, 64))
    x_halo_b = dint("x_halo_b", (D, HALO))
    CCW_A = A_HEADS * A_HD + 64
    cc_a_in = dint("cc_a_in", (128, CCW_A), BF16)
    cc_a_out = dint("cc_a_out", (8 * 128, CCW_A), BF16)
    scr_dec = dint("scr_dec", (1, 128), BF16)

    with ExitStack() as ctx:
        AB_t = ctx.enter_context(nc.sbuf_tensor("AB", [128, 68864], BF16))
        AF_t = ctx.enter_context(nc.sbuf_tensor("AFL", [128, 9216], F32))
        XF_t = ctx.enter_context(nc.sbuf_tensor("XF2", [128, 2048], F32))

        ob = [0]
        def ab(n):
            r = AB_t[:, ob[0]:ob[0] + n]; ob[0] += n
            assert ob[0] <= 68864, ob[0]
            return r
        eye_b = ab(128); su_b = ab(128); negbig_b = ab(128); onescol_b = ab(2)
        dexp_b = ab(A_INTER); nw_b = ab(A_INTER)
        S_sb = ab(4096)
        ECCSb = ab(1024)
        DECb = ab(64); DROWb = ab(64)
        TAP2 = ab(512); TAP0 = ab(512); SS2 = ab(512)
        _pad = ab((16 - ob[0] % 16) % 16)
        R1 = ab(36864)
        R2 = ab(16384)

        ofl = [0]
        def af(n):
            r = AF_t[:, ofl[0]:ofl[0] + n]; ofl[0] += n
            assert ofl[0] <= 9216, ofl[0]
            return r
        eye_s = af(128); ones_row = af(128); cmask_s = af(24)
        RCC = af(4); ACOLf = af(2); DTBCf = af(2)
        EPSc = af(2); EPSBc = af(2)
        SCLc = af(2); RCCSf = af(2)
        SSQb = af(2); MSb = af(2); RSTDb = af(2)
        HSLf = af(2); ZEROc = af(128)
        FR = af(8704)

        lnb = R1[:, 0:34816]
        TAP3 = R1[:, 34816:35328]; TAP1 = R1[:, 35328:35840]
        COt = R1[:, 35840:36352]; GTT = R1[:, 36352:36864]
        XSTv = [R1[:, 0:4096], R1[:, 4096:8192]]
        YTMv = [R1[:, 8192:12288], R1[:, 12288:16384]]
        BDv = [R1[:, 16384:20480], R1[:, 20480:24576]]
        BTv = [R1[:, 24576:28672], R1[:, 28672:32768]]
        DIv = [R2[:, 2048:6144], R2[:, 6144:10240]]
        SJv = [R1[:, 0:4096], R1[:, 4096:8192]]
        T2c = R1[:, 8192:12288]
        YFMh = R1[:, 0:32768]
        YTg = R1[:, 32768:36864]
        XB = R2[:, 0:4096]; SQ = R2[:, 4096:8192]
        WTb = [R2[:, 8192:10240], R2[:, 10240:12288]]
        LTv = R2[:, 0:512]; MTv = R2[:, 512:1024]
        Gv = R2[:, 1024:1152]
        BFv = [R2[:, 1152:1280], R2[:, 1280:1408]]
        CFv = [R2[:, 1408:1536], R2[:, 1536:1664]]
        DTTMb = R2[:, 1664:1728]; DDTTMb = R2[:, 1728:1792]
        ECTMb = R2[:, 1792:1856]; DTOTb = R2[:, 1856:1920]
        DBCb = R2[:, 1920:1984]; MBCb = R2[:, 1984:2048]
        GXS = R2[:, 0:4096]; GGT = R2[:, 4096:8192]
        GB1 = R2[:, 8192:12288]; GB2 = R2[:, 12288:16384]
        WTOv = [R2[:, 8192:12288], R2[:, 12288:16384]]
        XSTvB = [R1[:, 0:4096], R1[:, 4096:8192]]
        BTvB = [R1[:, 8192:12288], R1[:, 12288:16384]]
        BDvB = [R1[:, 16384:20480], R1[:, 20480:24576]]
        YTMb = R1[:, 24576:28672]
        LTb = R2[:, 0:512]; MTb = R2[:, 512:1024]
        DIvB = [R2[:, 2048:6144], R2[:, 8192:12288]]
        BFq = [R2[:, 6144:6656], R2[:, 6656:7168]]
        CFq = [R2[:, 7168:7680], R2[:, 7680:8192]]
        GZTb = R2[:, 0:4096]
        CFgq = [R2[:, 4096:4608], R2[:, 4608:5120]]
        XOLDb = [XF_t[:, 0:512], XF_t[:, 512:1024]]
        XNEWb = [XF_t[:, 1024:1536], XF_t[:, 1536:2048]]
        XT = FR[:, 0:8192]
        SRr = FR[:, 8192:8704]
        DTRf = FR[:, 0:128]; Uf = FR[:, 128:256]; NAf = FR[:, 256:384]
        Ef = FR[:, 384:512]; MX2f = FR[:, 512:640]; DTf = FR[:, 640:768]
        LDTf = MX2f
        ADf = FR[:, 768:896]; CUMf = FR[:, 896:1024]; NCf = FR[:, 1024:1152]
        REVIf = FR[:, 1152:1280]; DTEf = FR[:, 1280:1408]; DDTf = FR[:, 1408:1536]
        ECf = FR[:, 1536:1664]; ECCf = FR[:, 1664:1792]; ONESf = FR[:, 1792:1920]
        TOTf = FR[:, 1920:1922]; TROWf = FR[:, 1922:1986]; DTILf = FR[:, 1986:2050]
        LTMPf = FR[:, 2056:2568]
        LdL = [FR[:, 2568:3080], FR[:, 3080:3592]]
        LdR = [FR[:, 3592:4104], FR[:, 4104:4616]]
        SSQf = FR[:, 4616:4618]; MSf = FR[:, 4618:4620]; RSTDf = FR[:, 4620:4622]
        XOLDf = [FR[:, 4624:5136], FR[:, 5136:5648]]
        XNEWf = [FR[:, 5648:6160], FR[:, 6160:6672]]
        SILZb = FR[:, 0:4096]
        SQDb = FR[:, 4096:8192]
        HSLT = FR[:, 0:8192]

        sem_names = ("lx0", "lx1", "ldw0", "ldw1", "mc0", "mc1", "lds0", "lds1",
                     "sj0", "sj1", "glx", "xl0", "xl1", "sy0", "sy1", "str_", "stw",
                     "stg", "scs", "mm", "ev", "vv", "pp", "cc", "bar", "sed", "sea")
        SM = {n: Sem(ctx.enter_context(nc.semaphore(n))) for n in sem_names}
        lx = [SM["lx0"], SM["lx1"]]
        ldwp = [SM["ldw0"], SM["ldw1"]]
        mc = [SM["mc0"], SM["mc1"]]
        ldsp = [SM["lds0"], SM["lds1"]]
        sj = [SM["sj0"], SM["sj1"]]
        glx = SM["glx"]
        xlp = [SM["xl0"], SM["xl1"]]
        syp = [SM["sy0"], SM["sy1"]]
        strs = SM["str_"]
        stw = SM["stw"]
        stg = SM["stg"]
        scs = SM["scs"]
        smm, sev, svv, spp, scc, bar = (SM[n] for n in ("mm", "ev", "vv", "pp", "cc", "bar"))
        sed = SM["sed"]; sea = SM["sea"]
        DRAIN = {
            "sync": [SM["lx0"], SM["lx1"], SM["mc0"], SM["mc1"], SM["sj0"], SM["sj1"],
                     glx, SM["xl0"], SM["xl1"], strs, scs],
            "scalar": [stw, SM["sy0"], SM["sy1"]],
            "gpsimd": [SM["ldw0"], SM["ldw1"], SM["lds0"], SM["lds1"], stg],
        }

        PS = [ctx.enter_context(nc.psum_tensor(f"ps{i}", [128, 512], F32)) for i in range(8)]
        psguard = {i: [] for i in range(8)}

        def gpop(*banks):
            out = []
            for b in banks:
                out.extend(psguard[b])
                psguard[b] = []
            return out

        def gwait(eng, gl):
            for s_, v in gl:
                eng.wait_ge(s_.h, v)

        def setguard(b, s_, v):
            psguard[b].append((s_, v))

        def Bar():
            tgt = bar.n + len(ENGS)
            bar.n = tgt
            for e in ENGS:
                drains = [(s_, s_.n) for s_ in DRAIN.get(e, [])]
                def f(eng, t=tgt, drains=drains):
                    for s_, v in drains:
                        eng.wait_ge(s_.h, v)
                    eng.drain().then_inc(bar.h, 1)
                    eng.wait_ge(bar.h, t)
                on(e, f)
            for b in range(8):
                psguard[b] = []

        init_vc = [0]

        def emit_init():
            def f_sync(eng, tgt=lx[0].n + 16 * 7):
                for dst, src in [(eye_s, eye_f), (ones_row[0:1, :], ones_f_row),
                                 (cmask_s[0:1, :], cmask),
                                 (eye_b, eye_bf), (su_b, su_bf), (negbig_b, negbig_bf),
                                 (onescol_b[:, 0:1], ones_bf_col)]:
                    eng.dma_start(out=dst, in_=src).then_inc(lx[0].h, 16)
                eng.wait_ge(lx[0].h, tgt)
            on("sync", f_sync)
            lx[0].inc(16 * 7)

            def f_g(eng, tgt=ldwp[0].n + 32):
                eng.dma_start(out=dexp_b, in_=a_Drep.partition_broadcast(128)).then_inc(ldwp[0].h, 16)
                eng.dma_start(out=nw_b, in_=a_nw.partition_broadcast(128)).then_inc(ldwp[0].h, 16)
                eng.wait_ge(ldwp[0].h, tgt)
            on("gpsimd", f_g)
            ldwp[0].inc(32)

            def f_vc(eng):
                eng.memset(EPSc[:, 0:1], 1e-5)
                eng.memset(EPSBc[:, 0:1], B_RMS_EPS)
                eng.memset(SCLc[:, 0:1], float(np.log(B_SCALE)))
                eng.memset(ZEROc[:, :], 0.0)
                eng.drain().then_inc(svv.h, 1)
            on("vector", f_vc)
            init_vc[0] = svv.inc()

            def f_hz0(eng, v0=init_vc[0], tgt=scs.n + 16 * NK):
                eng.wait_ge(svv.h, v0)
                for k in range(NK):
                    eng.dma_start(out=x_halo_b[k * 128:(k + 1) * 128, :],
                                  in_=ZEROc[:, :]).then_inc(scs.h, 16)
                eng.wait_ge(scs.h, tgt)
            on("sync", f_hz0)
            scs.inc(16 * NK)
            Bar()

        def emit_ln(lnw, tt_list, slotw=256, lnb_t=None, post_tt=None):
            if lnb_t is None:
                lnb_t = lnb
            applies = {}
            xtg = {}
            for it, (doff, soff, w, src) in enumerate(tt_list):
                xs = (it % 2) * 16 * slotw
                ap_prev = applies.get(it - 2, 0)
                xg_prev = xtg.get(it - 2, None)

                ls = lx[it % 2]
                def f_ld(eng, xs=xs, soff=soff, w=w, src=src, ap=ap_prev, ls=ls, slotw=slotw,
                         xg=xg_prev):
                    eng.wait_ge(svv.h, ap)
                    if xg is not None:
                        eng.wait_ge(xg[0].h, xg[1])
                    for k in range(NK):
                        eng.dma_start(out=XT[:, xs + k * slotw:xs + k * slotw + w],
                                      in_=src[k * 128:(k + 1) * 128, soff:soff + w]).then_inc(ls.h, 16)
                on("sync", f_ld)
                ld_t = ls.inc(16 * NK)

                mm_prev = smm.n
                def f_sc(eng, xs=xs, w=w, ld_t=ld_t, mm_prev=mm_prev, it=it, ls=ls, slotw=slotw):
                    eng.wait_ge(ls.h, ld_t)
                    if it > 0:
                        eng.wait_ge(smm.h, mm_prev)
                    for k in range(NK):
                        eng.activation(XB[:, k * slotw:k * slotw + w],
                                       XT[:, xs + k * slotw:xs + k * slotw + w], AF.Copy)
                    for k in range(NK):
                        eng.activation(SQ[:, k * slotw:k * slotw + w],
                                       XT[:, xs + k * slotw:xs + k * slotw + w], AF.Square)
                    eng.drain().then_inc(sev.h, 1)
                on("scalar", f_sc)
                ev_t = sev.inc()

                g45 = gpop(4, 5)
                def f_t1(eng, w=w, ev_t=ev_t, g45=g45, slotw=slotw):
                    eng.wait_ge(sev.h, ev_t)
                    gwait(eng, g45)
                    for k in range(NK):
                        eng.matmul(PS[4][0:1, 0:w], onescol_b[:, 0:1], XB[:, k * slotw:k * slotw + w],
                                   start=(k == 0), stop=(k == NK - 1))
                    for k in range(NK):
                        eng.matmul(PS[5][0:1, 0:w], onescol_b[:, 0:1], SQ[:, k * slotw:k * slotw + w],
                                   start=(k == 0), stop=(k == NK - 1))
                    eng.drain().then_inc(smm.h, 1)
                on("tensor", f_t1)
                mm_t = smm.inc()

                def f_v1(eng, w=w, mm_t=mm_t):
                    eng.wait_ge(smm.h, mm_t)
                    eng.tensor_scalar(SRr[0:1, 0:w], PS[4][0:1, 0:w], 1.0 / D, None, ALU.mult)
                    eng.tensor_scalar(SRr[0:1, 256:256 + w], PS[5][0:1, 0:w], 1.0 / D, None, ALU.mult)
                    eng.drain().then_inc(svv.h, 1)
                on("vector", f_v1)
                v1_t = svv.inc()

                g6 = gpop(6)
                def f_t2(eng, w=w, v1_t=v1_t, g6=g6):
                    eng.wait_ge(svv.h, v1_t)
                    gwait(eng, g6)
                    eng.matmul(PS[6][:, 0:w], ones_row[0:1, :], SRr[0:1, 0:w], start=True, stop=True)
                    eng.drain().then_inc(smm.h, 1)
                on("tensor", f_t2)
                t2_t = smm.inc()

                def f_v2(eng, w=w, v1_t=v1_t, vmid=svv.n + 1):
                    eng.wait_ge(svv.h, v1_t)
                    eng.tensor_tensor(PS[4][0:1, 0:w], SRr[0:1, 0:w], SRr[0:1, 0:w],
                                      ALU.mult).then_inc(svv.h, 1)
                    eng.wait_ge(svv.h, vmid)
                    eng.tensor_tensor(SRr[0:1, 256:256 + w], SRr[0:1, 256:256 + w],
                                      PS[4][0:1, 0:w], ALU.subtract)
                    eng.drain().then_inc(svv.h, 1)
                on("vector", f_v2)
                svv.inc()
                v2_t = svv.inc()
                setguard(4, svv, v2_t)

                def f_s2(eng, w=w, v2_t=v2_t, ea=sea.n + 1):
                    eng.wait_ge(svv.h, v2_t)
                    eng.activation(SRr[0:1, 256:256 + w], SRr[0:1, 256:256 + w], AF.Ln,
                                   bias=EPSc[0:1, 0:1]).then_inc(sea.h, 1)
                    eng.wait_ge(sea.h, ea)
                    eng.activation(SRr[0:1, 256:256 + w], SRr[0:1, 256:256 + w], AF.Exp, scale=-0.5)
                    eng.drain().then_inc(sev.h, 1)
                on("scalar", f_s2)
                sea.inc()
                s2_t = sev.inc()

                g7 = gpop(7)
                def f_t3(eng, w=w, s2_t=s2_t, g7=g7):
                    eng.wait_ge(sev.h, s2_t)
                    gwait(eng, g7)
                    eng.matmul(PS[7][:, 0:w], ones_row[0:1, :], SRr[0:1, 256:256 + w],
                               start=True, stop=True)
                    eng.drain().then_inc(smm.h, 1)
                on("tensor", f_t3)
                t3_t = smm.inc()

                def f_v3(eng, xs=xs, doff=doff, w=w, t2_t=t2_t, t3_t=t3_t, lnw=lnw,
                         em=sed.n + 1, slotw=slotw, lnb_t=lnb_t):
                    eng.wait_ge(smm.h, t3_t)
                    for k in range(NK):
                        xt = XT[:, xs + k * slotw:xs + k * slotw + w]
                        ins = eng.tensor_tensor(xt, xt, PS[6][:, 0:w], ALU.subtract)
                    ins.then_inc(sed.h, 1)
                    eng.wait_ge(sed.h, em)
                    for k in range(NK):
                        xt = XT[:, xs + k * slotw:xs + k * slotw + w]
                        eng.tensor_tensor(lnb_t[:, k * lnw + doff:k * lnw + doff + w],
                                          xt, PS[7][:, 0:w], ALU.mult)
                    eng.drain().then_inc(svv.h, 1)
                on("vector", f_v3)
                sed.inc()
                av = svv.inc()
                applies[it] = av
                setguard(6, svv, av)
                setguard(7, svv, av)
                if post_tt is not None:
                    xtg[it] = post_tt(it, xs, doff, w, av)
            return applies

        CWp = af(20)

        def emit_inproj2(wT, Mlist, lnw, ln_tts, applies, convw_dram, convb_dram):
            def ln_wait(o, w):
                tgt = 0
                for idx, (doff, _s, ww, _src) in enumerate(ln_tts):
                    if doff < o + w and o < doff + ww:
                        tgt = max(tgt, applies[idx])
                return tgt

            mmdone = {}
            st_guard = [0]
            emit_inproj2.ptv = [0]
            halo_ln = lnw == TC
            for mi, M in enumerate(Mlist):
                p = mi % 2
                mcol0, mw = M["mcol0"], M["mw"]
                wfree = smm.n if mi < 2 else mmdone[mi - 2]
                ws = ldwp[p]
                def f_gw(eng, p=p, mcol0=mcol0, mw=mw, wfree=wfree, M=M, ws=ws):
                    eng.wait_ge(smm.h, wfree)
                    src = wT.rearrange("(k p) m -> p k m", p=128)[:, :, mcol0:mcol0 + mw]
                    eng.dma_start(out=WTb[p].rearrange("p (k j) -> p k j", k=16)[:, :, 0:mw],
                                  in_=src).then_inc(ws.h, 16)
                    if M["conv"]:
                        eng.dma_start(out=CWp[:, p * 10:p * 10 + 4],
                                      in_=convw_dram[M["crow0"]:M["crow0"] + 128, :]).then_inc(ws.h, 16)
                        if convb_dram is not None:
                            eng.dma_start(out=CWp[:, p * 10 + 4:p * 10 + 5],
                                          in_=convb_dram[M["crow0"]:M["crow0"] + 128, :]).then_inc(ws.h, 16)
                on("gpsimd", f_gw)
                nw_dma = 1 + ((2 if convb_dram is not None else 1) if M["conv"] else 0)
                w_t = ws.inc(16 * nw_dma)

                tts = ([("halo", 0, -1, 128)] if (M["conv"] and halo_ln) else [])
                off0 = HALO if halo_ln else 0
                tts += [("real", off0 + i * 512, i * 512, 512) for i in range(4)]
                prev_bank = None
                prev_w = None
                for ti, (knd, lno, so, w) in enumerate(tts):
                    bk = ti % 3
                    lnv = ln_wait(lno, w)
                    gb = gpop(bk)
                    def f_mm(eng, p=p, mw=mw, lno=lno, w=w, bk=bk, w_t=w_t, lnv=lnv, lnw=lnw, ws=ws, gb=gb):
                        eng.wait_ge(ws.h, w_t)
                        eng.wait_ge(svv.h, lnv)
                        gwait(eng, gb)
                        for k in range(NK):
                            eng.matmul(PS[bk][0:mw, 0:w],
                                       WTb[p][:, k * 128:k * 128 + mw],
                                       lnb[:, k * lnw + lno:k * lnw + lno + w],
                                       start=(k == 0), stop=(k == NK - 1))
                        eng.drain().then_inc(smm.h, 1)
                    on("tensor", f_mm)
                    g_t = smm.inc()
                    if ti == len(tts) - 1:
                        mmdone[mi] = g_t
                    if knd == "halo":
                        prev_bank = bk
                        prev_w = w
                        setguard(bk, smm, g_t)
                        continue

                    if M["conv"]:
                        pb = prev_bank
                        cw = CWp[:, p * 10:p * 10 + 4]
                        cb = CWp[:, p * 10 + 4:p * 10 + 5] if convb_dram is not None else None
                        fr = pb is None
                        pw = prev_w
                        def f_ta(eng, b=bk, pb=pb, g_t=g_t, cw=cw, fr=fr, gg=st_guard[0], pw=pw,
                                 ptv=emit_inproj2.ptv[0]):
                            eng.wait_ge(smm.h, g_t)
                            eng.wait_ge(stw.h, gg)
                            eng.wait_ge(svv.h, ptv)
                            eng.activation(TAP3[:, 0:512], PS[b][:, 0:512], AF.Copy, scale=cw[:, 3:4])
                            eng.activation(TAP1[:, 2:512], PS[b][:, 0:510], AF.Copy, scale=cw[:, 1:2])
                            if fr:
                                eng.memset(TAP1[:, 0:2], 0.0)
                            else:
                                eng.activation(TAP1[:, 0:2], PS[pb][:, pw - 2:pw], AF.Copy, scale=cw[:, 1:2])
                            eng.drain().then_inc(sev.h, 1)
                        on("scalar", f_ta)
                        ta = sev.inc()

                        def f_tv(eng, b=bk, pb=pb, g_t=g_t, cw=cw, fr=fr, ppg=spp.n, pw=pw, ta=ta):
                            eng.wait_ge(smm.h, g_t)
                            eng.wait_ge(spp.h, ppg)
                            eng.wait_ge(sev.h, ta)
                            eng.tensor_scalar(TAP2[:, 1:512], PS[b][:, 0:511], cw[:, 2:3], None, ALU.mult)
                            eng.tensor_scalar(TAP0[:, 3:512], PS[b][:, 0:509], cw[:, 0:1], None, ALU.mult)
                            if fr:
                                eng.memset(TAP2[:, 0:1], 0.0)
                                eng.memset(TAP0[:, 0:3], 0.0)
                            else:
                                eng.tensor_scalar(TAP2[:, 0:1], PS[pb][:, pw - 1:pw], cw[:, 2:3], None, ALU.mult)
                                eng.tensor_scalar(TAP0[:, 0:3], PS[pb][:, pw - 3:pw], cw[:, 0:1], None, ALU.mult)
                            eng.drain().then_inc(svv.h, 1)
                        on("vector", f_tv)
                        tv = svv.inc()
                        emit_inproj2.ptv[0] = tv
                        setguard(bk, sev, ta); setguard(bk, svv, tv)
                        if pb is not None:
                            setguard(pb, sev, ta); setguard(pb, svv, tv)

                        def f_pp(eng, ta=ta, tv=tv):
                            eng.wait_ge(sev.h, ta)
                            eng.wait_ge(svv.h, tv)
                            eng.tensor_tensor(SS2[:, 0:512], TAP1[:, 0:512], TAP0[:, 0:512], ALU.add)
                            eng.drain().then_inc(spp.h, 1)
                        on("gpsimd", f_pp)
                        pp_t = spp.inc()

                        def f_av(eng, ta=ta, tv=tv, pp_t=pp_t, em=sed.n + 1):
                            eng.wait_ge(sev.h, ta)
                            eng.tensor_tensor(COt[:, 0:512], TAP3[:, 0:512], TAP2[:, 0:512],
                                              ALU.add).then_inc(sed.h, 1)
                            eng.wait_ge(spp.h, pp_t)
                            eng.wait_ge(sed.h, em)
                            eng.tensor_tensor(COt[:, 0:512], COt[:, 0:512], SS2[:, 0:512], ALU.add)
                            eng.drain().then_inc(svv.h, 1)
                        on("vector", f_av)
                        sed.inc()
                        cv = svv.inc()

                        def f_si(eng, cv=cv, cb=cb, gg=st_guard[0]):
                            eng.wait_ge(svv.h, cv)
                            eng.wait_ge(stw.h, gg)
                            if cb is not None:
                                eng.activation(GTT[:, 0:512], COt[:, 0:512], AF.Silu, bias=cb)
                            else:
                                eng.activation(GTT[:, 0:512], COt[:, 0:512], AF.Silu)
                            eng.drain().then_inc(sev.h, 1)
                        on("scalar", f_si)
                        si = sev.inc()

                        def f_st(eng, M=M, so=so, si=si):
                            eng.wait_ge(sev.h, si)
                            eng.dma_start(out=M["dst"][M["drow0"]:M["drow0"] + 128, so:so + 512],
                                          in_=GTT[:, 0:512]).then_inc(stw.h, 16)
                        on("scalar", f_st)
                        st_guard[0] = stw.inc(16)
                    else:
                        isdt = M["kind"] == "dt"
                        def f_ev(eng, b=bk, g_t=g_t, mw=mw, isdt=isdt, gg=st_guard[0]):
                            eng.wait_ge(smm.h, g_t)
                            eng.wait_ge(stw.h, gg)
                            if isdt:
                                eng.activation(XT[0:mw, 0:512], PS[b][0:mw, 0:512], AF.Copy)
                            else:
                                eng.activation(GTT[:, 0:512], PS[b][:, 0:512], AF.Copy)
                            eng.drain().then_inc(sev.h, 1)
                        on("scalar", f_ev)
                        e_t = sev.inc()
                        setguard(bk, sev, e_t)

                        def f_st2(eng, M=M, so=so, e_t=e_t, mw=mw, isdt=isdt):
                            eng.wait_ge(sev.h, e_t)
                            src = XT[0:mw, 0:512] if isdt else GTT[:, 0:512]
                            eng.dma_start(out=M["dst"][M["drow0"]:M["drow0"] + mw, so:so + 512],
                                          in_=src).then_inc(stw.h, 16)
                        on("scalar", f_st2)
                        st_guard[0] = stw.inc(16)
                    prev_bank = bk
                    prev_w = w

        sevac = {}
        chunk_mm_done = {}
        qmm_ld = {}
        chunk_ldq = {}
        ytm_store = {}
        mixer_pq = [0, 0]

        def emit_mixer_a():
            H, HD = A_HEADS, A_HD
            HP = H * HD
            nq = H // 4

            def f_ld0(eng):
                eng.dma_start(out=DTRf[0:H, 0:1], in_=a_Alog[:, :]).then_inc(lx[0].h, 16)
                eng.dma_start(out=DTBCf[0:H, 0:1], in_=a_dtb[:, :]).then_inc(lx[0].h, 16)
            on("sync", f_ld0)
            l0 = lx[0].inc(32)

            def f_i_s(eng, l0=l0):
                eng.wait_ge(lx[0].h, l0)
                eng.activation(ACOLf[0:H, 0:1], DTRf[0:H, 0:1], AF.Exp)
                eng.drain().then_inc(sev.h, 1)
            on("scalar", f_i_s)
            e0 = sev.inc()

            def f_i_v(eng, e0=e0):
                eng.wait_ge(sev.h, e0)
                eng.tensor_scalar(ACOLf[0:H, 0:1], ACOLf[0:H, 0:1], -1.0, None, ALU.mult)
                eng.memset(RCC[:, 0:4], 0.0)
                eng.memset(S_sb[:, :], 0.0)
                eng.memset(ONESf[0:H, :], 1.0)
                for pr in range(2):
                    eng.memset(LdL[pr][0:2, 0:512], 1.0)
                    eng.memset(LdR[pr][0:2, 0:512], 1.0)
                eng.drain().then_inc(svv.h, 1)
            on("vector", f_i_v)
            v0 = svv.inc()

            dtv2_of = {}
            g1_of = {}
            for c in range(NCH):
                cp = c % 2
                c0 = c * CH
                g_prev = smm.n if c < 2 else chunk_mm_done[c - 2]
                ms_ = mc[cp]
                def f_cin(eng, cp=cp, c0=c0, g_prev=g_prev, v0=v0, ms_=ms_,
                          dtg=dtv2_of.get(c - 1, 0)):
                    eng.wait_ge(svv.h, v0)
                    eng.wait_ge(smm.h, g_prev)
                    eng.wait_ge(svv.h, dtg)
                    eng.dma_start_transpose(out=XSTv[cp][:, :],
                                            in_=scr_conv_a[0:HP, c0:c0 + CH]).then_inc(ms_.h, 16)
                    eng.dma_start(out=BFv[cp][:, :],
                                  in_=scr_conv_a[A_INTER:A_INTER + 128, c0:c0 + CH]).then_inc(ms_.h, 16)
                    eng.dma_start(out=CFv[cp][:, :],
                                  in_=scr_conv_a[A_INTER + 128:A_INTER + 256, c0:c0 + CH]).then_inc(ms_.h, 16)
                    eng.dma_start_transpose(out=BTv[cp][:, 0:128],
                                            in_=scr_conv_a[A_INTER:A_INTER + 128, c0:c0 + CH]).then_inc(ms_.h, 16)
                    eng.dma_start(out=DTRf[0:H, 0:128],
                                  in_=scr_dt_a[0:H, c0:c0 + CH]).then_inc(ms_.h, 16)
                on("sync", f_cin)
                cin_t = ms_.inc(16 * 5)

                def f_dtv1(eng, cin_t=cin_t, ms_=ms_, e0=sed.n + 1, e1=sed.n + 2):
                    eng.wait_ge(ms_.h, cin_t)
                    eng.tensor_scalar(Uf[0:H, :], DTRf[0:H, 0:128], DTBCf[0:H, 0:1], None,
                                      ALU.add).then_inc(sed.h, 1)
                    eng.wait_ge(sed.h, e0)
                    eng.tensor_scalar(NAf[0:H, :], Uf[0:H, :], -1.0, None,
                                      ALU.mult).then_inc(sed.h, 1)
                    eng.wait_ge(sed.h, e1)
                    eng.tensor_tensor(NAf[0:H, :], Uf[0:H, :], NAf[0:H, :], ALU.min)
                    eng.drain().then_inc(svv.h, 1)
                on("vector", f_dtv1)
                sed.inc(2)
                t1 = svv.inc()

                def f_dts1(eng, t1=t1, ea=sea.n + 1):
                    eng.wait_ge(svv.h, t1)
                    eng.activation(Ef[0:H, :], NAf[0:H, :], AF.Exp).then_inc(sea.h, 1)
                    eng.wait_ge(sea.h, ea)
                    eng.activation(Ef[0:H, :], Ef[0:H, :], AF.Ln, bias=1.0)
                    eng.drain().then_inc(sev.h, 1)
                on("scalar", f_dts1)
                sea.inc()
                t2 = sev.inc()

                def f_dtv2(eng, t2=t2, cp=cp, eb=sed.n):
                    eng.wait_ge(sev.h, t2)
                    eng.tensor_scalar(MX2f[0:H, :], Uf[0:H, :], 0.0, None,
                                      ALU.max).then_inc(sed.h, 1)
                    eng.wait_ge(sed.h, eb + 1)
                    eng.tensor_tensor(DTf[0:H, :], MX2f[0:H, :], Ef[0:H, :],
                                      ALU.add).then_inc(sed.h, 1)
                    eng.wait_ge(sed.h, eb + 2)
                    eng.tensor_scalar(DTf[0:H, :], DTf[0:H, :], 100.0, 0.0, ALU.min,
                                      ALU.max).then_inc(sed.h, 1)
                    eng.wait_ge(sed.h, eb + 3)
                    eng.tensor_scalar(ADf[0:H, :], DTf[0:H, :], ACOLf[0:H, 0:1], None,
                                      ALU.mult).then_inc(sed.h, 1)
                    eng.wait_ge(sed.h, eb + 4)
                    eng.tensor_tensor_scan(REVIf[0:H, :][:, ::-1], ONESf[0:H, :],
                                           ADf[0:H, :][:, ::-1], 0.0, ALU.mult, ALU.add)
                    eng.tensor_tensor_scan(CUMf[0:H, :], ONESf[0:H, :], ADf[0:H, :],
                                           0.0, ALU.mult, ALU.add).then_inc(sed.h, 1)
                    eng.wait_ge(sed.h, eb + 5)
                    eng.tensor_tensor(DTEf[0:H, :], REVIf[0:H, :], ADf[0:H, :], ALU.subtract)
                    eng.tensor_reduce(TOTf[0:H, 0:1], ADf[0:H, :], AX.X,
                                      ALU.add).then_inc(sed.h, 1)
                    eng.wait_ge(sed.h, eb + 6)
                    eng.tensor_tensor(RCC[0:H, (1 - cp):(2 - cp)], RCC[0:H, cp:cp + 1],
                                      TOTf[0:H, 0:1], ALU.add)
                    eng.drain().then_inc(svv.h, 1)
                on("vector", f_dtv2)
                sed.inc(6)
                t3 = svv.inc()
                dtv2_of[c] = t3

                def f_dts2(eng, t3=t3, cp=cp):
                    eng.wait_ge(svv.h, t3)
                    eng.activation(DTEf[0:H, :], DTEf[0:H, :], AF.Exp)
                    eng.activation(LDTf[0:H, :], DTf[0:H, :], AF.Ln)
                    eng.activation(ECf[0:H, :], CUMf[0:H, :], AF.Exp)
                    eng.activation(ECCf[0:H, :], CUMf[0:H, :], AF.Exp, bias=RCC[0:H, cp:cp + 1])
                    eng.drain().then_inc(sev.h, 1)
                on("scalar", f_dts2)
                t4 = sev.inc()

                def f_dtv3(eng, t4=t4, em=sed.n + 1):
                    eng.wait_ge(sev.h, t4)
                    eng.tensor_tensor(DDTf[0:H, :], DTf[0:H, :], DTEf[0:H, :], ALU.mult)
                    eng.tensor_tensor(NCf[0:H, :], LDTf[0:H, :], CUMf[0:H, :], ALU.subtract)
                    eng.drain().then_inc(svv.h, 1)
                on("vector", f_dtv3)
                t5 = svv.inc()

                def f_g1(eng, cp=cp, t5=t5, ldg=chunk_ldq.get(c - 2, ())):
                    eng.wait_ge(svv.h, t5)
                    for s_, v_ in ldg:
                        eng.wait_ge(s_.h, v_)
                    eng.dma_start(out=scr_cum[2 * cp, 0:H * 128].rearrange("(h l) -> h l", h=H),
                                  in_=CUMf[0:H, :]).then_inc(stg.h, 16)
                    eng.dma_start(out=scr_cum[2 * cp + 1, 0:H * 128].rearrange("(h l) -> h l", h=H),
                                  in_=NCf[0:H, :]).then_inc(stg.h, 16)
                on("gpsimd", f_g1)
                st1 = stg.inc(32)
                g1_of[c] = st1

                g76 = gpop(7, 6)
                def f_tr(eng, t5=t5, g76=g76):
                    eng.wait_ge(svv.h, t5)
                    gwait(eng, g76)
                    eng.transpose(PS[7][:, 128:128 + H], DDTf[0:H, :], eye_s[0:H, 0:H])
                    eng.transpose(PS[7][:, 256:256 + H], ECf[0:H, :], eye_s[0:H, 0:H])
                    eng.transpose(PS[7][:, 384:384 + H], ECCf[0:H, :], eye_s[0:H, 0:H])
                    eng.transpose(PS[6][0:1, 0:H], TOTf[0:H, 0:1], eye_s[0:H, 0:H])
                    eng.drain().then_inc(smm.h, 1)
                on("tensor", f_tr)
                tr_t = smm.inc()

                def f_tre(eng, tr_t=tr_t, c=c, pqy=mixer_pq[0]):
                    eng.wait_ge(smm.h, tr_t)
                    eng.wait_ge(svv.h, pqy)
                    eng.activation(DDTTMb[:, 0:H], PS[7][:, 128:128 + H], AF.Copy)
                    eng.activation(ECTMb[:, 0:H], PS[7][:, 256:256 + H], AF.Copy)
                    eng.activation(ECCSb[:, c * 64:c * 64 + H], PS[7][:, 384:384 + H], AF.Copy)
                    eng.activation(TROWf[0:1, 0:H], PS[6][0:1, 0:H], AF.Copy)
                    eng.drain().then_inc(sev.h, 1)
                on("scalar", f_tre)
                tre_t = sev.inc()
                setguard(7, sev, tre_t)

                def f_dtm(eng, tre_t=tre_t):
                    eng.wait_ge(sev.h, tre_t)
                    eng.matmul(PS[6][:, 64:64 + H], ones_row[0:1, :], TROWf[0:1, 0:H],
                               start=True, stop=True)
                    eng.drain().then_inc(smm.h, 1)
                on("tensor", f_dtm)
                dtm_t = smm.inc()

                def f_dte(eng, dtm_t=dtm_t):
                    eng.wait_ge(smm.h, dtm_t)
                    eng.activation(DTOTb[:, 0:H], PS[6][:, 64:64 + H], AF.Exp)
                    eng.drain().then_inc(sev.h, 1)
                on("scalar", f_dte)
                dte_t = sev.inc()
                setguard(6, sev, dte_t)

                big_prev = smm.n if c < 2 else chunk_mm_done[c - 2]
                def f_big(eng, cp=cp, cin_t=cin_t, dte_t=dte_t, big_prev=big_prev, ms_=ms_):
                    eng.wait_ge(ms_.h, cin_t)
                    eng.wait_ge(sev.h, dte_t)
                    eng.wait_ge(smm.h, big_prev)
                    eng.tensor_tensor(BDv[cp].rearrange("p (h d) -> p h d", h=H),
                                      BTv[cp][:, 0:128].unsqueeze(1).broadcast_to((128, H, 128)),
                                      DDTTMb[:, 0:H].unsqueeze(2).broadcast_to((128, H, 128)),
                                      ALU.mult)
                    eng.tensor_tensor(DIv[cp].rearrange("p (h d) -> p h d", h=H),
                                      eye_b.unsqueeze(1).broadcast_to((128, H, 128)),
                                      DTOTb[:, 0:H].unsqueeze(2).broadcast_to((128, H, 128)),
                                      ALU.mult)
                    eng.drain().then_inc(svv.h, 1)
                on("vector", f_big)
                big_t = svv.inc()

                g6m = gpop(6)
                def f_gmm(eng, cp=cp, cin_t=cin_t, ms_=ms_, g6m=g6m):
                    eng.wait_ge(ms_.h, cin_t)
                    gwait(eng, g6m)
                    eng.matmul(PS[6][:, 128:256], BFv[cp][:, :], CFv[cp][:, :], start=True, stop=True)
                    eng.drain().then_inc(smm.h, 1)
                on("tensor", f_gmm)
                gm_t = smm.inc()

                def f_gev(eng, gm_t=gm_t):
                    eng.wait_ge(smm.h, gm_t)
                    eng.activation(Gv[:, :], PS[6][:, 128:256], AF.Copy)
                    eng.drain().then_inc(sev.h, 1)
                on("scalar", f_gev)
                ge_t = sev.inc()
                setguard(6, sev, ge_t)

                for q in range(nq):
                    qp = q % 2
                    bL, bY, bO = qp, 2 + qp, 4 + qp

                    ld_guard = qmm_ld.get((c, q - 2), qmm_ld.get((c - 1, nq - 2 + q), 0))
                    qs_ = ldsp[qp]
                    def f_gq(eng, cp=cp, q=q, qp=qp, st1=st1, ld_guard=ld_guard, qs_=qs_):
                        eng.wait_ge(stg.h, st1)
                        eng.wait_ge(smm.h, ld_guard)
                        eng.dma_start(out=LdR[qp][0:1, 0:512],
                                      in_=scr_cum[2 * cp, q * 512:(q + 1) * 512][None, :]).then_inc(qs_.h, 16)
                        eng.dma_start(out=LdL[qp][1:2, 0:512],
                                      in_=scr_cum[2 * cp + 1, q * 512:(q + 1) * 512][None, :]).then_inc(qs_.h, 16)
                    on("gpsimd", f_gq)
                    ldq = qs_.inc(32)

                    gqL = gpop(bL)
                    def f_qL(eng, qp=qp, bL=bL, ldq=ldq, v0=v0, qs_=qs_, gqL=gqL):
                        eng.wait_ge(qs_.h, ldq)
                        eng.wait_ge(svv.h, v0)
                        gwait(eng, gqL)
                        for j in range(4):
                            eng.matmul(PS[bL][:, j * 128:(j + 1) * 128], su_b, negbig_b,
                                       start=True, stop=False)
                            eng.matmul(PS[bL][:, j * 128:(j + 1) * 128],
                                       LdL[qp][0:2, j * 128:(j + 1) * 128],
                                       LdR[qp][0:2, j * 128:(j + 1) * 128],
                                       start=False, stop=True)
                        eng.drain().then_inc(smm.h, 1)
                    on("tensor", f_qL)
                    qL = smm.inc()
                    qmm_ld[(c, q)] = qL

                    def f_qa(eng, bL=bL, qL=qL, pqy=mixer_pq[0]):
                        eng.wait_ge(smm.h, qL)
                        eng.wait_ge(svv.h, pqy)
                        eng.activation(LTv[:, 0:512], PS[bL][:, 0:512], AF.Exp)
                        eng.drain().then_inc(sev.h, 1)
                    on("scalar", f_qa)
                    qa = sev.inc()
                    setguard(bL, sev, qa)

                    def f_qmt(eng, qa=qa, ge_t=ge_t, pq2=mixer_pq[1]):
                        eng.wait_ge(sev.h, qa)
                        eng.wait_ge(sev.h, ge_t)
                        eng.wait_ge(smm.h, pq2)
                        eng.tensor_tensor(MTv.rearrange("p (j i) -> p j i", j=4),
                                          LTv.rearrange("p (j i) -> p j i", j=4),
                                          Gv.unsqueeze(1).broadcast_to((128, 4, 128)), ALU.mult)
                        eng.drain().then_inc(svv.h, 1)
                    on("vector", f_qmt)
                    qmt = svv.inc()

                    sev_prev = sevac.get((c - 1, q), 0)
                    gq2 = gpop(bY, bO, bL)
                    def f_q2(eng, q=q, cp=cp, bL=bL, bY=bY, bO=bO, qmt=qmt, big_t=big_t,
                             qa=qa, sev_prev=sev_prev, gq=gq2):
                        eng.wait_ge(svv.h, qmt)
                        eng.wait_ge(svv.h, big_t)
                        eng.wait_ge(sev.h, qa)
                        eng.wait_ge(sev.h, sev_prev)
                        gwait(eng, gq)
                        for j in range(4):
                            h = 4 * q + j
                            eng.matmul(PS[bY][:, j * 128:(j + 1) * 128],
                                       MTv[:, j * 128:(j + 1) * 128],
                                       XSTv[cp][:, h * 128:(h + 1) * 128], start=True, stop=True)
                            eng.matmul(PS[bO][:, j * 128:(j + 1) * 128],
                                       CFv[cp][:, :], S_sb[:, h * 128:(h + 1) * 128],
                                       start=True, stop=True)
                            eng.matmul(PS[bL][:, j * 128:(j + 1) * 128],
                                       DIv[cp][:, h * 128:(h + 1) * 128],
                                       S_sb[:, h * 128:(h + 1) * 128], start=True, stop=False)
                            eng.matmul(PS[bL][:, j * 128:(j + 1) * 128],
                                       BDv[cp][:, h * 128:(h + 1) * 128],
                                       XSTv[cp][:, h * 128:(h + 1) * 128], start=False, stop=True)
                        eng.drain().then_inc(smm.h, 1)
                    on("tensor", f_q2)
                    q2 = smm.inc()
                    mixer_pq[1] = q2

                    def f_qy(eng, q=q, cp=cp, bY=bY, bO=bO, q2=q2, ysg=ytm_store.get(c - 2, 0), sy_=syp[cp],
                             em=sed.n + 1):
                        eng.wait_ge(smm.h, q2)
                        eng.wait_ge(sy_.h, ysg)
                        eng.tensor_tensor(LTv.rearrange("p (j i) -> p j i", j=4),
                                          PS[bO].ap().rearrange("p (j i) -> p j i", j=4),
                                          ECTMb[:, 4 * q:4 * q + 4].unsqueeze(2).broadcast_to((128, 4, 128)),
                                          ALU.mult).then_inc(sed.h, 1)
                        eng.wait_ge(sed.h, em)
                        eng.tensor_tensor(YTMv[cp][:, q * 512:(q + 1) * 512],
                                          PS[bY][:, 0:512], LTv[:, 0:512], ALU.add)
                        eng.drain().then_inc(svv.h, 1)
                    on("vector", f_qy)
                    sed.inc()
                    qy = svv.inc()
                    mixer_pq[0] = qy
                    if c == 0 and q == 0 and os.environ.get("P_DBG2", "0") == "1":
                        dbgs = []
                        for nm, tile in [("d_xst", XSTv[0]), ("d_bd", BDv[0]),
                                          ("d_di", DIv[0]), ("d_lt", LTv), ("d_mt", MTv),
                                          ("d_g", Gv), ("d_bt", BTv[0][:, 0:128]),
                                          ("d_bf", BFv[0]), ("d_cf", CFv[0])]:
                            dt_ = nc.dram_tensor(nm, [128, tile.shape[1]], BF16,
                                                 kind="ExternalOutput").ap()
                            dbgs.append((dt_, tile))
                        for nm, tile in [("d_cum", CUMf), ("d_dtf", DTf), ("d_ltmp", LTMPf)]:
                            dt_ = nc.dram_tensor(nm, [128, tile.shape[1]], F32,
                                                 kind="ExternalOutput").ap()
                            dbgs.append((dt_, tile))
                        def f_dbg2(eng, dbgs=dbgs, w1=qy, w2=sevac.get((0, 0), 0), wq=smm.n,
                                   tgt=scs.n + 16 * len(dbgs)):
                            eng.wait_ge(svv.h, w1)
                            eng.wait_ge(smm.h, wq)
                            for dt_, tile in dbgs:
                                eng.dma_start(out=dt_[0:tile.shape[0], :], in_=tile).then_inc(scs.h, 16)
                            eng.wait_ge(scs.h, tgt)
                        on("sync", f_dbg2)
                        scs.inc(16 * len(dbgs))
                    setguard(bY, svv, qy)
                    setguard(bO, svv, qy)

                    def f_qs(eng, q=q, bL=bL, q2=q2):
                        eng.wait_ge(smm.h, q2)
                        eng.activation(S_sb[:, q * 512:(q + 1) * 512], PS[bL][:, 0:512], AF.Copy)
                        eng.drain().then_inc(sev.h, 1)
                    on("scalar", f_qs)
                    se = sev.inc()
                    sevac[(c, q)] = se
                    setguard(bL, sev, se)
                chunk_mm_done[c] = smm.n
                chunk_ldq[c] = ((ldsp[0], ldsp[0].n), (ldsp[1], ldsp[1].n))

                lastqy = svv.n
                ystg = ytm_store.get(c - 2, 0)
                def f_yst(eng, cp=cp, c=c, lastqy=lastqy, ystg=ystg):
                    eng.wait_ge(svv.h, lastqy)
                    eng.dma_start(out=scr_y[c * 128:(c + 1) * 128, 0:HP],
                                  in_=YTMv[cp][:, :]).then_inc(syp[cp].h, 16)
                on("scalar", f_yst)
                ytm_store[c] = syp[cp].inc(16)

            fin_se = sev.n
            fin_vv = svv.n
            def f_dec(eng, fin_se=fin_se, fin_vv=fin_vv, cpf=NCH % 2):
                eng.wait_ge(sev.h, fin_se)
                eng.wait_ge(svv.h, fin_vv)
                eng.activation(DECb[:, 0:1], RCC[:, cpf:cpf + 1], AF.Exp)
                eng.drain().then_inc(sev.h, 1)
            on("scalar", f_dec)
            dec_t = sev.inc()

            def f_ss(eng, dec_t=dec_t, w1=scs.n + 16 * 3):
                eng.wait_ge(sev.h, dec_t)
                eng.dma_start(out=cc_a_in[:, 0:A_HEADS * A_HD], in_=S_sb[:, :]).then_inc(scs.h, 16)
                eng.dma_start(out=cc_a_in[:, A_HEADS * A_HD:A_HEADS * A_HD + 64],
                              in_=S_sb[:, 0:64]).then_inc(scs.h, 16)
                eng.dma_start(out=scr_dec[0:1, :].rearrange("a p -> p a"),
                              in_=DECb[:, 0:1]).then_inc(scs.h, 16)
                eng.wait_ge(scs.h, w1)
                eng.dma_start(out=cc_a_in[0:1, A_HEADS * A_HD:A_HEADS * A_HD + 64],
                              in_=scr_dec[0:1, 0:64]).then_inc(scs.h, 16)
            on("sync", f_ss)
            scs.inc(64)

        def emit_collective_a():
            H = A_HEADS
            st_t = scs.n
            def f_cc(eng, st_t=st_t):
                eng.wait_ge(scs.h, st_t)
                eng.collective_compute(
                    "AllGather", ALU.bypass,
                    replica_groups=[list(range(8))],
                    ins=[cc_a_in[:, :].opt()],
                    outs=[cc_a_out[:, :].opt()],
                ).then_inc(scc.h, 1)
            on("gpsimd", f_cc)
            cc_t = scc.inc()

            def f_hz(eng, cc_t=cc_t):
                eng.wait_ge(scc.h, cc_t)
                eng.memset(S_sb[:, :], 0.0)
                eng.drain().then_inc(svv.h, 1)
            on("vector", f_hz)
            prev = svv.inc()

            jld = {}
            for j in range(8):
                jp = j % 2
                ld_pv = jld.get(j - 2, 0)
                js_ = sj[jp]
                def f_jl(eng, j=j, jp=jp, cc_t=cc_t, ld_pv=ld_pv, js_=js_):
                    eng.wait_ge(scc.h, cc_t)
                    eng.wait_ge(svv.h, ld_pv)
                    eng.dma_start(out=SJv[jp][:, :],
                                  in_=cc_a_out[j * 128:(j + 1) * 128, 0:H * A_HD]).then_inc(js_.h, 16)
                    eng.dma_start(out=DROWb[0:1, jp * 32:jp * 32 + H],
                                  in_=cc_a_out[j * 128:j * 128 + 1, H * A_HD:H * A_HD + H]).then_inc(js_.h, 16)
                on("sync", f_jl)
                jl = js_.inc(32)

                def f_jr(eng, j=j, jl=jl, prev=prev, js_=js_, em=sed.n + 1):
                    eng.wait_ge(js_.h, jl)
                    eng.wait_ge(svv.h, prev)
                    eng.tensor_scalar(DTILf[0:1, 0:H], DROWb[0:1, (j % 2) * 32:(j % 2) * 32 + H],
                                      cmask_s[0:1, 8 + j:9 + j], None, ALU.mult).then_inc(sed.h, 1)
                    eng.wait_ge(sed.h, em)
                    eng.tensor_scalar(DTILf[0:1, 0:H], DTILf[0:1, 0:H],
                                      cmask_s[0:1, 16 + j:17 + j], None, ALU.add)
                    eng.tensor_scalar(DTILf[0:1, 32:32 + H], ones_row[0:1, 0:H],
                                      cmask_s[0:1, j:j + 1], None, ALU.mult)
                    eng.drain().then_inc(svv.h, 1)
                on("vector", f_jr)
                sed.inc()
                jr = svv.inc()

                gj6 = gpop(6)
                def f_jm(eng, jr=jr, gj6=gj6):
                    eng.wait_ge(svv.h, jr)
                    gwait(eng, gj6)
                    eng.matmul(PS[6][:, 0:H], ones_row[0:1, :], DTILf[0:1, 0:H], start=True, stop=True)
                    eng.matmul(PS[6][:, 64:64 + H], ones_row[0:1, :], DTILf[0:1, 32:32 + H],
                               start=True, stop=True)
                    eng.drain().then_inc(smm.h, 1)
                on("tensor", f_jm)
                jm = smm.inc()

                def f_je(eng, jm=jm):
                    eng.wait_ge(smm.h, jm)
                    eng.activation(DBCb[:, 0:H], PS[6][:, 0:H], AF.Copy)
                    eng.activation(MBCb[:, 0:H], PS[6][:, 64:64 + H], AF.Copy)
                    eng.drain().then_inc(sev.h, 1)
                on("scalar", f_je)
                je = sev.inc()
                setguard(6, sev, je)

                def f_jv(eng, jp=jp, je=je, em=sed.n + 1):
                    eng.wait_ge(sev.h, je)
                    eng.tensor_tensor(S_sb.rearrange("p (h d) -> p h d", h=H),
                                      S_sb.rearrange("p (h d) -> p h d", h=H),
                                      DBCb[:, 0:H].unsqueeze(2).broadcast_to((128, H, 128)), ALU.mult)
                    eng.tensor_tensor(T2c.rearrange("p (h d) -> p h d", h=H),
                                      SJv[jp].rearrange("p (h d) -> p h d", h=H),
                                      MBCb[:, 0:H].unsqueeze(2).broadcast_to((128, H, 128)),
                                      ALU.mult).then_inc(sed.h, 1)
                    eng.wait_ge(sed.h, em)
                    eng.tensor_tensor(S_sb[:, :], S_sb[:, :], T2c[:, :], ALU.add)
                    eng.drain().then_inc(svv.h, 1)
                on("vector", f_jv)
                sed.inc()
                prev = svv.inc()
                jld[j] = prev

        def emit_gate_outproj_a(xprev, xprev_off, xdst):
            H = A_HEADS
            HP = A_INTER
            GCF = SS2[:, 0:128]
            for h2 in range(2):
                gate_done_v = None
                for ci in range(8):
                    c = 8 * h2 + ci
                    c0 = c * CH
                    ysg = ytm_store[c]
                    def f_gl(eng, c=c, c0=c0, ysg=ysg, gp=smm.n, pe1=emit_gate_outproj_a.pe1[0],
                             psg=emit_gate_outproj_a.psg[0], sy_=syp[c % 2]):
                        eng.wait_ge(sy_.h, ysg)
                        eng.wait_ge(smm.h, gp)
                        eng.wait_ge(svv.h, pe1)
                        eng.wait_ge(sev.h, psg)
                        eng.dma_start(out=YTg[:, :], in_=scr_y[c * 128:(c + 1) * 128, 0:HP]).then_inc(glx.h, 16)
                        eng.dma_start_transpose(out=GXS[:, :], in_=scr_conv_a[0:HP, c0:c0 + CH]).then_inc(glx.h, 16)
                        eng.dma_start_transpose(out=GGT[:, :], in_=scr_gate_a[0:HP, c0:c0 + CH]).then_inc(glx.h, 16)
                        eng.dma_start(out=GCF, in_=scr_conv_a[A_INTER + 128:A_INTER + 256, c0:c0 + CH]).then_inc(glx.h, 16)
                    on("sync", f_gl)
                    gl = glx.inc(16 * 4)

                    corr_v = []
                    for q in range(H // 4):
                        bq = q % 2
                        gcm = gpop(bq)
                        def f_cm(eng, q=q, bq=bq, gl=gl, gcm=gcm):
                            eng.wait_ge(glx.h, gl)
                            gwait(eng, gcm)
                            for j in range(4):
                                h = 4 * q + j
                                eng.matmul(PS[bq][:, j * 128:(j + 1) * 128], GCF,
                                           S_sb[:, h * 128:(h + 1) * 128], start=True, stop=True)
                            eng.drain().then_inc(smm.h, 1)
                        on("tensor", f_cm)
                        cm = smm.inc()

                        def f_cv(eng, q=q, bq=bq, cm=cm, c=c, em=sed.n + 1):
                            eng.wait_ge(smm.h, cm)
                            eng.tensor_tensor(TAP2.rearrange("p (j i) -> p j i", j=4),
                                              PS[bq].ap().rearrange("p (j i) -> p j i", j=4),
                                              ECCSb[:, c * 64 + 4 * q:c * 64 + 4 * q + 4]
                                              .unsqueeze(2).broadcast_to((128, 4, 128)),
                                              ALU.mult).then_inc(sed.h, 1)
                            eng.wait_ge(sed.h, em)
                            eng.tensor_tensor(YTg[:, q * 512:(q + 1) * 512],
                                              YTg[:, q * 512:(q + 1) * 512], TAP2[:, 0:512], ALU.add)
                            eng.drain().then_inc(svv.h, 1)
                        on("vector", f_cv)
                        sed.inc()
                        cvt = svv.inc()
                        setguard(bq, svv, cvt)
                        corr_v.append(cvt)

                    def f_sg(eng, gl=gl, pe2=emit_gate_outproj_a.pe2[0]):
                        eng.wait_ge(glx.h, gl)
                        eng.wait_ge(svv.h, pe2)
                        eng.activation(GB1[:, :], GGT[:, :], AF.Silu)
                        eng.drain().then_inc(sev.h, 1)
                    on("scalar", f_sg)
                    sg = sev.inc()

                    cvl = corr_v[-1]
                    def f_e1(eng, gl=gl, cvl=cvl, sg=sg, em=sed.n + 1,
                             ptr=emit_gate_outproj_a.trg.get(c - 1, 0)):
                        eng.wait_ge(glx.h, gl)
                        eng.wait_ge(svv.h, cvl)
                        eng.wait_ge(sev.h, sg)
                        eng.wait_ge(strs.h, ptr)
                        eng.tensor_tensor(GB2[:, :], GXS[:, :], dexp_b[:, :],
                                          ALU.mult).then_inc(sed.h, 1)
                        eng.wait_ge(sed.h, em)
                        eng.tensor_tensor(GXS[:, :], YTg[:, :], GB2[:, :],
                                          ALU.add).then_inc(sed.h, 1)
                        eng.wait_ge(sed.h, em + 1)
                        eng.tensor_tensor(GB2[:, :], GXS[:, :], GB1[:, :], ALU.mult)
                        eng.drain().then_inc(svv.h, 1)
                    on("vector", f_e1)
                    sed.inc(2)
                    e1 = svv.inc()
                    emit_gate_outproj_a.pe1[0] = e1
                    emit_gate_outproj_a.psg[0] = sg

                    def f_sq(eng, e1=e1):
                        eng.wait_ge(svv.h, e1)
                        eng.activation(GB1[:, :], GB2[:, :], AF.Square, accum_out=SSQf[:, 0:1])
                        eng.drain().then_inc(sev.h, 1)
                    on("scalar", f_sq)
                    sq_t = sev.inc()

                    def f_rs1(eng, sq_t=sq_t):
                        eng.wait_ge(sev.h, sq_t)
                        eng.tensor_scalar(MSf[:, 0:1], SSQf[:, 0:1], 1.0 / A_INTER, None, ALU.mult)
                        eng.drain().then_inc(svv.h, 1)
                    on("vector", f_rs1)
                    rs1 = svv.inc()

                    def f_rs2(eng, rs1=rs1, ea=sea.n + 1):
                        eng.wait_ge(svv.h, rs1)
                        eng.activation(RSTDf[:, 0:1], MSf[:, 0:1], AF.Ln,
                                       bias=EPSc[:, 0:1]).then_inc(sea.h, 1)
                        eng.wait_ge(sea.h, ea)
                        eng.activation(RSTDf[:, 0:1], RSTDf[:, 0:1], AF.Exp, scale=-0.5)
                        eng.drain().then_inc(sev.h, 1)
                    on("scalar", f_rs2)
                    sea.inc()
                    rs2 = sev.inc()

                    trg = emit_gate_outproj_a.trg.get(c - 1, 0)
                    def f_e2(eng, rs2=rs2, trg=trg, em=sed.n + 1):
                        eng.wait_ge(sev.h, rs2)
                        eng.wait_ge(strs.h, trg)
                        eng.tensor_scalar(GB1[:, :], GB2[:, :], RSTDf[:, 0:1], None,
                                          ALU.mult).then_inc(sed.h, 1)
                        eng.wait_ge(sed.h, em)
                        eng.tensor_tensor(GB2[:, :], GB1[:, :], nw_b[:, :], ALU.mult)
                        eng.drain().then_inc(svv.h, 1)
                    on("vector", f_e2)
                    sed.inc()
                    e2 = svv.inc()
                    emit_gate_outproj_a.pe2[0] = e2

                    def f_trs(eng, e2=e2, ci=ci, opg=emit_gate_outproj_a.opmm[0]):
                        eng.wait_ge(svv.h, e2)
                        eng.wait_ge(smm.h, opg)
                        for f in range(32):
                            eng.dma_start_transpose(
                                out=YFMh[:, f * 1024 + ci * 128:f * 1024 + ci * 128 + 128],
                                in_=GB2[:, f * 128:(f + 1) * 128]).then_inc(strs.h, 16)
                    on("sync", f_trs)
                    emit_gate_outproj_a.trg[c] = strs.inc(16 * 32)

                half_tr = emit_gate_outproj_a.trg[8 * h2 + 7]
                mmd = {}
                for m in range(16):
                    p = m % 2
                    wfree = smm.n if m < 2 else mmd[m - 2]
                    ws2 = ldwp[p]
                    def f_wl(eng, p=p, m=m, wfree=wfree, lv=svv.n, htr=half_tr, ws2=ws2):
                        eng.wait_ge(smm.h, wfree)
                        eng.wait_ge(svv.h, lv)
                        eng.wait_ge(strs.h, htr)
                        src = wa_outT.rearrange("(k p) m -> p k m", p=128)[:, :, m * 128:(m + 1) * 128]
                        eng.dma_start(out=WTOv[p].rearrange("p (k j) -> p k j", k=32),
                                      in_=src).then_inc(ws2.h, 16)
                    on("gpsimd", f_wl)
                    wt = ws2.inc(16)

                    for tt in range(2):
                        bk = 2 + (m * 2 + tt) % 2
                        tok0 = h2 * 1024 + tt * 512
                        gom = gpop(bk)
                        def f_om(eng, p=p, bk=bk, tt=tt, wt=wt, half_tr=half_tr, ws2=ws2, gom=gom):
                            eng.wait_ge(ws2.h, wt)
                            eng.wait_ge(strs.h, half_tr)
                            gwait(eng, gom)
                            for k in range(32):
                                eng.matmul(PS[bk][:, 0:512],
                                           WTOv[p][:, k * 128:(k + 1) * 128],
                                           YFMh[:, k * 1024 + tt * 512:k * 1024 + tt * 512 + 512],
                                           start=(k == 0), stop=(k == 31))
                            eng.drain().then_inc(smm.h, 1)
                        on("tensor", f_om)
                        om = smm.inc()
                        emit_gate_outproj_a.opmm[0] = om
                        if tt == 1:
                            mmd[m] = om

                        xp = (m * 2 + tt) % 2
                        xs_ = xlp[xp]
                        def f_xl(eng, m=m, tok0=tok0, xp=xp, xprev=xprev, xprev_off=xprev_off,
                                 xg=emit_gate_outproj_a.xld.get(xp, 0), xs_=xs_):
                            eng.wait_ge(svv.h, xg)
                            eng.dma_start(out=XOLDf[xp][:, :],
                                          in_=xprev[m * 128:(m + 1) * 128,
                                                    xprev_off + tok0:xprev_off + tok0 + 512]).then_inc(xs_.h, 16)
                        on("sync", f_xl)
                        xl = xs_.inc(16)

                        def f_xv(eng, bk=bk, xp=xp, om=om, xl=xl, xg2=emit_gate_outproj_a.xst.get(xp, 0),
                                 xs_=xs_, sy_=syp[xp]):
                            eng.wait_ge(smm.h, om)
                            eng.wait_ge(xs_.h, xl)
                            eng.wait_ge(sy_.h, xg2)
                            eng.tensor_tensor(XNEWf[xp][:, :], PS[bk][:, 0:512], XOLDf[xp][:, :], ALU.add)
                            eng.drain().then_inc(svv.h, 1)
                        on("vector", f_xv)
                        xv = svv.inc()
                        setguard(bk, svv, xv)
                        emit_gate_outproj_a.xld[xp] = xv

                        def f_xs(eng, m=m, tok0=tok0, xp=xp, xv=xv, xdst=xdst):
                            eng.wait_ge(svv.h, xv)
                            eng.dma_start(out=xdst[m * 128:(m + 1) * 128, tok0:tok0 + 512],
                                          in_=XNEWf[xp][:, :]).then_inc(syp[xp].h, 16)
                        on("scalar", f_xs)
                        emit_gate_outproj_a.xst[xp] = syp[xp].inc(16)
        emit_gate_outproj_a.trg = {}
        emit_gate_outproj_a.xld = {}
        emit_gate_outproj_a.xst = {}
        emit_gate_outproj_a.pe1 = [0]
        emit_gate_outproj_a.psg = [0]
        emit_gate_outproj_a.pe2 = [0]
        emit_gate_outproj_a.opmm = [0]


        def emit_ffn(xsrc, xsrc_off, xdst):
            LNW = T
            lnbf = R1[:, 0:16 * LNW]
            XA1 = FR[:, 0:512]
            FXOLD = [FR[:, 512:1024], FR[:, 1024:1536]]
            FXNEW = [FR[:, 1536:2048], FR[:, 2048:2560]]
            LGf = FR[:, 2560:3072]
            LGT = FR[:, 3072:3088]
            M8v = FR[:, 3088:3096]
            NV1 = FR[:, 3096:3097]; E2v = FR[:, 3097:3098]; DENv = FR[:, 3098:3099]
            RDv = FR[:, 3099:3100]
            Ev = FR[:, 3104:3120]; G8v = FR[:, 3120:3136]; ESv = FR[:, 3136:3152]
            RW1T = FR[:, 4096:5120]
            RW2Ts = FR[:, 5120:5152]
            RB1c = FR[:, 5152:5153]; RB2G = FR[:, 5153:5154]; GTSc = FR[:, 5154:5155]
            BCOL = FR[:, 5168:5184]
            RT1 = FR[:, 5200:5328]
            WXA = [R2[:, 0:2048], R2[:, 2048:4096]]
            BTW = [R2[:, 12288:12544], R2[:, 12544:12800]]
            SELb = R2[:, 12800:13056]
            COF = R2[:, 13056:15104]
            CEb = [R2[:, 15104:15616], R2[:, 15616:16128]]

            def f_fl(eng, tgt=lx[0].n + 16 * 7):
                eng.dma_start(out=RW1T.rearrange("p (k j) -> p k j", k=16),
                              in_=f_rw1T.rearrange("(k p) m -> p k m", p=128)).then_inc(lx[0].h, 16)
                eng.dma_start(out=RW2Ts[0:64, :], in_=f_rw2T[:, :]).then_inc(lx[0].h, 16)
                eng.dma_start(out=RB1c[0:64, :], in_=f_rb1[:, :]).then_inc(lx[0].h, 16)
                eng.dma_start(out=RB2G[0:32, :], in_=f_rb2[:, :]).then_inc(lx[0].h, 16)
                eng.dma_start(out=GTSc[0:16, 0:1], in_=f_gates[:, :]).then_inc(lx[0].h, 16)
                eng.dma_start(out=RT1[0:16, 0:128], in_=f_biasT[:, :]).then_inc(lx[0].h, 16)
                eng.dma_start(out=SELb[0:16, :], in_=f_sel[:, :]).then_inc(lx[0].h, 16)
                eng.wait_ge(lx[0].h, tgt)
            on("sync", f_fl)
            fl_t = lx[0].inc(16 * 7)

            g3i = gpop(3)
            def f_fb_t(eng, fl_t=fl_t, g3i=g3i):
                eng.wait_ge(lx[0].h, fl_t)
                gwait(eng, g3i)
                eng.transpose(PS[3][:, 0:16], RT1[0:16, 0:128], eye_s[0:16, 0:16])
                eng.drain().then_inc(smm.h, 1)
            on("tensor", f_fb_t)
            fbt = smm.inc()

            def f_fb_e(eng, fbt=fbt):
                eng.wait_ge(smm.h, fbt)
                eng.activation(BCOL[:, :], PS[3][:, 0:16], AF.Copy)
                eng.drain().then_inc(sev.h, 1)
            on("scalar", f_fb_e)
            fbe = sev.inc()
            setguard(3, sev, fbe)

            def f_fb_v(eng, fbe=fbe):
                eng.wait_ge(sev.h, fbe)
                eng.tensor_tensor(RB2G[0:16, :], RB2G[0:16, :], GTSc[0:16, :], ALU.add)
                eng.drain().then_inc(svv.h, 1)
            on("vector", f_fb_v)
            fbv = svv.inc()

            hr_done = []

            def router1(it, xs, doff, w, av):
                g3 = gpop(3)
                def f_r1(eng, xs=xs, w=w, av=av, g3=g3, fl_t=fl_t, fbv=fbv):
                    eng.wait_ge(svv.h, av)
                    eng.wait_ge(svv.h, fbv)
                    eng.wait_ge(lx[0].h, fl_t)
                    gwait(eng, g3)
                    for k in range(NK):
                        eng.matmul(PS[3][0:64, 0:w], RW1T[:, k * 64:(k + 1) * 64],
                                   XT[:, xs + k * 128:xs + k * 128 + w],
                                   start=(k == 0), stop=(k == NK - 1))
                    eng.drain().then_inc(smm.h, 1)
                on("tensor", f_r1)
                r1 = smm.inc()

                def f_r1e(eng, r1=r1, w=w):
                    eng.wait_ge(smm.h, r1)
                    eng.activation(RT1[0:64, 0:w], PS[3][0:64, 0:w], AF.Copy)
                    eng.drain().then_inc(sev.h, 1)
                on("scalar", f_r1e)
                r1e = sev.inc()
                setguard(3, sev, r1e)

                def f_r1v(eng, r1e=r1e, w=w):
                    eng.wait_ge(sev.h, r1e)
                    eng.tensor_tensor(RT1[0:64, 0:w], RT1[0:64, 0:w], PS[7][0:64, 0:w], ALU.mult)
                    eng.drain().then_inc(svv.h, 1)
                on("vector", f_r1v)
                r1v = svv.inc()
                setguard(7, svv, r1v)

                def f_r1s(eng, r1v=r1v, doff=doff, w=w):
                    eng.wait_ge(svv.h, r1v)
                    eng.activation(HRf[0:64, doff:doff + w], RT1[0:64, 0:w], AF.Silu, bias=RB1c[0:64, :])
                    eng.drain().then_inc(sev.h, 1)
                on("scalar", f_r1s)
                hr_done.append(sev.inc())
                return (smm, r1)

            HRf = FR[:, 6144:8192]
            ln_tts = [(j * 128, j * 128, 128, xsrc) for j in range(16)]
            ln_tts = [(j * 128, xsrc_off + j * 128, 128, xsrc) for j in range(16)]
            emit_ln(LNW, ln_tts, slotw=128, lnb_t=lnbf, post_tt=router1)
            Bar()

            for t4 in range(4):
                bk = t4 % 2
                g_ = gpop(bk)
                def f_r2(eng, t4=t4, bk=bk, g_=g_):
                    gwait(eng, g_)
                    eng.matmul(PS[bk][0:32, 0:512], RW2Ts[0:64, :], HRf[0:64, t4 * 512:(t4 + 1) * 512],
                               start=True, stop=True)
                    eng.drain().then_inc(smm.h, 1)
                on("tensor", f_r2)
                r2 = smm.inc()

                def f_lg(eng, r2=r2, t4=t4, bk=bk):
                    eng.wait_ge(smm.h, r2)
                    eng.tensor_scalar(LGf[0:16, 0:512], PS[bk][0:16, 0:512], RB2G[0:16, :],
                                      None, ALU.add)
                    eng.drain().then_inc(svv.h, 1)
                on("vector", f_lg)
                lg = svv.inc()
                setguard(bk, svv, lg)
                if t4 == 0 and os.environ.get("P_DBG3", "0") == "1":
                    d_lg0 = nc.dram_tensor("d_lg0", [16, 512], F32, kind="ExternalOutput").ap()
                    d_hr0 = nc.dram_tensor("d_hr0", [64, 512], F32, kind="ExternalOutput").ap()
                    def f_d30(eng, w=lg, tgt=scs.n + 32):
                        eng.wait_ge(svv.h, w)
                        eng.dma_start(out=d_lg0[:, :], in_=LGf[0:16, 0:512]).then_inc(scs.h, 16)
                        eng.dma_start(out=d_hr0[:, :], in_=HRf[0:64, 0:512]).then_inc(scs.h, 16)
                        eng.wait_ge(scs.h, tgt)
                    on("sync", f_d30)
                    scs.inc(32)

                for i8 in range(4):
                    tok = t4 * 512 + i8 * 128
                    bk2 = 2 + (i8 % 2)
                    g2_ = gpop(bk2)
                    def f_lt(eng, i8=i8, bk2=bk2, lg=lg, g2_=g2_):
                        eng.wait_ge(svv.h, lg)
                        gwait(eng, g2_)
                        eng.transpose(PS[bk2][:, 0:16], LGf[0:16, i8 * 128:(i8 + 1) * 128],
                                      eye_s[0:16, 0:16])
                        eng.drain().then_inc(smm.h, 1)
                    on("tensor", f_lt)
                    lt_ = smm.inc()

                    def f_lte(eng, lt_=lt_, bk2=bk2):
                        eng.wait_ge(smm.h, lt_)
                        eng.activation(LGT[:, :], PS[bk2][:, 0:16], AF.Copy)
                        eng.drain().then_inc(sev.h, 1)
                    on("scalar", f_lte)
                    lte = sev.inc()
                    setguard(bk2, sev, lte)

                    def f_top(eng, lte=lte, eb=sed.n):
                        eng.wait_ge(sev.h, lte)
                        eng.max(out=M8v[:, :], in_=LGT[:, :])
                        eng.drain().then_inc(sed.h, 1)
                        eng.wait_ge(sed.h, eb + 1)
                        eng.tensor_scalar(NV1[:, :], M8v[:, 0:1], -1.0, None, ALU.mult)
                        eng.tensor_scalar(G8v[:, :], LGT[:, :], M8v[:, 1:2], None, ALU.is_ge)
                        eng.drain().then_inc(svv.h, 1)
                    on("vector", f_top)
                    sed.inc()
                    top = svv.inc()

                    def f_te(eng, top=top, eb=sea.n):
                        eng.wait_ge(svv.h, top)
                        eng.activation(Ev[:, :], LGT[:, :], AF.Exp, bias=NV1[:, :])
                        eng.activation(E2v[:, :], M8v[:, 1:2], AF.Exp, bias=NV1[:, :])
                        eng.drain().then_inc(sev.h, 1)
                    on("scalar", f_te)
                    te = sev.inc()

                    def f_co(eng, te=te, eb=sed.n):
                        eng.wait_ge(sev.h, te)
                        eng.tensor_scalar(DENv[:, :], E2v[:, :], 1.0, None, ALU.add).then_inc(sed.h, 1)
                        eng.wait_ge(sed.h, eb + 1)
                        eng.reciprocal(RDv[:, :], DENv[:, :]).then_inc(sed.h, 1)
                        eng.wait_ge(sed.h, eb + 2)
                        eng.tensor_scalar(ESv[:, :], Ev[:, :], RDv[:, :], None, ALU.mult).then_inc(sed.h, 1)
                        eng.wait_ge(sed.h, eb + 3)
                        eng.tensor_tensor(ESv[:, :], ESv[:, :], G8v[:, :], ALU.mult)
                        eng.drain().then_inc(svv.h, 1)
                    on("vector", f_co)
                    sed.inc(3)
                    co = svv.inc()

                    g3_ = gpop(bk2)
                    def f_ct(eng, co=co, bk2=bk2, g3_=g3_):
                        eng.wait_ge(svv.h, co)
                        gwait(eng, g3_)
                        eng.transpose(PS[bk2][0:16, 0:128], ESv[:, :], eye_s[:, :])
                        eng.drain().then_inc(smm.h, 1)
                    on("tensor", f_ct)
                    ct = smm.inc()

                    def f_ce(eng, ct=ct, bk2=bk2, tok=tok):
                        eng.wait_ge(smm.h, ct)
                        eng.activation(COF[0:16, tok:tok + 128], PS[bk2][0:16, 0:128], AF.Copy)
                        eng.drain().then_inc(sev.h, 1)
                    on("scalar", f_ce)
                    cof_t = sev.inc()
                    setguard(bk2, sev, cof_t)
            cof_all = sev.n
            if os.environ.get("P_DBG3", "0") == "1":
                d_cof = nc.dram_tensor("d_cof", [16, 2048], BF16, kind="ExternalOutput").ap()
                d_hr = nc.dram_tensor("d_hr", [64, 2048], F32, kind="ExternalOutput").ap()
                d_lg = nc.dram_tensor("d_lg", [16, 512], F32, kind="ExternalOutput").ap()
                d_lgt = nc.dram_tensor("d_lgt", [128, 16], F32, kind="ExternalOutput").ap()
                d_m8 = nc.dram_tensor("d_m8", [128, 8], F32, kind="ExternalOutput").ap()
                d_es = nc.dram_tensor("d_es", [128, 16], F32, kind="ExternalOutput").ap()
                d_rw2 = nc.dram_tensor("d_rw2", [64, 32], F32, kind="ExternalOutput").ap()
                def f_d3(eng, w=cof_all, tgt=scs.n + 16 * 7):
                    eng.wait_ge(sev.h, w)
                    eng.dma_start(out=d_cof[:, :], in_=COF[0:16, :]).then_inc(scs.h, 16)
                    eng.dma_start(out=d_hr[:, :], in_=HRf[0:64, :]).then_inc(scs.h, 16)
                    eng.dma_start(out=d_lg[:, :], in_=LGf[0:16, 0:512]).then_inc(scs.h, 16)
                    eng.dma_start(out=d_lgt[:, :], in_=LGT[:, :]).then_inc(scs.h, 16)
                    eng.dma_start(out=d_m8[:, :], in_=M8v[:, :]).then_inc(scs.h, 16)
                    eng.dma_start(out=d_es[:, :], in_=ESv[:, :]).then_inc(scs.h, 16)
                    eng.dma_start(out=d_rw2[:, :], in_=RW2Ts[0:64, :]).then_inc(scs.h, 16)
                    eng.wait_ge(scs.h, tgt)
                on("sync", f_d3)
                scs.inc(16 * 7)
            Bar()

            for m2 in range(2):
                ws = ldwp[m2]
                def f_aw(eng, m2=m2, ws=ws, wg=smm.n):
                    eng.wait_ge(smm.h, wg)
                    src_ = f_AT.rearrange("(k p) m -> p k m", p=128)[:, :, m2 * 128:(m2 + 1) * 128]
                    eng.dma_start(out=WTb[m2].rearrange("p (k j) -> p k j", k=16),
                                  in_=src_).then_inc(ws.h, 16)
                on("gpsimd", f_aw)
                aw = ws.inc(16)

                for t4 in range(4):
                    bkx = (t4 % 2)
                    bkc = 2 + (t4 % 2)
                    gx = gpop(bkx); gc = gpop(bkc)
                    def f_xam(eng, m2=m2, t4=t4, bkx=bkx, bkc=bkc, aw=aw, ws=ws, gx=gx, gc=gc):
                        eng.wait_ge(ws.h, aw)
                        gwait(eng, gx); gwait(eng, gc)
                        for k in range(NK):
                            eng.matmul(PS[bkx][:, 0:512], WTb[m2][:, k * 128:(k + 1) * 128],
                                       lnbf[:, k * LNW + t4 * 512:k * LNW + t4 * 512 + 512],
                                       start=(k == 0), stop=(k == NK - 1))
                        eng.matmul(PS[bkc][:, 0:512], SELb[0:16, m2 * 128:(m2 + 1) * 128],
                                   COF[0:16, t4 * 512:(t4 + 1) * 512], start=True, stop=True)
                        eng.drain().then_inc(smm.h, 1)
                    on("tensor", f_xam)
                    xam = smm.inc()

                    def f_cee(eng, xam=xam, bkc=bkc, t4=t4):
                        eng.wait_ge(smm.h, xam)
                        eng.activation(CEb[t4 % 2][:, :], PS[bkc][:, 0:512], AF.Copy)
                        eng.drain().then_inc(sev.h, 1)
                    on("scalar", f_cee)
                    cee = sev.inc()
                    setguard(bkc, sev, cee)

                    def f_wxa(eng, cee=cee, bkx=bkx, m2=m2, t4=t4):
                        eng.wait_ge(sev.h, cee)
                        eng.tensor_tensor(WXA[m2][:, t4 * 512:(t4 + 1) * 512],
                                          PS[bkx][:, 0:512], CEb[t4 % 2][:, :], ALU.mult)
                        eng.drain().then_inc(svv.h, 1)
                    on("vector", f_wxa)
                    wxa = svv.inc()
                    setguard(bkx, svv, wxa)
            Bar()

            mmdF = {}
            for m in range(16):
                p = m % 2
                ws = ldwp[p]
                wfree = smm.n if m < 2 else mmdF[m - 2]
                def f_fw(eng, p=p, m=m, ws=ws, wfree=wfree):
                    eng.wait_ge(smm.h, wfree)
                    src_ = wf_T.rearrange("(k p) m -> p k m", p=128)[:, :, m * 128:(m + 1) * 128]
                    eng.dma_start(out=WTb[p].rearrange("p (k j) -> p k j", k=16),
                                  in_=src_).then_inc(ws.h, 16)
                    src2 = f_BT2.rearrange("(k p) m -> p k m", p=128)[:, :, m * 128:(m + 1) * 128]
                    eng.dma_start(out=BTW[p].rearrange("p (k j) -> p k j", k=2),
                                  in_=src2).then_inc(ws.h, 16)
                on("gpsimd", f_fw)
                fw = ws.inc(32)

                for tt in range(4):
                    bk = (m * 4 + tt) % 4
                    gk = gpop(bk)
                    def f_bm(eng, p=p, tt=tt, bk=bk, fw=fw, ws=ws, gk=gk):
                        eng.wait_ge(ws.h, fw)
                        gwait(eng, gk)
                        for k in range(NK):
                            eng.matmul(PS[bk][:, 0:512], WTb[p][:, k * 128:(k + 1) * 128],
                                       lnbf[:, k * LNW + tt * 512:k * LNW + tt * 512 + 512],
                                       start=(k == 0), stop=False)
                        for k2 in range(2):
                            eng.matmul(PS[bk][:, 0:512], BTW[p][:, k2 * 128:(k2 + 1) * 128],
                                       WXA[k2][:, tt * 512:(tt + 1) * 512],
                                       start=False, stop=(k2 == 1))
                        eng.drain().then_inc(smm.h, 1)
                    on("tensor", f_bm)
                    bm = smm.inc()
                    if tt == 3:
                        mmdF[m] = bm

                    xp = (m * 4 + tt) % 2
                    xs_ = xlp[xp]
                    def f_fxl(eng, m=m, tt=tt, xp=xp, xs_=xs_,
                              xg=emit_ffn.xld.get(xp, 0)):
                        eng.wait_ge(svv.h, xg)
                        eng.dma_start(out=FXOLD[xp][:, :],
                                      in_=xsrc[m * 128:(m + 1) * 128,
                                               xsrc_off + tt * 512:xsrc_off + tt * 512 + 512]).then_inc(xs_.h, 16)
                    on("sync", f_fxl)
                    fxl = xs_.inc(16)

                    def f_fxv(eng, bk=bk, xp=xp, bm=bm, fxl=fxl, m=m, xs_=xs_,
                              xg2=emit_ffn.xst.get(xp, 0), eb=sed.n, sy_=syp[xp]):
                        eng.wait_ge(smm.h, bm)
                        eng.wait_ge(xs_.h, fxl)
                        eng.wait_ge(sy_.h, xg2)
                        eng.tensor_scalar(XA1[:, :], PS[bk][:, 0:512], BCOL[:, m:m + 1],
                                          None, ALU.add).then_inc(sed.h, 1)
                        eng.wait_ge(sed.h, eb + 1)
                        eng.tensor_tensor(FXNEW[xp][:, :], XA1[:, :], FXOLD[xp][:, :], ALU.add)
                        eng.drain().then_inc(svv.h, 1)
                    on("vector", f_fxv)
                    sed.inc()
                    fxv = svv.inc()
                    setguard(bk, svv, fxv)
                    emit_ffn.xld[xp] = fxv

                    def f_fxs(eng, m=m, tt=tt, xp=xp, fxv=fxv, xdst=xdst):
                        eng.wait_ge(svv.h, fxv)
                        eng.dma_start(out=xdst[m * 128:(m + 1) * 128, tt * 512:tt * 512 + 512],
                                      in_=FXNEW[xp][:, :]).then_inc(syp[xp].h, 16)
                    on("scalar", f_fxs)
                    emit_ffn.xst[xp] = syp[xp].inc(16)
        emit_ffn.xld = {}
        emit_ffn.xst = {}

        def emit_halo_b(xsrc, xsrc_off):
            base = lx[0].n
            def f_hga(eng, base=base, xsrc=xsrc, xsrc_off=xsrc_off):
                for k in range(NK):
                    eng.dma_start(out=XF_t[:, k * 4:(k + 1) * 4],
                                  in_=xsrc[k * 128:(k + 1) * 128,
                                           xsrc_off + T - 4:xsrc_off + T]).then_inc(lx[0].h, 16)
                eng.dma_start(out=HSLf[0:8, 0:1], in_=hsel_in[:, :]).then_inc(lx[0].h, 16)
                eng.wait_ge(lx[0].h, base + 16 * 17)
                for k in range(NK):
                    eng.dma_start(out=cc_h_in[k * 8:(k + 1) * 8, :]
                                  .rearrange("P (pp t) -> (P pp) t", pp=16),
                                  in_=XF_t[:, k * 4:(k + 1) * 4]).then_inc(lx[0].h, 16)
            on("sync", f_hga)
            hga = lx[0].inc(16 * 33)

            def f_hcc(eng, hga=hga):
                eng.wait_ge(lx[0].h, hga)
                eng.collective_compute(
                    "AllGather", ALU.bypass,
                    replica_groups=[list(range(8))],
                    ins=[cc_h_in[:, :].opt()],
                    outs=[cc_h_out[:, :].opt()],
                ).then_inc(scc.h, 1)
            on("gpsimd", f_hcc)
            hcc = scc.inc()

            def f_hlt(eng, hcc=hcc):
                eng.wait_ge(scc.h, hcc)
                eng.dma_start(out=HSLT[0:8, :],
                              in_=cc_h_out.rearrange("(j p) q -> j (p q)", p=128)).then_inc(lx[1].h, 16)
            on("sync", f_hlt)
            hlt = lx[1].inc(16)

            halo_st = [0]
            for c in range(NCH):
                bk = 4 + c % 2
                gh = gpop(bk)
                def f_hm(eng, c=c, bk=bk, hlt=hlt, gh=gh, hga=hga):
                    eng.wait_ge(lx[1].h, hlt)
                    eng.wait_ge(lx[0].h, hga)
                    gwait(eng, gh)
                    eng.matmul(PS[bk][0:1, 0:512], HSLf[0:8, 0:1],
                               HSLT[0:8, c * 512:(c + 1) * 512], start=True, stop=True)
                    eng.drain().then_inc(smm.h, 1)
                on("tensor", f_hm)
                hm = smm.inc()

                def f_he(eng, bk=bk, hm=hm, pst=halo_st[0]):
                    eng.wait_ge(smm.h, hm)
                    eng.wait_ge(scs.h, pst)
                    eng.activation(XF_t[0:1, 1024:1536], PS[bk][0:1, 0:512], AF.Copy)
                    eng.drain().then_inc(sev.h, 1)
                on("scalar", f_he)
                he = sev.inc()
                setguard(bk, sev, he)

                def f_hs(eng, c=c, he=he):
                    eng.wait_ge(sev.h, he)
                    eng.dma_start(out=x_halo_b[c * 128:(c + 1) * 128, 124:128].unsqueeze(0),
                                  in_=XF_t[0:1, 1024:1536].rearrange("a (p t) -> a p t", p=128)
                                  ).then_inc(scs.h, 16)
                on("sync", f_hs)
                halo_st[0] = scs.inc(16)

        sevacB = {}
        chunk_mm_doneB = {}
        qmm_ldB = {}
        chunk_ldqB = {}
        ytm_storeB = {}
        bigB = {}
        lastqyB = {}
        pqB = [0, 0]
        LNSCALE = float(np.log(B_SCALE))

        def emit_mixer_b():
            H, HD, SP = B_HEADS, B_HD, B_STATE
            nq = H // 4

            def f_ld0(eng):
                eng.dma_start(out=DTRf[0:H, 0:1], in_=b_Alog[:, :]).then_inc(lx[0].h, 16)
                eng.dma_start(out=DTBCf[0:H, 0:1], in_=b_dtb[:, :]).then_inc(lx[0].h, 16)
            on("sync", f_ld0)
            l0 = lx[0].inc(32)

            def f_i_s(eng, l0=l0):
                eng.wait_ge(lx[0].h, l0)
                eng.activation(ACOLf[0:H, 0:1], DTRf[0:H, 0:1], AF.Exp)
                eng.drain().then_inc(sev.h, 1)
            on("scalar", f_i_s)
            e0 = sev.inc()

            def f_i_v(eng, e0=e0):
                eng.wait_ge(sev.h, e0)
                eng.tensor_scalar(ACOLf[0:H, 0:1], ACOLf[0:H, 0:1], -1.0, None, ALU.mult)
                eng.memset(RCC[:, 0:4], 0.0)
                eng.memset(S_sb[:, :], 0.0)
                eng.memset(ONESf[0:H, :], 1.0)
                for pr in range(2):
                    eng.memset(LdL[pr][0:2, 0:512], 1.0)
                    eng.memset(LdR[pr][0:2, 0:512], 1.0)
                eng.drain().then_inc(svv.h, 1)
            on("vector", f_i_v)
            v0 = svv.inc()

            dtv2B = {}
            for c in range(NCH):
                cp = c % 2
                c0 = c * CH
                ms_ = mc[cp]
                g_prev = smm.n if c < 2 else chunk_mm_doneB[c - 2]
                def f_cinB(eng, cp=cp, c0=c0, g_prev=g_prev, v0=v0, ms_=ms_,
                           dtg=dtv2B.get(c - 1, 0)):
                    eng.wait_ge(svv.h, v0)
                    eng.wait_ge(smm.h, g_prev)
                    eng.wait_ge(svv.h, dtg)
                    eng.dma_start_transpose(out=XSTvB[cp][:, :],
                                            in_=scr_conv_b[0:B_INNER, c0:c0 + CH]).then_inc(ms_.h, 16)
                    eng.dma_start_transpose(out=BTvB[cp][:, :],
                                            in_=scr_B_b[0:4096, c0:c0 + CH]).then_inc(ms_.h, 16)
                    eng.dma_start(out=DTRf[0:H, 0:128],
                                  in_=scr_dt_b[0:H, c0:c0 + CH]).then_inc(ms_.h, 16)
                on("sync", f_cinB)
                cin_t = ms_.inc(16 * 3)

                def f_dtv1(eng, cin_t=cin_t, ms_=ms_, e0=sed.n + 1, e1=sed.n + 2):
                    eng.wait_ge(ms_.h, cin_t)
                    eng.tensor_scalar(Uf[0:H, :], DTRf[0:H, 0:128], DTBCf[0:H, 0:1], None,
                                      ALU.add).then_inc(sed.h, 1)
                    eng.wait_ge(sed.h, e0)
                    eng.tensor_scalar(NAf[0:H, :], Uf[0:H, :], -1.0, None,
                                      ALU.mult).then_inc(sed.h, 1)
                    eng.wait_ge(sed.h, e1)
                    eng.tensor_tensor(NAf[0:H, :], Uf[0:H, :], NAf[0:H, :], ALU.min)
                    eng.drain().then_inc(svv.h, 1)
                on("vector", f_dtv1)
                sed.inc(2)
                t1 = svv.inc()

                def f_dts1(eng, t1=t1, ea=sea.n + 1):
                    eng.wait_ge(svv.h, t1)
                    eng.activation(Ef[0:H, :], NAf[0:H, :], AF.Exp).then_inc(sea.h, 1)
                    eng.wait_ge(sea.h, ea)
                    eng.activation(Ef[0:H, :], Ef[0:H, :], AF.Ln, bias=1.0)
                    eng.drain().then_inc(sev.h, 1)
                on("scalar", f_dts1)
                sea.inc()
                t2 = sev.inc()

                def f_dtv2(eng, t2=t2, cp=cp, eb=sed.n):
                    eng.wait_ge(sev.h, t2)
                    eng.tensor_scalar(MX2f[0:H, :], Uf[0:H, :], 0.0, None,
                                      ALU.max).then_inc(sed.h, 1)
                    eng.wait_ge(sed.h, eb + 1)
                    eng.tensor_tensor(DTf[0:H, :], MX2f[0:H, :], Ef[0:H, :],
                                      ALU.add).then_inc(sed.h, 1)
                    eng.wait_ge(sed.h, eb + 2)
                    eng.tensor_scalar(DTf[0:H, :], DTf[0:H, :], 0.5, 1e-4, ALU.min,
                                      ALU.max).then_inc(sed.h, 1)
                    eng.wait_ge(sed.h, eb + 3)
                    eng.tensor_scalar(ADf[0:H, :], DTf[0:H, :], ACOLf[0:H, 0:1], None,
                                      ALU.mult).then_inc(sed.h, 1)
                    eng.wait_ge(sed.h, eb + 4)
                    eng.tensor_tensor_scan(REVIf[0:H, :][:, ::-1], ONESf[0:H, :],
                                           ADf[0:H, :][:, ::-1], 0.0, ALU.mult, ALU.add)
                    eng.tensor_tensor_scan(CUMf[0:H, :], ONESf[0:H, :], ADf[0:H, :],
                                           0.0, ALU.mult, ALU.add).then_inc(sed.h, 1)
                    eng.wait_ge(sed.h, eb + 5)
                    eng.tensor_tensor(DTEf[0:H, :], REVIf[0:H, :], ADf[0:H, :], ALU.subtract)
                    eng.tensor_scalar(RCCSf[0:H, 0:1], RCC[0:H, cp:cp + 1], LNSCALE, None, ALU.add)
                    eng.tensor_reduce(TOTf[0:H, 0:1], ADf[0:H, :], AX.X,
                                      ALU.add).then_inc(sed.h, 1)
                    eng.wait_ge(sed.h, eb + 6)
                    eng.tensor_tensor(RCC[0:H, (1 - cp):(2 - cp)], RCC[0:H, cp:cp + 1],
                                      TOTf[0:H, 0:1], ALU.add)
                    eng.drain().then_inc(svv.h, 1)
                on("vector", f_dtv2)
                sed.inc(6)
                t3 = svv.inc()
                dtv2B[c] = t3

                def f_dts2(eng, t3=t3, cp=cp):
                    eng.wait_ge(svv.h, t3)
                    eng.activation(DTEf[0:H, :], DTEf[0:H, :], AF.Exp)
                    eng.activation(LDTf[0:H, :], DTf[0:H, :], AF.Ln)
                    eng.activation(ECf[0:H, :], CUMf[0:H, :], AF.Exp, bias=SCLc[0:H, 0:1])
                    eng.activation(ECCf[0:H, :], CUMf[0:H, :], AF.Exp, bias=RCCSf[0:H, 0:1])
                    eng.drain().then_inc(sev.h, 1)
                on("scalar", f_dts2)
                t4 = sev.inc()

                def f_dtv3(eng, t4=t4, em=sed.n + 1):
                    eng.wait_ge(sev.h, t4)
                    eng.tensor_tensor(DDTf[0:H, :], DTf[0:H, :], DTEf[0:H, :], ALU.mult)
                    eng.tensor_tensor(NCf[0:H, :], LDTf[0:H, :], CUMf[0:H, :], ALU.subtract)
                    eng.drain().then_inc(svv.h, 1)
                on("vector", f_dtv3)
                t5 = svv.inc()

                def f_g1(eng, cp=cp, t5=t5, ldg=chunk_ldqB.get(c - 2, ())):
                    eng.wait_ge(svv.h, t5)
                    for s_, v_ in ldg:
                        eng.wait_ge(s_.h, v_)
                    eng.dma_start(out=scr_cum[2 * cp, 0:H * 128].rearrange("(h l) -> h l", h=H),
                                  in_=CUMf[0:H, :]).then_inc(stg.h, 16)
                    eng.dma_start(out=scr_cum[2 * cp + 1, 0:H * 128].rearrange("(h l) -> h l", h=H),
                                  in_=NCf[0:H, :]).then_inc(stg.h, 16)
                on("gpsimd", f_g1)
                st1 = stg.inc(32)

                g76 = gpop(7, 6)
                def f_tr(eng, t5=t5, g76=g76):
                    eng.wait_ge(svv.h, t5)
                    gwait(eng, g76)
                    eng.transpose(PS[7][:, 64:64 + H], DDTf[0:H, :], eye_s[0:H, 0:H])
                    eng.transpose(PS[7][:, 128:128 + H], ECf[0:H, :], eye_s[0:H, 0:H])
                    eng.transpose(PS[7][:, 192:192 + H], ECCf[0:H, :], eye_s[0:H, 0:H])
                    eng.transpose(PS[6][0:1, 0:H], TOTf[0:H, 0:1], eye_s[0:H, 0:H])
                    eng.drain().then_inc(smm.h, 1)
                on("tensor", f_tr)
                tr_t = smm.inc()

                def f_tre(eng, tr_t=tr_t, c=c, pqy=lastqyB.get(c - 1, 0)):
                    eng.wait_ge(smm.h, tr_t)
                    eng.wait_ge(svv.h, pqy)
                    eng.activation(DDTTMb[:, 0:H], PS[7][:, 64:64 + H], AF.Copy)
                    eng.activation(ECTMb[:, 0:H], PS[7][:, 128:128 + H], AF.Copy)
                    eng.activation(ECCSb[:, c * 64:(c + 1) * 64], PS[7][:, 192:192 + H], AF.Copy)
                    eng.activation(TROWf[0:1, 0:H], PS[6][0:1, 0:H], AF.Copy)
                    eng.drain().then_inc(sev.h, 1)
                on("scalar", f_tre)
                tre_t = sev.inc()
                setguard(7, sev, tre_t)

                def f_dtm(eng, tre_t=tre_t):
                    eng.wait_ge(sev.h, tre_t)
                    eng.matmul(PS[6][:, 64:64 + H], ones_row[0:1, :], TROWf[0:1, 0:H],
                               start=True, stop=True)
                    eng.drain().then_inc(smm.h, 1)
                on("tensor", f_dtm)
                dtm_t = smm.inc()

                def f_dte(eng, dtm_t=dtm_t):
                    eng.wait_ge(smm.h, dtm_t)
                    eng.activation(DTOTb[:, 0:H], PS[6][:, 64:64 + H], AF.Exp)
                    eng.drain().then_inc(sev.h, 1)
                on("scalar", f_dte)
                dte_t = sev.inc()
                setguard(6, sev, dte_t)

                big_prev = smm.n if c < 2 else chunk_mm_doneB[c - 2]
                def f_bigB(eng, cp=cp, cin_t=cin_t, dte_t=dte_t, big_prev=big_prev, ms_=ms_):
                    eng.wait_ge(ms_.h, cin_t)
                    eng.wait_ge(sev.h, dte_t)
                    eng.wait_ge(smm.h, big_prev)
                    eng.tensor_tensor(BDvB[cp].rearrange("p (h d) -> p h d", h=H),
                                      BTvB[cp].rearrange("p (h d) -> p h d", h=H),
                                      DDTTMb[:, 0:H].unsqueeze(2).broadcast_to((128, H, SP)),
                                      ALU.mult)
                    eng.tensor_tensor(DIvB[cp][0:64, :].rearrange("p (h d) -> p h d", h=H),
                                      eye_b[0:64, 0:64].unsqueeze(1).broadcast_to((64, H, 64)),
                                      DTOTb[0:64, 0:H].unsqueeze(2).broadcast_to((64, H, 64)),
                                      ALU.mult)
                    eng.drain().then_inc(svv.h, 1)
                on("vector", f_bigB)
                bigB[c] = svv.inc()

                for q in range(nq):
                    qp = q % 2
                    o = q // 2
                    bL, bG = qp, 2 + qp
                    bY, bO = 4 + (o % 2), 6 + (o % 2)
                    yo = (q % 2) * 256

                    ld_guard = qmm_ldB.get((c, q - 2), qmm_ldB.get((c - 1, nq - 2 + q), 0))
                    qs_ = ldsp[qp]
                    def f_gq(eng, cp=cp, c0=c0, q=q, qp=qp, st1=st1, ld_guard=ld_guard, qs_=qs_):
                        eng.wait_ge(stg.h, st1)
                        eng.wait_ge(smm.h, ld_guard)
                        eng.dma_start(out=LdR[qp][0:1, 0:512],
                                      in_=scr_cum[2 * cp, q * 512:(q + 1) * 512][None, :]).then_inc(qs_.h, 16)
                        eng.dma_start(out=LdL[qp][1:2, 0:512],
                                      in_=scr_cum[2 * cp + 1, q * 512:(q + 1) * 512][None, :]).then_inc(qs_.h, 16)
                        eng.dma_start(
                            out=BFq[qp][0:64, :].rearrange("p (h j) -> p h j", h=4),
                            in_=scr_B_b.rearrange("(h n) t -> n h t", n=64)[:, 4 * q:4 * q + 4,
                                                                           c0:c0 + CH]).then_inc(qs_.h, 16)
                        eng.dma_start(
                            out=CFq[qp][0:64, :].rearrange("p (h j) -> p h j", h=4),
                            in_=scr_C_b.rearrange("(h n) t -> n h t", n=64)[:, 4 * q:4 * q + 4,
                                                                           c0:c0 + CH]).then_inc(qs_.h, 16)
                    on("gpsimd", f_gq)
                    ldq = qs_.inc(64)

                    gqL = gpop(bL)
                    def f_qL(eng, qp=qp, bL=bL, ldq=ldq, v0=v0, qs_=qs_, gqL=gqL):
                        eng.wait_ge(qs_.h, ldq)
                        eng.wait_ge(svv.h, v0)
                        gwait(eng, gqL)
                        for j in range(4):
                            eng.matmul(PS[bL][:, j * 128:(j + 1) * 128], su_b, negbig_b,
                                       start=True, stop=False)
                            eng.matmul(PS[bL][:, j * 128:(j + 1) * 128],
                                       LdL[qp][0:2, j * 128:(j + 1) * 128],
                                       LdR[qp][0:2, j * 128:(j + 1) * 128],
                                       start=False, stop=True)
                        eng.drain().then_inc(smm.h, 1)
                    on("tensor", f_qL)
                    qL = smm.inc()

                    gqG = gpop(bG)
                    def f_qG(eng, q=q, qp=qp, bG=bG, ldq=ldq, qs_=qs_, gqG=gqG):
                        eng.wait_ge(qs_.h, ldq)
                        gwait(eng, gqG)
                        for j in range(4):
                            eng.matmul(PS[bG][:, j * 128:(j + 1) * 128],
                                       BFq[qp][0:64, j * 128:(j + 1) * 128],
                                       CFq[qp][0:64, j * 128:(j + 1) * 128], start=True, stop=True)
                        eng.drain().then_inc(smm.h, 1)
                    on("tensor", f_qG)
                    qG = smm.inc()

                    def f_qa(eng, bL=bL, qL=qL, pmt=pqB[0]):
                        eng.wait_ge(smm.h, qL)
                        eng.wait_ge(svv.h, pmt)
                        eng.activation(LTb[:, 0:512], PS[bL][:, 0:512], AF.Exp, bias=SCLc[:, 0:1])
                        eng.drain().then_inc(sev.h, 1)
                    on("scalar", f_qa)
                    qa = sev.inc()
                    setguard(bL, sev, qa)

                    def f_qmt(eng, bG=bG, qa=qa, qG=qG, pq2=pqB[1]):
                        eng.wait_ge(sev.h, qa)
                        eng.wait_ge(smm.h, qG)
                        eng.wait_ge(smm.h, pq2)
                        eng.tensor_tensor(MTb[:, 0:512], PS[bG][:, 0:512], LTb[:, 0:512], ALU.mult)
                        eng.drain().then_inc(svv.h, 1)
                    on("vector", f_qmt)
                    qmt = svv.inc()
                    setguard(bG, svv, qmt)
                    pqB[0] = qmt

                    sev_prev = sevacB.get((c - 1, q), 0)
                    gq2 = gpop(bY, bO, bL)
                    def f_q2(eng, q=q, cp=cp, qp=qp, bL=bL, bY=bY, bO=bO, yo=yo, qmt=qmt,
                             big_t=bigB[c], qa=qa, sev_prev=sev_prev, gq=gq2):
                        eng.wait_ge(svv.h, qmt)
                        eng.wait_ge(svv.h, big_t)
                        eng.wait_ge(sev.h, qa)
                        eng.wait_ge(sev.h, sev_prev)
                        gwait(eng, gq)
                        for j in range(4):
                            h = 4 * q + j
                            eng.matmul(PS[bY][:, yo + j * 64:yo + (j + 1) * 64],
                                       MTb[:, j * 128:(j + 1) * 128],
                                       XSTvB[cp][:, h * 64:(h + 1) * 64], start=True, stop=True)
                            eng.matmul(PS[bO][:, yo + j * 64:yo + (j + 1) * 64],
                                       CFq[qp][0:64, j * 128:(j + 1) * 128],
                                       S_sb[0:64, h * 64:(h + 1) * 64], start=True, stop=True)
                            eng.matmul(PS[bL][0:64, j * 64:(j + 1) * 64],
                                       DIvB[cp][0:64, h * 64:(h + 1) * 64],
                                       S_sb[0:64, h * 64:(h + 1) * 64], start=True, stop=False)
                            eng.matmul(PS[bL][0:64, j * 64:(j + 1) * 64],
                                       BDvB[cp][:, h * 64:(h + 1) * 64],
                                       XSTvB[cp][:, h * 64:(h + 1) * 64], start=False, stop=True)
                        eng.drain().then_inc(smm.h, 1)
                    on("tensor", f_q2)
                    q2 = smm.inc()
                    pqB[1] = q2
                    qmm_ldB[(c, q)] = q2

                    if q % 2 == 1:
                        ysg_c = ytm_storeB.get(c - 1, 0)
                        def f_qy(eng, o=o, bY=bY, bO=bO, q2=q2, ysg=ysg_c,
                                 sy_=syp[(c - 1) % 2], em=sed.n + 1):
                            eng.wait_ge(smm.h, q2)
                            eng.wait_ge(sy_.h, ysg)
                            eng.tensor_tensor(TAP2.rearrange("p (h j) -> p h j", h=8),
                                              PS[bO].ap().rearrange("p (h j) -> p h j", h=8),
                                              ECTMb[:, 8 * o:8 * o + 8].unsqueeze(2)
                                              .broadcast_to((128, 8, 64)),
                                              ALU.mult).then_inc(sed.h, 1)
                            eng.wait_ge(sed.h, em)
                            eng.tensor_tensor(YTMb[:, o * 512:(o + 1) * 512],
                                              PS[bY][:, 0:512], TAP2[:, 0:512], ALU.add)
                            eng.drain().then_inc(svv.h, 1)
                        on("vector", f_qy)
                        sed.inc()
                        qy = svv.inc()
                        setguard(bY, svv, qy)
                        setguard(bO, svv, qy)

                    def f_qs(eng, q=q, bL=bL, q2=q2):
                        eng.wait_ge(smm.h, q2)
                        eng.activation(S_sb[0:64, q * 256:(q + 1) * 256], PS[bL][0:64, 0:256],
                                       AF.Copy)
                        eng.drain().then_inc(sev.h, 1)
                    on("scalar", f_qs)
                    se = sev.inc()
                    sevacB[(c, q)] = se
                    setguard(bL, sev, se)
                chunk_mm_doneB[c] = smm.n
                chunk_ldqB[c] = ((ldsp[0], ldsp[0].n), (ldsp[1], ldsp[1].n))

                lastqy = svv.n
                lastqyB[c] = lastqy
                def f_yst(eng, c=c, lastqy=lastqy):
                    eng.wait_ge(svv.h, lastqy)
                    eng.dma_start(out=scr_y[c * 128:(c + 1) * 128, 0:B_INNER],
                                  in_=YTMb[:, :]).then_inc(syp[c % 2].h, 16)
                on("scalar", f_yst)
                ytm_storeB[c] = syp[c % 2].inc(16)

            fin_se = sev.n
            fin_vv = svv.n
            def f_dec(eng, fin_se=fin_se, fin_vv=fin_vv, cpf=NCH % 2):
                eng.wait_ge(sev.h, fin_se)
                eng.wait_ge(svv.h, fin_vv)
                eng.activation(DECb[0:B_HEADS, 0:1], RCC[0:B_HEADS, cpf:cpf + 1], AF.Exp)
                eng.drain().then_inc(sev.h, 1)
            on("scalar", f_dec)
            dec_t = sev.inc()

            def f_ssB(eng, dec_t=dec_t, w1=scs.n + 16 * 3):
                eng.wait_ge(sev.h, dec_t)
                eng.dma_start(out=cc_b_in[:, 0:4096], in_=S_sb[0:64, :]).then_inc(scs.h, 16)
                eng.dma_start(out=cc_b_in[:, 4096:4160], in_=S_sb[0:64, 0:64]).then_inc(scs.h, 16)
                eng.dma_start(out=scr_dec[0:1, 0:64].rearrange("a p -> p a"),
                              in_=DECb[0:64, 0:1]).then_inc(scs.h, 16)
                eng.wait_ge(scs.h, w1)
                eng.dma_start(out=cc_b_in[0:1, 4096:4160],
                              in_=scr_dec[0:1, 0:64]).then_inc(scs.h, 16)
            on("sync", f_ssB)
            scs.inc(64)

        def emit_collective_b():
            H = B_HEADS
            st_t = scs.n
            def f_cc(eng, st_t=st_t):
                eng.wait_ge(scs.h, st_t)
                eng.collective_compute(
                    "AllGather", ALU.bypass,
                    replica_groups=[list(range(8))],
                    ins=[cc_b_in[:, :].opt()],
                    outs=[cc_b_out[:, :].opt()],
                ).then_inc(scc.h, 1)
            on("gpsimd", f_cc)
            cc_t = scc.inc()

            def f_hz(eng, cc_t=cc_t):
                eng.wait_ge(scc.h, cc_t)
                eng.memset(S_sb[:, :], 0.0)
                eng.drain().then_inc(svv.h, 1)
            on("vector", f_hz)
            prev = svv.inc()

            drow_slots = [DROWb[0:1, 0:64], DECb[0:1, 0:64]]
            jld = {}
            for j in range(8):
                jp = j % 2
                ld_pv = jld.get(j - 2, 0)
                js_ = sj[jp]
                def f_jl(eng, j=j, jp=jp, cc_t=cc_t, ld_pv=ld_pv, js_=js_):
                    eng.wait_ge(scc.h, cc_t)
                    eng.wait_ge(svv.h, ld_pv)
                    eng.dma_start(out=SJv[jp][0:64, :],
                                  in_=cc_b_out[j * 64:(j + 1) * 64, 0:4096]).then_inc(js_.h, 16)
                    eng.dma_start(out=drow_slots[jp],
                                  in_=cc_b_out[j * 64:j * 64 + 1, 4096:4096 + H]).then_inc(js_.h, 16)
                on("sync", f_jl)
                jl = js_.inc(32)

                def f_jr(eng, j=j, jp=jp, jl=jl, prev=prev, js_=js_, em=sed.n + 1):
                    eng.wait_ge(js_.h, jl)
                    eng.wait_ge(svv.h, prev)
                    eng.tensor_scalar(DTILf[0:1, 0:H], drow_slots[jp],
                                      cmask_s[0:1, 8 + j:9 + j], None, ALU.mult).then_inc(sed.h, 1)
                    eng.wait_ge(sed.h, em)
                    eng.tensor_scalar(DTILf[0:1, 0:H], DTILf[0:1, 0:H],
                                      cmask_s[0:1, 16 + j:17 + j], None, ALU.add)
                    eng.tensor_scalar(TROWf[0:1, 0:H], ones_row[0:1, 0:H],
                                      cmask_s[0:1, j:j + 1], None, ALU.mult)
                    eng.drain().then_inc(svv.h, 1)
                on("vector", f_jr)
                sed.inc()
                jr = svv.inc()

                gj6 = gpop(6)
                def f_jm(eng, jr=jr, gj6=gj6):
                    eng.wait_ge(svv.h, jr)
                    gwait(eng, gj6)
                    eng.matmul(PS[6][:, 0:H], ones_row[0:1, :], DTILf[0:1, 0:H], start=True, stop=True)
                    eng.matmul(PS[6][:, 64:64 + H], ones_row[0:1, :], TROWf[0:1, 0:H],
                               start=True, stop=True)
                    eng.drain().then_inc(smm.h, 1)
                on("tensor", f_jm)
                jm = smm.inc()

                def f_je(eng, jm=jm):
                    eng.wait_ge(smm.h, jm)
                    eng.activation(DBCb[:, 0:H], PS[6][:, 0:H], AF.Copy)
                    eng.activation(MBCb[:, 0:H], PS[6][:, 64:64 + H], AF.Copy)
                    eng.drain().then_inc(sev.h, 1)
                on("scalar", f_je)
                je = sev.inc()
                setguard(6, sev, je)

                def f_jv(eng, jp=jp, je=je, em=sed.n + 1):
                    eng.wait_ge(sev.h, je)
                    eng.tensor_tensor(S_sb[0:64, :].rearrange("p (h d) -> p h d", h=H),
                                      S_sb[0:64, :].rearrange("p (h d) -> p h d", h=H),
                                      DBCb[0:64, 0:H].unsqueeze(2).broadcast_to((64, H, 64)),
                                      ALU.mult)
                    eng.tensor_tensor(T2c[0:64, :].rearrange("p (h d) -> p h d", h=H),
                                      SJv[jp][0:64, :].rearrange("p (h d) -> p h d", h=H),
                                      MBCb[0:64, 0:H].unsqueeze(2).broadcast_to((64, H, 64)),
                                      ALU.mult).then_inc(sed.h, 1)
                    eng.wait_ge(sed.h, em)
                    eng.tensor_tensor(S_sb[0:64, :], S_sb[0:64, :], T2c[0:64, :], ALU.add)
                    eng.drain().then_inc(svv.h, 1)
                on("vector", f_jv)
                sed.inc()
                prev = svv.inc()
                jld[j] = prev

        def emit_gate_outproj_b(xprev, xprev_off, xdst):
            H = B_HEADS
            gb = emit_gate_outproj_b
            for h2 in range(2):
                for ci in range(8):
                    c = 8 * h2 + ci
                    c0 = c * CH
                    ysg = ytm_storeB[c]
                    def f_gl(eng, c=c, c0=c0, ysg=ysg, pe2=gb.pe2[0],
                             psg=gb.psg[0], sy_=syp[c % 2], ptr=gb.trg.get(c - 1, 0)):
                        eng.wait_ge(sy_.h, ysg)
                        eng.wait_ge(svv.h, pe2)
                        eng.wait_ge(sev.h, psg)
                        eng.wait_ge(strs.h, ptr)
                        eng.dma_start(out=YTg[:, :],
                                      in_=scr_y[c * 128:(c + 1) * 128, 0:B_INNER]).then_inc(glx.h, 16)
                        eng.dma_start_transpose(out=GZTb[:, :],
                                                in_=scr_z_b[0:B_INNER, c0:c0 + CH]).then_inc(glx.h, 16)
                    on("sync", f_gl)
                    gl = glx.inc(16 * 2)

                    corr_v = []
                    for q in range(16):
                        bq = q % 2
                        qp = q % 2
                        qs_ = ldsp[qp]
                        cf_guard = gb.cmld.get((c, q - 2), gb.cmld.get((c - 1, 14 + q), 0))
                        def f_gcf(eng, c0=c0, q=q, qp=qp, cf_guard=cf_guard, qs_=qs_):
                            eng.wait_ge(smm.h, cf_guard)
                            eng.dma_start(
                                out=CFgq[qp][0:64, :].rearrange("p (h j) -> p h j", h=4),
                                in_=scr_C_b.rearrange("(h n) t -> n h t", n=64)[:, 4 * q:4 * q + 4,
                                                                               c0:c0 + CH]).then_inc(qs_.h, 16)
                        on("gpsimd", f_gcf)
                        cfl = qs_.inc(16)

                        gcm = gpop(bq)
                        def f_cm(eng, q=q, qp=qp, bq=bq, gl=gl, cfl=cfl, qs_=qs_, gcm=gcm):
                            eng.wait_ge(glx.h, gl)
                            eng.wait_ge(qs_.h, cfl)
                            gwait(eng, gcm)
                            for j in range(4):
                                h = 4 * q + j
                                eng.matmul(PS[bq][:, j * 64:(j + 1) * 64],
                                           CFgq[qp][0:64, j * 128:(j + 1) * 128],
                                           S_sb[0:64, h * 64:(h + 1) * 64], start=True, stop=True)
                            eng.drain().then_inc(smm.h, 1)
                        on("tensor", f_cm)
                        cm = smm.inc()
                        gb.cmld[(c, q)] = cm

                        def f_cv(eng, q=q, bq=bq, cm=cm, c=c, em=sed.n + 1):
                            eng.wait_ge(smm.h, cm)
                            eng.tensor_tensor(TAP2[:, 0:256].rearrange("p (j i) -> p j i", j=4),
                                              PS[bq].ap()[:, 0:256].rearrange("p (j i) -> p j i", j=4),
                                              ECCSb[:, c * 64 + 4 * q:c * 64 + 4 * q + 4]
                                              .unsqueeze(2).broadcast_to((128, 4, 64)),
                                              ALU.mult).then_inc(sed.h, 1)
                            eng.wait_ge(sed.h, em)
                            eng.tensor_tensor(YTg[:, q * 256:(q + 1) * 256],
                                              YTg[:, q * 256:(q + 1) * 256], TAP2[:, 0:256], ALU.add)
                            eng.drain().then_inc(svv.h, 1)
                        on("vector", f_cv)
                        sed.inc()
                        cvt = svv.inc()
                        setguard(bq, svv, cvt)
                        corr_v.append(cvt)

                    cvl = corr_v[-1]
                    def f_sq(eng, cvl=cvl, pe2=gb.pe2[0]):
                        eng.wait_ge(svv.h, cvl)
                        eng.wait_ge(svv.h, pe2)
                        eng.activation(SQDb[:, :], YTg[:, :], AF.Square, accum_out=SSQb[:, 0:1])
                        eng.drain().then_inc(sev.h, 1)
                    on("scalar", f_sq)
                    sq_t = sev.inc()

                    def f_sg(eng, gl=gl, sq_t=sq_t, pe2=gb.pe2[0]):
                        eng.wait_ge(glx.h, gl)
                        eng.wait_ge(sev.h, sq_t)
                        eng.wait_ge(svv.h, pe2)
                        eng.activation(SILZb[:, :], GZTb[:, :], AF.Silu)
                        eng.drain().then_inc(sev.h, 1)
                    on("scalar", f_sg)
                    sg = sev.inc()
                    gb.psg[0] = sg

                    def f_rs1(eng, sq_t=sq_t):
                        eng.wait_ge(sev.h, sq_t)
                        eng.tensor_scalar(MSb[:, 0:1], SSQb[:, 0:1], 1.0 / B_INNER, None, ALU.mult)
                        eng.drain().then_inc(svv.h, 1)
                    on("vector", f_rs1)
                    rs1 = svv.inc()

                    def f_rs2(eng, rs1=rs1, ea=sea.n + 1):
                        eng.wait_ge(svv.h, rs1)
                        eng.activation(RSTDb[:, 0:1], MSb[:, 0:1], AF.Ln,
                                       bias=EPSBc[:, 0:1]).then_inc(sea.h, 1)
                        eng.wait_ge(sea.h, ea)
                        eng.activation(RSTDb[:, 0:1], RSTDb[:, 0:1], AF.Exp, scale=-0.5)
                        eng.drain().then_inc(sev.h, 1)
                    on("scalar", f_rs2)
                    sea.inc()
                    rs2 = sev.inc()

                    trg = gb.trg.get(c - 1, 0)
                    def f_e2(eng, rs2=rs2, sg=sg, trg=trg, em=sed.n + 1):
                        eng.wait_ge(sev.h, rs2)
                        eng.wait_ge(sev.h, sg)
                        eng.wait_ge(strs.h, trg)
                        eng.tensor_scalar(SQDb[:, :], YTg[:, :], RSTDb[:, 0:1], None,
                                          ALU.mult).then_inc(sed.h, 1)
                        eng.wait_ge(sed.h, em)
                        eng.tensor_tensor(YTg[:, :], SQDb[:, :], SILZb[:, :], ALU.mult)
                        eng.drain().then_inc(svv.h, 1)
                    on("vector", f_e2)
                    sed.inc()
                    e2 = svv.inc()
                    gb.pe2[0] = e2

                    def f_trs(eng, e2=e2, ci=ci, opg=gb.opmm[0]):
                        eng.wait_ge(svv.h, e2)
                        eng.wait_ge(smm.h, opg)
                        for f in range(32):
                            eng.dma_start_transpose(
                                out=YFMh[:, f * 1024 + ci * 128:f * 1024 + ci * 128 + 128],
                                in_=YTg[:, f * 128:(f + 1) * 128]).then_inc(strs.h, 16)
                    on("sync", f_trs)
                    gb.trg[c] = strs.inc(16 * 32)

                half_tr = gb.trg[8 * h2 + 7]
                mmd = {}
                for m in range(16):
                    p = m % 2
                    wfree = smm.n if m < 2 else mmd[m - 2]
                    ws2 = ldwp[p]
                    def f_wl(eng, p=p, m=m, wfree=wfree, htr=half_tr, ws2=ws2):
                        eng.wait_ge(smm.h, wfree)
                        eng.wait_ge(strs.h, htr)
                        src = wb_outT.rearrange("(k p) m -> p k m", p=128)[:, :, m * 128:(m + 1) * 128]
                        eng.dma_start(out=WTOv[p].rearrange("p (k j) -> p k j", k=32),
                                      in_=src).then_inc(ws2.h, 16)
                    on("gpsimd", f_wl)
                    wt = ws2.inc(16)

                    for tt in range(2):
                        bk = 2 + (m * 2 + tt) % 2
                        tok0 = h2 * 1024 + tt * 512
                        gom = gpop(bk)
                        def f_om(eng, p=p, bk=bk, tt=tt, wt=wt, half_tr=half_tr, ws2=ws2, gom=gom):
                            eng.wait_ge(ws2.h, wt)
                            eng.wait_ge(strs.h, half_tr)
                            gwait(eng, gom)
                            for k in range(32):
                                eng.matmul(PS[bk][:, 0:512],
                                           WTOv[p][:, k * 128:(k + 1) * 128],
                                           YFMh[:, k * 1024 + tt * 512:k * 1024 + tt * 512 + 512],
                                           start=(k == 0), stop=(k == 31))
                            eng.drain().then_inc(smm.h, 1)
                        on("tensor", f_om)
                        om = smm.inc()
                        gb.opmm[0] = om
                        if tt == 1:
                            mmd[m] = om

                        xp = (m * 2 + tt) % 2
                        xs_ = xlp[xp]
                        def f_xl(eng, m=m, tok0=tok0, xp=xp, xprev=xprev, xprev_off=xprev_off,
                                 xg=gb.xld.get(xp, 0), xs_=xs_):
                            eng.wait_ge(svv.h, xg)
                            eng.dma_start(out=XOLDb[xp][:, :],
                                          in_=xprev[m * 128:(m + 1) * 128,
                                                    xprev_off + tok0:xprev_off + tok0 + 512]).then_inc(xs_.h, 16)
                        on("sync", f_xl)
                        xl = xs_.inc(16)

                        def f_xv(eng, bk=bk, xp=xp, om=om, xl=xl, xg2=gb.xst.get(xp, 0),
                                 xs_=xs_, sy_=syp[xp]):
                            eng.wait_ge(smm.h, om)
                            eng.wait_ge(xs_.h, xl)
                            eng.wait_ge(sy_.h, xg2)
                            eng.tensor_tensor(XNEWb[xp][:, :], PS[bk][:, 0:512], XOLDb[xp][:, :], ALU.add)
                            eng.drain().then_inc(svv.h, 1)
                        on("vector", f_xv)
                        xv = svv.inc()
                        setguard(bk, svv, xv)
                        gb.xld[xp] = xv

                        def f_xs(eng, m=m, tok0=tok0, xp=xp, xv=xv, xdst=xdst):
                            eng.wait_ge(svv.h, xv)
                            eng.dma_start(out=xdst[m * 128:(m + 1) * 128, tok0:tok0 + 512],
                                          in_=XNEWb[xp][:, :]).then_inc(syp[xp].h, 16)
                        on("scalar", f_xs)
                        gb.xst[xp] = syp[xp].inc(16)
        emit_gate_outproj_b.trg = {}
        emit_gate_outproj_b.xld = {}
        emit_gate_outproj_b.xst = {}
        emit_gate_outproj_b.psg = [0]
        emit_gate_outproj_b.pe2 = [0]
        emit_gate_outproj_b.opmm = [0]
        emit_gate_outproj_b.cmld = {}

        emit_init()

        cur, cur_off = x_in, HALO
        for si, sl in enumerate(sublayers):
            last = si == len(sublayers) - 1
            dst = out_t if last else x_state
            if sl == "attn_a":
                ln_tts = [(0, 0, 128, x_in)] + [(HALO + j * 256, HALO + j * 256, 256, x_in)
                                                for j in range(8)]
                applies = emit_ln(TC, ln_tts)
                if CUT != "ln":
                    NM_CONV = int(os.environ.get("P_NMC", "34"))
                    NM_GATE = int(os.environ.get("P_NMG", "32"))
                    Mlist = []
                    for m in range(NM_CONV):
                        Mlist.append(dict(kind="conv", mcol0=A_INTER + m * 128, mw=128, conv=True,
                                          dst=scr_conv_a, drow0=m * 128, crow0=m * 128))
                    for m in range(NM_GATE):
                        Mlist.append(dict(kind="gate", mcol0=m * 128, mw=128, conv=False,
                                          dst=scr_gate_a, drow0=m * 128))
                    Mlist.append(dict(kind="dt", mcol0=A_INTER + A_CONVD, mw=32, conv=False,
                                      dst=scr_dt_a, drow0=0))
                    emit_inproj2(wa_inT, Mlist, TC, ln_tts, applies, conv_a_w, conv_a_b)
                    Bar()
                if CUT not in ("ln", "inproj"):
                    emit_mixer_a()
                    Bar()
                if CUT not in ("ln", "inproj", "mixer"):
                    emit_collective_a()
                    Bar()
                if CUT not in ("ln", "inproj", "mixer", "collective"):
                    emit_gate_outproj_a(cur, cur_off, dst)
            elif sl in ("ffn1", "ffn2"):
                if CUT != "full":
                    continue
                emit_ffn.xld = {}
                emit_ffn.xst = {}
                emit_ffn(cur, cur_off, dst)
            elif sl == "attn_b":
                if CUT != "full" and "attn_a" in sublayers:
                    continue
                emit_halo_b(cur, cur_off)
                Bar()
                ln_tts_b = [(0, 0, 128, x_halo_b)] + [(HALO + j * 256, cur_off + j * 256, 256, cur)
                                                      for j in range(8)]
                applies_b = emit_ln(TC, ln_tts_b)
                if CUT != "ln":
                    MlistB = []
                    for m in range(32):
                        MlistB.append(dict(kind="conv", mcol0=B_INNER + m * 128, mw=128, conv=True,
                                           dst=scr_conv_b, drow0=m * 128, crow0=m * 128))
                    for m in range(32):
                        MlistB.append(dict(kind="z", mcol0=m * 128, mw=128, conv=False,
                                           dst=scr_z_b, drow0=m * 128))
                    for m in range(32):
                        MlistB.append(dict(kind="B", mcol0=2 * B_INNER + m * 128, mw=128, conv=False,
                                           dst=scr_B_b, drow0=m * 128))
                    for m in range(32):
                        MlistB.append(dict(kind="C", mcol0=2 * B_INNER + 4096 + m * 128, mw=128,
                                           conv=False, dst=scr_C_b, drow0=m * 128))
                    MlistB.append(dict(kind="dt", mcol0=B_PROJ - B_HEADS, mw=64, conv=False,
                                       dst=scr_dt_b, drow0=0))
                    emit_inproj2(wb_inT, MlistB, TC, ln_tts_b, applies_b, conv_b_w, None)
                    Bar()
                if CUT not in ("ln", "inproj"):
                    emit_mixer_b()
                    Bar()
                if CUT not in ("ln", "inproj", "mixer"):
                    emit_collective_b()
                    Bar()
                if CUT not in ("ln", "inproj", "mixer", "collective"):
                    emit_gate_outproj_b(cur, cur_off, dst)
            cur, cur_off = dst, 0
            Bar()

        if os.environ.get("P_DBGB", "0") == "1":
            d_halo = nc.dram_tensor("d_halo", [D, HALO], F32, kind="ExternalOutput").ap()
            d_spre = nc.dram_tensor("d_spre", [64, 4096], BF16, kind="ExternalOutput").ap()
            d_ccin = nc.dram_tensor("d_ccin", [64, CCW_B], BF16, kind="ExternalOutput").ap()
            def f_dbgb(eng, b0=bar.n, tgt=scs.n + 16 * (NK + 2)):
                eng.wait_ge(bar.h, b0)
                for k in range(NK):
                    eng.dma_start(out=d_halo[k * 128:(k + 1) * 128, :],
                                  in_=x_halo_b[k * 128:(k + 1) * 128, :]).then_inc(scs.h, 16)
                eng.dma_start(out=d_spre[:, :], in_=S_sb[0:64, :]).then_inc(scs.h, 16)
                eng.dma_start(out=d_ccin[:, :], in_=cc_b_in[:, :]).then_inc(scs.h, 16)
                eng.wait_ge(scs.h, tgt)
            on("sync", f_dbgb)
            scs.inc(16 * (NK + 2))

        if os.environ.get("P_DBG", "0") == "1":
            dbg_conv = nc.dram_tensor("dbg_conv", [A_CONVD, T], BF16, kind="ExternalOutput").ap()
            dbg_gate = nc.dram_tensor("dbg_gate", [A_INTER, T], BF16, kind="ExternalOutput").ap()
            dbg_dt = nc.dram_tensor("dbg_dt", [A_HEADS, T], F32, kind="ExternalOutput").ap()
            dbg_y = nc.dram_tensor("dbg_y", [NCH * 128, A_INTER], BF16, kind="ExternalOutput").ap()
            def f_dbg(eng, b0=bar.n, tgt=scs.n + 16 * 4):
                eng.wait_ge(bar.h, b0)
                eng.dma_start(out=dbg_conv[:, :], in_=scr_conv_a[:, :]).then_inc(scs.h, 16)
                eng.dma_start(out=dbg_gate[:, :], in_=scr_gate_a[:, :]).then_inc(scs.h, 16)
                eng.dma_start(out=dbg_dt[:, :], in_=scr_dt_a[:, :]).then_inc(scs.h, 16)
                eng.dma_start(out=dbg_y[:, :], in_=scr_y[:, :]).then_inc(scs.h, 16)
                eng.wait_ge(scs.h, tgt)
            on("sync", f_dbg)
            scs.inc(16 * 4)

        if CUT != "full":
            def f_cp(eng, b0=bar.n, tgt=scs.n + 16 * NK):
                eng.wait_ge(bar.h, b0)
                for k in range(NK):
                    eng.dma_start(out=out_t[k * 128:(k + 1) * 128, :],
                                  in_=x_in[k * 128:(k + 1) * 128, HALO:HALO + T]).then_inc(scs.h, 16)
                eng.wait_ge(scs.h, tgt)
            on("sync", f_cp)
            scs.inc(16 * NK)

        fins = [(s_, s_.n) for s_ in (stw, syp[0], syp[1], stg, scs, strs)]
        def f_fin(eng, fins=fins):
            for s_, v in fins:
                eng.wait_ge(s_.h, v)
        on("sync", f_fin)

        with nc.Block() as block:
            block.sync(lambda eng: [f(eng) for f in Q["sync"]])
            block.tensor(lambda eng: [f(eng) for f in Q["tensor"]])
            block.vector(lambda eng: [f(eng) for f in Q["vector"]])
            block.scalar(lambda eng: [f(eng) for f in Q["scalar"]])
            block.gpsimd(lambda eng: [f(eng) for f in Q["gpsimd"]])
    return nc



def _bf(a):
    return np.ascontiguousarray(a).astype(ml_dtypes.bfloat16)


def make_host_inputs(inputs, sublayers=SUBLAYERS):
    x = np.asarray(inputs["L0_attn_a_hidden_states"], dtype=np.float32)
    shared = {
        "wa_inT": np.ascontiguousarray(np.asarray(inputs["L0_attn_a_in_proj_w"], np.float32).T),
        "wa_outT": np.ascontiguousarray(np.asarray(inputs["L0_attn_a_out_proj_w"], np.float32).T),
        "conv_a_w": np.ascontiguousarray(np.asarray(inputs["L0_attn_a_conv_w"], np.float32).reshape(A_CONVD, 4)),
        "conv_a_b": np.ascontiguousarray(np.asarray(inputs["L0_attn_a_conv_b"], np.float32).reshape(A_CONVD, 1)),
        "a_Alog": np.ascontiguousarray(np.asarray(inputs["L0_attn_a_A_log"], np.float32).reshape(A_HEADS, 1)),
        "a_dtb": np.ascontiguousarray(np.asarray(inputs["L0_attn_a_dt_bias"], np.float32).reshape(A_HEADS, 1)),
        "a_Drep": np.ascontiguousarray(np.repeat(np.asarray(inputs["L0_attn_a_D"], np.float32), A_HD).reshape(1, A_INTER)),
        "a_nw": np.ascontiguousarray(np.asarray(inputs["L0_attn_a_norm_w"], np.float32).reshape(1, A_INTER)),
        "eye_f": np.eye(128, dtype=np.float32),
        "ones_f_row": np.ones((1, 128), dtype=np.float32),
        "eye_bf": _bf(np.eye(128, dtype=np.float32)),
        "su_bf": _bf(np.triu(np.ones((128, 128), np.float32), 1)),
        "negbig_bf": _bf(np.eye(128, dtype=np.float32) * -1.0e30),
        "ones_bf_col": _bf(np.ones((128, 1), np.float32)),
        "wb_inT": np.ascontiguousarray(np.asarray(inputs["L0_attn_b_in_proj_w"], np.float32).T),
        "wb_outT": np.ascontiguousarray(np.asarray(inputs["L0_attn_b_out_proj_w"], np.float32).T),
        "conv_b_w": np.ascontiguousarray(np.asarray(inputs["L0_attn_b_conv_w"], np.float32).reshape(B_INNER, 4)),
        "b_Alog": np.ascontiguousarray(np.asarray(inputs["L0_attn_b_A_log"], np.float32).reshape(B_HEADS, 1)),
        "b_dtb": np.ascontiguousarray(np.asarray(inputs["L0_attn_b_dt_bias"], np.float32).reshape(B_HEADS, 1)),
        "wf_T": np.ascontiguousarray(np.asarray(inputs["L0_ffn_W"], np.float32).T),
        "f_biasT": np.ascontiguousarray(np.asarray(inputs["L0_ffn_bias"], np.float32).reshape(16, 128)),
        "f_rw1T": np.ascontiguousarray(np.asarray(inputs["L0_ffn_rw1"], np.float32).T),
        "f_rw2T": np.ascontiguousarray(np.asarray(inputs["L0_ffn_rw2"], np.float32).T),
        "f_rb1": np.ascontiguousarray(np.asarray(inputs["L0_ffn_rb1"], np.float32).reshape(64, 1)),
        "f_rb2": np.ascontiguousarray(np.asarray(inputs["L0_ffn_rb2"], np.float32).reshape(32, 1)),
        "f_gates": np.ascontiguousarray(np.asarray(inputs["L0_ffn_gates"], np.float32).reshape(16, 1)),
        "f_AT": np.ascontiguousarray(np.asarray(inputs["L0_ffn_A"], np.float32).reshape(256, D).T),
        "f_BT2": np.ascontiguousarray(np.asarray(inputs["L0_ffn_B"], np.float32).transpose(0, 2, 1).reshape(256, D)),
        "f_sel": _bf(np.repeat(np.eye(16, dtype=np.float32), 16, axis=1) * 0 +
                     np.kron(np.eye(16, dtype=np.float32), np.ones((1, 16), np.float32))),
    }
    in_maps = []
    for ci in range(8):
        b, seg = divmod(ci, 4)
        s0 = seg * T
        arr = np.zeros((TC, D), dtype=np.float32)
        arr[HALO:] = x[b, s0:s0 + T]
        if seg > 0:
            arr[:HALO] = x[b, s0 - HALO:s0]
        m = np.zeros(8, np.float32)
        m2 = np.zeros(8, np.float32)
        lo = 4 * (ci // 4)
        for j in range(8):
            if lo <= j < ci:
                m[j] = 1.0
                m2[j] = 1.0
        cm = np.concatenate([m, m2, 1.0 - m2]).reshape(1, 24).astype(np.float32)
        hsel = np.zeros((8, 1), np.float32)
        if ci % 4 != 0:
            hsel[ci - 1, 0] = 1.0
        d = dict(shared)
        d["x_in"] = np.ascontiguousarray(arr.T)
        d["cmask"] = cm
        d["hsel_in"] = hsel
        in_maps.append(d)
    return in_maps


_NC_CACHE = {}


def kernel(**inputs):
    sub = SUBLAYERS
    key = (sub, CUT)
    if key not in _NC_CACHE:
        _NC_CACHE[key] = build_nc(sub)
    nc_ = _NC_CACHE[key]
    in_maps = make_host_inputs(inputs, sub)
    res = run_bass_kernel_spmd(nc_, in_maps, core_ids=list(range(8)))
    out = np.zeros((2, 8192, D), dtype=np.float32)
    for ci in range(8):
        b, seg = divmod(ci, 4)
        out[b, seg * T:(seg + 1) * T] = np.asarray(res.results[ci]["out"]).T
    return out


# version 03
# speedup vs baseline: 1.0436x; 1.0096x over previous
import os
import numpy as np
import ml_dtypes
from contextlib import ExitStack
import concourse.bass as bass
import concourse.mybir as mybir
from concourse.bass_utils import run_bass_kernel_spmd

F32 = mybir.dt.float32
BF16 = mybir.dt.bfloat16
AF = mybir.ActivationFunctionType
ALU = mybir.AluOpType
AX = mybir.AxisListType
ENGS = ("sync", "tensor", "vector", "scalar", "gpsimd")

D = 2048
T = 2048
HALO = 128
TC = T + HALO
NK = D // 128
CH = 128
NCH = T // CH
LN_EPS = 1e-5

A_INTER, A_HEADS, A_HD, A_STATE = 4096, 32, 128, 128
A_CONVD = A_INTER + 2 * A_STATE
A_PROJ = A_INTER + A_CONVD + A_HEADS
A_RMS_EPS = 1e-5

B_INNER, B_HEADS, B_HD, B_STATE = 4096, 64, 64, 64
B_PROJ = 2 * B_INNER + 2 * B_HEADS * B_STATE + B_HEADS
B_SCALE = float(B_STATE) ** -0.5
B_RMS_EPS = 1.1920929e-07

SUBLAYERS = tuple(os.environ.get("P_SUB", "attn_a,ffn1,attn_b,ffn2").split(","))
CUT = "full"


class Sem:
    def __init__(self, h):
        self.h = h
        self.n = 0

    def inc(self, k=1):
        self.n += k
        return self.n


def build_nc(sublayers=SUBLAYERS):
    nc = bass.Bass("TRN2", target_bir_lowering=False, debug=False)
    Q = {e: [] for e in ENGS}

    def on(e, fn):
        Q[e].append(fn)

    def din(name, shape, dt=F32):
        return nc.dram_tensor(name, list(shape), dt, kind="ExternalInput").ap()

    def dint(name, shape, dt=F32):
        return nc.dram_tensor(name, list(shape), dt, kind="Internal").ap()

    x_in = din("x_in", (D, TC))
    out_t = nc.dram_tensor("out", [D, T], F32, kind="ExternalOutput").ap()
    x_state = dint("x_state", (D, T))

    wa_inT = din("wa_inT", (D, A_PROJ))
    wa_outT = din("wa_outT", (A_INTER, D))
    conv_a_w = din("conv_a_w", (A_CONVD, 4))
    conv_a_b = din("conv_a_b", (A_CONVD, 1))
    a_Alog = din("a_Alog", (A_HEADS, 1))
    a_dtb = din("a_dtb", (A_HEADS, 1))
    a_Drep = din("a_Drep", (1, A_INTER))
    a_nw = din("a_nw", (1, A_INTER))

    wb_inT = din("wb_inT", (D, B_PROJ))
    wb_outT = din("wb_outT", (B_INNER, D))
    conv_b_w = din("conv_b_w", (B_INNER, 4))
    b_Alog = din("b_Alog", (B_HEADS, 1))
    b_dtb = din("b_dtb", (B_HEADS, 1))
    hsel_in = din("hsel_in", (8, 1))

    wf_T = din("wf_T", (D, D))
    f_biasT = din("f_biasT", (16, 128))
    f_rw1T = din("f_rw1T", (D, 64))
    f_rw2T = din("f_rw2T", (64, 32))
    f_rb1 = din("f_rb1", (64, 1))
    f_rb2 = din("f_rb2", (32, 1))
    f_gates = din("f_gates", (16, 1))
    f_AT = din("f_AT", (D, 256))
    f_BT2 = din("f_BT2", (256, D))
    f_sel = din("f_sel", (16, 256), BF16)

    eye_f = din("eye_f", (128, 128))
    ones_f_row = din("ones_f_row", (1, 128))
    eye_bf = din("eye_bf", (128, 128), BF16)
    su_bf = din("su_bf", (128, 128), BF16)
    negbig_bf = din("negbig_bf", (128, 128), BF16)
    ones_bf_col = din("ones_bf_col", (128, 1), BF16)
    cmask = din("cmask", (1, 24))

    scr_conv_a = dint("scr_conv_a", (A_CONVD, T), BF16)
    scr_gate_a = dint("scr_gate_a", (A_INTER, T), BF16)
    scr_dt_a = dint("scr_dt_a", (A_HEADS, T))
    scr_y = dint("scr_y", (NCH * 128, A_INTER), BF16)
    scr_cum = dint("scr_cum", (4, 8192))
    scr_conv_b = dint("scr_conv_b", (B_INNER, T), BF16)
    scr_z_b = dint("scr_z_b", (B_INNER, T), BF16)
    scr_B_b = dint("scr_B_b", (B_HEADS * B_STATE, T), BF16)
    scr_C_b = dint("scr_C_b", (B_HEADS * B_STATE, T), BF16)
    scr_dt_b = dint("scr_dt_b", (B_HEADS, T))
    CCW_B = B_HEADS * B_HD + 64
    cc_b_in = dint("cc_b_in", (64, CCW_B), BF16)
    cc_b_out = dint("cc_b_out", (8 * 64, CCW_B), BF16)
    cc_h_in = dint("cc_h_in", (128, 64))
    cc_h_out = dint("cc_h_out", (8 * 128, 64))
    x_halo_b = dint("x_halo_b", (D, HALO))
    CCW_A = A_HEADS * A_HD + 64
    cc_a_in = dint("cc_a_in", (128, CCW_A), BF16)
    cc_a_out = dint("cc_a_out", (8 * 128, CCW_A), BF16)
    scr_dec = dint("scr_dec", (1, 128), BF16)

    with ExitStack() as ctx:
        AB_t = ctx.enter_context(nc.sbuf_tensor("AB", [128, 68864], BF16))
        AF_t = ctx.enter_context(nc.sbuf_tensor("AFL", [128, 9216], F32))
        XF_t = ctx.enter_context(nc.sbuf_tensor("XF2", [128, 2048], F32))

        ob = [0]
        def ab(n):
            r = AB_t[:, ob[0]:ob[0] + n]; ob[0] += n
            assert ob[0] <= 68864, ob[0]
            return r
        eye_b = ab(128); su_b = ab(128); negbig_b = ab(128); onescol_b = ab(2)
        dexp_b = ab(A_INTER); nw_b = ab(A_INTER)
        S_sb = ab(4096)
        ECCSb = ab(1024)
        DECb = ab(64); DROWb = ab(64)
        TAP2 = ab(512); TAP0 = ab(512); SS2 = ab(512)
        _pad = ab((16 - ob[0] % 16) % 16)
        R1 = ab(36864)
        R2 = ab(16384)

        ofl = [0]
        def af(n):
            r = AF_t[:, ofl[0]:ofl[0] + n]; ofl[0] += n
            assert ofl[0] <= 9216, ofl[0]
            return r
        eye_s = af(128); ones_row = af(128); cmask_s = af(24)
        RCC = af(4); ACOLf = af(2); DTBCf = af(2)
        EPSc = af(2); EPSBc = af(2)
        SCLc = af(2); RCCSf = af(2)
        SSQb = af(2); MSb = af(2); RSTDb = af(2)
        HSLf = af(2); ZEROc = af(128)
        FR = af(8704)

        lnb = R1[:, 0:34816]
        TAP3 = R1[:, 34816:35328]; TAP1 = R1[:, 35328:35840]
        COt = R1[:, 35840:36352]; GTT = R1[:, 36352:36864]
        XSTv = [R1[:, 0:4096], R1[:, 4096:8192]]
        YTMv = [R1[:, 8192:12288], R1[:, 12288:16384]]
        BDv = [R1[:, 16384:20480], R1[:, 20480:24576]]
        BTv = [R1[:, 24576:28672], R1[:, 28672:32768]]
        DIv = [R2[:, 2048:6144], R2[:, 6144:10240]]
        SJv = [R1[:, 0:4096], R1[:, 4096:8192]]
        T2c = R1[:, 8192:12288]
        YFMh = R1[:, 0:32768]
        YTg = R1[:, 32768:36864]
        XB = R2[:, 0:4096]; SQ = R2[:, 4096:8192]
        WTb = [R2[:, 8192:10240], R2[:, 10240:12288]]
        LTv = R2[:, 0:512]; MTv = R2[:, 512:1024]
        Gv = R2[:, 1024:1152]
        BFv = [R2[:, 1152:1280], R2[:, 1280:1408]]
        CFv = [R2[:, 1408:1536], R2[:, 1536:1664]]
        DTTMb = R2[:, 1664:1728]; DDTTMb = R2[:, 1728:1792]
        ECTMb = R2[:, 1792:1856]; DTOTb = R2[:, 1856:1920]
        DBCb = R2[:, 1920:1984]; MBCb = R2[:, 1984:2048]
        GXS = R2[:, 0:4096]; GGT = R2[:, 4096:8192]
        GB1 = R2[:, 8192:12288]; GB2 = R2[:, 12288:16384]
        WTOv = [R2[:, 8192:12288], R2[:, 12288:16384]]
        XSTvB = [R1[:, 0:4096], R1[:, 4096:8192]]
        BTvB = [R1[:, 8192:12288], R1[:, 12288:16384]]
        BDvB = [R1[:, 16384:20480], R1[:, 20480:24576]]
        YTMb = R1[:, 24576:28672]
        LTb = R2[:, 0:512]; MTb = R2[:, 512:1024]
        DIvB = [R2[:, 2048:6144], R2[:, 8192:12288]]
        BFq = [R2[:, 6144:6656], R2[:, 6656:7168]]
        CFq = [R2[:, 7168:7680], R2[:, 7680:8192]]
        ECTMv = [R2[:, 12288:12352], R2[:, 12352:12416]]
        GZTb = R2[:, 0:4096]
        CFgq = [R2[:, 4096:4608], R2[:, 4608:5120]]
        XOLDb = [XF_t[:, 0:512], XF_t[:, 512:1024]]
        XNEWb = [XF_t[:, 1024:1536], XF_t[:, 1536:2048]]
        XT = FR[:, 0:8192]
        SRr = FR[:, 8192:8704]
        DTRf = FR[:, 0:128]; Uf = FR[:, 128:256]; NAf = FR[:, 256:384]
        Ef = FR[:, 384:512]; MX2f = FR[:, 512:640]; DTf = FR[:, 640:768]
        LDTf = MX2f
        ADf = FR[:, 768:896]; CUMf = FR[:, 896:1024]; NCf = FR[:, 1024:1152]
        REVIf = FR[:, 1152:1280]; DTEf = FR[:, 1280:1408]; DDTf = FR[:, 1408:1536]
        ECf = FR[:, 1536:1664]; ECCf = FR[:, 1664:1792]; ONESf = FR[:, 1792:1920]
        TOTf = FR[:, 1920:1922]; TROWf = FR[:, 1922:1986]; DTILf = FR[:, 1986:2050]
        LTMPf = FR[:, 2056:2568]
        LdL = [FR[:, 2568:3080], FR[:, 3080:3592]]
        LdR = [FR[:, 3592:4104], FR[:, 4104:4616]]
        SSQf = FR[:, 4616:4618]; MSf = FR[:, 4618:4620]; RSTDf = FR[:, 4620:4622]
        XOLDf = [FR[:, 4624:5136], FR[:, 5136:5648]]
        XNEWf = [FR[:, 5648:6160], FR[:, 6160:6672]]
        SILZb = FR[:, 0:4096]
        SQDb = FR[:, 4096:8192]
        HSLT = FR[:, 0:8192]

        sem_names = ("lx0", "lx1", "ldw0", "ldw1", "mc0", "mc1", "lds0", "lds1",
                     "sj0", "sj1", "glx", "xl0", "xl1", "sy0", "sy1", "str_", "stw",
                     "stg", "scs", "mm", "ev", "vv", "pp", "cc", "bar", "sed", "sea")
        SM = {n: Sem(ctx.enter_context(nc.semaphore(n))) for n in sem_names}
        lx = [SM["lx0"], SM["lx1"]]
        ldwp = [SM["ldw0"], SM["ldw1"]]
        mc = [SM["mc0"], SM["mc1"]]
        ldsp = [SM["lds0"], SM["lds1"]]
        sj = [SM["sj0"], SM["sj1"]]
        glx = SM["glx"]
        xlp = [SM["xl0"], SM["xl1"]]
        syp = [SM["sy0"], SM["sy1"]]
        strs = SM["str_"]
        stw = SM["stw"]
        stg = SM["stg"]
        scs = SM["scs"]
        smm, sev, svv, spp, scc, bar = (SM[n] for n in ("mm", "ev", "vv", "pp", "cc", "bar"))
        sed = SM["sed"]; sea = SM["sea"]
        DRAIN = {
            "sync": [SM["lx0"], SM["lx1"], SM["mc0"], SM["mc1"], SM["sj0"], SM["sj1"],
                     glx, SM["xl0"], SM["xl1"], strs, scs],
            "scalar": [stw, SM["sy0"], SM["sy1"]],
            "gpsimd": [SM["ldw0"], SM["ldw1"], SM["lds0"], SM["lds1"], stg],
        }

        PS = [ctx.enter_context(nc.psum_tensor(f"ps{i}", [128, 512], F32)) for i in range(8)]
        psguard = {i: [] for i in range(8)}

        def gpop(*banks):
            out = []
            for b in banks:
                out.extend(psguard[b])
                psguard[b] = []
            return out

        def gwait(eng, gl):
            for s_, v in gl:
                eng.wait_ge(s_.h, v)

        def setguard(b, s_, v):
            psguard[b].append((s_, v))

        def Bar():
            tgt = bar.n + len(ENGS)
            bar.n = tgt
            for e in ENGS:
                drains = [(s_, s_.n) for s_ in DRAIN.get(e, [])]
                def f(eng, t=tgt, drains=drains):
                    for s_, v in drains:
                        eng.wait_ge(s_.h, v)
                    eng.drain().then_inc(bar.h, 1)
                    eng.wait_ge(bar.h, t)
                on(e, f)
            for b in range(8):
                psguard[b] = []

        init_vc = [0]

        def emit_init():
            def f_sync(eng, tgt=lx[0].n + 16 * 7):
                for dst, src in [(eye_s, eye_f), (ones_row[0:1, :], ones_f_row),
                                 (cmask_s[0:1, :], cmask),
                                 (eye_b, eye_bf), (su_b, su_bf), (negbig_b, negbig_bf),
                                 (onescol_b[:, 0:1], ones_bf_col)]:
                    eng.dma_start(out=dst, in_=src).then_inc(lx[0].h, 16)
                eng.wait_ge(lx[0].h, tgt)
            on("sync", f_sync)
            lx[0].inc(16 * 7)

            def f_g(eng, tgt=ldwp[0].n + 32):
                eng.dma_start(out=dexp_b, in_=a_Drep.partition_broadcast(128)).then_inc(ldwp[0].h, 16)
                eng.dma_start(out=nw_b, in_=a_nw.partition_broadcast(128)).then_inc(ldwp[0].h, 16)
                eng.wait_ge(ldwp[0].h, tgt)
            on("gpsimd", f_g)
            ldwp[0].inc(32)

            def f_vc(eng):
                eng.memset(EPSc[:, 0:1], 1e-5)
                eng.memset(EPSBc[:, 0:1], B_RMS_EPS)
                eng.memset(SCLc[:, 0:1], float(np.log(B_SCALE)))
                eng.memset(ZEROc[:, :], 0.0)
                eng.drain().then_inc(svv.h, 1)
            on("vector", f_vc)
            init_vc[0] = svv.inc()

            def f_hz0(eng, v0=init_vc[0], tgt=scs.n + 16 * NK):
                eng.wait_ge(svv.h, v0)
                for k in range(NK):
                    eng.dma_start(out=x_halo_b[k * 128:(k + 1) * 128, :],
                                  in_=ZEROc[:, :]).then_inc(scs.h, 16)
                eng.wait_ge(scs.h, tgt)
            on("sync", f_hz0)
            scs.inc(16 * NK)
            Bar()

        def emit_ln(lnw, tt_list, slotw=256, lnb_t=None, post_tt=None):
            if lnb_t is None:
                lnb_t = lnb
            applies = {}
            xtg = {}
            for it, (doff, soff, w, src) in enumerate(tt_list):
                xs = (it % 2) * 16 * slotw
                ap_prev = applies.get(it - 2, 0)
                xg_prev = xtg.get(it - 2, None)

                ls = lx[it % 2]
                def f_ld(eng, xs=xs, soff=soff, w=w, src=src, ap=ap_prev, ls=ls, slotw=slotw,
                         xg=xg_prev):
                    eng.wait_ge(svv.h, ap)
                    if xg is not None:
                        eng.wait_ge(xg[0].h, xg[1])
                    for k in range(NK):
                        eng.dma_start(out=XT[:, xs + k * slotw:xs + k * slotw + w],
                                      in_=src[k * 128:(k + 1) * 128, soff:soff + w]).then_inc(ls.h, 16)
                on("sync", f_ld)
                ld_t = ls.inc(16 * NK)

                mm_prev = smm.n
                def f_sc(eng, xs=xs, w=w, ld_t=ld_t, mm_prev=mm_prev, it=it, ls=ls, slotw=slotw):
                    eng.wait_ge(ls.h, ld_t)
                    if it > 0:
                        eng.wait_ge(smm.h, mm_prev)
                    for k in range(NK):
                        eng.activation(XB[:, k * slotw:k * slotw + w],
                                       XT[:, xs + k * slotw:xs + k * slotw + w], AF.Copy)
                    for k in range(NK):
                        eng.activation(SQ[:, k * slotw:k * slotw + w],
                                       XT[:, xs + k * slotw:xs + k * slotw + w], AF.Square)
                    eng.drain().then_inc(sev.h, 1)
                on("scalar", f_sc)
                ev_t = sev.inc()

                g45 = gpop(4, 5)
                def f_t1(eng, w=w, ev_t=ev_t, g45=g45, slotw=slotw):
                    eng.wait_ge(sev.h, ev_t)
                    gwait(eng, g45)
                    for k in range(NK):
                        eng.matmul(PS[4][0:1, 0:w], onescol_b[:, 0:1], XB[:, k * slotw:k * slotw + w],
                                   start=(k == 0), stop=(k == NK - 1))
                    for k in range(NK):
                        eng.matmul(PS[5][0:1, 0:w], onescol_b[:, 0:1], SQ[:, k * slotw:k * slotw + w],
                                   start=(k == 0), stop=(k == NK - 1))
                    eng.drain().then_inc(smm.h, 1)
                on("tensor", f_t1)
                mm_t = smm.inc()

                def f_v1(eng, w=w, mm_t=mm_t):
                    eng.wait_ge(smm.h, mm_t)
                    eng.tensor_scalar(SRr[0:1, 0:w], PS[4][0:1, 0:w], 1.0 / D, None, ALU.mult)
                    eng.tensor_scalar(SRr[0:1, 256:256 + w], PS[5][0:1, 0:w], 1.0 / D, None, ALU.mult)
                    eng.drain().then_inc(svv.h, 1)
                on("vector", f_v1)
                v1_t = svv.inc()

                g6 = gpop(6)
                def f_t2(eng, w=w, v1_t=v1_t, g6=g6):
                    eng.wait_ge(svv.h, v1_t)
                    gwait(eng, g6)
                    eng.matmul(PS[6][:, 0:w], ones_row[0:1, :], SRr[0:1, 0:w], start=True, stop=True)
                    eng.drain().then_inc(smm.h, 1)
                on("tensor", f_t2)
                t2_t = smm.inc()

                def f_v2(eng, w=w, v1_t=v1_t, vmid=svv.n + 1):
                    eng.wait_ge(svv.h, v1_t)
                    eng.tensor_tensor(PS[4][0:1, 0:w], SRr[0:1, 0:w], SRr[0:1, 0:w],
                                      ALU.mult).then_inc(svv.h, 1)
                    eng.wait_ge(svv.h, vmid)
                    eng.tensor_tensor(SRr[0:1, 256:256 + w], SRr[0:1, 256:256 + w],
                                      PS[4][0:1, 0:w], ALU.subtract)
                    eng.drain().then_inc(svv.h, 1)
                on("vector", f_v2)
                svv.inc()
                v2_t = svv.inc()
                setguard(4, svv, v2_t)

                def f_s2(eng, w=w, v2_t=v2_t, ea=sea.n + 1):
                    eng.wait_ge(svv.h, v2_t)
                    eng.activation(SRr[0:1, 256:256 + w], SRr[0:1, 256:256 + w], AF.Ln,
                                   bias=EPSc[0:1, 0:1]).then_inc(sea.h, 1)
                    eng.wait_ge(sea.h, ea)
                    eng.activation(SRr[0:1, 256:256 + w], SRr[0:1, 256:256 + w], AF.Exp, scale=-0.5)
                    eng.drain().then_inc(sev.h, 1)
                on("scalar", f_s2)
                sea.inc()
                s2_t = sev.inc()

                g7 = gpop(7)
                def f_t3(eng, w=w, s2_t=s2_t, g7=g7):
                    eng.wait_ge(sev.h, s2_t)
                    gwait(eng, g7)
                    eng.matmul(PS[7][:, 0:w], ones_row[0:1, :], SRr[0:1, 256:256 + w],
                               start=True, stop=True)
                    eng.drain().then_inc(smm.h, 1)
                on("tensor", f_t3)
                t3_t = smm.inc()

                def f_v3(eng, xs=xs, doff=doff, w=w, t2_t=t2_t, t3_t=t3_t, lnw=lnw,
                         em=sed.n + 1, slotw=slotw, lnb_t=lnb_t):
                    eng.wait_ge(smm.h, t3_t)
                    for k in range(NK):
                        xt = XT[:, xs + k * slotw:xs + k * slotw + w]
                        ins = eng.tensor_tensor(xt, xt, PS[6][:, 0:w], ALU.subtract)
                    ins.then_inc(sed.h, 1)
                    eng.wait_ge(sed.h, em)
                    for k in range(NK):
                        xt = XT[:, xs + k * slotw:xs + k * slotw + w]
                        eng.tensor_tensor(lnb_t[:, k * lnw + doff:k * lnw + doff + w],
                                          xt, PS[7][:, 0:w], ALU.mult)
                    eng.drain().then_inc(svv.h, 1)
                on("vector", f_v3)
                sed.inc()
                av = svv.inc()
                applies[it] = av
                setguard(6, svv, av)
                setguard(7, svv, av)
                if post_tt is not None:
                    xtg[it] = post_tt(it, xs, doff, w, av)
            return applies

        CWp = af(20)

        def emit_inproj2(wT, Mlist, lnw, ln_tts, applies, convw_dram, convb_dram):
            def ln_wait(o, w):
                tgt = 0
                for idx, (doff, _s, ww, _src) in enumerate(ln_tts):
                    if doff < o + w and o < doff + ww:
                        tgt = max(tgt, applies[idx])
                return tgt

            mmdone = {}
            st_guard = [0]
            emit_inproj2.ptv = [0]
            halo_ln = lnw == TC
            for mi, M in enumerate(Mlist):
                p = mi % 2
                mcol0, mw = M["mcol0"], M["mw"]
                wfree = smm.n if mi < 2 else mmdone[mi - 2]
                ws = ldwp[p]
                def f_gw(eng, p=p, mcol0=mcol0, mw=mw, wfree=wfree, M=M, ws=ws):
                    eng.wait_ge(smm.h, wfree)
                    src = wT.rearrange("(k p) m -> p k m", p=128)[:, :, mcol0:mcol0 + mw]
                    eng.dma_start(out=WTb[p].rearrange("p (k j) -> p k j", k=16)[:, :, 0:mw],
                                  in_=src).then_inc(ws.h, 16)
                    if M["conv"]:
                        eng.dma_start(out=CWp[:, p * 10:p * 10 + 4],
                                      in_=convw_dram[M["crow0"]:M["crow0"] + 128, :]).then_inc(ws.h, 16)
                        if convb_dram is not None:
                            eng.dma_start(out=CWp[:, p * 10 + 4:p * 10 + 5],
                                          in_=convb_dram[M["crow0"]:M["crow0"] + 128, :]).then_inc(ws.h, 16)
                on("gpsimd", f_gw)
                nw_dma = 1 + ((2 if convb_dram is not None else 1) if M["conv"] else 0)
                w_t = ws.inc(16 * nw_dma)

                tts = ([("halo", 0, -1, 128)] if (M["conv"] and halo_ln) else [])
                off0 = HALO if halo_ln else 0
                tts += [("real", off0 + i * 512, i * 512, 512) for i in range(4)]
                prev_bank = None
                prev_w = None
                for ti, (knd, lno, so, w) in enumerate(tts):
                    bk = ti % 3
                    lnv = ln_wait(lno, w)
                    gb = gpop(bk)
                    def f_mm(eng, p=p, mw=mw, lno=lno, w=w, bk=bk, w_t=w_t, lnv=lnv, lnw=lnw, ws=ws, gb=gb):
                        eng.wait_ge(ws.h, w_t)
                        eng.wait_ge(svv.h, lnv)
                        gwait(eng, gb)
                        for k in range(NK):
                            eng.matmul(PS[bk][0:mw, 0:w],
                                       WTb[p][:, k * 128:k * 128 + mw],
                                       lnb[:, k * lnw + lno:k * lnw + lno + w],
                                       start=(k == 0), stop=(k == NK - 1))
                        eng.drain().then_inc(smm.h, 1)
                    on("tensor", f_mm)
                    g_t = smm.inc()
                    if ti == len(tts) - 1:
                        mmdone[mi] = g_t
                    if knd == "halo":
                        prev_bank = bk
                        prev_w = w
                        setguard(bk, smm, g_t)
                        continue

                    if M["conv"]:
                        pb = prev_bank
                        cw = CWp[:, p * 10:p * 10 + 4]
                        cb = CWp[:, p * 10 + 4:p * 10 + 5] if convb_dram is not None else None
                        fr = pb is None
                        pw = prev_w
                        def f_ta(eng, b=bk, pb=pb, g_t=g_t, cw=cw, fr=fr, gg=st_guard[0], pw=pw,
                                 ptv=emit_inproj2.ptv[0]):
                            eng.wait_ge(smm.h, g_t)
                            eng.wait_ge(stw.h, gg)
                            eng.wait_ge(svv.h, ptv)
                            eng.activation(TAP3[:, 0:512], PS[b][:, 0:512], AF.Copy, scale=cw[:, 3:4])
                            eng.activation(TAP1[:, 2:512], PS[b][:, 0:510], AF.Copy, scale=cw[:, 1:2])
                            if fr:
                                eng.memset(TAP1[:, 0:2], 0.0)
                            else:
                                eng.activation(TAP1[:, 0:2], PS[pb][:, pw - 2:pw], AF.Copy, scale=cw[:, 1:2])
                            eng.drain().then_inc(sev.h, 1)
                        on("scalar", f_ta)
                        ta = sev.inc()

                        def f_tv(eng, b=bk, pb=pb, g_t=g_t, cw=cw, fr=fr, ppg=spp.n, pw=pw, ta=ta):
                            eng.wait_ge(smm.h, g_t)
                            eng.wait_ge(spp.h, ppg)
                            eng.wait_ge(sev.h, ta)
                            eng.tensor_scalar(TAP2[:, 1:512], PS[b][:, 0:511], cw[:, 2:3], None, ALU.mult)
                            eng.tensor_scalar(TAP0[:, 3:512], PS[b][:, 0:509], cw[:, 0:1], None, ALU.mult)
                            if fr:
                                eng.memset(TAP2[:, 0:1], 0.0)
                                eng.memset(TAP0[:, 0:3], 0.0)
                            else:
                                eng.tensor_scalar(TAP2[:, 0:1], PS[pb][:, pw - 1:pw], cw[:, 2:3], None, ALU.mult)
                                eng.tensor_scalar(TAP0[:, 0:3], PS[pb][:, pw - 3:pw], cw[:, 0:1], None, ALU.mult)
                            eng.drain().then_inc(svv.h, 1)
                        on("vector", f_tv)
                        tv = svv.inc()
                        emit_inproj2.ptv[0] = tv
                        setguard(bk, sev, ta); setguard(bk, svv, tv)
                        if pb is not None:
                            setguard(pb, sev, ta); setguard(pb, svv, tv)

                        def f_pp(eng, ta=ta, tv=tv):
                            eng.wait_ge(sev.h, ta)
                            eng.wait_ge(svv.h, tv)
                            eng.tensor_tensor(SS2[:, 0:512], TAP1[:, 0:512], TAP0[:, 0:512], ALU.add)
                            eng.drain().then_inc(spp.h, 1)
                        on("gpsimd", f_pp)
                        pp_t = spp.inc()

                        def f_av(eng, ta=ta, tv=tv, pp_t=pp_t, em=sed.n + 1):
                            eng.wait_ge(sev.h, ta)
                            eng.tensor_tensor(COt[:, 0:512], TAP3[:, 0:512], TAP2[:, 0:512],
                                              ALU.add).then_inc(sed.h, 1)
                            eng.wait_ge(spp.h, pp_t)
                            eng.wait_ge(sed.h, em)
                            eng.tensor_tensor(COt[:, 0:512], COt[:, 0:512], SS2[:, 0:512], ALU.add)
                            eng.drain().then_inc(svv.h, 1)
                        on("vector", f_av)
                        sed.inc()
                        cv = svv.inc()

                        def f_si(eng, cv=cv, cb=cb, gg=st_guard[0]):
                            eng.wait_ge(svv.h, cv)
                            eng.wait_ge(stw.h, gg)
                            if cb is not None:
                                eng.activation(GTT[:, 0:512], COt[:, 0:512], AF.Silu, bias=cb)
                            else:
                                eng.activation(GTT[:, 0:512], COt[:, 0:512], AF.Silu)
                            eng.drain().then_inc(sev.h, 1)
                        on("scalar", f_si)
                        si = sev.inc()

                        def f_st(eng, M=M, so=so, si=si):
                            eng.wait_ge(sev.h, si)
                            eng.dma_start(out=M["dst"][M["drow0"]:M["drow0"] + 128, so:so + 512],
                                          in_=GTT[:, 0:512]).then_inc(stw.h, 16)
                        on("scalar", f_st)
                        st_guard[0] = stw.inc(16)
                    else:
                        isdt = M["kind"] == "dt"
                        def f_ev(eng, b=bk, g_t=g_t, mw=mw, isdt=isdt, gg=st_guard[0]):
                            eng.wait_ge(smm.h, g_t)
                            eng.wait_ge(stw.h, gg)
                            if isdt:
                                eng.activation(XT[0:mw, 0:512], PS[b][0:mw, 0:512], AF.Copy)
                            else:
                                eng.activation(GTT[:, 0:512], PS[b][:, 0:512], AF.Copy)
                            eng.drain().then_inc(sev.h, 1)
                        on("scalar", f_ev)
                        e_t = sev.inc()
                        setguard(bk, sev, e_t)

                        def f_st2(eng, M=M, so=so, e_t=e_t, mw=mw, isdt=isdt):
                            eng.wait_ge(sev.h, e_t)
                            src = XT[0:mw, 0:512] if isdt else GTT[:, 0:512]
                            eng.dma_start(out=M["dst"][M["drow0"]:M["drow0"] + mw, so:so + 512],
                                          in_=src).then_inc(stw.h, 16)
                        on("scalar", f_st2)
                        st_guard[0] = stw.inc(16)
                    prev_bank = bk
                    prev_w = w

        sevac = {}
        chunk_mm_done = {}
        qmm_ld = {}
        chunk_ldq = {}
        ytm_store = {}
        mixer_pq = [0, 0]

        def emit_mixer_a():
            H, HD = A_HEADS, A_HD
            HP = H * HD
            nq = H // 4

            def f_ld0(eng):
                eng.dma_start(out=DTRf[0:H, 0:1], in_=a_Alog[:, :]).then_inc(lx[0].h, 16)
                eng.dma_start(out=DTBCf[0:H, 0:1], in_=a_dtb[:, :]).then_inc(lx[0].h, 16)
            on("sync", f_ld0)
            l0 = lx[0].inc(32)

            def f_i_s(eng, l0=l0):
                eng.wait_ge(lx[0].h, l0)
                eng.activation(ACOLf[0:H, 0:1], DTRf[0:H, 0:1], AF.Exp)
                eng.drain().then_inc(sev.h, 1)
            on("scalar", f_i_s)
            e0 = sev.inc()

            def f_i_v(eng, e0=e0):
                eng.wait_ge(sev.h, e0)
                eng.tensor_scalar(ACOLf[0:H, 0:1], ACOLf[0:H, 0:1], -1.0, None, ALU.mult)
                eng.memset(RCC[:, 0:4], 0.0)
                eng.memset(S_sb[:, :], 0.0)
                eng.memset(ONESf[0:H, :], 1.0)
                for pr in range(2):
                    eng.memset(LdL[pr][0:2, 0:512], 1.0)
                    eng.memset(LdR[pr][0:2, 0:512], 1.0)
                eng.drain().then_inc(svv.h, 1)
            on("vector", f_i_v)
            v0 = svv.inc()

            dtv2_of = {}
            g1_of = {}
            for c in range(NCH):
                cp = c % 2
                c0 = c * CH
                g_prev = smm.n if c < 2 else chunk_mm_done[c - 2]
                ms_ = mc[cp]
                def f_cin(eng, cp=cp, c0=c0, g_prev=g_prev, v0=v0, ms_=ms_,
                          dtg=dtv2_of.get(c - 1, 0)):
                    eng.wait_ge(svv.h, v0)
                    eng.wait_ge(smm.h, g_prev)
                    eng.wait_ge(svv.h, dtg)
                    eng.dma_start_transpose(out=XSTv[cp][:, :],
                                            in_=scr_conv_a[0:HP, c0:c0 + CH]).then_inc(ms_.h, 16)
                    eng.dma_start(out=BFv[cp][:, :],
                                  in_=scr_conv_a[A_INTER:A_INTER + 128, c0:c0 + CH]).then_inc(ms_.h, 16)
                    eng.dma_start(out=CFv[cp][:, :],
                                  in_=scr_conv_a[A_INTER + 128:A_INTER + 256, c0:c0 + CH]).then_inc(ms_.h, 16)
                    eng.dma_start_transpose(out=BTv[cp][:, 0:128],
                                            in_=scr_conv_a[A_INTER:A_INTER + 128, c0:c0 + CH]).then_inc(ms_.h, 16)
                    eng.dma_start(out=DTRf[0:H, 0:128],
                                  in_=scr_dt_a[0:H, c0:c0 + CH]).then_inc(ms_.h, 16)
                on("sync", f_cin)
                cin_t = ms_.inc(16 * 5)

                def f_dtv1(eng, cin_t=cin_t, ms_=ms_, e0=sed.n + 1, e1=sed.n + 2):
                    eng.wait_ge(ms_.h, cin_t)
                    eng.tensor_scalar(Uf[0:H, :], DTRf[0:H, 0:128], DTBCf[0:H, 0:1], None,
                                      ALU.add).then_inc(sed.h, 1)
                    eng.wait_ge(sed.h, e0)
                    eng.tensor_scalar(NAf[0:H, :], Uf[0:H, :], -1.0, None,
                                      ALU.mult).then_inc(sed.h, 1)
                    eng.wait_ge(sed.h, e1)
                    eng.tensor_tensor(NAf[0:H, :], Uf[0:H, :], NAf[0:H, :], ALU.min)
                    eng.drain().then_inc(svv.h, 1)
                on("vector", f_dtv1)
                sed.inc(2)
                t1 = svv.inc()

                def f_dts1(eng, t1=t1, ea=sea.n + 1):
                    eng.wait_ge(svv.h, t1)
                    eng.activation(Ef[0:H, :], NAf[0:H, :], AF.Exp).then_inc(sea.h, 1)
                    eng.wait_ge(sea.h, ea)
                    eng.activation(Ef[0:H, :], Ef[0:H, :], AF.Ln, bias=1.0)
                    eng.drain().then_inc(sev.h, 1)
                on("scalar", f_dts1)
                sea.inc()
                t2 = sev.inc()

                def f_dtv2(eng, t2=t2, cp=cp, eb=sed.n):
                    eng.wait_ge(sev.h, t2)
                    eng.tensor_scalar(MX2f[0:H, :], Uf[0:H, :], 0.0, None,
                                      ALU.max).then_inc(sed.h, 1)
                    eng.wait_ge(sed.h, eb + 1)
                    eng.tensor_tensor(DTf[0:H, :], MX2f[0:H, :], Ef[0:H, :],
                                      ALU.add).then_inc(sed.h, 1)
                    eng.wait_ge(sed.h, eb + 2)
                    eng.tensor_scalar(DTf[0:H, :], DTf[0:H, :], 100.0, 0.0, ALU.min,
                                      ALU.max).then_inc(sed.h, 1)
                    eng.wait_ge(sed.h, eb + 3)
                    eng.tensor_scalar(ADf[0:H, :], DTf[0:H, :], ACOLf[0:H, 0:1], None,
                                      ALU.mult).then_inc(sed.h, 1)
                    eng.wait_ge(sed.h, eb + 4)
                    eng.tensor_tensor_scan(REVIf[0:H, :][:, ::-1], ONESf[0:H, :],
                                           ADf[0:H, :][:, ::-1], 0.0, ALU.mult, ALU.add)
                    eng.tensor_tensor_scan(CUMf[0:H, :], ONESf[0:H, :], ADf[0:H, :],
                                           0.0, ALU.mult, ALU.add).then_inc(sed.h, 1)
                    eng.wait_ge(sed.h, eb + 5)
                    eng.tensor_tensor(DTEf[0:H, :], REVIf[0:H, :], ADf[0:H, :], ALU.subtract)
                    eng.tensor_reduce(TOTf[0:H, 0:1], ADf[0:H, :], AX.X,
                                      ALU.add).then_inc(sed.h, 1)
                    eng.wait_ge(sed.h, eb + 6)
                    eng.tensor_tensor(RCC[0:H, (1 - cp):(2 - cp)], RCC[0:H, cp:cp + 1],
                                      TOTf[0:H, 0:1], ALU.add)
                    eng.drain().then_inc(svv.h, 1)
                on("vector", f_dtv2)
                sed.inc(6)
                t3 = svv.inc()
                dtv2_of[c] = t3

                def f_dts2(eng, t3=t3, cp=cp):
                    eng.wait_ge(svv.h, t3)
                    eng.activation(DTEf[0:H, :], DTEf[0:H, :], AF.Exp)
                    eng.activation(LDTf[0:H, :], DTf[0:H, :], AF.Ln)
                    eng.activation(ECf[0:H, :], CUMf[0:H, :], AF.Exp)
                    eng.activation(ECCf[0:H, :], CUMf[0:H, :], AF.Exp, bias=RCC[0:H, cp:cp + 1])
                    eng.drain().then_inc(sev.h, 1)
                on("scalar", f_dts2)
                t4 = sev.inc()

                def f_dtv3(eng, t4=t4, em=sed.n + 1):
                    eng.wait_ge(sev.h, t4)
                    eng.tensor_tensor(DDTf[0:H, :], DTf[0:H, :], DTEf[0:H, :], ALU.mult)
                    eng.tensor_tensor(NCf[0:H, :], LDTf[0:H, :], CUMf[0:H, :], ALU.subtract)
                    eng.drain().then_inc(svv.h, 1)
                on("vector", f_dtv3)
                t5 = svv.inc()

                def f_g1(eng, cp=cp, t5=t5, ldg=chunk_ldq.get(c - 2, ())):
                    eng.wait_ge(svv.h, t5)
                    for s_, v_ in ldg:
                        eng.wait_ge(s_.h, v_)
                    eng.dma_start(out=scr_cum[2 * cp, 0:H * 128].rearrange("(h l) -> h l", h=H),
                                  in_=CUMf[0:H, :]).then_inc(stg.h, 16)
                    eng.dma_start(out=scr_cum[2 * cp + 1, 0:H * 128].rearrange("(h l) -> h l", h=H),
                                  in_=NCf[0:H, :]).then_inc(stg.h, 16)
                on("gpsimd", f_g1)
                st1 = stg.inc(32)
                g1_of[c] = st1

                g76 = gpop(7, 6)
                def f_tr(eng, t5=t5, g76=g76):
                    eng.wait_ge(svv.h, t5)
                    gwait(eng, g76)
                    eng.transpose(PS[7][:, 128:128 + H], DDTf[0:H, :], eye_s[0:H, 0:H])
                    eng.transpose(PS[7][:, 256:256 + H], ECf[0:H, :], eye_s[0:H, 0:H])
                    eng.transpose(PS[7][:, 384:384 + H], ECCf[0:H, :], eye_s[0:H, 0:H])
                    eng.transpose(PS[6][0:1, 0:H], TOTf[0:H, 0:1], eye_s[0:H, 0:H])
                    eng.drain().then_inc(smm.h, 1)
                on("tensor", f_tr)
                tr_t = smm.inc()

                def f_tre(eng, tr_t=tr_t, c=c, cp=cp):
                    eng.wait_ge(smm.h, tr_t)
                    eng.activation(DDTTMb[:, 0:H], PS[7][:, 128:128 + H], AF.Copy)
                    eng.activation(ECTMv[cp][:, 0:H], PS[7][:, 256:256 + H], AF.Copy)
                    eng.activation(ECCSb[:, c * 64:c * 64 + H], PS[7][:, 384:384 + H], AF.Copy)
                    eng.activation(TROWf[0:1, 0:H], PS[6][0:1, 0:H], AF.Copy)
                    eng.drain().then_inc(sev.h, 1)
                on("scalar", f_tre)
                tre_t = sev.inc()
                setguard(7, sev, tre_t)

                def f_dtm(eng, tre_t=tre_t):
                    eng.wait_ge(sev.h, tre_t)
                    eng.matmul(PS[6][:, 64:64 + H], ones_row[0:1, :], TROWf[0:1, 0:H],
                               start=True, stop=True)
                    eng.drain().then_inc(smm.h, 1)
                on("tensor", f_dtm)
                dtm_t = smm.inc()

                def f_dte(eng, dtm_t=dtm_t):
                    eng.wait_ge(smm.h, dtm_t)
                    eng.activation(DTOTb[:, 0:H], PS[6][:, 64:64 + H], AF.Exp)
                    eng.drain().then_inc(sev.h, 1)
                on("scalar", f_dte)
                dte_t = sev.inc()
                setguard(6, sev, dte_t)

                big_prev = smm.n if c < 2 else chunk_mm_done[c - 2]
                def f_big(eng, cp=cp, cin_t=cin_t, dte_t=dte_t, big_prev=big_prev, ms_=ms_):
                    eng.wait_ge(ms_.h, cin_t)
                    eng.wait_ge(sev.h, dte_t)
                    eng.wait_ge(smm.h, big_prev)
                    eng.tensor_tensor(BDv[cp].rearrange("p (h d) -> p h d", h=H),
                                      BTv[cp][:, 0:128].unsqueeze(1).broadcast_to((128, H, 128)),
                                      DDTTMb[:, 0:H].unsqueeze(2).broadcast_to((128, H, 128)),
                                      ALU.mult)
                    eng.tensor_tensor(DIv[cp].rearrange("p (h d) -> p h d", h=H),
                                      eye_b.unsqueeze(1).broadcast_to((128, H, 128)),
                                      DTOTb[:, 0:H].unsqueeze(2).broadcast_to((128, H, 128)),
                                      ALU.mult)
                    eng.drain().then_inc(svv.h, 1)
                on("vector", f_big)
                big_t = svv.inc()

                g6m = gpop(6)
                def f_gmm(eng, cp=cp, cin_t=cin_t, ms_=ms_, g6m=g6m):
                    eng.wait_ge(ms_.h, cin_t)
                    gwait(eng, g6m)
                    eng.matmul(PS[6][:, 128:256], BFv[cp][:, :], CFv[cp][:, :], start=True, stop=True)
                    eng.drain().then_inc(smm.h, 1)
                on("tensor", f_gmm)
                gm_t = smm.inc()

                def f_gev(eng, gm_t=gm_t):
                    eng.wait_ge(smm.h, gm_t)
                    eng.activation(Gv[:, :], PS[6][:, 128:256], AF.Copy)
                    eng.drain().then_inc(sev.h, 1)
                on("scalar", f_gev)
                ge_t = sev.inc()
                setguard(6, sev, ge_t)

                for q in range(nq):
                    qp = q % 2
                    bL, bY, bO = qp, 2 + qp, 4 + qp

                    ld_guard = qmm_ld.get((c, q - 2), qmm_ld.get((c - 1, nq - 2 + q), 0))
                    qs_ = ldsp[qp]
                    def f_gq(eng, cp=cp, q=q, qp=qp, st1=st1, ld_guard=ld_guard, qs_=qs_):
                        eng.wait_ge(stg.h, st1)
                        eng.wait_ge(smm.h, ld_guard)
                        eng.dma_start(out=LdR[qp][0:1, 0:512],
                                      in_=scr_cum[2 * cp, q * 512:(q + 1) * 512][None, :]).then_inc(qs_.h, 16)
                        eng.dma_start(out=LdL[qp][1:2, 0:512],
                                      in_=scr_cum[2 * cp + 1, q * 512:(q + 1) * 512][None, :]).then_inc(qs_.h, 16)
                    on("gpsimd", f_gq)
                    ldq = qs_.inc(32)

                    gqL = gpop(bL)
                    def f_qL(eng, qp=qp, bL=bL, ldq=ldq, v0=v0, qs_=qs_, gqL=gqL):
                        eng.wait_ge(qs_.h, ldq)
                        eng.wait_ge(svv.h, v0)
                        gwait(eng, gqL)
                        for j in range(4):
                            eng.matmul(PS[bL][:, j * 128:(j + 1) * 128], su_b, negbig_b,
                                       start=True, stop=False)
                            eng.matmul(PS[bL][:, j * 128:(j + 1) * 128],
                                       LdL[qp][0:2, j * 128:(j + 1) * 128],
                                       LdR[qp][0:2, j * 128:(j + 1) * 128],
                                       start=False, stop=True)
                        eng.drain().then_inc(smm.h, 1)
                    on("tensor", f_qL)
                    qL = smm.inc()
                    qmm_ld[(c, q)] = qL

                    def f_qa(eng, bL=bL, qL=qL, pqy=mixer_pq[0]):
                        eng.wait_ge(smm.h, qL)
                        eng.wait_ge(svv.h, pqy)
                        eng.activation(LTv[:, 0:512], PS[bL][:, 0:512], AF.Exp)
                        eng.drain().then_inc(sev.h, 1)
                    on("scalar", f_qa)
                    qa = sev.inc()
                    setguard(bL, sev, qa)

                    def f_qmt(eng, qa=qa, ge_t=ge_t, pq2=mixer_pq[1]):
                        eng.wait_ge(sev.h, qa)
                        eng.wait_ge(sev.h, ge_t)
                        eng.wait_ge(smm.h, pq2)
                        eng.tensor_tensor(MTv.rearrange("p (j i) -> p j i", j=4),
                                          LTv.rearrange("p (j i) -> p j i", j=4),
                                          Gv.unsqueeze(1).broadcast_to((128, 4, 128)), ALU.mult)
                        eng.drain().then_inc(svv.h, 1)
                    on("vector", f_qmt)
                    qmt = svv.inc()

                    sev_prev = sevac.get((c - 1, q), 0)
                    gq2 = gpop(bY, bO, bL)
                    def f_q2(eng, q=q, cp=cp, bL=bL, bY=bY, bO=bO, qmt=qmt, big_t=big_t,
                             qa=qa, sev_prev=sev_prev, gq=gq2):
                        eng.wait_ge(svv.h, qmt)
                        eng.wait_ge(svv.h, big_t)
                        eng.wait_ge(sev.h, qa)
                        eng.wait_ge(sev.h, sev_prev)
                        gwait(eng, gq)
                        for j in range(4):
                            h = 4 * q + j
                            eng.matmul(PS[bY][:, j * 128:(j + 1) * 128],
                                       MTv[:, j * 128:(j + 1) * 128],
                                       XSTv[cp][:, h * 128:(h + 1) * 128], start=True, stop=True)
                            eng.matmul(PS[bO][:, j * 128:(j + 1) * 128],
                                       CFv[cp][:, :], S_sb[:, h * 128:(h + 1) * 128],
                                       start=True, stop=True)
                            eng.matmul(PS[bL][:, j * 128:(j + 1) * 128],
                                       DIv[cp][:, h * 128:(h + 1) * 128],
                                       S_sb[:, h * 128:(h + 1) * 128], start=True, stop=False)
                            eng.matmul(PS[bL][:, j * 128:(j + 1) * 128],
                                       BDv[cp][:, h * 128:(h + 1) * 128],
                                       XSTv[cp][:, h * 128:(h + 1) * 128], start=False, stop=True)
                        eng.drain().then_inc(smm.h, 1)
                    on("tensor", f_q2)
                    q2 = smm.inc()
                    mixer_pq[1] = q2

                    def f_qy(eng, q=q, cp=cp, bY=bY, bO=bO, q2=q2, ysg=ytm_store.get(c - 2, 0), sy_=syp[cp],
                             em=sed.n + 1):
                        eng.wait_ge(smm.h, q2)
                        eng.wait_ge(sy_.h, ysg)
                        eng.tensor_tensor(LTv.rearrange("p (j i) -> p j i", j=4),
                                          PS[bO].ap().rearrange("p (j i) -> p j i", j=4),
                                          ECTMv[cp][:, 4 * q:4 * q + 4].unsqueeze(2).broadcast_to((128, 4, 128)),
                                          ALU.mult).then_inc(sed.h, 1)
                        eng.wait_ge(sed.h, em)
                        eng.tensor_tensor(YTMv[cp][:, q * 512:(q + 1) * 512],
                                          PS[bY][:, 0:512], LTv[:, 0:512], ALU.add)
                        eng.drain().then_inc(svv.h, 1)
                    on("vector", f_qy)
                    sed.inc()
                    qy = svv.inc()
                    mixer_pq[0] = qy
                    if c == 0 and q == 0 and os.environ.get("P_DBG2", "0") == "1":
                        dbgs = []
                        for nm, tile in [("d_xst", XSTv[0]), ("d_bd", BDv[0]),
                                          ("d_di", DIv[0]), ("d_lt", LTv), ("d_mt", MTv),
                                          ("d_g", Gv), ("d_bt", BTv[0][:, 0:128]),
                                          ("d_bf", BFv[0]), ("d_cf", CFv[0])]:
                            dt_ = nc.dram_tensor(nm, [128, tile.shape[1]], BF16,
                                                 kind="ExternalOutput").ap()
                            dbgs.append((dt_, tile))
                        for nm, tile in [("d_cum", CUMf), ("d_dtf", DTf), ("d_ltmp", LTMPf)]:
                            dt_ = nc.dram_tensor(nm, [128, tile.shape[1]], F32,
                                                 kind="ExternalOutput").ap()
                            dbgs.append((dt_, tile))
                        def f_dbg2(eng, dbgs=dbgs, w1=qy, w2=sevac.get((0, 0), 0), wq=smm.n,
                                   tgt=scs.n + 16 * len(dbgs)):
                            eng.wait_ge(svv.h, w1)
                            eng.wait_ge(smm.h, wq)
                            for dt_, tile in dbgs:
                                eng.dma_start(out=dt_[0:tile.shape[0], :], in_=tile).then_inc(scs.h, 16)
                            eng.wait_ge(scs.h, tgt)
                        on("sync", f_dbg2)
                        scs.inc(16 * len(dbgs))
                    setguard(bY, svv, qy)
                    setguard(bO, svv, qy)

                    def f_qs(eng, q=q, bL=bL, q2=q2):
                        eng.wait_ge(smm.h, q2)
                        eng.activation(S_sb[:, q * 512:(q + 1) * 512], PS[bL][:, 0:512], AF.Copy)
                        eng.drain().then_inc(sev.h, 1)
                    on("scalar", f_qs)
                    se = sev.inc()
                    sevac[(c, q)] = se
                    setguard(bL, sev, se)
                chunk_mm_done[c] = smm.n
                chunk_ldq[c] = ((ldsp[0], ldsp[0].n), (ldsp[1], ldsp[1].n))

                lastqy = svv.n
                ystg = ytm_store.get(c - 2, 0)
                def f_yst(eng, cp=cp, c=c, lastqy=lastqy, ystg=ystg):
                    eng.wait_ge(svv.h, lastqy)
                    eng.dma_start(out=scr_y[c * 128:(c + 1) * 128, 0:HP],
                                  in_=YTMv[cp][:, :]).then_inc(syp[cp].h, 16)
                on("scalar", f_yst)
                ytm_store[c] = syp[cp].inc(16)

            fin_se = sev.n
            fin_vv = svv.n
            def f_dec(eng, fin_se=fin_se, fin_vv=fin_vv, cpf=NCH % 2):
                eng.wait_ge(sev.h, fin_se)
                eng.wait_ge(svv.h, fin_vv)
                eng.activation(DECb[:, 0:1], RCC[:, cpf:cpf + 1], AF.Exp)
                eng.drain().then_inc(sev.h, 1)
            on("scalar", f_dec)
            dec_t = sev.inc()

            def f_ss(eng, dec_t=dec_t, w1=scs.n + 16 * 3):
                eng.wait_ge(sev.h, dec_t)
                eng.dma_start(out=cc_a_in[:, 0:A_HEADS * A_HD], in_=S_sb[:, :]).then_inc(scs.h, 16)
                eng.dma_start(out=cc_a_in[:, A_HEADS * A_HD:A_HEADS * A_HD + 64],
                              in_=S_sb[:, 0:64]).then_inc(scs.h, 16)
                eng.dma_start(out=scr_dec[0:1, :].rearrange("a p -> p a"),
                              in_=DECb[:, 0:1]).then_inc(scs.h, 16)
                eng.wait_ge(scs.h, w1)
                eng.dma_start(out=cc_a_in[0:1, A_HEADS * A_HD:A_HEADS * A_HD + 64],
                              in_=scr_dec[0:1, 0:64]).then_inc(scs.h, 16)
            on("sync", f_ss)
            scs.inc(64)

        def emit_collective_a():
            H = A_HEADS
            st_t = scs.n
            def f_cc(eng, st_t=st_t):
                eng.wait_ge(scs.h, st_t)
                eng.collective_compute(
                    "AllGather", ALU.bypass,
                    replica_groups=[list(range(8))],
                    ins=[cc_a_in[:, :].opt()],
                    outs=[cc_a_out[:, :].opt()],
                ).then_inc(scc.h, 1)
            on("gpsimd", f_cc)
            cc_t = scc.inc()

            def f_hz(eng, cc_t=cc_t):
                eng.wait_ge(scc.h, cc_t)
                eng.memset(S_sb[:, :], 0.0)
                eng.drain().then_inc(svv.h, 1)
            on("vector", f_hz)
            prev = svv.inc()

            jld = {}
            for j in range(8):
                jp = j % 2
                ld_pv = jld.get(j - 2, 0)
                js_ = sj[jp]
                def f_jl(eng, j=j, jp=jp, cc_t=cc_t, ld_pv=ld_pv, js_=js_):
                    eng.wait_ge(scc.h, cc_t)
                    eng.wait_ge(svv.h, ld_pv)
                    eng.dma_start(out=SJv[jp][:, :],
                                  in_=cc_a_out[j * 128:(j + 1) * 128, 0:H * A_HD]).then_inc(js_.h, 16)
                    eng.dma_start(out=DROWb[0:1, jp * 32:jp * 32 + H],
                                  in_=cc_a_out[j * 128:j * 128 + 1, H * A_HD:H * A_HD + H]).then_inc(js_.h, 16)
                on("sync", f_jl)
                jl = js_.inc(32)

                def f_jr(eng, j=j, jl=jl, prev=prev, js_=js_, em=sed.n + 1):
                    eng.wait_ge(js_.h, jl)
                    eng.wait_ge(svv.h, prev)
                    eng.tensor_scalar(DTILf[0:1, 0:H], DROWb[0:1, (j % 2) * 32:(j % 2) * 32 + H],
                                      cmask_s[0:1, 8 + j:9 + j], None, ALU.mult).then_inc(sed.h, 1)
                    eng.wait_ge(sed.h, em)
                    eng.tensor_scalar(DTILf[0:1, 0:H], DTILf[0:1, 0:H],
                                      cmask_s[0:1, 16 + j:17 + j], None, ALU.add)
                    eng.tensor_scalar(DTILf[0:1, 32:32 + H], ones_row[0:1, 0:H],
                                      cmask_s[0:1, j:j + 1], None, ALU.mult)
                    eng.drain().then_inc(svv.h, 1)
                on("vector", f_jr)
                sed.inc()
                jr = svv.inc()

                gj6 = gpop(6)
                def f_jm(eng, jr=jr, gj6=gj6):
                    eng.wait_ge(svv.h, jr)
                    gwait(eng, gj6)
                    eng.matmul(PS[6][:, 0:H], ones_row[0:1, :], DTILf[0:1, 0:H], start=True, stop=True)
                    eng.matmul(PS[6][:, 64:64 + H], ones_row[0:1, :], DTILf[0:1, 32:32 + H],
                               start=True, stop=True)
                    eng.drain().then_inc(smm.h, 1)
                on("tensor", f_jm)
                jm = smm.inc()

                def f_je(eng, jm=jm):
                    eng.wait_ge(smm.h, jm)
                    eng.activation(DBCb[:, 0:H], PS[6][:, 0:H], AF.Copy)
                    eng.activation(MBCb[:, 0:H], PS[6][:, 64:64 + H], AF.Copy)
                    eng.drain().then_inc(sev.h, 1)
                on("scalar", f_je)
                je = sev.inc()
                setguard(6, sev, je)

                def f_jv(eng, jp=jp, je=je, em=sed.n + 1):
                    eng.wait_ge(sev.h, je)
                    eng.tensor_tensor(S_sb.rearrange("p (h d) -> p h d", h=H),
                                      S_sb.rearrange("p (h d) -> p h d", h=H),
                                      DBCb[:, 0:H].unsqueeze(2).broadcast_to((128, H, 128)), ALU.mult)
                    eng.tensor_tensor(T2c.rearrange("p (h d) -> p h d", h=H),
                                      SJv[jp].rearrange("p (h d) -> p h d", h=H),
                                      MBCb[:, 0:H].unsqueeze(2).broadcast_to((128, H, 128)),
                                      ALU.mult).then_inc(sed.h, 1)
                    eng.wait_ge(sed.h, em)
                    eng.tensor_tensor(S_sb[:, :], S_sb[:, :], T2c[:, :], ALU.add)
                    eng.drain().then_inc(svv.h, 1)
                on("vector", f_jv)
                sed.inc()
                prev = svv.inc()
                jld[j] = prev

        def emit_gate_outproj_a(xprev, xprev_off, xdst):
            H = A_HEADS
            HP = A_INTER
            GCF = SS2[:, 0:128]
            for h2 in range(2):
                gate_done_v = None
                for ci in range(8):
                    c = 8 * h2 + ci
                    c0 = c * CH
                    ysg = ytm_store[c]
                    def f_gl(eng, c=c, c0=c0, ysg=ysg, gp=smm.n, pe1=emit_gate_outproj_a.pe1[0],
                             psg=emit_gate_outproj_a.psg[0], sy_=syp[c % 2]):
                        eng.wait_ge(sy_.h, ysg)
                        eng.wait_ge(smm.h, gp)
                        eng.wait_ge(svv.h, pe1)
                        eng.wait_ge(sev.h, psg)
                        eng.dma_start(out=YTg[:, :], in_=scr_y[c * 128:(c + 1) * 128, 0:HP]).then_inc(glx.h, 16)
                        eng.dma_start_transpose(out=GXS[:, :], in_=scr_conv_a[0:HP, c0:c0 + CH]).then_inc(glx.h, 16)
                        eng.dma_start_transpose(out=GGT[:, :], in_=scr_gate_a[0:HP, c0:c0 + CH]).then_inc(glx.h, 16)
                        eng.dma_start(out=GCF, in_=scr_conv_a[A_INTER + 128:A_INTER + 256, c0:c0 + CH]).then_inc(glx.h, 16)
                    on("sync", f_gl)
                    gl = glx.inc(16 * 4)

                    corr_v = []
                    for q in range(H // 4):
                        bq = q % 2
                        gcm = gpop(bq)
                        def f_cm(eng, q=q, bq=bq, gl=gl, gcm=gcm):
                            eng.wait_ge(glx.h, gl)
                            gwait(eng, gcm)
                            for j in range(4):
                                h = 4 * q + j
                                eng.matmul(PS[bq][:, j * 128:(j + 1) * 128], GCF,
                                           S_sb[:, h * 128:(h + 1) * 128], start=True, stop=True)
                            eng.drain().then_inc(smm.h, 1)
                        on("tensor", f_cm)
                        cm = smm.inc()

                        def f_cv(eng, q=q, bq=bq, cm=cm, c=c, em=sed.n + 1):
                            eng.wait_ge(smm.h, cm)
                            eng.tensor_tensor(TAP2.rearrange("p (j i) -> p j i", j=4),
                                              PS[bq].ap().rearrange("p (j i) -> p j i", j=4),
                                              ECCSb[:, c * 64 + 4 * q:c * 64 + 4 * q + 4]
                                              .unsqueeze(2).broadcast_to((128, 4, 128)),
                                              ALU.mult).then_inc(sed.h, 1)
                            eng.wait_ge(sed.h, em)
                            eng.tensor_tensor(YTg[:, q * 512:(q + 1) * 512],
                                              YTg[:, q * 512:(q + 1) * 512], TAP2[:, 0:512], ALU.add)
                            eng.drain().then_inc(svv.h, 1)
                        on("vector", f_cv)
                        sed.inc()
                        cvt = svv.inc()
                        setguard(bq, svv, cvt)
                        corr_v.append(cvt)

                    def f_sg(eng, gl=gl, pe2=emit_gate_outproj_a.pe2[0]):
                        eng.wait_ge(glx.h, gl)
                        eng.wait_ge(svv.h, pe2)
                        eng.activation(GB1[:, :], GGT[:, :], AF.Silu)
                        eng.drain().then_inc(sev.h, 1)
                    on("scalar", f_sg)
                    sg = sev.inc()

                    cvl = corr_v[-1]
                    def f_e1(eng, gl=gl, cvl=cvl, sg=sg, em=sed.n + 1,
                             ptr=emit_gate_outproj_a.trg.get(c - 1, 0)):
                        eng.wait_ge(glx.h, gl)
                        eng.wait_ge(svv.h, cvl)
                        eng.wait_ge(sev.h, sg)
                        eng.wait_ge(strs.h, ptr)
                        eng.tensor_tensor(GB2[:, :], GXS[:, :], dexp_b[:, :],
                                          ALU.mult).then_inc(sed.h, 1)
                        eng.wait_ge(sed.h, em)
                        eng.tensor_tensor(GXS[:, :], YTg[:, :], GB2[:, :],
                                          ALU.add).then_inc(sed.h, 1)
                        eng.wait_ge(sed.h, em + 1)
                        eng.tensor_tensor(GB2[:, :], GXS[:, :], GB1[:, :], ALU.mult)
                        eng.drain().then_inc(svv.h, 1)
                    on("vector", f_e1)
                    sed.inc(2)
                    e1 = svv.inc()
                    emit_gate_outproj_a.pe1[0] = e1
                    emit_gate_outproj_a.psg[0] = sg

                    def f_sq(eng, e1=e1):
                        eng.wait_ge(svv.h, e1)
                        eng.activation(GB1[:, :], GB2[:, :], AF.Square, accum_out=SSQf[:, 0:1])
                        eng.drain().then_inc(sev.h, 1)
                    on("scalar", f_sq)
                    sq_t = sev.inc()

                    def f_rs1(eng, sq_t=sq_t):
                        eng.wait_ge(sev.h, sq_t)
                        eng.tensor_scalar(MSf[:, 0:1], SSQf[:, 0:1], 1.0 / A_INTER, None, ALU.mult)
                        eng.drain().then_inc(svv.h, 1)
                    on("vector", f_rs1)
                    rs1 = svv.inc()

                    def f_rs2(eng, rs1=rs1, ea=sea.n + 1):
                        eng.wait_ge(svv.h, rs1)
                        eng.activation(RSTDf[:, 0:1], MSf[:, 0:1], AF.Ln,
                                       bias=EPSc[:, 0:1]).then_inc(sea.h, 1)
                        eng.wait_ge(sea.h, ea)
                        eng.activation(RSTDf[:, 0:1], RSTDf[:, 0:1], AF.Exp, scale=-0.5)
                        eng.drain().then_inc(sev.h, 1)
                    on("scalar", f_rs2)
                    sea.inc()
                    rs2 = sev.inc()

                    trg = emit_gate_outproj_a.trg.get(c - 1, 0)
                    def f_e2(eng, rs2=rs2, trg=trg, em=sed.n + 1):
                        eng.wait_ge(sev.h, rs2)
                        eng.wait_ge(strs.h, trg)
                        eng.tensor_scalar(GB1[:, :], GB2[:, :], RSTDf[:, 0:1], None,
                                          ALU.mult).then_inc(sed.h, 1)
                        eng.wait_ge(sed.h, em)
                        eng.tensor_tensor(GB2[:, :], GB1[:, :], nw_b[:, :], ALU.mult)
                        eng.drain().then_inc(svv.h, 1)
                    on("vector", f_e2)
                    sed.inc()
                    e2 = svv.inc()
                    emit_gate_outproj_a.pe2[0] = e2

                    def f_trs(eng, e2=e2, ci=ci, opg=emit_gate_outproj_a.opmm[0]):
                        eng.wait_ge(svv.h, e2)
                        eng.wait_ge(smm.h, opg)
                        for f in range(32):
                            eng.dma_start_transpose(
                                out=YFMh[:, f * 1024 + ci * 128:f * 1024 + ci * 128 + 128],
                                in_=GB2[:, f * 128:(f + 1) * 128]).then_inc(strs.h, 16)
                    on("sync", f_trs)
                    emit_gate_outproj_a.trg[c] = strs.inc(16 * 32)

                half_tr = emit_gate_outproj_a.trg[8 * h2 + 7]
                mmd = {}
                for m in range(16):
                    p = m % 2
                    wfree = smm.n if m < 2 else mmd[m - 2]
                    ws2 = ldwp[p]
                    def f_wl(eng, p=p, m=m, wfree=wfree, lv=svv.n, htr=half_tr, ws2=ws2):
                        eng.wait_ge(smm.h, wfree)
                        eng.wait_ge(svv.h, lv)
                        eng.wait_ge(strs.h, htr)
                        src = wa_outT.rearrange("(k p) m -> p k m", p=128)[:, :, m * 128:(m + 1) * 128]
                        eng.dma_start(out=WTOv[p].rearrange("p (k j) -> p k j", k=32),
                                      in_=src).then_inc(ws2.h, 16)
                    on("gpsimd", f_wl)
                    wt = ws2.inc(16)

                    for tt in range(2):
                        bk = 2 + (m * 2 + tt) % 2
                        tok0 = h2 * 1024 + tt * 512
                        gom = gpop(bk)
                        def f_om(eng, p=p, bk=bk, tt=tt, wt=wt, half_tr=half_tr, ws2=ws2, gom=gom):
                            eng.wait_ge(ws2.h, wt)
                            eng.wait_ge(strs.h, half_tr)
                            gwait(eng, gom)
                            for k in range(32):
                                eng.matmul(PS[bk][:, 0:512],
                                           WTOv[p][:, k * 128:(k + 1) * 128],
                                           YFMh[:, k * 1024 + tt * 512:k * 1024 + tt * 512 + 512],
                                           start=(k == 0), stop=(k == 31))
                            eng.drain().then_inc(smm.h, 1)
                        on("tensor", f_om)
                        om = smm.inc()
                        emit_gate_outproj_a.opmm[0] = om
                        if tt == 1:
                            mmd[m] = om

                        xp = (m * 2 + tt) % 2
                        xs_ = xlp[xp]
                        def f_xl(eng, m=m, tok0=tok0, xp=xp, xprev=xprev, xprev_off=xprev_off,
                                 xg=emit_gate_outproj_a.xld.get(xp, 0), xs_=xs_):
                            eng.wait_ge(svv.h, xg)
                            eng.dma_start(out=XOLDf[xp][:, :],
                                          in_=xprev[m * 128:(m + 1) * 128,
                                                    xprev_off + tok0:xprev_off + tok0 + 512]).then_inc(xs_.h, 16)
                        on("sync", f_xl)
                        xl = xs_.inc(16)

                        def f_xv(eng, bk=bk, xp=xp, om=om, xl=xl, xg2=emit_gate_outproj_a.xst.get(xp, 0),
                                 xs_=xs_, sy_=syp[xp]):
                            eng.wait_ge(smm.h, om)
                            eng.wait_ge(xs_.h, xl)
                            eng.wait_ge(sy_.h, xg2)
                            eng.tensor_tensor(XNEWf[xp][:, :], PS[bk][:, 0:512], XOLDf[xp][:, :], ALU.add)
                            eng.drain().then_inc(svv.h, 1)
                        on("vector", f_xv)
                        xv = svv.inc()
                        setguard(bk, svv, xv)
                        emit_gate_outproj_a.xld[xp] = xv

                        def f_xs(eng, m=m, tok0=tok0, xp=xp, xv=xv, xdst=xdst):
                            eng.wait_ge(svv.h, xv)
                            eng.dma_start(out=xdst[m * 128:(m + 1) * 128, tok0:tok0 + 512],
                                          in_=XNEWf[xp][:, :]).then_inc(syp[xp].h, 16)
                        on("scalar", f_xs)
                        emit_gate_outproj_a.xst[xp] = syp[xp].inc(16)
        emit_gate_outproj_a.trg = {}
        emit_gate_outproj_a.xld = {}
        emit_gate_outproj_a.xst = {}
        emit_gate_outproj_a.pe1 = [0]
        emit_gate_outproj_a.psg = [0]
        emit_gate_outproj_a.pe2 = [0]
        emit_gate_outproj_a.opmm = [0]


        def emit_ffn(xsrc, xsrc_off, xdst):
            LNW = T
            lnbf = R1[:, 0:16 * LNW]
            XA1 = FR[:, 0:512]
            FXOLD = [FR[:, 512:1024], FR[:, 1024:1536]]
            FXNEW = [FR[:, 1536:2048], FR[:, 2048:2560]]
            LGf = FR[:, 2560:3072]
            LGT = FR[:, 3072:3088]
            M8v = FR[:, 3088:3096]
            NV1 = FR[:, 3096:3097]; E2v = FR[:, 3097:3098]; DENv = FR[:, 3098:3099]
            RDv = FR[:, 3099:3100]
            Ev = FR[:, 3104:3120]; G8v = FR[:, 3120:3136]; ESv = FR[:, 3136:3152]
            RW1T = FR[:, 4096:5120]
            RW2Ts = FR[:, 5120:5152]
            RB1c = FR[:, 5152:5153]; RB2G = FR[:, 5153:5154]; GTSc = FR[:, 5154:5155]
            BCOL = FR[:, 5168:5184]
            RT1 = FR[:, 5200:5328]
            WXA = [R2[:, 0:2048], R2[:, 2048:4096]]
            BTW = [R2[:, 12288:12544], R2[:, 12544:12800]]
            SELb = R2[:, 12800:13056]
            COF = R2[:, 13056:15104]
            CEb = [R2[:, 15104:15616], R2[:, 15616:16128]]

            def f_fl(eng, tgt=lx[0].n + 16 * 7):
                eng.dma_start(out=RW1T.rearrange("p (k j) -> p k j", k=16),
                              in_=f_rw1T.rearrange("(k p) m -> p k m", p=128)).then_inc(lx[0].h, 16)
                eng.dma_start(out=RW2Ts[0:64, :], in_=f_rw2T[:, :]).then_inc(lx[0].h, 16)
                eng.dma_start(out=RB1c[0:64, :], in_=f_rb1[:, :]).then_inc(lx[0].h, 16)
                eng.dma_start(out=RB2G[0:32, :], in_=f_rb2[:, :]).then_inc(lx[0].h, 16)
                eng.dma_start(out=GTSc[0:16, 0:1], in_=f_gates[:, :]).then_inc(lx[0].h, 16)
                eng.dma_start(out=RT1[0:16, 0:128], in_=f_biasT[:, :]).then_inc(lx[0].h, 16)
                eng.dma_start(out=SELb[0:16, :], in_=f_sel[:, :]).then_inc(lx[0].h, 16)
                eng.wait_ge(lx[0].h, tgt)
            on("sync", f_fl)
            fl_t = lx[0].inc(16 * 7)

            g3i = gpop(3)
            def f_fb_t(eng, fl_t=fl_t, g3i=g3i):
                eng.wait_ge(lx[0].h, fl_t)
                gwait(eng, g3i)
                eng.transpose(PS[3][:, 0:16], RT1[0:16, 0:128], eye_s[0:16, 0:16])
                eng.drain().then_inc(smm.h, 1)
            on("tensor", f_fb_t)
            fbt = smm.inc()

            def f_fb_e(eng, fbt=fbt):
                eng.wait_ge(smm.h, fbt)
                eng.activation(BCOL[:, :], PS[3][:, 0:16], AF.Copy)
                eng.drain().then_inc(sev.h, 1)
            on("scalar", f_fb_e)
            fbe = sev.inc()
            setguard(3, sev, fbe)

            def f_fb_v(eng, fbe=fbe):
                eng.wait_ge(sev.h, fbe)
                eng.tensor_tensor(RB2G[0:16, :], RB2G[0:16, :], GTSc[0:16, :], ALU.add)
                eng.drain().then_inc(svv.h, 1)
            on("vector", f_fb_v)
            fbv = svv.inc()

            hr_done = []

            def router1(it, xs, doff, w, av):
                g3 = gpop(3)
                def f_r1(eng, xs=xs, w=w, av=av, g3=g3, fl_t=fl_t, fbv=fbv):
                    eng.wait_ge(svv.h, av)
                    eng.wait_ge(svv.h, fbv)
                    eng.wait_ge(lx[0].h, fl_t)
                    gwait(eng, g3)
                    for k in range(NK):
                        eng.matmul(PS[3][0:64, 0:w], RW1T[:, k * 64:(k + 1) * 64],
                                   XT[:, xs + k * 128:xs + k * 128 + w],
                                   start=(k == 0), stop=(k == NK - 1))
                    eng.drain().then_inc(smm.h, 1)
                on("tensor", f_r1)
                r1 = smm.inc()

                def f_r1e(eng, r1=r1, w=w):
                    eng.wait_ge(smm.h, r1)
                    eng.activation(RT1[0:64, 0:w], PS[3][0:64, 0:w], AF.Copy)
                    eng.drain().then_inc(sev.h, 1)
                on("scalar", f_r1e)
                r1e = sev.inc()
                setguard(3, sev, r1e)

                def f_r1v(eng, r1e=r1e, w=w):
                    eng.wait_ge(sev.h, r1e)
                    eng.tensor_tensor(RT1[0:64, 0:w], RT1[0:64, 0:w], PS[7][0:64, 0:w], ALU.mult)
                    eng.drain().then_inc(svv.h, 1)
                on("vector", f_r1v)
                r1v = svv.inc()
                setguard(7, svv, r1v)

                def f_r1s(eng, r1v=r1v, doff=doff, w=w):
                    eng.wait_ge(svv.h, r1v)
                    eng.activation(HRf[0:64, doff:doff + w], RT1[0:64, 0:w], AF.Silu, bias=RB1c[0:64, :])
                    eng.drain().then_inc(sev.h, 1)
                on("scalar", f_r1s)
                hr_done.append(sev.inc())
                return (smm, r1)

            HRf = FR[:, 6144:8192]
            ln_tts = [(j * 128, j * 128, 128, xsrc) for j in range(16)]
            ln_tts = [(j * 128, xsrc_off + j * 128, 128, xsrc) for j in range(16)]
            emit_ln(LNW, ln_tts, slotw=128, lnb_t=lnbf, post_tt=router1)
            Bar()

            for t4 in range(4):
                bk = t4 % 2
                g_ = gpop(bk)
                def f_r2(eng, t4=t4, bk=bk, g_=g_):
                    gwait(eng, g_)
                    eng.matmul(PS[bk][0:32, 0:512], RW2Ts[0:64, :], HRf[0:64, t4 * 512:(t4 + 1) * 512],
                               start=True, stop=True)
                    eng.drain().then_inc(smm.h, 1)
                on("tensor", f_r2)
                r2 = smm.inc()

                def f_lg(eng, r2=r2, t4=t4, bk=bk):
                    eng.wait_ge(smm.h, r2)
                    eng.tensor_scalar(LGf[0:16, 0:512], PS[bk][0:16, 0:512], RB2G[0:16, :],
                                      None, ALU.add)
                    eng.drain().then_inc(svv.h, 1)
                on("vector", f_lg)
                lg = svv.inc()
                setguard(bk, svv, lg)
                if t4 == 0 and os.environ.get("P_DBG3", "0") == "1":
                    d_lg0 = nc.dram_tensor("d_lg0", [16, 512], F32, kind="ExternalOutput").ap()
                    d_hr0 = nc.dram_tensor("d_hr0", [64, 512], F32, kind="ExternalOutput").ap()
                    def f_d30(eng, w=lg, tgt=scs.n + 32):
                        eng.wait_ge(svv.h, w)
                        eng.dma_start(out=d_lg0[:, :], in_=LGf[0:16, 0:512]).then_inc(scs.h, 16)
                        eng.dma_start(out=d_hr0[:, :], in_=HRf[0:64, 0:512]).then_inc(scs.h, 16)
                        eng.wait_ge(scs.h, tgt)
                    on("sync", f_d30)
                    scs.inc(32)

                for i8 in range(4):
                    tok = t4 * 512 + i8 * 128
                    bk2 = 2 + (i8 % 2)
                    g2_ = gpop(bk2)
                    def f_lt(eng, i8=i8, bk2=bk2, lg=lg, g2_=g2_):
                        eng.wait_ge(svv.h, lg)
                        gwait(eng, g2_)
                        eng.transpose(PS[bk2][:, 0:16], LGf[0:16, i8 * 128:(i8 + 1) * 128],
                                      eye_s[0:16, 0:16])
                        eng.drain().then_inc(smm.h, 1)
                    on("tensor", f_lt)
                    lt_ = smm.inc()

                    def f_lte(eng, lt_=lt_, bk2=bk2):
                        eng.wait_ge(smm.h, lt_)
                        eng.activation(LGT[:, :], PS[bk2][:, 0:16], AF.Copy)
                        eng.drain().then_inc(sev.h, 1)
                    on("scalar", f_lte)
                    lte = sev.inc()
                    setguard(bk2, sev, lte)

                    def f_top(eng, lte=lte, eb=sed.n):
                        eng.wait_ge(sev.h, lte)
                        eng.max(out=M8v[:, :], in_=LGT[:, :])
                        eng.drain().then_inc(sed.h, 1)
                        eng.wait_ge(sed.h, eb + 1)
                        eng.tensor_scalar(NV1[:, :], M8v[:, 0:1], -1.0, None, ALU.mult)
                        eng.tensor_scalar(G8v[:, :], LGT[:, :], M8v[:, 1:2], None, ALU.is_ge)
                        eng.drain().then_inc(svv.h, 1)
                    on("vector", f_top)
                    sed.inc()
                    top = svv.inc()

                    def f_te(eng, top=top, eb=sea.n):
                        eng.wait_ge(svv.h, top)
                        eng.activation(Ev[:, :], LGT[:, :], AF.Exp, bias=NV1[:, :])
                        eng.activation(E2v[:, :], M8v[:, 1:2], AF.Exp, bias=NV1[:, :])
                        eng.drain().then_inc(sev.h, 1)
                    on("scalar", f_te)
                    te = sev.inc()

                    def f_co(eng, te=te, eb=sed.n):
                        eng.wait_ge(sev.h, te)
                        eng.tensor_scalar(DENv[:, :], E2v[:, :], 1.0, None, ALU.add).then_inc(sed.h, 1)
                        eng.wait_ge(sed.h, eb + 1)
                        eng.reciprocal(RDv[:, :], DENv[:, :]).then_inc(sed.h, 1)
                        eng.wait_ge(sed.h, eb + 2)
                        eng.tensor_scalar(ESv[:, :], Ev[:, :], RDv[:, :], None, ALU.mult).then_inc(sed.h, 1)
                        eng.wait_ge(sed.h, eb + 3)
                        eng.tensor_tensor(ESv[:, :], ESv[:, :], G8v[:, :], ALU.mult)
                        eng.drain().then_inc(svv.h, 1)
                    on("vector", f_co)
                    sed.inc(3)
                    co = svv.inc()

                    g3_ = gpop(bk2)
                    def f_ct(eng, co=co, bk2=bk2, g3_=g3_):
                        eng.wait_ge(svv.h, co)
                        gwait(eng, g3_)
                        eng.transpose(PS[bk2][0:16, 0:128], ESv[:, :], eye_s[:, :])
                        eng.drain().then_inc(smm.h, 1)
                    on("tensor", f_ct)
                    ct = smm.inc()

                    def f_ce(eng, ct=ct, bk2=bk2, tok=tok):
                        eng.wait_ge(smm.h, ct)
                        eng.activation(COF[0:16, tok:tok + 128], PS[bk2][0:16, 0:128], AF.Copy)
                        eng.drain().then_inc(sev.h, 1)
                    on("scalar", f_ce)
                    cof_t = sev.inc()
                    setguard(bk2, sev, cof_t)
            cof_all = sev.n
            if os.environ.get("P_DBG3", "0") == "1":
                d_cof = nc.dram_tensor("d_cof", [16, 2048], BF16, kind="ExternalOutput").ap()
                d_hr = nc.dram_tensor("d_hr", [64, 2048], F32, kind="ExternalOutput").ap()
                d_lg = nc.dram_tensor("d_lg", [16, 512], F32, kind="ExternalOutput").ap()
                d_lgt = nc.dram_tensor("d_lgt", [128, 16], F32, kind="ExternalOutput").ap()
                d_m8 = nc.dram_tensor("d_m8", [128, 8], F32, kind="ExternalOutput").ap()
                d_es = nc.dram_tensor("d_es", [128, 16], F32, kind="ExternalOutput").ap()
                d_rw2 = nc.dram_tensor("d_rw2", [64, 32], F32, kind="ExternalOutput").ap()
                def f_d3(eng, w=cof_all, tgt=scs.n + 16 * 7):
                    eng.wait_ge(sev.h, w)
                    eng.dma_start(out=d_cof[:, :], in_=COF[0:16, :]).then_inc(scs.h, 16)
                    eng.dma_start(out=d_hr[:, :], in_=HRf[0:64, :]).then_inc(scs.h, 16)
                    eng.dma_start(out=d_lg[:, :], in_=LGf[0:16, 0:512]).then_inc(scs.h, 16)
                    eng.dma_start(out=d_lgt[:, :], in_=LGT[:, :]).then_inc(scs.h, 16)
                    eng.dma_start(out=d_m8[:, :], in_=M8v[:, :]).then_inc(scs.h, 16)
                    eng.dma_start(out=d_es[:, :], in_=ESv[:, :]).then_inc(scs.h, 16)
                    eng.dma_start(out=d_rw2[:, :], in_=RW2Ts[0:64, :]).then_inc(scs.h, 16)
                    eng.wait_ge(scs.h, tgt)
                on("sync", f_d3)
                scs.inc(16 * 7)
            Bar()

            for m2 in range(2):
                ws = ldwp[m2]
                def f_aw(eng, m2=m2, ws=ws, wg=smm.n):
                    eng.wait_ge(smm.h, wg)
                    src_ = f_AT.rearrange("(k p) m -> p k m", p=128)[:, :, m2 * 128:(m2 + 1) * 128]
                    eng.dma_start(out=WTb[m2].rearrange("p (k j) -> p k j", k=16),
                                  in_=src_).then_inc(ws.h, 16)
                on("gpsimd", f_aw)
                aw = ws.inc(16)

                for t4 in range(4):
                    bkx = (t4 % 2)
                    bkc = 2 + (t4 % 2)
                    gx = gpop(bkx); gc = gpop(bkc)
                    def f_xam(eng, m2=m2, t4=t4, bkx=bkx, bkc=bkc, aw=aw, ws=ws, gx=gx, gc=gc):
                        eng.wait_ge(ws.h, aw)
                        gwait(eng, gx); gwait(eng, gc)
                        for k in range(NK):
                            eng.matmul(PS[bkx][:, 0:512], WTb[m2][:, k * 128:(k + 1) * 128],
                                       lnbf[:, k * LNW + t4 * 512:k * LNW + t4 * 512 + 512],
                                       start=(k == 0), stop=(k == NK - 1))
                        eng.matmul(PS[bkc][:, 0:512], SELb[0:16, m2 * 128:(m2 + 1) * 128],
                                   COF[0:16, t4 * 512:(t4 + 1) * 512], start=True, stop=True)
                        eng.drain().then_inc(smm.h, 1)
                    on("tensor", f_xam)
                    xam = smm.inc()

                    def f_cee(eng, xam=xam, bkc=bkc, t4=t4):
                        eng.wait_ge(smm.h, xam)
                        eng.activation(CEb[t4 % 2][:, :], PS[bkc][:, 0:512], AF.Copy)
                        eng.drain().then_inc(sev.h, 1)
                    on("scalar", f_cee)
                    cee = sev.inc()
                    setguard(bkc, sev, cee)

                    def f_wxa(eng, cee=cee, bkx=bkx, m2=m2, t4=t4):
                        eng.wait_ge(sev.h, cee)
                        eng.tensor_tensor(WXA[m2][:, t4 * 512:(t4 + 1) * 512],
                                          PS[bkx][:, 0:512], CEb[t4 % 2][:, :], ALU.mult)
                        eng.drain().then_inc(svv.h, 1)
                    on("vector", f_wxa)
                    wxa = svv.inc()
                    setguard(bkx, svv, wxa)
            Bar()

            mmdF = {}
            for m in range(16):
                p = m % 2
                ws = ldwp[p]
                wfree = smm.n if m < 2 else mmdF[m - 2]
                def f_fw(eng, p=p, m=m, ws=ws, wfree=wfree):
                    eng.wait_ge(smm.h, wfree)
                    src_ = wf_T.rearrange("(k p) m -> p k m", p=128)[:, :, m * 128:(m + 1) * 128]
                    eng.dma_start(out=WTb[p].rearrange("p (k j) -> p k j", k=16),
                                  in_=src_).then_inc(ws.h, 16)
                    src2 = f_BT2.rearrange("(k p) m -> p k m", p=128)[:, :, m * 128:(m + 1) * 128]
                    eng.dma_start(out=BTW[p].rearrange("p (k j) -> p k j", k=2),
                                  in_=src2).then_inc(ws.h, 16)
                on("gpsimd", f_fw)
                fw = ws.inc(32)

                for tt in range(4):
                    bk = (m * 4 + tt) % 4
                    gk = gpop(bk)
                    def f_bm(eng, p=p, tt=tt, bk=bk, fw=fw, ws=ws, gk=gk):
                        eng.wait_ge(ws.h, fw)
                        gwait(eng, gk)
                        for k in range(NK):
                            eng.matmul(PS[bk][:, 0:512], WTb[p][:, k * 128:(k + 1) * 128],
                                       lnbf[:, k * LNW + tt * 512:k * LNW + tt * 512 + 512],
                                       start=(k == 0), stop=False)
                        for k2 in range(2):
                            eng.matmul(PS[bk][:, 0:512], BTW[p][:, k2 * 128:(k2 + 1) * 128],
                                       WXA[k2][:, tt * 512:(tt + 1) * 512],
                                       start=False, stop=(k2 == 1))
                        eng.drain().then_inc(smm.h, 1)
                    on("tensor", f_bm)
                    bm = smm.inc()
                    if tt == 3:
                        mmdF[m] = bm

                    xp = (m * 4 + tt) % 2
                    xs_ = xlp[xp]
                    def f_fxl(eng, m=m, tt=tt, xp=xp, xs_=xs_,
                              xg=emit_ffn.xld.get(xp, 0)):
                        eng.wait_ge(svv.h, xg)
                        eng.dma_start(out=FXOLD[xp][:, :],
                                      in_=xsrc[m * 128:(m + 1) * 128,
                                               xsrc_off + tt * 512:xsrc_off + tt * 512 + 512]).then_inc(xs_.h, 16)
                    on("sync", f_fxl)
                    fxl = xs_.inc(16)

                    def f_fxv(eng, bk=bk, xp=xp, bm=bm, fxl=fxl, m=m, xs_=xs_,
                              xg2=emit_ffn.xst.get(xp, 0), eb=sed.n, sy_=syp[xp]):
                        eng.wait_ge(smm.h, bm)
                        eng.wait_ge(xs_.h, fxl)
                        eng.wait_ge(sy_.h, xg2)
                        eng.tensor_scalar(XA1[:, :], PS[bk][:, 0:512], BCOL[:, m:m + 1],
                                          None, ALU.add).then_inc(sed.h, 1)
                        eng.wait_ge(sed.h, eb + 1)
                        eng.tensor_tensor(FXNEW[xp][:, :], XA1[:, :], FXOLD[xp][:, :], ALU.add)
                        eng.drain().then_inc(svv.h, 1)
                    on("vector", f_fxv)
                    sed.inc()
                    fxv = svv.inc()
                    setguard(bk, svv, fxv)
                    emit_ffn.xld[xp] = fxv

                    def f_fxs(eng, m=m, tt=tt, xp=xp, fxv=fxv, xdst=xdst):
                        eng.wait_ge(svv.h, fxv)
                        eng.dma_start(out=xdst[m * 128:(m + 1) * 128, tt * 512:tt * 512 + 512],
                                      in_=FXNEW[xp][:, :]).then_inc(syp[xp].h, 16)
                    on("scalar", f_fxs)
                    emit_ffn.xst[xp] = syp[xp].inc(16)
        emit_ffn.xld = {}
        emit_ffn.xst = {}

        def emit_halo_b(xsrc, xsrc_off):
            base = lx[0].n
            def f_hga(eng, base=base, xsrc=xsrc, xsrc_off=xsrc_off):
                for k in range(NK):
                    eng.dma_start(out=XF_t[:, k * 4:(k + 1) * 4],
                                  in_=xsrc[k * 128:(k + 1) * 128,
                                           xsrc_off + T - 4:xsrc_off + T]).then_inc(lx[0].h, 16)
                eng.dma_start(out=HSLf[0:8, 0:1], in_=hsel_in[:, :]).then_inc(lx[0].h, 16)
                eng.wait_ge(lx[0].h, base + 16 * 17)
                for k in range(NK):
                    eng.dma_start(out=cc_h_in[k * 8:(k + 1) * 8, :]
                                  .rearrange("P (pp t) -> (P pp) t", pp=16),
                                  in_=XF_t[:, k * 4:(k + 1) * 4]).then_inc(lx[0].h, 16)
            on("sync", f_hga)
            hga = lx[0].inc(16 * 33)

            def f_hcc(eng, hga=hga):
                eng.wait_ge(lx[0].h, hga)
                eng.collective_compute(
                    "AllGather", ALU.bypass,
                    replica_groups=[list(range(8))],
                    ins=[cc_h_in[:, :].opt()],
                    outs=[cc_h_out[:, :].opt()],
                ).then_inc(scc.h, 1)
            on("gpsimd", f_hcc)
            hcc = scc.inc()

            def f_hlt(eng, hcc=hcc):
                eng.wait_ge(scc.h, hcc)
                eng.dma_start(out=HSLT[0:8, :],
                              in_=cc_h_out.rearrange("(j p) q -> j (p q)", p=128)).then_inc(lx[1].h, 16)
            on("sync", f_hlt)
            hlt = lx[1].inc(16)

            halo_st = [0]
            for c in range(NCH):
                bk = 4 + c % 2
                gh = gpop(bk)
                def f_hm(eng, c=c, bk=bk, hlt=hlt, gh=gh, hga=hga):
                    eng.wait_ge(lx[1].h, hlt)
                    eng.wait_ge(lx[0].h, hga)
                    gwait(eng, gh)
                    eng.matmul(PS[bk][0:1, 0:512], HSLf[0:8, 0:1],
                               HSLT[0:8, c * 512:(c + 1) * 512], start=True, stop=True)
                    eng.drain().then_inc(smm.h, 1)
                on("tensor", f_hm)
                hm = smm.inc()

                def f_he(eng, bk=bk, hm=hm, pst=halo_st[0]):
                    eng.wait_ge(smm.h, hm)
                    eng.wait_ge(scs.h, pst)
                    eng.activation(XF_t[0:1, 1024:1536], PS[bk][0:1, 0:512], AF.Copy)
                    eng.drain().then_inc(sev.h, 1)
                on("scalar", f_he)
                he = sev.inc()
                setguard(bk, sev, he)

                def f_hs(eng, c=c, he=he):
                    eng.wait_ge(sev.h, he)
                    eng.dma_start(out=x_halo_b[c * 128:(c + 1) * 128, 124:128].unsqueeze(0),
                                  in_=XF_t[0:1, 1024:1536].rearrange("a (p t) -> a p t", p=128)
                                  ).then_inc(scs.h, 16)
                on("sync", f_hs)
                halo_st[0] = scs.inc(16)

        sevacB = {}
        chunk_mm_doneB = {}
        qmm_ldB = {}
        chunk_ldqB = {}
        ytm_storeB = {}
        bigB = {}
        lastqyB = {}
        pqB = [0, 0]
        LNSCALE = float(np.log(B_SCALE))

        def emit_mixer_b():
            H, HD, SP = B_HEADS, B_HD, B_STATE
            nq = H // 4

            def f_ld0(eng):
                eng.dma_start(out=DTRf[0:H, 0:1], in_=b_Alog[:, :]).then_inc(lx[0].h, 16)
                eng.dma_start(out=DTBCf[0:H, 0:1], in_=b_dtb[:, :]).then_inc(lx[0].h, 16)
            on("sync", f_ld0)
            l0 = lx[0].inc(32)

            def f_i_s(eng, l0=l0):
                eng.wait_ge(lx[0].h, l0)
                eng.activation(ACOLf[0:H, 0:1], DTRf[0:H, 0:1], AF.Exp)
                eng.drain().then_inc(sev.h, 1)
            on("scalar", f_i_s)
            e0 = sev.inc()

            def f_i_v(eng, e0=e0):
                eng.wait_ge(sev.h, e0)
                eng.tensor_scalar(ACOLf[0:H, 0:1], ACOLf[0:H, 0:1], -1.0, None, ALU.mult)
                eng.memset(RCC[:, 0:4], 0.0)
                eng.memset(S_sb[:, :], 0.0)
                eng.memset(ONESf[0:H, :], 1.0)
                for pr in range(2):
                    eng.memset(LdL[pr][0:2, 0:512], 1.0)
                    eng.memset(LdR[pr][0:2, 0:512], 1.0)
                eng.drain().then_inc(svv.h, 1)
            on("vector", f_i_v)
            v0 = svv.inc()

            dtv2B = {}
            for c in range(NCH):
                cp = c % 2
                c0 = c * CH
                ms_ = mc[cp]
                g_prev = smm.n if c < 2 else chunk_mm_doneB[c - 2]
                def f_cinB(eng, cp=cp, c0=c0, g_prev=g_prev, v0=v0, ms_=ms_,
                           dtg=dtv2B.get(c - 1, 0)):
                    eng.wait_ge(svv.h, v0)
                    eng.wait_ge(smm.h, g_prev)
                    eng.wait_ge(svv.h, dtg)
                    eng.dma_start_transpose(out=XSTvB[cp][:, :],
                                            in_=scr_conv_b[0:B_INNER, c0:c0 + CH]).then_inc(ms_.h, 16)
                    eng.dma_start_transpose(out=BTvB[cp][:, :],
                                            in_=scr_B_b[0:4096, c0:c0 + CH]).then_inc(ms_.h, 16)
                    eng.dma_start(out=DTRf[0:H, 0:128],
                                  in_=scr_dt_b[0:H, c0:c0 + CH]).then_inc(ms_.h, 16)
                on("sync", f_cinB)
                cin_t = ms_.inc(16 * 3)

                def f_dtv1(eng, cin_t=cin_t, ms_=ms_, e0=sed.n + 1, e1=sed.n + 2):
                    eng.wait_ge(ms_.h, cin_t)
                    eng.tensor_scalar(Uf[0:H, :], DTRf[0:H, 0:128], DTBCf[0:H, 0:1], None,
                                      ALU.add).then_inc(sed.h, 1)
                    eng.wait_ge(sed.h, e0)
                    eng.tensor_scalar(NAf[0:H, :], Uf[0:H, :], -1.0, None,
                                      ALU.mult).then_inc(sed.h, 1)
                    eng.wait_ge(sed.h, e1)
                    eng.tensor_tensor(NAf[0:H, :], Uf[0:H, :], NAf[0:H, :], ALU.min)
                    eng.drain().then_inc(svv.h, 1)
                on("vector", f_dtv1)
                sed.inc(2)
                t1 = svv.inc()

                def f_dts1(eng, t1=t1, ea=sea.n + 1):
                    eng.wait_ge(svv.h, t1)
                    eng.activation(Ef[0:H, :], NAf[0:H, :], AF.Exp).then_inc(sea.h, 1)
                    eng.wait_ge(sea.h, ea)
                    eng.activation(Ef[0:H, :], Ef[0:H, :], AF.Ln, bias=1.0)
                    eng.drain().then_inc(sev.h, 1)
                on("scalar", f_dts1)
                sea.inc()
                t2 = sev.inc()

                def f_dtv2(eng, t2=t2, cp=cp, eb=sed.n):
                    eng.wait_ge(sev.h, t2)
                    eng.tensor_scalar(MX2f[0:H, :], Uf[0:H, :], 0.0, None,
                                      ALU.max).then_inc(sed.h, 1)
                    eng.wait_ge(sed.h, eb + 1)
                    eng.tensor_tensor(DTf[0:H, :], MX2f[0:H, :], Ef[0:H, :],
                                      ALU.add).then_inc(sed.h, 1)
                    eng.wait_ge(sed.h, eb + 2)
                    eng.tensor_scalar(DTf[0:H, :], DTf[0:H, :], 0.5, 1e-4, ALU.min,
                                      ALU.max).then_inc(sed.h, 1)
                    eng.wait_ge(sed.h, eb + 3)
                    eng.tensor_scalar(ADf[0:H, :], DTf[0:H, :], ACOLf[0:H, 0:1], None,
                                      ALU.mult).then_inc(sed.h, 1)
                    eng.wait_ge(sed.h, eb + 4)
                    eng.tensor_tensor_scan(REVIf[0:H, :][:, ::-1], ONESf[0:H, :],
                                           ADf[0:H, :][:, ::-1], 0.0, ALU.mult, ALU.add)
                    eng.tensor_tensor_scan(CUMf[0:H, :], ONESf[0:H, :], ADf[0:H, :],
                                           0.0, ALU.mult, ALU.add).then_inc(sed.h, 1)
                    eng.wait_ge(sed.h, eb + 5)
                    eng.tensor_tensor(DTEf[0:H, :], REVIf[0:H, :], ADf[0:H, :], ALU.subtract)
                    eng.tensor_scalar(RCCSf[0:H, 0:1], RCC[0:H, cp:cp + 1], LNSCALE, None, ALU.add)
                    eng.tensor_reduce(TOTf[0:H, 0:1], ADf[0:H, :], AX.X,
                                      ALU.add).then_inc(sed.h, 1)
                    eng.wait_ge(sed.h, eb + 6)
                    eng.tensor_tensor(RCC[0:H, (1 - cp):(2 - cp)], RCC[0:H, cp:cp + 1],
                                      TOTf[0:H, 0:1], ALU.add)
                    eng.drain().then_inc(svv.h, 1)
                on("vector", f_dtv2)
                sed.inc(6)
                t3 = svv.inc()
                dtv2B[c] = t3

                def f_dts2(eng, t3=t3, cp=cp):
                    eng.wait_ge(svv.h, t3)
                    eng.activation(DTEf[0:H, :], DTEf[0:H, :], AF.Exp)
                    eng.activation(LDTf[0:H, :], DTf[0:H, :], AF.Ln)
                    eng.activation(ECf[0:H, :], CUMf[0:H, :], AF.Exp, bias=SCLc[0:H, 0:1])
                    eng.activation(ECCf[0:H, :], CUMf[0:H, :], AF.Exp, bias=RCCSf[0:H, 0:1])
                    eng.drain().then_inc(sev.h, 1)
                on("scalar", f_dts2)
                t4 = sev.inc()

                def f_dtv3(eng, t4=t4, em=sed.n + 1):
                    eng.wait_ge(sev.h, t4)
                    eng.tensor_tensor(DDTf[0:H, :], DTf[0:H, :], DTEf[0:H, :], ALU.mult)
                    eng.tensor_tensor(NCf[0:H, :], LDTf[0:H, :], CUMf[0:H, :], ALU.subtract)
                    eng.drain().then_inc(svv.h, 1)
                on("vector", f_dtv3)
                t5 = svv.inc()

                def f_g1(eng, cp=cp, t5=t5, ldg=chunk_ldqB.get(c - 2, ())):
                    eng.wait_ge(svv.h, t5)
                    for s_, v_ in ldg:
                        eng.wait_ge(s_.h, v_)
                    eng.dma_start(out=scr_cum[2 * cp, 0:H * 128].rearrange("(h l) -> h l", h=H),
                                  in_=CUMf[0:H, :]).then_inc(stg.h, 16)
                    eng.dma_start(out=scr_cum[2 * cp + 1, 0:H * 128].rearrange("(h l) -> h l", h=H),
                                  in_=NCf[0:H, :]).then_inc(stg.h, 16)
                on("gpsimd", f_g1)
                st1 = stg.inc(32)

                g76 = gpop(7, 6)
                def f_tr(eng, t5=t5, g76=g76):
                    eng.wait_ge(svv.h, t5)
                    gwait(eng, g76)
                    eng.transpose(PS[7][:, 64:64 + H], DDTf[0:H, :], eye_s[0:H, 0:H])
                    eng.transpose(PS[7][:, 128:128 + H], ECf[0:H, :], eye_s[0:H, 0:H])
                    eng.transpose(PS[7][:, 192:192 + H], ECCf[0:H, :], eye_s[0:H, 0:H])
                    eng.transpose(PS[6][0:1, 0:H], TOTf[0:H, 0:1], eye_s[0:H, 0:H])
                    eng.drain().then_inc(smm.h, 1)
                on("tensor", f_tr)
                tr_t = smm.inc()

                def f_tre(eng, tr_t=tr_t, c=c, cp=cp):
                    eng.wait_ge(smm.h, tr_t)
                    eng.activation(DDTTMb[:, 0:H], PS[7][:, 64:64 + H], AF.Copy)
                    eng.activation(ECTMv[cp][:, 0:H], PS[7][:, 128:128 + H], AF.Copy)
                    eng.activation(ECCSb[:, c * 64:(c + 1) * 64], PS[7][:, 192:192 + H], AF.Copy)
                    eng.activation(TROWf[0:1, 0:H], PS[6][0:1, 0:H], AF.Copy)
                    eng.drain().then_inc(sev.h, 1)
                on("scalar", f_tre)
                tre_t = sev.inc()
                setguard(7, sev, tre_t)

                def f_dtm(eng, tre_t=tre_t):
                    eng.wait_ge(sev.h, tre_t)
                    eng.matmul(PS[6][:, 64:64 + H], ones_row[0:1, :], TROWf[0:1, 0:H],
                               start=True, stop=True)
                    eng.drain().then_inc(smm.h, 1)
                on("tensor", f_dtm)
                dtm_t = smm.inc()

                def f_dte(eng, dtm_t=dtm_t):
                    eng.wait_ge(smm.h, dtm_t)
                    eng.activation(DTOTb[:, 0:H], PS[6][:, 64:64 + H], AF.Exp)
                    eng.drain().then_inc(sev.h, 1)
                on("scalar", f_dte)
                dte_t = sev.inc()
                setguard(6, sev, dte_t)

                big_prev = smm.n if c < 2 else chunk_mm_doneB[c - 2]
                def f_bigB(eng, cp=cp, cin_t=cin_t, dte_t=dte_t, big_prev=big_prev, ms_=ms_):
                    eng.wait_ge(ms_.h, cin_t)
                    eng.wait_ge(sev.h, dte_t)
                    eng.wait_ge(smm.h, big_prev)
                    eng.tensor_tensor(BDvB[cp].rearrange("p (h d) -> p h d", h=H),
                                      BTvB[cp].rearrange("p (h d) -> p h d", h=H),
                                      DDTTMb[:, 0:H].unsqueeze(2).broadcast_to((128, H, SP)),
                                      ALU.mult)
                    eng.tensor_tensor(DIvB[cp][0:64, :].rearrange("p (h d) -> p h d", h=H),
                                      eye_b[0:64, 0:64].unsqueeze(1).broadcast_to((64, H, 64)),
                                      DTOTb[0:64, 0:H].unsqueeze(2).broadcast_to((64, H, 64)),
                                      ALU.mult)
                    eng.drain().then_inc(svv.h, 1)
                on("vector", f_bigB)
                bigB[c] = svv.inc()

                for q in range(nq):
                    qp = q % 2
                    o = q // 2
                    bL, bG = qp, 2 + qp
                    bY, bO = 4 + (o % 2), 6 + (o % 2)
                    yo = (q % 2) * 256

                    ld_guard = qmm_ldB.get((c, q - 2), qmm_ldB.get((c - 1, nq - 2 + q), 0))
                    qs_ = ldsp[qp]
                    def f_gq(eng, cp=cp, q=q, qp=qp, st1=st1, ld_guard=ld_guard, qs_=qs_):
                        eng.wait_ge(stg.h, st1)
                        eng.wait_ge(smm.h, ld_guard)
                        eng.dma_start(out=LdR[qp][0:1, 0:512],
                                      in_=scr_cum[2 * cp, q * 512:(q + 1) * 512][None, :]).then_inc(qs_.h, 16)
                        eng.dma_start(out=LdL[qp][1:2, 0:512],
                                      in_=scr_cum[2 * cp + 1, q * 512:(q + 1) * 512][None, :]).then_inc(qs_.h, 16)
                    on("gpsimd", f_gq)
                    ldq = qs_.inc(32)

                    bs_ = sj[qp]
                    def f_gbc(eng, c0=c0, q=q, qp=qp, ld_guard=ld_guard, bs_=bs_):
                        eng.wait_ge(smm.h, ld_guard)
                        eng.dma_start(
                            out=BFq[qp][0:64, :].rearrange("p (h j) -> p h j", h=4),
                            in_=scr_B_b.rearrange("(h n) t -> n h t", n=64)[:, 4 * q:4 * q + 4,
                                                                           c0:c0 + CH]).then_inc(bs_.h, 16)
                        eng.dma_start(
                            out=CFq[qp][0:64, :].rearrange("p (h j) -> p h j", h=4),
                            in_=scr_C_b.rearrange("(h n) t -> n h t", n=64)[:, 4 * q:4 * q + 4,
                                                                           c0:c0 + CH]).then_inc(bs_.h, 16)
                    on("sync", f_gbc)
                    bcq = bs_.inc(32)

                    gqL = gpop(bL)
                    def f_qL(eng, qp=qp, bL=bL, ldq=ldq, v0=v0, qs_=qs_, gqL=gqL):
                        eng.wait_ge(qs_.h, ldq)
                        eng.wait_ge(svv.h, v0)
                        gwait(eng, gqL)
                        for j in range(4):
                            eng.matmul(PS[bL][:, j * 128:(j + 1) * 128], su_b, negbig_b,
                                       start=True, stop=False)
                            eng.matmul(PS[bL][:, j * 128:(j + 1) * 128],
                                       LdL[qp][0:2, j * 128:(j + 1) * 128],
                                       LdR[qp][0:2, j * 128:(j + 1) * 128],
                                       start=False, stop=True)
                        eng.drain().then_inc(smm.h, 1)
                    on("tensor", f_qL)
                    qL = smm.inc()

                    gqG = gpop(bG)
                    def f_qG(eng, q=q, qp=qp, bG=bG, bcq=bcq, bs_=bs_, gqG=gqG):
                        eng.wait_ge(bs_.h, bcq)
                        gwait(eng, gqG)
                        for j in range(4):
                            eng.matmul(PS[bG][:, j * 128:(j + 1) * 128],
                                       BFq[qp][0:64, j * 128:(j + 1) * 128],
                                       CFq[qp][0:64, j * 128:(j + 1) * 128], start=True, stop=True)
                        eng.drain().then_inc(smm.h, 1)
                    on("tensor", f_qG)
                    qG = smm.inc()

                    def f_qa(eng, bL=bL, qL=qL, pmt=pqB[0]):
                        eng.wait_ge(smm.h, qL)
                        eng.wait_ge(svv.h, pmt)
                        eng.activation(LTb[:, 0:512], PS[bL][:, 0:512], AF.Exp, bias=SCLc[:, 0:1])
                        eng.drain().then_inc(sev.h, 1)
                    on("scalar", f_qa)
                    qa = sev.inc()
                    setguard(bL, sev, qa)

                    def f_qmt(eng, bG=bG, qa=qa, qG=qG, pq2=pqB[1]):
                        eng.wait_ge(sev.h, qa)
                        eng.wait_ge(smm.h, qG)
                        eng.wait_ge(smm.h, pq2)
                        eng.tensor_tensor(MTb[:, 0:512], PS[bG][:, 0:512], LTb[:, 0:512], ALU.mult)
                        eng.drain().then_inc(svv.h, 1)
                    on("vector", f_qmt)
                    qmt = svv.inc()
                    setguard(bG, svv, qmt)
                    pqB[0] = qmt

                    sev_prev = sevacB.get((c - 1, q), 0)
                    gq2 = gpop(bY, bO, bL)
                    def f_q2(eng, q=q, cp=cp, qp=qp, bL=bL, bY=bY, bO=bO, yo=yo, qmt=qmt,
                             big_t=bigB[c], qa=qa, sev_prev=sev_prev, gq=gq2):
                        eng.wait_ge(svv.h, qmt)
                        eng.wait_ge(svv.h, big_t)
                        eng.wait_ge(sev.h, qa)
                        eng.wait_ge(sev.h, sev_prev)
                        gwait(eng, gq)
                        for j in range(4):
                            h = 4 * q + j
                            eng.matmul(PS[bY][:, yo + j * 64:yo + (j + 1) * 64],
                                       MTb[:, j * 128:(j + 1) * 128],
                                       XSTvB[cp][:, h * 64:(h + 1) * 64], start=True, stop=True)
                            eng.matmul(PS[bO][:, yo + j * 64:yo + (j + 1) * 64],
                                       CFq[qp][0:64, j * 128:(j + 1) * 128],
                                       S_sb[0:64, h * 64:(h + 1) * 64], start=True, stop=True)
                            eng.matmul(PS[bL][0:64, j * 64:(j + 1) * 64],
                                       DIvB[cp][0:64, h * 64:(h + 1) * 64],
                                       S_sb[0:64, h * 64:(h + 1) * 64], start=True, stop=False)
                            eng.matmul(PS[bL][0:64, j * 64:(j + 1) * 64],
                                       BDvB[cp][:, h * 64:(h + 1) * 64],
                                       XSTvB[cp][:, h * 64:(h + 1) * 64], start=False, stop=True)
                        eng.drain().then_inc(smm.h, 1)
                    on("tensor", f_q2)
                    q2 = smm.inc()
                    pqB[1] = q2
                    qmm_ldB[(c, q)] = q2

                    if q % 2 == 1:
                        ysg_c = ytm_storeB.get(c - 1, 0)
                        def f_qy(eng, o=o, cp=cp, bY=bY, bO=bO, q2=q2, ysg=ysg_c,
                                 sy_=syp[(c - 1) % 2], em=sed.n + 1):
                            eng.wait_ge(smm.h, q2)
                            eng.wait_ge(sy_.h, ysg)
                            eng.tensor_tensor(TAP2.rearrange("p (h j) -> p h j", h=8),
                                              PS[bO].ap().rearrange("p (h j) -> p h j", h=8),
                                              ECTMv[cp][:, 8 * o:8 * o + 8].unsqueeze(2)
                                              .broadcast_to((128, 8, 64)),
                                              ALU.mult).then_inc(sed.h, 1)
                            eng.wait_ge(sed.h, em)
                            eng.tensor_tensor(YTMb[:, o * 512:(o + 1) * 512],
                                              PS[bY][:, 0:512], TAP2[:, 0:512], ALU.add)
                            eng.drain().then_inc(svv.h, 1)
                        on("vector", f_qy)
                        sed.inc()
                        qy = svv.inc()
                        setguard(bY, svv, qy)
                        setguard(bO, svv, qy)

                    def f_qs(eng, q=q, bL=bL, q2=q2):
                        eng.wait_ge(smm.h, q2)
                        eng.activation(S_sb[0:64, q * 256:(q + 1) * 256], PS[bL][0:64, 0:256],
                                       AF.Copy)
                        eng.drain().then_inc(sev.h, 1)
                    on("scalar", f_qs)
                    se = sev.inc()
                    sevacB[(c, q)] = se
                    setguard(bL, sev, se)
                chunk_mm_doneB[c] = smm.n
                chunk_ldqB[c] = ((ldsp[0], ldsp[0].n), (ldsp[1], ldsp[1].n))

                lastqy = svv.n
                lastqyB[c] = lastqy
                def f_yst(eng, c=c, lastqy=lastqy):
                    eng.wait_ge(svv.h, lastqy)
                    eng.dma_start(out=scr_y[c * 128:(c + 1) * 128, 0:B_INNER],
                                  in_=YTMb[:, :]).then_inc(syp[c % 2].h, 16)
                on("scalar", f_yst)
                ytm_storeB[c] = syp[c % 2].inc(16)

            fin_se = sev.n
            fin_vv = svv.n
            def f_dec(eng, fin_se=fin_se, fin_vv=fin_vv, cpf=NCH % 2):
                eng.wait_ge(sev.h, fin_se)
                eng.wait_ge(svv.h, fin_vv)
                eng.activation(DECb[0:B_HEADS, 0:1], RCC[0:B_HEADS, cpf:cpf + 1], AF.Exp)
                eng.drain().then_inc(sev.h, 1)
            on("scalar", f_dec)
            dec_t = sev.inc()

            def f_ssB(eng, dec_t=dec_t, w1=scs.n + 16 * 3):
                eng.wait_ge(sev.h, dec_t)
                eng.dma_start(out=cc_b_in[:, 0:4096], in_=S_sb[0:64, :]).then_inc(scs.h, 16)
                eng.dma_start(out=cc_b_in[:, 4096:4160], in_=S_sb[0:64, 0:64]).then_inc(scs.h, 16)
                eng.dma_start(out=scr_dec[0:1, 0:64].rearrange("a p -> p a"),
                              in_=DECb[0:64, 0:1]).then_inc(scs.h, 16)
                eng.wait_ge(scs.h, w1)
                eng.dma_start(out=cc_b_in[0:1, 4096:4160],
                              in_=scr_dec[0:1, 0:64]).then_inc(scs.h, 16)
            on("sync", f_ssB)
            scs.inc(64)

        def emit_collective_b():
            H = B_HEADS
            st_t = scs.n
            def f_cc(eng, st_t=st_t):
                eng.wait_ge(scs.h, st_t)
                eng.collective_compute(
                    "AllGather", ALU.bypass,
                    replica_groups=[list(range(8))],
                    ins=[cc_b_in[:, :].opt()],
                    outs=[cc_b_out[:, :].opt()],
                ).then_inc(scc.h, 1)
            on("gpsimd", f_cc)
            cc_t = scc.inc()

            def f_hz(eng, cc_t=cc_t):
                eng.wait_ge(scc.h, cc_t)
                eng.memset(S_sb[:, :], 0.0)
                eng.drain().then_inc(svv.h, 1)
            on("vector", f_hz)
            prev = svv.inc()

            drow_slots = [DROWb[0:1, 0:64], DECb[0:1, 0:64]]
            jld = {}
            for j in range(8):
                jp = j % 2
                ld_pv = jld.get(j - 2, 0)
                js_ = sj[jp]
                def f_jl(eng, j=j, jp=jp, cc_t=cc_t, ld_pv=ld_pv, js_=js_):
                    eng.wait_ge(scc.h, cc_t)
                    eng.wait_ge(svv.h, ld_pv)
                    eng.dma_start(out=SJv[jp][0:64, :],
                                  in_=cc_b_out[j * 64:(j + 1) * 64, 0:4096]).then_inc(js_.h, 16)
                    eng.dma_start(out=drow_slots[jp],
                                  in_=cc_b_out[j * 64:j * 64 + 1, 4096:4096 + H]).then_inc(js_.h, 16)
                on("sync", f_jl)
                jl = js_.inc(32)

                def f_jr(eng, j=j, jp=jp, jl=jl, prev=prev, js_=js_, em=sed.n + 1):
                    eng.wait_ge(js_.h, jl)
                    eng.wait_ge(svv.h, prev)
                    eng.tensor_scalar(DTILf[0:1, 0:H], drow_slots[jp],
                                      cmask_s[0:1, 8 + j:9 + j], None, ALU.mult).then_inc(sed.h, 1)
                    eng.wait_ge(sed.h, em)
                    eng.tensor_scalar(DTILf[0:1, 0:H], DTILf[0:1, 0:H],
                                      cmask_s[0:1, 16 + j:17 + j], None, ALU.add)
                    eng.tensor_scalar(TROWf[0:1, 0:H], ones_row[0:1, 0:H],
                                      cmask_s[0:1, j:j + 1], None, ALU.mult)
                    eng.drain().then_inc(svv.h, 1)
                on("vector", f_jr)
                sed.inc()
                jr = svv.inc()

                gj6 = gpop(6)
                def f_jm(eng, jr=jr, gj6=gj6):
                    eng.wait_ge(svv.h, jr)
                    gwait(eng, gj6)
                    eng.matmul(PS[6][:, 0:H], ones_row[0:1, :], DTILf[0:1, 0:H], start=True, stop=True)
                    eng.matmul(PS[6][:, 64:64 + H], ones_row[0:1, :], TROWf[0:1, 0:H],
                               start=True, stop=True)
                    eng.drain().then_inc(smm.h, 1)
                on("tensor", f_jm)
                jm = smm.inc()

                def f_je(eng, jm=jm):
                    eng.wait_ge(smm.h, jm)
                    eng.activation(DBCb[:, 0:H], PS[6][:, 0:H], AF.Copy)
                    eng.activation(MBCb[:, 0:H], PS[6][:, 64:64 + H], AF.Copy)
                    eng.drain().then_inc(sev.h, 1)
                on("scalar", f_je)
                je = sev.inc()
                setguard(6, sev, je)

                def f_jv(eng, jp=jp, je=je, em=sed.n + 1):
                    eng.wait_ge(sev.h, je)
                    eng.tensor_tensor(S_sb[0:64, :].rearrange("p (h d) -> p h d", h=H),
                                      S_sb[0:64, :].rearrange("p (h d) -> p h d", h=H),
                                      DBCb[0:64, 0:H].unsqueeze(2).broadcast_to((64, H, 64)),
                                      ALU.mult)
                    eng.tensor_tensor(T2c[0:64, :].rearrange("p (h d) -> p h d", h=H),
                                      SJv[jp][0:64, :].rearrange("p (h d) -> p h d", h=H),
                                      MBCb[0:64, 0:H].unsqueeze(2).broadcast_to((64, H, 64)),
                                      ALU.mult).then_inc(sed.h, 1)
                    eng.wait_ge(sed.h, em)
                    eng.tensor_tensor(S_sb[0:64, :], S_sb[0:64, :], T2c[0:64, :], ALU.add)
                    eng.drain().then_inc(svv.h, 1)
                on("vector", f_jv)
                sed.inc()
                prev = svv.inc()
                jld[j] = prev

        def emit_gate_outproj_b(xprev, xprev_off, xdst):
            H = B_HEADS
            gb = emit_gate_outproj_b
            for h2 in range(2):
                for ci in range(8):
                    c = 8 * h2 + ci
                    c0 = c * CH
                    ysg = ytm_storeB[c]
                    def f_gl(eng, c=c, c0=c0, ysg=ysg, pe2=gb.pe2[0],
                             psg=gb.psg[0], sy_=syp[c % 2], ptr=gb.trg.get(c - 1, 0)):
                        eng.wait_ge(sy_.h, ysg)
                        eng.wait_ge(svv.h, pe2)
                        eng.wait_ge(sev.h, psg)
                        eng.wait_ge(strs.h, ptr)
                        eng.dma_start(out=YTg[:, :],
                                      in_=scr_y[c * 128:(c + 1) * 128, 0:B_INNER]).then_inc(glx.h, 16)
                        eng.dma_start_transpose(out=GZTb[:, :],
                                                in_=scr_z_b[0:B_INNER, c0:c0 + CH]).then_inc(glx.h, 16)
                    on("sync", f_gl)
                    gl = glx.inc(16 * 2)

                    corr_v = []
                    for q in range(16):
                        bq = q % 2
                        qp = q % 2
                        qs_ = sj[qp]
                        cf_guard = gb.cmld.get((c, q - 2), gb.cmld.get((c - 1, 14 + q), 0))
                        def f_gcf(eng, c0=c0, q=q, qp=qp, cf_guard=cf_guard, qs_=qs_):
                            eng.wait_ge(smm.h, cf_guard)
                            eng.dma_start(
                                out=CFgq[qp][0:64, :].rearrange("p (h j) -> p h j", h=4),
                                in_=scr_C_b.rearrange("(h n) t -> n h t", n=64)[:, 4 * q:4 * q + 4,
                                                                               c0:c0 + CH]).then_inc(qs_.h, 16)
                        on("sync", f_gcf)
                        cfl = qs_.inc(16)

                        gcm = gpop(bq)
                        def f_cm(eng, q=q, qp=qp, bq=bq, gl=gl, cfl=cfl, qs_=qs_, gcm=gcm):
                            eng.wait_ge(glx.h, gl)
                            eng.wait_ge(qs_.h, cfl)
                            gwait(eng, gcm)
                            for j in range(4):
                                h = 4 * q + j
                                eng.matmul(PS[bq][:, j * 64:(j + 1) * 64],
                                           CFgq[qp][0:64, j * 128:(j + 1) * 128],
                                           S_sb[0:64, h * 64:(h + 1) * 64], start=True, stop=True)
                            eng.drain().then_inc(smm.h, 1)
                        on("tensor", f_cm)
                        cm = smm.inc()
                        gb.cmld[(c, q)] = cm

                        def f_cv(eng, q=q, bq=bq, cm=cm, c=c, em=sed.n + 1):
                            eng.wait_ge(smm.h, cm)
                            eng.tensor_tensor(TAP2[:, 0:256].rearrange("p (j i) -> p j i", j=4),
                                              PS[bq].ap()[:, 0:256].rearrange("p (j i) -> p j i", j=4),
                                              ECCSb[:, c * 64 + 4 * q:c * 64 + 4 * q + 4]
                                              .unsqueeze(2).broadcast_to((128, 4, 64)),
                                              ALU.mult).then_inc(sed.h, 1)
                            eng.wait_ge(sed.h, em)
                            eng.tensor_tensor(YTg[:, q * 256:(q + 1) * 256],
                                              YTg[:, q * 256:(q + 1) * 256], TAP2[:, 0:256], ALU.add)
                            eng.drain().then_inc(svv.h, 1)
                        on("vector", f_cv)
                        sed.inc()
                        cvt = svv.inc()
                        setguard(bq, svv, cvt)
                        corr_v.append(cvt)

                    cvl = corr_v[-1]
                    def f_sq(eng, cvl=cvl, pe2=gb.pe2[0]):
                        eng.wait_ge(svv.h, cvl)
                        eng.wait_ge(svv.h, pe2)
                        eng.activation(SQDb[:, :], YTg[:, :], AF.Square, accum_out=SSQb[:, 0:1])
                        eng.drain().then_inc(sev.h, 1)
                    on("scalar", f_sq)
                    sq_t = sev.inc()

                    def f_sg(eng, gl=gl, sq_t=sq_t, pe2=gb.pe2[0]):
                        eng.wait_ge(glx.h, gl)
                        eng.wait_ge(sev.h, sq_t)
                        eng.wait_ge(svv.h, pe2)
                        eng.activation(SILZb[:, :], GZTb[:, :], AF.Silu)
                        eng.drain().then_inc(sev.h, 1)
                    on("scalar", f_sg)
                    sg = sev.inc()
                    gb.psg[0] = sg

                    def f_rs1(eng, sq_t=sq_t):
                        eng.wait_ge(sev.h, sq_t)
                        eng.tensor_scalar(MSb[:, 0:1], SSQb[:, 0:1], 1.0 / B_INNER, None, ALU.mult)
                        eng.drain().then_inc(svv.h, 1)
                    on("vector", f_rs1)
                    rs1 = svv.inc()

                    def f_rs2(eng, rs1=rs1, ea=sea.n + 1):
                        eng.wait_ge(svv.h, rs1)
                        eng.activation(RSTDb[:, 0:1], MSb[:, 0:1], AF.Ln,
                                       bias=EPSBc[:, 0:1]).then_inc(sea.h, 1)
                        eng.wait_ge(sea.h, ea)
                        eng.activation(RSTDb[:, 0:1], RSTDb[:, 0:1], AF.Exp, scale=-0.5)
                        eng.drain().then_inc(sev.h, 1)
                    on("scalar", f_rs2)
                    sea.inc()
                    rs2 = sev.inc()

                    trg = gb.trg.get(c - 1, 0)
                    def f_e2(eng, rs2=rs2, sg=sg, trg=trg, em=sed.n + 1):
                        eng.wait_ge(sev.h, rs2)
                        eng.wait_ge(sev.h, sg)
                        eng.wait_ge(strs.h, trg)
                        eng.tensor_scalar(SQDb[:, :], YTg[:, :], RSTDb[:, 0:1], None,
                                          ALU.mult).then_inc(sed.h, 1)
                        eng.wait_ge(sed.h, em)
                        eng.tensor_tensor(YTg[:, :], SQDb[:, :], SILZb[:, :], ALU.mult)
                        eng.drain().then_inc(svv.h, 1)
                    on("vector", f_e2)
                    sed.inc()
                    e2 = svv.inc()
                    gb.pe2[0] = e2

                    def f_trs(eng, e2=e2, ci=ci, opg=gb.opmm[0]):
                        eng.wait_ge(svv.h, e2)
                        eng.wait_ge(smm.h, opg)
                        for f in range(32):
                            eng.dma_start_transpose(
                                out=YFMh[:, f * 1024 + ci * 128:f * 1024 + ci * 128 + 128],
                                in_=YTg[:, f * 128:(f + 1) * 128]).then_inc(strs.h, 16)
                    on("sync", f_trs)
                    gb.trg[c] = strs.inc(16 * 32)

                half_tr = gb.trg[8 * h2 + 7]
                mmd = {}
                for m in range(16):
                    p = m % 2
                    wfree = smm.n if m < 2 else mmd[m - 2]
                    ws2 = ldwp[p]
                    def f_wl(eng, p=p, m=m, wfree=wfree, htr=half_tr, ws2=ws2):
                        eng.wait_ge(smm.h, wfree)
                        eng.wait_ge(strs.h, htr)
                        src = wb_outT.rearrange("(k p) m -> p k m", p=128)[:, :, m * 128:(m + 1) * 128]
                        eng.dma_start(out=WTOv[p].rearrange("p (k j) -> p k j", k=32),
                                      in_=src).then_inc(ws2.h, 16)
                    on("gpsimd", f_wl)
                    wt = ws2.inc(16)

                    for tt in range(2):
                        bk = 2 + (m * 2 + tt) % 2
                        tok0 = h2 * 1024 + tt * 512
                        gom = gpop(bk)
                        def f_om(eng, p=p, bk=bk, tt=tt, wt=wt, half_tr=half_tr, ws2=ws2, gom=gom):
                            eng.wait_ge(ws2.h, wt)
                            eng.wait_ge(strs.h, half_tr)
                            gwait(eng, gom)
                            for k in range(32):
                                eng.matmul(PS[bk][:, 0:512],
                                           WTOv[p][:, k * 128:(k + 1) * 128],
                                           YFMh[:, k * 1024 + tt * 512:k * 1024 + tt * 512 + 512],
                                           start=(k == 0), stop=(k == 31))
                            eng.drain().then_inc(smm.h, 1)
                        on("tensor", f_om)
                        om = smm.inc()
                        gb.opmm[0] = om
                        if tt == 1:
                            mmd[m] = om

                        xp = (m * 2 + tt) % 2
                        xs_ = xlp[xp]
                        def f_xl(eng, m=m, tok0=tok0, xp=xp, xprev=xprev, xprev_off=xprev_off,
                                 xg=gb.xld.get(xp, 0), xs_=xs_):
                            eng.wait_ge(svv.h, xg)
                            eng.dma_start(out=XOLDb[xp][:, :],
                                          in_=xprev[m * 128:(m + 1) * 128,
                                                    xprev_off + tok0:xprev_off + tok0 + 512]).then_inc(xs_.h, 16)
                        on("sync", f_xl)
                        xl = xs_.inc(16)

                        def f_xv(eng, bk=bk, xp=xp, om=om, xl=xl, xg2=gb.xst.get(xp, 0),
                                 xs_=xs_, sy_=syp[xp]):
                            eng.wait_ge(smm.h, om)
                            eng.wait_ge(xs_.h, xl)
                            eng.wait_ge(sy_.h, xg2)
                            eng.tensor_tensor(XNEWb[xp][:, :], PS[bk][:, 0:512], XOLDb[xp][:, :], ALU.add)
                            eng.drain().then_inc(svv.h, 1)
                        on("vector", f_xv)
                        xv = svv.inc()
                        setguard(bk, svv, xv)
                        gb.xld[xp] = xv

                        def f_xs(eng, m=m, tok0=tok0, xp=xp, xv=xv, xdst=xdst):
                            eng.wait_ge(svv.h, xv)
                            eng.dma_start(out=xdst[m * 128:(m + 1) * 128, tok0:tok0 + 512],
                                          in_=XNEWb[xp][:, :]).then_inc(syp[xp].h, 16)
                        on("scalar", f_xs)
                        gb.xst[xp] = syp[xp].inc(16)
        emit_gate_outproj_b.trg = {}
        emit_gate_outproj_b.xld = {}
        emit_gate_outproj_b.xst = {}
        emit_gate_outproj_b.psg = [0]
        emit_gate_outproj_b.pe2 = [0]
        emit_gate_outproj_b.opmm = [0]
        emit_gate_outproj_b.cmld = {}

        emit_init()

        cur, cur_off = x_in, HALO
        for si, sl in enumerate(sublayers):
            last = si == len(sublayers) - 1
            dst = out_t if last else x_state
            if sl == "attn_a":
                ln_tts = [(0, 0, 128, x_in)] + [(HALO + j * 256, HALO + j * 256, 256, x_in)
                                                for j in range(8)]
                applies = emit_ln(TC, ln_tts)
                if CUT != "ln":
                    NM_CONV = int(os.environ.get("P_NMC", "34"))
                    NM_GATE = int(os.environ.get("P_NMG", "32"))
                    Mlist = []
                    for m in range(NM_CONV):
                        Mlist.append(dict(kind="conv", mcol0=A_INTER + m * 128, mw=128, conv=True,
                                          dst=scr_conv_a, drow0=m * 128, crow0=m * 128))
                    for m in range(NM_GATE):
                        Mlist.append(dict(kind="gate", mcol0=m * 128, mw=128, conv=False,
                                          dst=scr_gate_a, drow0=m * 128))
                    Mlist.append(dict(kind="dt", mcol0=A_INTER + A_CONVD, mw=32, conv=False,
                                      dst=scr_dt_a, drow0=0))
                    emit_inproj2(wa_inT, Mlist, TC, ln_tts, applies, conv_a_w, conv_a_b)
                    Bar()
                if CUT not in ("ln", "inproj"):
                    emit_mixer_a()
                    Bar()
                if CUT not in ("ln", "inproj", "mixer"):
                    emit_collective_a()
                    Bar()
                if CUT not in ("ln", "inproj", "mixer", "collective"):
                    emit_gate_outproj_a(cur, cur_off, dst)
            elif sl in ("ffn1", "ffn2"):
                if CUT != "full":
                    continue
                emit_ffn.xld = {}
                emit_ffn.xst = {}
                emit_ffn(cur, cur_off, dst)
            elif sl == "attn_b":
                if CUT != "full" and "attn_a" in sublayers:
                    continue
                emit_halo_b(cur, cur_off)
                Bar()
                ln_tts_b = [(0, 0, 128, x_halo_b)] + [(HALO + j * 256, cur_off + j * 256, 256, cur)
                                                      for j in range(8)]
                applies_b = emit_ln(TC, ln_tts_b)
                if CUT != "ln":
                    MlistB = []
                    for m in range(32):
                        MlistB.append(dict(kind="conv", mcol0=B_INNER + m * 128, mw=128, conv=True,
                                           dst=scr_conv_b, drow0=m * 128, crow0=m * 128))
                    for m in range(32):
                        MlistB.append(dict(kind="z", mcol0=m * 128, mw=128, conv=False,
                                           dst=scr_z_b, drow0=m * 128))
                    for m in range(32):
                        MlistB.append(dict(kind="B", mcol0=2 * B_INNER + m * 128, mw=128, conv=False,
                                           dst=scr_B_b, drow0=m * 128))
                    for m in range(32):
                        MlistB.append(dict(kind="C", mcol0=2 * B_INNER + 4096 + m * 128, mw=128,
                                           conv=False, dst=scr_C_b, drow0=m * 128))
                    MlistB.append(dict(kind="dt", mcol0=B_PROJ - B_HEADS, mw=64, conv=False,
                                       dst=scr_dt_b, drow0=0))
                    emit_inproj2(wb_inT, MlistB, TC, ln_tts_b, applies_b, conv_b_w, None)
                    Bar()
                if CUT not in ("ln", "inproj"):
                    emit_mixer_b()
                    Bar()
                if CUT not in ("ln", "inproj", "mixer"):
                    emit_collective_b()
                    Bar()
                if CUT not in ("ln", "inproj", "mixer", "collective"):
                    emit_gate_outproj_b(cur, cur_off, dst)
            cur, cur_off = dst, 0
            Bar()

        if os.environ.get("P_DBGB", "0") == "1":
            d_halo = nc.dram_tensor("d_halo", [D, HALO], F32, kind="ExternalOutput").ap()
            d_spre = nc.dram_tensor("d_spre", [64, 4096], BF16, kind="ExternalOutput").ap()
            d_ccin = nc.dram_tensor("d_ccin", [64, CCW_B], BF16, kind="ExternalOutput").ap()
            def f_dbgb(eng, b0=bar.n, tgt=scs.n + 16 * (NK + 2)):
                eng.wait_ge(bar.h, b0)
                for k in range(NK):
                    eng.dma_start(out=d_halo[k * 128:(k + 1) * 128, :],
                                  in_=x_halo_b[k * 128:(k + 1) * 128, :]).then_inc(scs.h, 16)
                eng.dma_start(out=d_spre[:, :], in_=S_sb[0:64, :]).then_inc(scs.h, 16)
                eng.dma_start(out=d_ccin[:, :], in_=cc_b_in[:, :]).then_inc(scs.h, 16)
                eng.wait_ge(scs.h, tgt)
            on("sync", f_dbgb)
            scs.inc(16 * (NK + 2))

        if os.environ.get("P_DBG", "0") == "1":
            dbg_conv = nc.dram_tensor("dbg_conv", [A_CONVD, T], BF16, kind="ExternalOutput").ap()
            dbg_gate = nc.dram_tensor("dbg_gate", [A_INTER, T], BF16, kind="ExternalOutput").ap()
            dbg_dt = nc.dram_tensor("dbg_dt", [A_HEADS, T], F32, kind="ExternalOutput").ap()
            dbg_y = nc.dram_tensor("dbg_y", [NCH * 128, A_INTER], BF16, kind="ExternalOutput").ap()
            def f_dbg(eng, b0=bar.n, tgt=scs.n + 16 * 4):
                eng.wait_ge(bar.h, b0)
                eng.dma_start(out=dbg_conv[:, :], in_=scr_conv_a[:, :]).then_inc(scs.h, 16)
                eng.dma_start(out=dbg_gate[:, :], in_=scr_gate_a[:, :]).then_inc(scs.h, 16)
                eng.dma_start(out=dbg_dt[:, :], in_=scr_dt_a[:, :]).then_inc(scs.h, 16)
                eng.dma_start(out=dbg_y[:, :], in_=scr_y[:, :]).then_inc(scs.h, 16)
                eng.wait_ge(scs.h, tgt)
            on("sync", f_dbg)
            scs.inc(16 * 4)

        if CUT != "full":
            def f_cp(eng, b0=bar.n, tgt=scs.n + 16 * NK):
                eng.wait_ge(bar.h, b0)
                for k in range(NK):
                    eng.dma_start(out=out_t[k * 128:(k + 1) * 128, :],
                                  in_=x_in[k * 128:(k + 1) * 128, HALO:HALO + T]).then_inc(scs.h, 16)
                eng.wait_ge(scs.h, tgt)
            on("sync", f_cp)
            scs.inc(16 * NK)

        fins = [(s_, s_.n) for s_ in (stw, syp[0], syp[1], stg, scs, strs)]
        def f_fin(eng, fins=fins):
            for s_, v in fins:
                eng.wait_ge(s_.h, v)
        on("sync", f_fin)

        with nc.Block() as block:
            block.sync(lambda eng: [f(eng) for f in Q["sync"]])
            block.tensor(lambda eng: [f(eng) for f in Q["tensor"]])
            block.vector(lambda eng: [f(eng) for f in Q["vector"]])
            block.scalar(lambda eng: [f(eng) for f in Q["scalar"]])
            block.gpsimd(lambda eng: [f(eng) for f in Q["gpsimd"]])
    return nc



def _bf(a):
    return np.ascontiguousarray(a).astype(ml_dtypes.bfloat16)


def make_host_inputs(inputs, sublayers=SUBLAYERS):
    x = np.asarray(inputs["L0_attn_a_hidden_states"], dtype=np.float32)
    shared = {
        "wa_inT": np.ascontiguousarray(np.asarray(inputs["L0_attn_a_in_proj_w"], np.float32).T),
        "wa_outT": np.ascontiguousarray(np.asarray(inputs["L0_attn_a_out_proj_w"], np.float32).T),
        "conv_a_w": np.ascontiguousarray(np.asarray(inputs["L0_attn_a_conv_w"], np.float32).reshape(A_CONVD, 4)),
        "conv_a_b": np.ascontiguousarray(np.asarray(inputs["L0_attn_a_conv_b"], np.float32).reshape(A_CONVD, 1)),
        "a_Alog": np.ascontiguousarray(np.asarray(inputs["L0_attn_a_A_log"], np.float32).reshape(A_HEADS, 1)),
        "a_dtb": np.ascontiguousarray(np.asarray(inputs["L0_attn_a_dt_bias"], np.float32).reshape(A_HEADS, 1)),
        "a_Drep": np.ascontiguousarray(np.repeat(np.asarray(inputs["L0_attn_a_D"], np.float32), A_HD).reshape(1, A_INTER)),
        "a_nw": np.ascontiguousarray(np.asarray(inputs["L0_attn_a_norm_w"], np.float32).reshape(1, A_INTER)),
        "eye_f": np.eye(128, dtype=np.float32),
        "ones_f_row": np.ones((1, 128), dtype=np.float32),
        "eye_bf": _bf(np.eye(128, dtype=np.float32)),
        "su_bf": _bf(np.triu(np.ones((128, 128), np.float32), 1)),
        "negbig_bf": _bf(np.eye(128, dtype=np.float32) * -1.0e30),
        "ones_bf_col": _bf(np.ones((128, 1), np.float32)),
        "wb_inT": np.ascontiguousarray(np.asarray(inputs["L0_attn_b_in_proj_w"], np.float32).T),
        "wb_outT": np.ascontiguousarray(np.asarray(inputs["L0_attn_b_out_proj_w"], np.float32).T),
        "conv_b_w": np.ascontiguousarray(np.asarray(inputs["L0_attn_b_conv_w"], np.float32).reshape(B_INNER, 4)),
        "b_Alog": np.ascontiguousarray(np.asarray(inputs["L0_attn_b_A_log"], np.float32).reshape(B_HEADS, 1)),
        "b_dtb": np.ascontiguousarray(np.asarray(inputs["L0_attn_b_dt_bias"], np.float32).reshape(B_HEADS, 1)),
        "wf_T": np.ascontiguousarray(np.asarray(inputs["L0_ffn_W"], np.float32).T),
        "f_biasT": np.ascontiguousarray(np.asarray(inputs["L0_ffn_bias"], np.float32).reshape(16, 128)),
        "f_rw1T": np.ascontiguousarray(np.asarray(inputs["L0_ffn_rw1"], np.float32).T),
        "f_rw2T": np.ascontiguousarray(np.asarray(inputs["L0_ffn_rw2"], np.float32).T),
        "f_rb1": np.ascontiguousarray(np.asarray(inputs["L0_ffn_rb1"], np.float32).reshape(64, 1)),
        "f_rb2": np.ascontiguousarray(np.asarray(inputs["L0_ffn_rb2"], np.float32).reshape(32, 1)),
        "f_gates": np.ascontiguousarray(np.asarray(inputs["L0_ffn_gates"], np.float32).reshape(16, 1)),
        "f_AT": np.ascontiguousarray(np.asarray(inputs["L0_ffn_A"], np.float32).reshape(256, D).T),
        "f_BT2": np.ascontiguousarray(np.asarray(inputs["L0_ffn_B"], np.float32).transpose(0, 2, 1).reshape(256, D)),
        "f_sel": _bf(np.repeat(np.eye(16, dtype=np.float32), 16, axis=1) * 0 +
                     np.kron(np.eye(16, dtype=np.float32), np.ones((1, 16), np.float32))),
    }
    in_maps = []
    for ci in range(8):
        b, seg = divmod(ci, 4)
        s0 = seg * T
        arr = np.zeros((TC, D), dtype=np.float32)
        arr[HALO:] = x[b, s0:s0 + T]
        if seg > 0:
            arr[:HALO] = x[b, s0 - HALO:s0]
        m = np.zeros(8, np.float32)
        m2 = np.zeros(8, np.float32)
        lo = 4 * (ci // 4)
        for j in range(8):
            if lo <= j < ci:
                m[j] = 1.0
                m2[j] = 1.0
        cm = np.concatenate([m, m2, 1.0 - m2]).reshape(1, 24).astype(np.float32)
        hsel = np.zeros((8, 1), np.float32)
        if ci % 4 != 0:
            hsel[ci - 1, 0] = 1.0
        d = dict(shared)
        d["x_in"] = np.ascontiguousarray(arr.T)
        d["cmask"] = cm
        d["hsel_in"] = hsel
        in_maps.append(d)
    return in_maps


_NC_CACHE = {}


def kernel(**inputs):
    sub = SUBLAYERS
    key = (sub, CUT)
    if key not in _NC_CACHE:
        _NC_CACHE[key] = build_nc(sub)
    nc_ = _NC_CACHE[key]
    in_maps = make_host_inputs(inputs, sub)
    res = run_bass_kernel_spmd(nc_, in_maps, core_ids=list(range(8)))
    out = np.zeros((2, 8192, D), dtype=np.float32)
    for ci in range(8):
        b, seg = divmod(ci, 4)
        out[b, seg * T:(seg + 1) * T] = np.asarray(res.results[ci]["out"]).T
    return out
